# version 02
# baseline (speedup 1.0000x reference)
import numpy as np
import ml_dtypes
from contextlib import ExitStack

import concourse.bass as bass
import concourse.mybir as mybir
from concourse.bass_utils import run_bass_kernel_spmd

F32 = mybir.dt.float32
BF16 = mybir.dt.bfloat16
AF = mybir.ActivationFunctionType
ALU = mybir.AluOpType
AXX = mybir.AxisListType.X
f32 = np.float32

T = 2048
D = 1024
C = 8
NW = 4
W512 = 512
K9 = 9
NH = 8
QKN = 12
HID = 2730
HIDP = 2816
K2N = 22
W1N = 44
NK2A = 4
NK2B = K2N - NK2A
NK2H = NK2B // 2
LNEPS = 1e-5

_CONST_CACHE = {}


def _constants():
    if _CONST_CACHE:
        return _CONST_CACHE
    c = {}
    c["ident_bf"] = np.eye(128, dtype=ml_dtypes.bfloat16)
    c["ident_f32"] = np.eye(128, dtype=np.float32)
    R = np.zeros((64, 64), np.float32)
    for d in range(32):
        R[d + 32, d] = -1.0
        R[d, d + 32] = 1.0
    Rb = np.zeros((128, 128), np.float32)
    Rb[0:64, 0:64] = R
    Rb[64:128, 64:128] = R
    c["rotmat"] = Rb.astype(ml_dtypes.bfloat16)
    d = np.arange(64, dtype=np.float64)
    half = (d % 32) * 2.0
    inv_freq = 1.0 / (10000.0 ** (half / 64.0))
    t = np.arange(T, dtype=np.float64)
    freqs = t[None, :] * inv_freq[:, None]
    base = (half + 0.4 * 64.0) / (1.4 * 64.0)
    power = (t - T // 2) / 256.0
    scale = base[:, None] ** power[None, :]
    cs, sn = np.cos(freqs), np.sin(freqs)

    def dup(a):
        return np.vstack([a, a]).astype(np.float32).astype(ml_dtypes.bfloat16)
    c["cosq"] = dup(8.0 * cs * scale)
    c["sinq"] = dup(8.0 * sn * scale)
    c["cosk"] = dup(cs / scale)
    c["sink"] = dup(sn / scale)
    r = np.arange(128)
    c["tri_diag"] = (r[:, None] <= r[None, :]).astype(ml_dtypes.bfloat16)
    c["tri_edge"] = (r[:, None] >= r[None, :]).astype(ml_dtypes.bfloat16)
    hb = np.zeros((128, 128), np.float32)
    hb[0:64, 0:64] = 1.0
    hb[64:128, 64:128] = 1.0
    c["hones"] = hb.astype(ml_dtypes.bfloat16)
    c["lncol_f32"] = np.full((128, 128), 1.0 / D, np.float32)
    c["lncol_bf"] = np.full((128, 128), 1.0 / D, np.float32).astype(ml_dtypes.bfloat16)
    c["ones1_bf"] = np.ones((1, 128), ml_dtypes.bfloat16)
    c["ones1_f32"] = np.ones((128, 128), np.float32)
    c["zeros_bf"] = np.zeros((128, 128), ml_dtypes.bfloat16)
    c["onescol_bf"] = np.ones((128, 1), ml_dtypes.bfloat16)
    s4 = np.zeros((128, 8), np.float32)
    for ct in range(4):
        for hp in range(2):
            for kk in range(K9):
                s4[32 * ct + 9 * hp + kk, 2 * ct + hp] = 1.0
    c["sel4"] = s4.astype(ml_dtypes.bfloat16)
    sz = np.zeros((8, 128), np.float32)
    for ct in range(4):
        for hp in range(2):
            for kk in range(K9):
                sz[2 * ct + hp, 32 * ct + 9 * hp + kk] = 1.0
    c["selzrow"] = sz
    B = np.zeros((K9, 4, 128, 128), np.float32)
    for kk in range(K9):
        for ct in range(4):
            for hp in range(2):
                B[kk, ct, 32 * ct + 9 * hp + kk, 64 * hp:64 * (hp + 1)] = 1.0
    c["bsel"] = np.ascontiguousarray(B.transpose(2, 0, 1, 3)).astype(ml_dtypes.bfloat16)
    _CONST_CACHE.update(c)
    return c


CONST_NAMES = ("ident_bf", "ident_f32", "rotmat", "cosq", "sinq", "cosk", "sink",
               "tri_diag", "tri_edge", "hones", "lncol_f32", "lncol_bf",
               "ones1_bf", "ones1_f32", "onescol_bf", "zeros_bf", "sel4",
               "selzrow", "bsel")
WEIGHT_NAMES = ("cw_stat", "wqkv", "wo", "w1", "w2", "b1a", "b1g", "qscol", "kscol",
                "lng", "lnb", "b2")


def _prep_weights(inp):
    w = {}
    cw = np.asarray(inp["L0_attn_conv_weight"], np.float32)
    Wk = np.zeros((128, C, K9, 18), np.float32)
    for h in range(16):
        cc, hp = h // 2, h % 2
        for kk in range(K9):
            Wk[64 * hp:64 * (hp + 1), cc, kk, 9 * hp:9 * (hp + 1)] = \
                cw[h * 9:(h + 1) * 9, :, kk].T
    w["cw_stat"] = np.ascontiguousarray(Wk.reshape(128, C * K9 * 18))
    wqkv = np.asarray(inp["L0_ffn_wqkv"], np.float32)
    w["wqkv"] = np.ascontiguousarray(
        wqkv.reshape(2, C, 128, QKN, 128).transpose(0, 2, 3, 1, 4))
    wo = np.asarray(inp["L0_ffn_wo"], np.float32)
    w["wo"] = np.ascontiguousarray(
        wo.reshape(2, 4, 128, 8, 128).transpose(0, 2, 3, 1, 4))
    w1 = np.asarray(inp["L0_ffn_w1"], np.float32)
    a, g = w1[:, :, :HID], w1[:, :, HID:]
    ap = np.zeros((2, D, HIDP), np.float32)
    gp = np.zeros((2, D, HIDP), np.float32)
    ap[:, :, :HID] = a
    gp[:, :, :HID] = g
    w1p = np.zeros((2, D, W1N, 128), np.float32)
    w1p[:, :, 0::2, :] = ap.reshape(2, D, K2N, 128)
    w1p[:, :, 1::2, :] = gp.reshape(2, D, K2N, 128)
    w["w1"] = np.ascontiguousarray(
        w1p.reshape(2, C, 128, W1N, 128).transpose(0, 2, 3, 1, 4))
    b1 = np.asarray(inp["L0_ffn_b1"], np.float32)
    b1a = np.zeros((2, HIDP), np.float32)
    b1g = np.zeros((2, HIDP), np.float32)
    b1a[:, :HID] = b1[:, :HID]
    b1g[:, :HID] = b1[:, HID:]
    w["b1a"] = np.ascontiguousarray(b1a.reshape(2, K2N, 128).transpose(0, 2, 1))
    w["b1g"] = np.ascontiguousarray(b1g.reshape(2, K2N, 128).transpose(0, 2, 1))
    w2 = np.asarray(inp["L0_ffn_w2"], np.float32)
    w2p = np.zeros((2, HIDP, D), np.float32)
    w2p[:, :HID, :] = w2
    w["w2"] = np.ascontiguousarray(
        w2p.reshape(2, K2N, 128, 8, 128).transpose(0, 2, 3, 1, 4))
    qs = np.asarray(inp["L0_ffn_q_scale"], np.float32)
    ks = np.asarray(inp["L0_ffn_k_scale"], np.float32)
    w["qscol"] = np.ascontiguousarray(np.concatenate([qs, qs], 1).reshape(2, 128, 1))
    w["kscol"] = np.ascontiguousarray(np.concatenate([ks, ks], 1).reshape(2, 128, 1))
    for nm, key in (("lng", "L0_ffn_ln_g"), ("lnb", "L0_ffn_ln_b"), ("b2", "L0_ffn_b2")):
        v = np.asarray(inp[key], np.float32)
        w[nm] = np.ascontiguousarray(v.reshape(2, C, 128).transpose(0, 2, 1))
    return w


ENGINES = ("tensor", "vector", "scalar", "sync", "gpsimd")
DMA_ENG = ("sync", "gpsimd")


class _Step:
    __slots__ = ("engine", "fn", "name", "deps", "need_inc", "inc_val", "waits",
                 "vdeps")

    def __init__(self, engine, fn, name):
        self.engine = engine
        self.fn = fn
        self.name = name
        self.deps = []
        self.need_inc = False
        self.inc_val = None
        self.waits = None
        self.vdeps = None


SERIAL = False


class Sched:
    def __init__(self):
        self.steps = {e: [] for e in ENGINES}
        self._writer = {}
        self._readers = {}
        self.n = 0
        self._prev = None

    def _collect(self, reads, writes):
        dep = set()
        for r in reads:
            w = self._writer.get(r)
            if w is not None:
                dep.add(w)
        for wk in writes:
            w = self._writer.get(wk)
            if w is not None:
                dep.add(w)
            for r in self._readers.get(wk, ()):
                dep.add(r)
        out = set()
        for d in dep:
            if d.vdeps is not None:
                out.update(d.vdeps)
            else:
                out.add(d)
        return out

    def add(self, engine, fn, reads=(), writes=(), name=""):
        s = _Step(engine, fn, name or f"s{self.n}")
        self.n += 1
        dep = self._collect(reads, writes)
        if SERIAL and self._prev is not None:
            dep.add(self._prev)
        self._prev = s
        for r in reads:
            self._readers.setdefault(r, []).append(s)
        for wk in writes:
            self._writer[wk] = s
            self._readers[wk] = []
        for d in dep:
            if d is s:
                continue
            if d.engine == engine and d.engine not in DMA_ENG:
                continue
            d.need_inc = True
            s.deps.append(d)
        if engine in DMA_ENG:
            s.need_inc = True
        self.steps[engine].append(s)
        return s

    def virt(self, writes=(), reads=(), name="virt"):
        s = _Step(None, None, name)
        s.vdeps = self._collect(reads, writes)
        for r in reads:
            self._readers.setdefault(r, []).append(s)
        for wk in writes:
            self._writer[wk] = s
            self._readers[wk] = []
        return s

    def emit(self, blk, sems):
        for e in ENGINES:
            cnt = 0
            amt = 16 if e in DMA_ENG else 1
            for s in self.steps[e]:
                if s.need_inc:
                    cnt += amt
                    s.inc_val = cnt
        for e in ENGINES:
            hw = {x: 0 for x in ENGINES}
            for s in self.steps[e]:
                ws = []
                for d in sorted(s.deps, key=lambda d: d.inc_val):
                    if d.inc_val > hw[d.engine]:
                        hw[d.engine] = d.inc_val
                        ws.append((d.engine, d.inc_val))
                s.waits = ws
        amt = {e: (16 if e in DMA_ENG else 1) for e in ENGINES}

        def mk(e_name):
            def body(eng):
                for s in self.steps[e_name]:
                    for (de, v) in s.waits:
                        eng.wait_ge(sems[de], v)
                    inst = s.fn(eng)
                    if s.need_inc:
                        assert inst is not None, f"step {s.name} returned None"
                        inst.then_inc(sems[e_name], amt[e_name])
            return body
        blk.tensor(mk("tensor"))
        blk.vector(mk("vector"))
        blk.scalar(mk("scalar"))
        blk.sync(mk("sync"))
        blk.gpsimd(mk("gpsimd"))


def build_program(stage=99, debug_taps=()):
    cst = _constants()
    nc = bass.Bass("TRN2", target_bir_lowering=False)
    taps = set(debug_taps)

    xin = nc.dram_tensor("x", [T, D], F32, kind="ExternalInput").ap()
    out = nc.dram_tensor("out", [T, D], F32, kind="ExternalOutput").ap()
    d_cw = nc.dram_tensor("cw_stat", [128, C * K9 * 18], F32, kind="ExternalInput").ap()
    d_wqkv = nc.dram_tensor("wqkv", [2, 128, QKN, C, 128], F32, kind="ExternalInput").ap()
    d_wo = nc.dram_tensor("wo", [2, 128, 8, 4, 128], F32, kind="ExternalInput").ap()
    d_w1 = nc.dram_tensor("w1", [2, 128, W1N, C, 128], F32, kind="ExternalInput").ap()
    d_w2 = nc.dram_tensor("w2", [2, 128, 8, K2N, 128], F32, kind="ExternalInput").ap()
    d_b1a = nc.dram_tensor("b1a", [2, 128, K2N], F32, kind="ExternalInput").ap()
    d_b1g = nc.dram_tensor("b1g", [2, 128, K2N], F32, kind="ExternalInput").ap()
    d_qs = nc.dram_tensor("qscol", [2, 128, 1], F32, kind="ExternalInput").ap()
    d_ks = nc.dram_tensor("kscol", [2, 128, 1], F32, kind="ExternalInput").ap()
    d_lng = nc.dram_tensor("lng", [2, 128, C], F32, kind="ExternalInput").ap()
    d_lnb = nc.dram_tensor("lnb", [2, 128, C], F32, kind="ExternalInput").ap()
    d_b2 = nc.dram_tensor("b2", [2, 128, C], F32, kind="ExternalInput").ap()
    dc = {}
    for nm in CONST_NAMES:
        a = np.asarray(cst[nm])
        dt = BF16 if a.dtype == ml_dtypes.bfloat16 else F32
        dc[nm] = nc.dram_tensor(nm, list(a.shape), dt, kind="ExternalInput").ap()
    d_x1 = nc.dram_tensor("x1sp", [128, C, T], F32, kind="Internal").ap()
    d_m = nc.dram_tensor("msp", [128, NK2B, T], BF16, kind="Internal").ap()

    dbg = {}

    def tap(name):
        return name in taps

    def tapdram(name, shape, dtype=F32):
        dbg[name] = nc.dram_tensor("dbg_" + name, shape, dtype,
                                   kind="ExternalOutput").ap()
        return dbg[name]

    ctx = ExitStack()
    E = ctx.enter_context

    XR = E(nc.sbuf_tensor("XR", [128, C * T], F32))
    ABF = E(nc.sbuf_tensor("ABF", [128, C * (T + 8)], BF16))
    LBF = E(nc.sbuf_tensor("LBF", [128, C * T], BF16))
    R2 = E(nc.sbuf_tensor("R2", [128, 53376], mybir.dt.uint8))
    r2base = nc.lookup_mloc(R2).addr

    def r2v(name, shape, dtype, off):
        return nc.alloc_sbuf_tensor_at(name, shape, dtype, offset=r2base + off)

    BSEL = r2v("BSEL", [128, K9, 4, 128], BF16, 0)
    CWS = r2v("CWS", [128, C * K9 * 18], BF16, 9216)
    CONVS = r2v("CONVS", [128, 2, 512], BF16, 12096)
    ZR = r2v("ZR", [8, 2 * 512], F32, 14144)
    ACCW = r2v("ACCW", [128, C, W512], F32, 18432)
    STG = r2v("STG", [128, 4, D], F32, 34816)
    COSQ = r2v("COSQ", [128, T], BF16, 0)
    SINQ = r2v("SINQ", [128, T], BF16, 4096)
    COSK = r2v("COSK", [128, T], BF16, 8192)
    SINK = r2v("SINK", [128, T], BF16, 12288)
    QW = r2v("QW", [128, 1, 4, W512], BF16, 16384)
    KW = r2v("KW", [128, 2, 4, W512], BF16, 20480)
    VW = r2v("VW", [128, 2, 4, 520], BF16, 28672)
    OFM = r2v("OFM", [128, 4, T], BF16, 36992)
    MRES = r2v("MRES", [128, NK2A, T], BF16, 0)
    MRB = r2v("MRB", [128, NK2H, W512], BF16, 16384)
    X1RR = r2v("X1RR", [128, 2, 512], F32, 0)
    STGO = r2v("STGO", [128, 4, D], F32, 36992)

    PSTG = E(nc.sbuf_tensor("PSTG", [128, 2, 512], BF16))
    MSTG = E(nc.sbuf_tensor("MSTG", [1, 2, W512], BF16))
    STT = E(nc.sbuf_tensor("STT", [128, 3, 512], F32))
    SQB = E(nc.sbuf_tensor("SQB", [128, 2, 512], BF16))
    QKS = SQB[:, 1, :]
    MAXC = E(nc.sbuf_tensor("MAXC", [128, 2], F32))
    IDB = E(nc.sbuf_tensor("IDB", [128, 128], BF16))
    IDF = E(nc.sbuf_tensor("IDF", [128, 128], F32))
    ROTM = E(nc.sbuf_tensor("ROTM", [128, 128], BF16))
    TRID = E(nc.sbuf_tensor("TRID", [128, 128], BF16))
    TRIE = E(nc.sbuf_tensor("TRIE", [128, 128], BF16))
    HON = E(nc.sbuf_tensor("HON", [128, 128], BF16))
    LNC32 = E(nc.sbuf_tensor("LNC32", [128, 128], F32))
    LNCBF = E(nc.sbuf_tensor("LNCBF", [128, 128], BF16))
    ONES1 = E(nc.sbuf_tensor("ONES1", [1, 128], BF16))
    ONE1F = E(nc.sbuf_tensor("ONE1F", [128, 128], F32))
    ONECB = E(nc.sbuf_tensor("ONECB", [128, 1], BF16))
    ZERB = E(nc.sbuf_tensor("ZERB", [128, 128], BF16))
    SEL4 = E(nc.sbuf_tensor("SEL4", [128, 8], BF16))
    SELZ = E(nc.sbuf_tensor("SELZ", [8, 128], F32))
    WQB = E(nc.sbuf_tensor("WQB", [128, 2, C * 128], BF16))
    WOB = E(nc.sbuf_tensor("WOB", [128, 4 * 128], BF16))
    WPB = E(nc.sbuf_tensor("WPB", [128, 3, C * 128], BF16))
    QSC = E(nc.sbuf_tensor("QSC", [128, 2], F32))
    GBC = E(nc.sbuf_tensor("GBC", [128, 3 * C], F32))
    B1C = E(nc.sbuf_tensor("B1C", [128, 2 * K2N], F32))

    PB = E(nc.psum_tensor("PB", [128, 3584], F32))
    PBF = E(nc.psum_tensor("PBF", [128, 1024], BF16))

    def bank(i, col0=0, ncol=512):
        return PB[:, 512 * i + col0: 512 * i + col0 + ncol]

    sems = {e: E(nc.semaphore("sem_" + e)) for e in ENGINES}
    blk = E(nc.Block())
    S = Sched()

    def XRc(c, lo, n):
        return XR[:, c * T + lo: c * T + lo + n]

    def ABFc(c, lo, n):
        return ABF[:, c * (T + 8) + lo: c * (T + 8) + lo + n]

    def LBFc(c, lo, n):
        return LBF[:, c * T + lo: c * T + lo + n]

    _consts = ((IDB[:], dc["ident_bf"]), (IDF[:], dc["ident_f32"]),
               (ROTM[:], dc["rotmat"]), (TRID[:], dc["tri_diag"]),
               (TRIE[:], dc["tri_edge"]), (HON[:], dc["hones"]),
               (LNC32[:], dc["lncol_f32"]), (LNCBF[:], dc["lncol_bf"]),
               (ONES1[:], dc["ones1_bf"]), (ONE1F[:], dc["ones1_f32"]),
               (ONECB[:], dc["onescol_bf"]), (ZERB[:], dc["zeros_bf"]),
               (SEL4[:], dc["sel4"]), (SELZ[:], dc["selzrow"]))
    for i, (dst, src) in enumerate(_consts):
        S.add("sync", (lambda e, dst=dst, src=src: e.dma_start(out=dst, in_=src)),
              writes=["const"] if i == len(_consts) - 1 else [],
              name=f"ld_const{i}")
    S.add("gpsimd", lambda e: e.dma_start(out=BSEL[:], in_=dc["bsel"]),
          writes=[], name="ld_bsel")
    S.add("gpsimd", lambda e: e.dma_start(out=CWS[:], in_=d_cw),
          writes=["convw"], name="ld_convw")

    def z_guards(e):
        for c in range(C):
            e.memset(ABF[:, c * (T + 8):c * (T + 8) + 4], 0)
            inst = e.memset(ABF[:, c * (T + 8) + T + 4:c * (T + 8) + T + 8], 0)
        return inst
    S.add("vector", z_guards, writes=["guards"], name="zguards")

    xv = xin.rearrange("(tt p) f -> p tt f", p=128)
    for qtr in range(4):
        def ld_q(e, qtr=qtr):
            return e.dma_start(out=STG[:], in_=xv[:, 4 * qtr:4 * (qtr + 1), :])
        S.add("sync", ld_q, writes=[("stg",)], name=f"ldx{qtr}")
        for tt4 in range(4):
            tt = 4 * qtr + tt4
            for cq in range(2):
                bk = (tt4 * 2 + cq) % 2
                def tp(e, tt4=tt4, cq=cq, bk=bk):
                    for c4 in range(4):
                        c = cq * 4 + c4
                        inst = e.transpose(
                            PB[:, 512 * bk + 128 * c4:512 * bk + 128 * (c4 + 1)],
                            STG[:, tt4, 128 * c:128 * (c + 1)], IDF[:])
                    return inst
                S.add("tensor", tp, reads=[("stg",), "const", ("pb", bk)],
                      writes=[("pb", bk)], name=f"tpx{tt}_{cq}")
                def ev(e, tt=tt, cq=cq, bk=bk):
                    for c4 in range(4):
                        c = cq * 4 + c4
                        inst = e.tensor_copy(
                            XRc(c, tt * 128, 128),
                            PB[:, 512 * bk + 128 * c4:512 * bk + 128 * (c4 + 1)])
                    return inst
                S.add("vector", ev, reads=[("pb", bk), ("stg",)],
                      writes=[("xr", tt // 4), ("pb", bk)], name=f"evx{tt}_{cq}")

    if tap("xr"):
        tapdram("xr", [128, C * T])
        S.add("gpsimd", lambda e: e.dma_start(out=dbg["xr"], in_=XR[:]),
              reads=[("xr", i) for i in range(4)], name="tap_xr")

    def ln_stats(tag, rhs, reads, bf=False):
        def mm_mean(e):
            for c in range(C):
                inst = e.matmul(bank(5), LNCBF[:] if bf else LNC32[:], rhs(c),
                                start=(c == 0), stop=(c == C - 1))
            return inst
        S.add("tensor", mm_mean, reads=list(reads) + ["const", ("pb", 5)],
              writes=[("pb", 5)], name=f"lnm_{tag}")
        for c in range(C):
            sl = 0
            def sq1(e, c=c, sl=sl):
                return e.activation(out=SQB[:, sl, :], in_=rhs(c), func=AF.Square)
            S.add("scalar", sq1, reads=list(reads) + [("sqb", sl)],
                  writes=[("sqbd", sl)], name=f"lnsq_{tag}_{c}")
            def mmsq(e, c=c, sl=sl):
                return e.matmul(bank(6), LNCBF[:], SQB[:, sl, :],
                                start=(c == 0), stop=(c == C - 1))
            S.add("tensor", mmsq, reads=[("sqbd", sl), "const", ("pb", 6)],
                  writes=[("pb", 6), ("sqb", sl)], name=f"lnsqm_{tag}_{c}")

        def stats(e):
            e.tensor_copy(STT[:, 2, :], bank(5))
            e.tensor_tensor(STT[:, 1, :], STT[:, 2, :], STT[:, 2, :], ALU.mult)
            e.scalar_tensor_tensor(STT[:, 1, :], STT[:, 1, :], -1.0,
                                   bank(6), ALU.mult, ALU.add)
            e.tensor_scalar_add(STT[:, 1, :], STT[:, 1, :], LNEPS)
            return e.reciprocal(STT[:, 1, :], STT[:, 1, :])
        S.add("vector", stats,
              reads=[("pb", 5), ("pb", 6), ("stt", 2), ("stt", 1), ("stt", 0)],
              writes=[("stt", 2), ("stt", 1), ("stt", 0), ("pb", 6), ("pb", 5)],
              name=f"lnv_{tag}")
        S.add("scalar",
              lambda e: e.activation(out=STT[:, 0, :], in_=STT[:, 1, :], func=AF.Sqrt),
              reads=[("stt", 1), ("stt", 0)], writes=[("stt", 0)], name=f"lnr_{tag}")

        def negm(e):
            return e.scalar_tensor_tensor(STT[:, 1, :], STT[:, 2, :], -1.0,
                                          STT[:, 0, :], ALU.mult, ALU.mult)
        S.add("vector", negm, reads=[("stt", 0), ("stt", 2), ("stt", 1)],
              writes=[("stt", 1)], name=f"lnn_{tag}")

    for w in range(NW):
        ln_stats(f"l1w{w}", lambda c, w=w: XRc(c, w * W512, W512), [("xr", w)])
        for c in range(C):
            def apa(e, c=c, w=w):
                return e.tensor_tensor(STT[:, 2, :], XRc(c, w * W512, W512),
                                       STT[:, 0, :], ALU.mult)
            S.add("vector", apa, reads=[("stt", 0), ("xr", w), ("stt", 2)],
                  writes=[("stt", 2)], name=f"l1a{w}_{c}")
            def apb(e, c=c, w=w):
                return e.tensor_tensor(ABFc(c, w * W512 + 4, W512),
                                       STT[:, 2, :], STT[:, 1, :], ALU.add)
            S.add("vector", apb, reads=[("stt", 2), ("stt", 1), "guards"],
                  writes=[("ln1", c, w)], name=f"l1b{w}_{c}")

    if tap("ln1"):
        tapdram("ln1", [128, C * (T + 8)])
        S.add("gpsimd", lambda e: e.dma_start(out=dbg["ln1"], in_=ABF[:]),
              reads=[("ln1", c, w) for c in range(C) for w in range(NW)],
              name="tap_ln1")

    if stage >= 2:
        for w in range(NW):
            for half in range(2):
                def mml(e, w=w, half=half):
                    for ct in range(4):
                        c = half * 4 + ct
                        tp = (0, 32 * ct) if ct > 0 else None
                        for k in range(K9):
                            inst = e.matmul(
                                PB[32 * ct:32 * ct + 18,
                                   512 * half:512 * half + 512],
                                CWS[:, (c * K9 + k) * 18:(c * K9 + k) * 18 + 18],
                                ABFc(c, w * W512 + k, W512),
                                start=(k == 0), stop=(k == K9 - 1),
                                tile_position=tp)
                    return inst
                S.add("tensor", mml,
                      reads=[("ln1", half * 4 + ct, w) for ct in range(4)] +
                            ["convw", ("pb", half)],
                      writes=[("pb", half)], name=f"cvl{w}_{half}")
                def clamp(e, half=half):
                    return e.tensor_scalar_min(bank(half), bank(half), 60.0)
                S.add("vector", clamp, reads=[("pb", half)], writes=[("pb", half)],
                      name=f"cvc{w}_{half}")
                def expE(e, half=half):
                    return e.activation(out=CONVS[:, half, :], in_=bank(half),
                                        func=AF.Exp)
                S.add("scalar", expE, reads=[("pb", half), ("cv_e", half)],
                      writes=[("cv_e", half), ("pb", half)], name=f"cve{w}_{half}")
                def mmz(e, half=half):
                    return e.matmul(PB[0:8, 1024:1536], SEL4[:], CONVS[:, half, :],
                                    start=True, stop=True)
                S.add("tensor", mmz, reads=[("cv_e", half), "const", ("pb", 2)],
                      writes=[("pb", 2)], name=f"cvz{w}_{half}")
                def zinv(e, half=half):
                    return e.reciprocal(ZR[:, half * 512:(half + 1) * 512],
                                        PB[0:8, 1024:1536])
                S.add("vector", zinv, reads=[("pb", 2), ("cv_zi", half), ("stt", 2)],
                      writes=[("cv_zi", half), ("pb", 2), ("stt", 2)],
                      name=f"cvzi{w}_{half}")
                def mmzr(e, half=half):
                    return e.matmul(bank(3), SELZ[:],
                                    ZR[:, half * 512:(half + 1) * 512],
                                    start=True, stop=True)
                S.add("tensor", mmzr, reads=[("cv_zi", half), "const", ("pb", 3)],
                      writes=[("pb", 3)], name=f"cvzr{w}_{half}")
                def wnorm(e, half=half):
                    return e.tensor_tensor(CONVS[:, half, :], CONVS[:, half, :],
                                           bank(3), ALU.mult)
                S.add("vector", wnorm, reads=[("cv_e", half), ("pb", 3)],
                      writes=[("cv_w", half), ("cv_e", half), ("pb", 3)],
                      name=f"cvw{w}_{half}")
            for c in range(C):
                half, ct = c // 4, c % 4
                for k in range(K9):
                    pb = 3 + (k % 2)
                    def mmb(e, k=k, ct=ct, half=half, pb=pb):
                        return e.matmul(bank(pb), BSEL[:, k, ct, :],
                                        CONVS[:, half, :], start=True, stop=True)
                    S.add("tensor", mmb,
                          reads=[("cv_w", half), "convw", ("pb", pb)],
                          writes=[("pb", pb)], name=f"cvb{w}_{c}_{k}")
                    def mix(e, k=k, c=c, w=w, pb=pb):
                        if k == 0:
                            return e.tensor_tensor(ACCW[:, c, :],
                                                   ABFc(c, w * W512 + k, W512),
                                                   bank(pb), ALU.mult)
                        e.tensor_tensor(STT[:, 2, :], ABFc(c, w * W512 + k, W512),
                                        bank(pb), ALU.mult)
                        return e.tensor_tensor(ACCW[:, c, :], ACCW[:, c, :],
                                               STT[:, 2, :], ALU.add)
                    S.add("vector", mix,
                          reads=[("pb", pb), ("ln1", c, w), ("acc", c), ("stt", 2)],
                          writes=[("acc", c), ("pb", pb), ("stt", 2)],
                          name=f"cvm{w}_{c}_{k}")
            if tap("conv") and w == 0:
                tapdram("conv", [128, C, W512])
                S.add("gpsimd", lambda e: e.dma_start(out=dbg["conv"], in_=ACCW[:]),
                      reads=[("acc", c) for c in range(C)], name="tap_conv")
            ln_stats(f"lcw{w}", lambda c: ACCW[:, c, :],
                     [("acc", c) for c in range(C)])
            for c in range(C):
                def xa(e, c=c):
                    return e.tensor_tensor(STT[:, 2, :], ACCW[:, c, :],
                                           STT[:, 0, :], ALU.mult)
                S.add("vector", xa, reads=[("stt", 0), ("acc", c), ("stt", 2)],
                      writes=[("stt", 2)], name=f"x1a{w}_{c}")
                def xb(e, c=c):
                    return e.tensor_tensor(STT[:, 2, :], STT[:, 2, :],
                                           STT[:, 1, :], ALU.add)
                S.add("vector", xb, reads=[("stt", 1)], writes=[("stt", 2)],
                      name=f"x1b{w}_{c}")
                def xc(e, c=c, w=w):
                    return e.tensor_tensor(XRc(c, w * W512, W512),
                                           XRc(c, w * W512, W512),
                                           STT[:, 2, :], ALU.add)
                S.add("vector", xc, reads=[("stt", 2), ("xr", w)],
                      writes=[("xr", w), ("acc", c)], name=f"x1c{w}_{c}")

        if tap("x1"):
            tapdram("x1", [128, C * T])
            S.add("gpsimd", lambda e: e.dma_start(out=dbg["x1"], in_=XR[:]),
                  reads=[("xr", i) for i in range(4)], name="tap_x1")

    if stage >= 3:
        def spill(e):
            return e.dma_start(out=d_x1, in_=XR[:].rearrange("p (c t) -> p c t", t=T))
        S.add("sync", spill, reads=[("xr", i) for i in range(4)], writes=["x1sp"],
              name="x1spill")
        for w in range(NW):
            ln_stats(f"l2w{w}", lambda c, w=w: XRc(c, w * W512, W512), [("xr", w)])
            for c in range(C):
                def ya(e, c=c, w=w):
                    return e.tensor_tensor(XRc(c, w * W512, W512),
                                           XRc(c, w * W512, W512),
                                           STT[:, 0, :], ALU.mult)
                S.add("vector", ya, reads=[("stt", 0), "x1sp"],
                      writes=[("xr", w)], name=f"y2a{w}_{c}")
                def yb(e, c=c, w=w):
                    return e.tensor_tensor(XRc(c, w * W512, W512),
                                           XRc(c, w * W512, W512),
                                           STT[:, 1, :], ALU.add)
                S.add("vector", yb, reads=[("stt", 1)], writes=[("xr", w)],
                      name=f"y2b{w}_{c}")
                def ymir(e, c=c, w=w):
                    return e.copy(out=ABFc(c, w * W512 + 4, W512),
                                  in_=XRc(c, w * W512, W512))
                S.add("scalar", ymir,
                      reads=[("xr", w)],
                      writes=[("ybf", c, w), ("ln1", c, w)], name=f"ymir{w}_{c}")

        if tap("y0"):
            tapdram("y0", [128, C * T])
            S.add("gpsimd", lambda e: e.dma_start(out=dbg["y0"], in_=XR[:]),
                  reads=[("xr", i) for i in range(4)], name="tap_y0")

    QWK = [("qw", c2) for c2 in range(4)]
    KWK = [("kw", s, c2) for s in range(2) for c2 in range(4)]
    VWK = [("vw", s, v) for s in range(2) for v in range(4)] + \
          [("vw1", s) for s in range(2)]
    OFK = [("ofm", c2, w, hp) for c2 in range(4) for w in range(NW)
           for hp in range(2)]
    MRK = [("mres", j, tq) for j in range(NK2A) for tq in range(NW)] + \
          [("mresd", j, tq) for j in range(NK2A) for tq in range(NW)] + \
          [("mrb", j) for j in range(NK2B)] + [("mrbd", j) for j in range(NK2B)] + \
          [("mrl", j) for j in range(NK2H)]

    if stage >= 4:
        nlayers = 2 if stage >= 5 else 1
        for l in range(nlayers):
            if l == 0:
                S.virt(writes=(["rot", ("stg",), "convw",
                                ("cv_w", 0), ("cv_w", 1), ("cv_e", 0), ("cv_e", 1),
                                ("cv_zi", 0), ("cv_zi", 1)] +
                               [("acc", c) for c in range(C)] +
                               QWK + KWK + VWK + OFK), name="retag_conv2attn")
            else:
                S.virt(writes=["rot"] + QWK + KWK + VWK + OFK + MRK,
                       name=f"retag_ffn2attn{l}")
            _ffn_layer(S, locals(), l)

    if stage >= 9:
        S.virt(writes=[("x1r", 0), ("x1r", 1), ("stgo2",)] + OFK + MRK,
               name="retag_exit")
        outv = out.rearrange("(tt p) f -> p tt f", p=128)
        for w in range(NW):
            ln_stats(f"lfw{w}", lambda c, w=w: XRc(c, w * W512, W512), [("xr", w)])
            for c in range(C):
                def fa(e, c=c, w=w):
                    return e.tensor_tensor(STT[:, 2, :], XRc(c, w * W512, W512),
                                           STT[:, 0, :], ALU.mult)
                S.add("vector", fa, reads=[("stt", 0), ("xr", w), ("stt", 2)],
                      writes=[("stt", 2)], name=f"fna{w}_{c}")
                def fb(e, c=c, w=w):
                    return e.tensor_tensor(STT[:, 2, :], STT[:, 2, :],
                                           STT[:, 1, :], ALU.add)
                S.add("vector", fb, reads=[("stt", 1)], writes=[("stt", 2)],
                      name=f"fnb{w}_{c}")
                sl = c % 2
                def ldx1(e, c=c, w=w, sl=sl):
                    return e.dma_start(out=X1RR[:, sl, :],
                                       in_=d_x1[:, c, w * W512:(w + 1) * W512])
                S.add("sync", ldx1, reads=["x1sp", ("x1r", sl)],
                      writes=[("x1rd", sl)], name=f"ldx1_{w}_{c}")
                def fc(e, c=c, sl=sl):
                    return e.tensor_tensor(STT[:, 2, :], STT[:, 2, :],
                                           X1RR[:, sl, :], ALU.add)
                S.add("vector", fc, reads=[("x1rd", sl)],
                      writes=[("stt", 2), ("x1r", sl)], name=f"fnc{w}_{c}")
                bk = c % 2
                def ftp(e, c=c, bk=bk):
                    for q in range(4):
                        inst = e.transpose(
                            PB[:, 512 * bk + 128 * q:512 * bk + 128 * (q + 1)],
                            STT[:, 2, 128 * q:128 * (q + 1)], IDF[:])
                    return inst
                S.add("tensor", ftp, reads=[("stt", 2), "const", ("pb", bk)],
                      writes=[("pb", bk)], name=f"ftp{w}_{c}")
                def fev(e, c=c, bk=bk):
                    for q in range(4):
                        inst = e.tensor_copy(
                            STGO[:, q, 128 * c:128 * (c + 1)],
                            PB[:, 512 * bk + 128 * q:512 * bk + 128 * (q + 1)])
                    return inst
                S.add("vector", fev, reads=[("pb", bk), ("stgo2",)],
                      writes=[("pb", bk), ("stgo", c)], name=f"fev{w}_{c}")
            def sto(e, w=w):
                return e.dma_start(out=outv[:, 4 * w:4 * (w + 1), :], in_=STGO[:])
            S.add("sync", sto, reads=[("stgo", c) for c in range(C)],
                  writes=["outd", ("stgo2",)], name=f"sto{w}")
        S.add("sync", lambda e: e.nop(), reads=["outd"], name="outwait")

    S.emit(blk, sems)
    ctx.close()
    return nc, dbg


def _ffn_layer(S, L, l):
    nc = L["nc"]
    PB, PBF = L["PB"], L["PBF"]
    bank = L["bank"]
    XRc, ABFc, LBFc = L["XRc"], L["ABFc"], L["LBFc"]
    STT, SQB, QKS, MAXC = L["STT"], L["SQB"], L["QKS"], L["MAXC"]
    WPB = L["WPB"]
    QW, KW, VW, OFM = L["QW"], L["KW"], L["VW"], L["OFM"]
    COSQ, SINQ, COSK, SINK = L["COSQ"], L["SINQ"], L["COSK"], L["SINK"]
    MRES, MRB = L["MRES"], L["MRB"]
    PSTG, MSTG = L["PSTG"], L["MSTG"]
    WQB, WOB = L["WQB"], L["WOB"]
    QSC, GBC, B1C = L["QSC"], L["GBC"], L["B1C"]
    HON, ROTM, IDB, IDF = L["HON"], L["ROTM"], L["IDB"], L["IDF"]
    TRID, TRIE = L["TRID"], L["TRIE"]
    ONES1, ONE1F, ONECB = L["ONES1"], L["ONE1F"], L["ONECB"]
    ZERB = L["ZERB"]
    d_wqkv, d_wo, d_w1, d_w2 = L["d_wqkv"], L["d_wo"], L["d_w1"], L["d_w2"]
    d_qs, d_ks = L["d_qs"], L["d_ks"]
    d_lng, d_lnb, d_b2 = L["d_lng"], L["d_lnb"], L["d_b2"]
    d_b1a, d_b1g, d_m = L["d_b1a"], L["d_b1g"], L["d_m"]
    dc = L["dc"]
    dbg, taps, tapdram = L["dbg"], L["taps"], L["tapdram"]
    ln_stats = L["ln_stats"]
    XRt = L["XR"]

    def tap(name):
        return name in taps

    YK = lambda c, w: ("ybf", c, w)

    for i, (dst, src) in enumerate((
            (QSC[:, 0:1], d_qs[l]), (QSC[:, 1:2], d_ks[l]),
            (GBC[:, 0:C], d_lng[l]), (GBC[:, C:2 * C], d_lnb[l]),
            (GBC[:, 2 * C:3 * C], d_b2[l]),
            (B1C[:, 0:K2N], d_b1a[l]), (B1C[:, K2N:2 * K2N], d_b1g[l]))):
        S.add("sync", (lambda e, dst=dst, src=src: e.dma_start(out=dst, in_=src)),
              writes=[("lsm",)], name=f"ldsm{l}_{i}")

    for i, (dst, src) in enumerate(((COSQ[:], dc["cosq"]), (SINQ[:], dc["sinq"]),
                                    (COSK[:], dc["cosk"]), (SINK[:], dc["sink"]))):
        S.add("sync", (lambda e, dst=dst, src=src: e.dma_start(out=dst, in_=src)),
              writes=["rot"], name=f"ld_rot{l}_{i}")

    for w in range(NW):
        for n in range(QKN):
            wsl = n % 2
            def ldwq(e, n=n, l=l, wsl=wsl):
                return e.dma_start(out=WQB[:, wsl, :], in_=d_wqkv[l, :, n, :, :])
            S.add("gpsimd", ldwq, reads=[("wqb", wsl)], writes=[("wqbd", wsl)],
                  name=f"ldwq{l}_{w}_{n}")
            pbk = n % 2
            yreads = [YK(c, w) for c in range(C)]
            if w > 0:
                yreads += [YK(c, w - 1) for c in range(4, C)]
            def mmq(e, n=n, w=w, pbk=pbk, wsl=wsl):
                for c in range(C):
                    off = w * W512 + (3 if c >= 4 else 4)
                    inst = e.matmul(bank(pbk), WQB[:, wsl, 128 * c:128 * (c + 1)],
                                    ABFc(c, off, W512),
                                    start=(c == 0), stop=(c == C - 1))
                return inst
            S.add("tensor", mmq, reads=[("wqbd", wsl), ("pb", pbk)] + yreads,
                  writes=[("pb", pbk), ("wqb", wsl)], name=f"mmqkv{l}_{w}_{n}")

            if n < 8:
                isq = n < 4
                c2 = n % 4
                tbl_c, tbl_s = (COSQ, SINQ) if isq else (COSK, SINK)
                scol = QSC[:, 0:1] if isq else QSC[:, 1:2]
                def sq(e, pbk=pbk):
                    return e.activation(out=SQB[:, 0, :], in_=bank(pbk),
                                        func=AF.Square)
                S.add("scalar", sq, reads=[("pb", pbk), ("sqb", 0)],
                      writes=[("sqbd", 0)], name=f"qksq{l}_{w}_{n}")
                def mmn(e):
                    return e.matmul(bank(2), HON[:], SQB[:, 0, :],
                                    start=True, stop=True)
                S.add("tensor", mmn, reads=[("sqbd", 0), "const", ("pb", 2)],
                      writes=[("pb", 2), ("sqb", 0)], name=f"qkn{l}_{w}_{n}")
                def ninv(e):
                    e.tensor_scalar_max(bank(2), bank(2), 1e-24)
                    return e.reciprocal(STT[:, 2, :], bank(2))
                S.add("vector", ninv, reads=[("pb", 2), ("stt", 2), ("stt", 1)],
                      writes=[("pb", 2), ("stt", 2), ("stt", 1)],
                      name=f"qkni{l}_{w}_{n}")
                def nsq(e):
                    return e.activation(out=STT[:, 0, :], in_=STT[:, 2, :],
                                        func=AF.Sqrt)
                S.add("scalar", nsq, reads=[("stt", 2), ("stt", 0)],
                      writes=[("stt", 0)], name=f"qkns{l}_{w}_{n}")
                def qbf(e, pbk=pbk, scol=scol):
                    return e.activation(out=QKS, in_=bank(pbk),
                                        func=AF.Copy, scale=scol)
                S.add("scalar", qbf, reads=[("pb", pbk), ("lsm",), ("sqb", 1)],
                      writes=[("sqbd", 1), ("pb", pbk)], name=f"qkb{l}_{w}_{n}")
                def mmrot(e):
                    return e.matmul(bank(3), ROTM[:], QKS,
                                    start=True, stop=True)
                S.add("tensor", mmrot, reads=[("sqbd", 1), "const", ("pb", 3)],
                      writes=[("pb", 3)], name=f"qkr{l}_{w}_{n}")
                def rot1(e, w=w, tbl_c=tbl_c):
                    return e.tensor_tensor(STT[:, 2, :], QKS,
                                           tbl_c[:, w * W512:(w + 1) * W512],
                                           ALU.mult)
                S.add("vector", rot1, reads=[("sqbd", 1), "rot", ("stt", 2)],
                      writes=[("stt", 2), ("sqb", 1)], name=f"qkr1{l}_{w}_{n}")
                def rot2(e, w=w, tbl_s=tbl_s):
                    e.tensor_tensor(STT[:, 1, :], bank(3),
                                    tbl_s[:, w * W512:(w + 1) * W512], ALU.mult)
                    return e.tensor_tensor(STT[:, 2, :], STT[:, 2, :],
                                           STT[:, 1, :], ALU.add)
                S.add("vector", rot2,
                      reads=[("pb", 3), "rot", ("stt", 2), ("stt", 1)],
                      writes=[("stt", 2), ("stt", 1), ("pb", 3)],
                      name=f"qkr2{l}_{w}_{n}")
                if isq:
                    dkey, dst = ("qw", c2), (lambda e, c2=c2: QW[:, 0, c2, :])
                else:
                    dkey, dst = ("kw", w % 2, c2), \
                        (lambda e, w=w, c2=c2: KW[:, w % 2, c2, :])
                def rot3(e, dst=dst):
                    return e.tensor_tensor(dst(e), STT[:, 2, :],
                                           STT[:, 0, :], ALU.mult)
                S.add("vector", rot3, reads=[("stt", 2), ("stt", 0), "rot", dkey],
                      writes=[dkey], name=f"qkr3{l}_{w}_{n}")
            else:
                vc = n - 8
                vsl = w % 2
                def vbf(e, pbk=pbk):
                    return e.activation(out=QKS, in_=bank(pbk), func=AF.Copy)
                S.add("scalar", vbf, reads=[("pb", pbk), ("sqb", 1)],
                      writes=[("sqbd", 1), ("pb", pbk)], name=f"vb{l}_{w}_{vc}")
                def vtp(e):
                    for q in range(4):
                        inst = e.transpose(PBF[:, 128 * q:128 * (q + 1)],
                                           SQB[:, 1, 128 * q:128 * (q + 1)], IDB[:])
                    return inst
                S.add("tensor", vtp, reads=[("sqbd", 1), "const", ("pb", 7)],
                      writes=[("pb", 7)], name=f"vt{l}_{w}_{vc}")
                def vev(e, vsl=vsl, vc=vc):
                    for q in range(4):
                        src = PBF[:, 128 * q:128 * (q + 1)].rearrange(
                            "p (hh dd) -> p hh dd", hh=2)
                        inst = e.tensor_copy(
                            VW[:, vsl, q, 65 * 2 * vc:65 * 2 * vc + 130].rearrange(
                                "p (hh dd) -> p hh dd", dd=65)[:, :, 0:64], src)
                    return inst
                S.add("vector", vev,
                      reads=[("pb", 7), "rot", ("vw", vsl, vc), ("sqb", 1)],
                      writes=[("vw", vsl, vc), ("pb", 7), ("sqb", 1)],
                      name=f"vev{l}_{w}_{vc}")
        def vones(e, vsl=w % 2):
            for q in range(4):
                inst = e.memset(
                    VW[:, vsl, q, :].rearrange("p (hh dd) -> p hh dd",
                                               dd=65)[:, :, 64:65], 1.0)
            return inst
        S.add("vector", vones,
              reads=[("vw", w % 2, vc) for vc in range(4)] + ["rot", ("vw1", w % 2)],
              writes=[("vw1", w % 2)], name=f"vones{l}_{w}")

        if w == 0 and l == 0:
            for nm, buf, keys in (
                    ("qw0", QW, [("qw", c2) for c2 in range(4)]),
                    ("kw0", KW, [("kw", 0, c2) for c2 in range(4)]),
                    ("vw0", VW, [("vw", 0, vc) for vc in range(4)] + [("vw1", 0)])):
                if tap(nm):
                    tapdram(nm, [128, buf.shape[1] * buf.shape[2] * buf.shape[3]])
                    S.add("gpsimd",
                          (lambda e, nm=nm, buf=buf:
                           e.dma_start(out=dbg[nm], in_=buf[:].rearrange(
                               "p a b c -> p (a b c)"))),
                          reads=keys, name=f"tap_{nm}")

        for h in range(NH):
            c2, hp = h // 2, h % 2
            rows = slice(64 * hp, 64 * hp + 64)
            for qi in range(4):
                kj_lo = max(0, 4 * w + qi - 4)
                kj_hi = 4 * w + qi
                ncol = 128 * (kj_hi - kj_lo + 1)
                kread = sorted({("kw", (kj // 4) % 2, c2)
                                for kj in range(kj_lo, kj_hi + 1)})
                def mmqk1(e, qi=qi, kj_lo=kj_lo, kj_hi=kj_hi, c2=c2, rows=rows):
                    inst = None
                    for i, kj in enumerate(range(kj_lo, kj_hi + 1)):
                        inst = e.matmul(
                            PB[:, 128 * i:128 * (i + 1)],
                            QW[rows, 0, c2, 128 * qi:128 * (qi + 1)],
                            KW[rows, (kj // 4) % 2, c2,
                               128 * (kj % 4):128 * (kj % 4) + 128],
                            start=True, stop=True)
                    return inst
                S.add("tensor", mmqk1,
                      reads=[("qw", c2), ("pb", 0), ("pb", 1)] + kread,
                      writes=[("pb", 0), ("pb", 1)], name=f"qk1_{l}_{w}_{h}_{qi}")
                def redm(e, ncol=ncol, qi=qi):
                    return e.reduce_max(MAXC[:, qi % 2:qi % 2 + 1],
                                        PB[:, 0:ncol], axis=AXX, negate=True)
                S.add("vector", redm, reads=[("pb", 0), ("pb", 1), ("mxc", qi % 2)],
                      writes=[("mxcd", qi % 2), ("pb", 0), ("pb", 1)],
                      name=f"rmx{l}_{w}_{h}_{qi}")
                def tpm(e, qi=qi):
                    return e.transpose(
                        PB[0:1, 3072 + 128 * qi:3072 + 128 * (qi + 1)],
                        MAXC[:, qi % 2:qi % 2 + 1], IDF[:])
                S.add("tensor", tpm, reads=[("mxcd", qi % 2), "const", ("pb", 6)],
                      writes=[("pb", 6), ("mxc", qi % 2)],
                      name=f"tpm{l}_{w}_{h}_{qi}")
            msl = h % 2
            def mev(e, msl=msl):
                return e.activation(out=MSTG[:, msl, :], in_=PB[0:1, 3072:3584],
                                    func=AF.Copy)
            S.add("scalar", mev, reads=[("pb", 6), ("mst", msl)],
                  writes=[("mstd", msl), ("pb", 6)], name=f"mev{l}_{w}_{h}")
            if l == 0 and w == 0 and h <= 1 and tap("mst0"):
                nm = f"mst{h}"
                tapdram(nm, [1, 512])
                S.add("gpsimd",
                      (lambda e, msl=msl, nm=nm:
                       e.dma_start(out=dbg[nm], in_=MSTG[:, msl, :])),
                      reads=[("mstd", msl)], name=f"tap_{nm}")

            pv_started = {"d": False, "z": False}
            for i_kj, kj in enumerate(range(max(0, 4 * w - 4), 4 * w + 4)):
                qi_lo = max(0, kj - 4 * w)
                qi_hi = min(3, kj - 4 * w + 4)
                M = 128 * (qi_hi - qi_lo + 1)
                qb = 2 + (i_kj % 2)
                def mmqk2(e, kj=kj, qi_lo=qi_lo, M=M, c2=c2, rows=rows,
                          qb=qb, msl=msl):
                    e.matmul(bank(qb, 0, M),
                             KW[rows, (kj // 4) % 2, c2,
                                128 * (kj % 4):128 * (kj % 4) + 128],
                             QW[rows, 0, c2, 128 * qi_lo:128 * qi_lo + M],
                             start=True, stop=False)
                    return e.matmul(bank(qb, 0, M), ONES1[:],
                                    MSTG[:, msl, 128 * qi_lo:128 * qi_lo + M],
                                    start=False, stop=True)
                S.add("tensor", mmqk2,
                      reads=[("qw", c2), ("kw", (kj // 4) % 2, c2),
                             ("mstd", msl), ("pb", qb)],
                      writes=[("pb", qb)], name=f"qk2_{l}_{w}_{h}_{kj}")
                psl = i_kj % 2
                def pexp(e, qb=qb, M=M, psl=psl):
                    return e.activation(out=PSTG[:, psl, 0:M], in_=bank(qb, 0, M),
                                        func=AF.Exp)
                S.add("scalar", pexp, reads=[("pb", qb), ("pst", psl)],
                      writes=[("pstd", psl), ("pb", qb)],
                      name=f"pex{l}_{w}_{h}_{kj}")
                qi_d = kj - 4 * w
                qi_e = kj - 4 * w + 4
                if qi_lo <= qi_d <= qi_hi or qi_lo <= qi_e <= qi_hi:
                    def tri(e, qi_d=qi_d, qi_e=qi_e, qi_lo=qi_lo, qi_hi=qi_hi,
                            psl=psl):
                        inst = None
                        if qi_lo <= qi_d <= qi_hi:
                            co = 128 * (qi_d - qi_lo)
                            inst = e.tensor_tensor(PSTG[:, psl, co:co + 128],
                                                   PSTG[:, psl, co:co + 128],
                                                   TRID[:], ALU.mult)
                        if qi_lo <= qi_e <= qi_hi:
                            co = 128 * (qi_e - qi_lo)
                            inst = e.tensor_tensor(PSTG[:, psl, co:co + 128],
                                                   PSTG[:, psl, co:co + 128],
                                                   TRIE[:], ALU.mult)
                        return inst
                    S.add("vector", tri, reads=[("pstd", psl), "const"],
                          writes=[("pstd", psl)], name=f"tri{l}_{w}_{h}_{kj}")
                if l == 0 and w == 0 and h <= 1 and kj == 0 and tap("pst0"):
                    nm = f"pst{h}"
                    tapdram(nm, [128, 512])
                    S.add("gpsimd",
                          (lambda e, psl=psl, M=M, nm=nm:
                           e.dma_start(out=dbg[nm][:, 0:M],
                                       in_=PSTG[:, psl, 0:M])),
                          reads=[("pstd", psl)], name=f"tap_{nm}")
                def pv(e, kj=kj, qi_lo=qi_lo, qi_hi=qi_hi, psl=psl, h=h, hp=hp,
                       w=w, c2=c2, pv_started=pv_started):
                    vsl = (kj // 4) % 2
                    vq = kj % 4
                    inst = None
                    if hp == 1 and not pv_started["d"]:
                        e.matmul(PB[:, 2048:2560], ZERB[:],
                                 KW[:, vsl, c2, :], start=True, stop=False)
                        pv_started["d"] = True
                    for qi in range(qi_lo, qi_hi + 1):
                        st = (hp == 0) and (not pv_started["d"])
                        pv_started["d"] = True
                        sp = (kj == 4 * w + 3 and qi == qi_hi)
                        co = 128 * (qi - qi_lo)
                        if hp == 0:
                            inst = e.matmul(
                                PB[0:65, 2048 + 128 * qi:2048 + 128 * (qi + 1)],
                                VW[:, vsl, vq, 65 * h:65 * h + 65],
                                PSTG[:, psl, co:co + 128], start=st, stop=sp)
                        else:
                            inst = e.matmul(
                                PB[0:1, 2048 + 128 * qi:2048 + 128 * (qi + 1)],
                                ONECB[:], PSTG[:, psl, co:co + 128],
                                start=False, stop=sp)
                            inst = e.matmul(
                                PB[64:128, 2048 + 128 * qi:2048 + 128 * (qi + 1)],
                                VW[:, vsl, vq, 65 * h:65 * h + 64],
                                PSTG[:, psl, co:co + 128], start=False, stop=sp,
                                tile_position=(0, 64))
                    return inst
                S.add("tensor", pv,
                      reads=[("pstd", psl), ("vw", (kj // 4) % 2, kj % 4),
                             ("vw1", (kj // 4) % 2), "const", ("pb", 4)],
                      writes=[("pb", 4), ("pst", psl)],
                      name=f"pv{l}_{w}_{h}_{kj}")
            if l == 0 and w == 0 and h <= 1 and tap("pv0"):
                nm = f"pv{h}"
                tapdram(nm, [128, 512])
                def pvdump(e):
                    return e.tensor_copy(STT[:, 2, :], PB[:, 2048:2560])
                S.add("vector", pvdump, reads=[("pb", 4), ("stt", 2)],
                      writes=[("stt", 2), ("pvdmp",)], name=f"pvd{h}")
                S.add("gpsimd",
                      (lambda e, nm=nm: e.dma_start(out=dbg[nm], in_=STT[:, 2, :])),
                      reads=[("pvdmp",), ("stt", 2)], name=f"tap_{nm}")
            zrow = 64 if hp == 0 else 0
            def zi(e, zrow=zrow):
                return e.reciprocal(STT[zrow:zrow + 1, 2, :],
                                    PB[zrow:zrow + 1, 2048:2560])
            S.add("vector", zi, reads=[("pb", 4), ("stt", 2), ("stt", 1)],
                  writes=[("sttz",), ("stt", 2), ("stt", 1)], name=f"zi{l}_{w}_{h}")
            def zb(e, zrow=zrow, hp=hp):
                tp = (0, 64) if hp else None
                return e.matmul(PB[64 * hp:64 * hp + 64, 2560:3072],
                                ONE1F[zrow:zrow + 1, 0:64],
                                STT[zrow:zrow + 1, 2, :],
                                start=True, stop=True, tile_position=tp)
            S.add("tensor", zb, reads=[("sttz",), "const", ("pb", 5)],
                  writes=[("pb", 5)], name=f"zb{l}_{w}_{h}")
            def oev(e, hp=hp, c2=c2, w=w):
                e.tensor_copy(STT[64 * hp:64 * hp + 64, 2, :],
                              PB[64 * hp:64 * hp + 64, 2048:2560])
                return e.tensor_tensor(
                    OFM[64 * hp:64 * hp + 64, c2, w * W512:(w + 1) * W512],
                    STT[64 * hp:64 * hp + 64, 2, :],
                    PB[64 * hp:64 * hp + 64, 2560:3072], ALU.mult)
            S.add("vector", oev,
                  reads=[("pb", 4), ("pb", 5), ("ofm", c2, w, hp), ("stt", 2)],
                  writes=[("ofm", c2, w, hp), ("pb", 4), ("pb", 5), ("stt", 2)],
                  name=f"oev{l}_{w}_{h}")

    if tap(f"ofm{l}"):
        tapdram(f"ofm{l}", [128, 4, T])
        S.add("gpsimd", lambda e: e.dma_start(out=dbg[f"ofm{l}"], in_=OFM[:]),
              reads=[("ofm", c2, w, hp) for c2 in range(4) for w in range(NW)
                     for hp in range(2)], name=f"tap_ofm{l}")

    for n in range(C):
        def ldwo(e, n=n, l=l):
            return e.dma_start(out=WOB[:], in_=d_wo[l, :, n, :, :])
        S.add("gpsimd", ldwo, reads=[("wob",)], writes=[("wobd",)],
              name=f"ldwo{l}_{n}")
        for w in range(NW):
            pbk = (n * NW + w) % 2
            def mmo(e, n=n, w=w, pbk=pbk):
                for c2 in range(4):
                    inst = e.matmul(bank(pbk), WOB[:, 128 * c2:128 * (c2 + 1)],
                                    OFM[:, c2, w * W512:(w + 1) * W512],
                                    start=(c2 == 0), stop=(c2 == 3))
                return inst
            S.add("tensor", mmo,
                  reads=[("wobd",), ("pb", pbk)] +
                        [("ofm", c2, w, hp) for c2 in range(4) for hp in range(2)],
                  writes=[("pb", pbk)] + ([("wob",)] if w == NW - 1 else []),
                  name=f"mmo{l}_{n}_{w}")
            def yadd(e, n=n, w=w, pbk=pbk):
                return e.tensor_tensor(XRc(n, w * W512, W512),
                                       XRc(n, w * W512, W512), bank(pbk), ALU.add)
            S.add("vector", yadd, reads=[("pb", pbk), ("xr", w)],
                  writes=[("xr", w), ("pb", pbk), ("wo_done", n, w)],
                  name=f"yadd{l}_{n}_{w}")
            def ymir2(e, n=n, w=w):
                return e.copy(out=ABFc(n, w * W512 + 4, W512),
                              in_=XRc(n, w * W512, W512))
            S.add("scalar", ymir2, reads=[("wo_done", n, w), ("xr", w)],
                  writes=[YK(n, w)], name=f"ym2{l}_{n}_{w}")

    if tap(f"ywo{l}"):
        tapdram(f"ywo{l}", [128, C * T])
        S.add("gpsimd", lambda e: e.dma_start(out=dbg[f"ywo{l}"], in_=XRt[:]),
              reads=[("wo_done", n, w) for n in range(C) for w in range(NW)],
              name=f"tap_ywo{l}")

    for w in range(NW):
        def rhs_bf(c, w=w):
            off = w * W512 + (3 if c >= 4 else 4)
            return ABFc(c, off, W512)
        rd = [YK(c, w) for c in range(C)]
        if w > 0:
            rd += [YK(c, w - 1) for c in range(4, C)]
        ln_stats(f"li{l}w{w}", rhs_bf, rd, bf=True)
        for c in range(C):
            def ia(e, c=c, w=w):
                off = w * W512 + (3 if c >= 4 else 4)
                return e.tensor_tensor(STT[:, 2, :], ABFc(c, off, W512),
                                       STT[:, 0, :], ALU.mult)
            S.add("vector", ia, reads=[("stt", 0), YK(c, w), ("stt", 2)] +
                  ([YK(c, w - 1)] if (w > 0 and c >= 4) else []),
                  writes=[("stt", 2)], name=f"ila{l}_{w}_{c}")
            def ib(e):
                return e.tensor_tensor(STT[:, 2, :], STT[:, 2, :],
                                       STT[:, 1, :], ALU.add)
            S.add("vector", ib, reads=[("stt", 1)], writes=[("stt", 2)],
                  name=f"ilb{l}_{w}_{c}")
            def ic(e, c=c, w=w):
                return e.activation(out=LBFc(c, w * W512, W512),
                                    in_=STT[:, 2, :], func=AF.Identity,
                                    scale=GBC[:, c:c + 1],
                                    bias=GBC[:, C + c:C + c + 1])
            S.add("scalar", ic, reads=[("stt", 2), ("lsm",)],
                  writes=[("lbf", c, w)], name=f"ilc{l}_{w}_{c}")

    if tap(f"lbf{l}"):
        tapdram(f"lbf{l}", [128, C * T])
        S.add("gpsimd", lambda e: e.dma_start(out=dbg[f"lbf{l}"], in_=L["LBF"][:]),
              reads=[("lbf", c, w) for c in range(C) for w in range(NW)],
              name=f"tap_lbf{l}")

    S.virt(writes=(["rot"] + [("qw", c2) for c2 in range(4)] +
                   [("kw", s, c2) for s in range(2) for c2 in range(4)] +
                   [("vw", s, v) for s in range(2) for v in range(4)] +
                   [("vw1", s) for s in range(2)] +
                   [("mres", j, tq) for j in range(NK2A) for tq in range(NW)] +
                   [("mresd", j, tq) for j in range(NK2A) for tq in range(NW)] +
                   [("mrb", j) for j in range(NK2B)] +
                   [("mrbd", j) for j in range(NK2B)] +
                   [("mrl", j) for j in range(NK2B)]),
           name=f"retag_attn2ffn{l}")

    for j in range(K2N):
        for tq in range(NW):
            for part in range(2):
                n = 2 * j + part
                wsl = n % 3
                fkey = ("wp", wsl)
                dkey = ("wpd", wsl)
                if tq == 0:
                    def ldw1(e, n=n, l=l, wsl=wsl):
                        return e.dma_start(out=WPB[:, wsl, :],
                                           in_=d_w1[l, :, n, :, :])
                    S.add("gpsimd", ldw1, reads=[fkey], writes=[dkey],
                          name=f"ldw1{l}_{n}")
                pbk = part * 2 + tq % 2
                def mm1(e, n=n, tq=tq, pbk=pbk, wsl=wsl):
                    for c in range(C):
                        inst = e.matmul(bank(pbk),
                                        WPB[:, wsl, 128 * c:128 * (c + 1)],
                                        LBFc(c, tq * W512, W512),
                                        start=(c == 0), stop=(c == C - 1))
                    return inst
                S.add("tensor", mm1,
                      reads=[dkey, ("pb", pbk)] +
                            [("lbf", c, tq) for c in range(C)],
                      writes=[("pb", pbk)] + ([fkey] if tq == NW - 1 else []),
                      name=f"mm1{l}_{n}_{tq}")
            pa = tq % 2
            pg = 2 + tq % 2
            def gact(e, j=j, pg=pg):
                return e.activation(out=SQB[:, 0, :], in_=bank(pg), func=AF.Gelu,
                                    bias=B1C[:, K2N + j:K2N + j + 1])
            S.add("scalar", gact, reads=[("pb", pg), ("sqb", 0), ("lsm",)],
                  writes=[("sqbd", 0), ("pb", pg)], name=f"ga{l}_{j}_{tq}")
            if j < NK2A:
                mdst = (lambda j=j, tq=tq: MRES[:, j, tq * W512:(tq + 1) * W512])
                mkey, mdkey = ("mres", j, tq), ("mresd", j, tq)
            else:
                msl2 = (j - NK2A) % NK2H
                mdst = (lambda msl2=msl2: MRB[:, msl2, :])
                mkey, mdkey = ("mrb", msl2), ("mrbd", msl2)
            def mmul(e, j=j, pa=pa, mdst=mdst):
                return e.scalar_tensor_tensor(mdst(), bank(pa),
                                              B1C[:, j:j + 1], SQB[:, 0, :],
                                              ALU.add, ALU.mult)
            S.add("vector", mmul,
                  reads=[("pb", pa), ("sqbd", 0), ("lsm",), mkey],
                  writes=[("pb", pa), ("sqb", 0), mdkey, mkey],
                  name=f"mmul{l}_{j}_{tq}")
            if j >= NK2A:
                def msp(e, j=j, tq=tq):
                    return e.dma_start(
                        out=d_m[:, j - NK2A, tq * W512:(tq + 1) * W512],
                        in_=MRB[:, (j - NK2A) % NK2H, :])
                S.add("sync", msp, reads=[mdkey],
                      writes=[("msp", j - NK2A, tq), mkey],
                      name=f"msp{l}_{j}_{tq}")

    for n in range(C):
        wsl = n % 3
        def ldw2(e, n=n, l=l, wsl=wsl):
            return e.dma_start(out=WPB[:, wsl, 0:128 * NK2A],
                               in_=d_w2[l, :, n, 0:NK2A, :])
        S.add("gpsimd", ldw2, reads=[("wp", wsl)], writes=[("wpd", wsl)],
              name=f"ldw2A{l}_{n}")
        for tq in range(NW):
            pbk = (n * NW + tq) % 2
            def mm2(e, n=n, tq=tq, pbk=pbk, wsl=wsl):
                for j in range(NK2A):
                    inst = e.matmul(bank(pbk), WPB[:, wsl, 128 * j:128 * (j + 1)],
                                    MRES[:, j, tq * W512:(tq + 1) * W512],
                                    start=(j == 0), stop=(j == NK2A - 1))
                return inst
            S.add("tensor", mm2,
                  reads=[("wpd", wsl), ("pb", pbk)] +
                        [("mresd", j, tq) for j in range(NK2A)],
                  writes=[("pb", pbk)] + ([("wp", wsl)] if tq == NW - 1 else []),
                  name=f"mm2A{l}_{n}_{tq}")
            def y2a(e, n=n, tq=tq, pbk=pbk):
                return e.scalar_tensor_tensor(
                    XRc(n, tq * W512, W512), bank(pbk),
                    GBC[:, 2 * C + n:2 * C + n + 1],
                    XRc(n, tq * W512, W512), ALU.add, ALU.add)
            S.add("vector", y2a, reads=[("pb", pbk), ("xr", tq), ("lsm",)],
                  writes=[("xr", tq), ("pb", pbk), ("w2p", n, tq)],
                  name=f"y2a{l}_{n}_{tq}")
    for tq in range(NW):
        for gh in range(2):
            jlo = NK2A + gh * NK2H
            jhi = jlo + NK2H
            for j in range(jlo, jhi):
                def mrl(e, j=j, tq=tq, jlo=jlo):
                    return e.dma_start(
                        out=MRB[:, j - jlo, :],
                        in_=d_m[:, j - NK2A, tq * W512:(tq + 1) * W512])
                S.add("sync", mrl, reads=[("msp", j - NK2A, tq), ("mrb", j - jlo)],
                      writes=[("mrl", j - jlo)], name=f"mrl{l}_{tq}_{j}")
            for n in range(C):
                def ldw2b(e, n=n, l=l, jlo=jlo, jhi=jhi):
                    return e.dma_start(
                        out=WPB[:, 0, :].tensor.ap()[:, 0:128 * NK2H] if False
                        else WPB[:].rearrange("p s n -> p (s n)")[:, 0:128 * NK2H],
                        in_=d_w2[l, :, n, jlo:jhi, :])
                S.add("gpsimd", ldw2b, reads=[("wp", 0), ("wp", 1)],
                      writes=[("wpd", 0), ("wpd", 1)], name=f"ldw2B{l}_{tq}_{gh}_{n}")
                pbk = (n + tq) % 2
                def mm2b(e, n=n, tq=tq, pbk=pbk, jlo=jlo, jhi=jhi):
                    wflat = WPB[:].rearrange("p s n -> p (s n)")
                    for j in range(jlo, jhi):
                        inst = e.matmul(
                            bank(pbk),
                            wflat[:, 128 * (j - jlo):128 * (j - jlo + 1)],
                            MRB[:, j - jlo, :],
                            start=(j == jlo), stop=(j == jhi - 1))
                    return inst
                S.add("tensor", mm2b,
                      reads=[("wpd", 0), ("wpd", 1), ("pb", pbk)] +
                            [("mrl", j) for j in range(NK2H)],
                      writes=[("pb", pbk), ("wp", 0), ("wp", 1)] +
                             ([("mrb", j) for j in range(NK2H)]
                              if n == C - 1 else []),
                      name=f"mm2B{l}_{tq}_{gh}_{n}")
                def y2c(e, n=n, tq=tq, pbk=pbk):
                    return e.tensor_tensor(XRc(n, tq * W512, W512),
                                           XRc(n, tq * W512, W512),
                                           bank(pbk), ALU.add)
                S.add("vector", y2c,
                      reads=[("pb", pbk), ("xr", tq), ("w2p", n, tq)],
                      writes=[("xr", tq), ("pb", pbk), ("w2d", n, tq, gh)],
                      name=f"y2c{l}_{tq}_{gh}_{n}")
                if gh == 1:
                    def ymir3(e, n=n, tq=tq):
                        return e.copy(out=ABFc(n, tq * W512 + 4, W512),
                                      in_=XRc(n, tq * W512, W512))
                    S.add("scalar", ymir3,
                          reads=[("w2d", n, tq, 1), ("xr", tq)],
                          writes=[YK(n, tq)], name=f"ym3{l}_{tq}_{n}")

    if tap(f"y{l + 1}"):
        tapdram(f"y{l + 1}", [128, C * T])
        S.add("gpsimd", lambda e: e.dma_start(out=dbg[f"y{l + 1}"], in_=XRt[:]),
              reads=[("w2d", n, tq, 1) for n in range(C) for tq in range(NW)],
              name=f"tap_y{l + 1}")


def kernel(**inputs):
    cst = _constants()
    w = _prep_weights(inputs)
    x = np.asarray(inputs["L0_attn_query"], np.float32)
    nc, _ = build_program()
    base = {}
    for k in WEIGHT_NAMES:
        base[k] = w[k]
    for k in CONST_NAMES:
        base[k] = np.asarray(cst[k])
    in_maps = []
    for b in range(8):
        m = dict(base)
        m["x"] = np.ascontiguousarray(x[b])
        in_maps.append(m)
    res = run_bass_kernel_spmd(nc, in_maps, core_ids=list(range(8)))
    return np.stack([res.results[b]["out"] for b in range(8)], 0)


# version 03
# speedup vs baseline: 1.0565x; 1.0565x over previous
import numpy as np
import ml_dtypes
from contextlib import ExitStack

import concourse.bass as bass
import concourse.mybir as mybir
from concourse.bass_utils import run_bass_kernel_spmd

F32 = mybir.dt.float32
BF16 = mybir.dt.bfloat16
AF = mybir.ActivationFunctionType
ALU = mybir.AluOpType
AXX = mybir.AxisListType.X
f32 = np.float32

T = 2048
D = 1024
C = 8
NW = 4
W512 = 512
K9 = 9
NH = 8
QKN = 12
HID = 2730
HIDP = 2816
K2N = 22
W1N = 44
NK2A = 4
NK2B = K2N - NK2A
NK2H = NK2B // 2
LNEPS = 1e-5

_CONST_CACHE = {}


def _constants():
    if _CONST_CACHE:
        return _CONST_CACHE
    c = {}
    c["ident_bf"] = np.eye(128, dtype=ml_dtypes.bfloat16)
    c["ident_f32"] = np.eye(128, dtype=np.float32)
    R = np.zeros((64, 64), np.float32)
    for d in range(32):
        R[d + 32, d] = -1.0
        R[d, d + 32] = 1.0
    Rb = np.zeros((128, 128), np.float32)
    Rb[0:64, 0:64] = R
    Rb[64:128, 64:128] = R
    c["rotmat"] = Rb.astype(ml_dtypes.bfloat16)
    d = np.arange(64, dtype=np.float64)
    half = (d % 32) * 2.0
    inv_freq = 1.0 / (10000.0 ** (half / 64.0))
    t = np.arange(T, dtype=np.float64)
    freqs = t[None, :] * inv_freq[:, None]
    base = (half + 0.4 * 64.0) / (1.4 * 64.0)
    power = (t - T // 2) / 256.0
    scale = base[:, None] ** power[None, :]
    cs, sn = np.cos(freqs), np.sin(freqs)

    def dup(a):
        return np.vstack([a, a]).astype(np.float32).astype(ml_dtypes.bfloat16)
    c["cosq"] = dup(8.0 * cs * scale)
    c["sinq"] = dup(8.0 * sn * scale)
    c["cosk"] = dup(cs / scale)
    c["sink"] = dup(sn / scale)
    r = np.arange(128)
    c["tri_diag"] = (r[:, None] <= r[None, :]).astype(ml_dtypes.bfloat16)
    c["tri_edge"] = (r[:, None] >= r[None, :]).astype(ml_dtypes.bfloat16)
    hb = np.zeros((128, 128), np.float32)
    hb[0:64, 0:64] = 1.0
    hb[64:128, 64:128] = 1.0
    c["hones"] = hb.astype(ml_dtypes.bfloat16)
    c["lncol_f32"] = np.full((128, 128), 1.0 / D, np.float32)
    c["lncol_bf"] = np.full((128, 128), 1.0 / D, np.float32).astype(ml_dtypes.bfloat16)
    c["ones1_bf"] = np.ones((1, 128), ml_dtypes.bfloat16)
    c["ones1_f32"] = np.ones((128, 128), np.float32)
    c["zeros_bf"] = np.zeros((128, 128), ml_dtypes.bfloat16)
    c["onescol_bf"] = np.ones((128, 1), ml_dtypes.bfloat16)
    s4 = np.zeros((128, 8), np.float32)
    for ct in range(4):
        for hp in range(2):
            for kk in range(K9):
                s4[32 * ct + 9 * hp + kk, 2 * ct + hp] = 1.0
    c["sel4"] = s4.astype(ml_dtypes.bfloat16)
    sz = np.zeros((8, 128), np.float32)
    for ct in range(4):
        for hp in range(2):
            for kk in range(K9):
                sz[2 * ct + hp, 32 * ct + 9 * hp + kk] = 1.0
    c["selzrow"] = sz
    B = np.zeros((K9, 4, 128, 128), np.float32)
    for kk in range(K9):
        for ct in range(4):
            for hp in range(2):
                B[kk, ct, 32 * ct + 9 * hp + kk, 64 * hp:64 * (hp + 1)] = 1.0
    c["bsel"] = np.ascontiguousarray(B.transpose(2, 0, 1, 3)).astype(ml_dtypes.bfloat16)
    _CONST_CACHE.update(c)
    return c


CONST_NAMES = ("ident_bf", "ident_f32", "rotmat", "cosq", "sinq", "cosk", "sink",
               "tri_diag", "tri_edge", "hones", "lncol_f32", "lncol_bf",
               "ones1_bf", "ones1_f32", "onescol_bf", "zeros_bf", "sel4",
               "selzrow", "bsel")
WEIGHT_NAMES = ("cw_stat", "wqkv", "wo", "w1", "w2", "b1a", "b1g", "qscol", "kscol",
                "lng", "lnb", "b2")


def _prep_weights(inp):
    w = {}
    cw = np.asarray(inp["L0_attn_conv_weight"], np.float32)
    Wk = np.zeros((128, C, K9, 18), np.float32)
    for h in range(16):
        cc, hp = h // 2, h % 2
        for kk in range(K9):
            Wk[64 * hp:64 * (hp + 1), cc, kk, 9 * hp:9 * (hp + 1)] = \
                cw[h * 9:(h + 1) * 9, :, kk].T
    w["cw_stat"] = np.ascontiguousarray(Wk.reshape(128, C * K9 * 18))
    wqkv = np.asarray(inp["L0_ffn_wqkv"], np.float32)
    w["wqkv"] = np.ascontiguousarray(
        wqkv.reshape(2, C, 128, QKN, 128).transpose(0, 2, 3, 1, 4))
    wo = np.asarray(inp["L0_ffn_wo"], np.float32)
    w["wo"] = np.ascontiguousarray(
        wo.reshape(2, 4, 128, 8, 128).transpose(0, 2, 3, 1, 4))
    w1 = np.asarray(inp["L0_ffn_w1"], np.float32)
    a, g = w1[:, :, :HID], w1[:, :, HID:]
    ap = np.zeros((2, D, HIDP), np.float32)
    gp = np.zeros((2, D, HIDP), np.float32)
    ap[:, :, :HID] = a
    gp[:, :, :HID] = g
    w1p = np.zeros((2, D, W1N, 128), np.float32)
    w1p[:, :, 0::2, :] = ap.reshape(2, D, K2N, 128)
    w1p[:, :, 1::2, :] = gp.reshape(2, D, K2N, 128)
    w["w1"] = np.ascontiguousarray(
        w1p.reshape(2, C, 128, W1N, 128).transpose(0, 2, 3, 1, 4))
    b1 = np.asarray(inp["L0_ffn_b1"], np.float32)
    b1a = np.zeros((2, HIDP), np.float32)
    b1g = np.zeros((2, HIDP), np.float32)
    b1a[:, :HID] = b1[:, :HID]
    b1g[:, :HID] = b1[:, HID:]
    w["b1a"] = np.ascontiguousarray(b1a.reshape(2, K2N, 128).transpose(0, 2, 1))
    w["b1g"] = np.ascontiguousarray(b1g.reshape(2, K2N, 128).transpose(0, 2, 1))
    w2 = np.asarray(inp["L0_ffn_w2"], np.float32)
    w2p = np.zeros((2, HIDP, D), np.float32)
    w2p[:, :HID, :] = w2
    w["w2"] = np.ascontiguousarray(
        w2p.reshape(2, K2N, 128, 8, 128).transpose(0, 2, 3, 1, 4))
    qs = np.asarray(inp["L0_ffn_q_scale"], np.float32)
    ks = np.asarray(inp["L0_ffn_k_scale"], np.float32)
    w["qscol"] = np.ascontiguousarray(np.concatenate([qs, qs], 1).reshape(2, 128, 1))
    w["kscol"] = np.ascontiguousarray(np.concatenate([ks, ks], 1).reshape(2, 128, 1))
    for nm, key in (("lng", "L0_ffn_ln_g"), ("lnb", "L0_ffn_ln_b"), ("b2", "L0_ffn_b2")):
        v = np.asarray(inp[key], np.float32)
        w[nm] = np.ascontiguousarray(v.reshape(2, C, 128).transpose(0, 2, 1))
    return w


ENGINES = ("tensor", "vector", "scalar", "sync", "gpsimd")
DMA_ENG = ("sync", "gpsimd")


class _Step:
    __slots__ = ("engine", "fn", "name", "deps", "need_inc", "inc_val", "waits",
                 "vdeps")

    def __init__(self, engine, fn, name):
        self.engine = engine
        self.fn = fn
        self.name = name
        self.deps = []
        self.need_inc = False
        self.inc_val = None
        self.waits = None
        self.vdeps = None


SERIAL = False


class Sched:
    def __init__(self):
        self.steps = {e: [] for e in ENGINES}
        self._writer = {}
        self._readers = {}
        self.n = 0
        self._prev = None

    def _collect(self, reads, writes):
        dep = set()
        for r in reads:
            w = self._writer.get(r)
            if w is not None:
                dep.add(w)
        for wk in writes:
            w = self._writer.get(wk)
            if w is not None:
                dep.add(w)
            for r in self._readers.get(wk, ()):
                dep.add(r)
        out = set()
        for d in dep:
            if d.vdeps is not None:
                out.update(d.vdeps)
            else:
                out.add(d)
        return out

    def add(self, engine, fn, reads=(), writes=(), name=""):
        s = _Step(engine, fn, name or f"s{self.n}")
        self.n += 1
        dep = self._collect(reads, writes)
        if SERIAL and self._prev is not None:
            dep.add(self._prev)
        self._prev = s
        for r in reads:
            self._readers.setdefault(r, []).append(s)
        for wk in writes:
            self._writer[wk] = s
            self._readers[wk] = []
        for d in dep:
            if d is s:
                continue
            if d.engine == engine and d.engine not in DMA_ENG:
                continue
            d.need_inc = True
            s.deps.append(d)
        if engine in DMA_ENG:
            s.need_inc = True
        self.steps[engine].append(s)
        return s

    def virt(self, writes=(), reads=(), name="virt"):
        s = _Step(None, None, name)
        s.vdeps = self._collect(reads, writes)
        for r in reads:
            self._readers.setdefault(r, []).append(s)
        for wk in writes:
            self._writer[wk] = s
            self._readers[wk] = []
        return s

    def emit(self, blk, sems):
        for e in ENGINES:
            cnt = 0
            amt = 16 if e in DMA_ENG else 1
            for s in self.steps[e]:
                if s.need_inc:
                    cnt += amt
                    s.inc_val = cnt
        for e in ENGINES:
            hw = {x: 0 for x in ENGINES}
            for s in self.steps[e]:
                ws = []
                for d in sorted(s.deps, key=lambda d: d.inc_val):
                    if d.inc_val > hw[d.engine]:
                        hw[d.engine] = d.inc_val
                        ws.append((d.engine, d.inc_val))
                s.waits = ws
        amt = {e: (16 if e in DMA_ENG else 1) for e in ENGINES}

        def mk(e_name):
            def body(eng):
                for s in self.steps[e_name]:
                    for (de, v) in s.waits:
                        eng.wait_ge(sems[de], v)
                    inst = s.fn(eng)
                    if s.need_inc:
                        assert inst is not None, f"step {s.name} returned None"
                        inst.then_inc(sems[e_name], amt[e_name])
            return body
        blk.tensor(mk("tensor"))
        blk.vector(mk("vector"))
        blk.scalar(mk("scalar"))
        blk.sync(mk("sync"))
        blk.gpsimd(mk("gpsimd"))


def build_program(stage=99, debug_taps=()):
    cst = _constants()
    nc = bass.Bass("TRN2", target_bir_lowering=False)
    taps = set(debug_taps)

    xin = nc.dram_tensor("x", [T, D], F32, kind="ExternalInput").ap()
    out = nc.dram_tensor("out", [T, D], F32, kind="ExternalOutput").ap()
    d_cw = nc.dram_tensor("cw_stat", [128, C * K9 * 18], F32, kind="ExternalInput").ap()
    d_wqkv = nc.dram_tensor("wqkv", [2, 128, QKN, C, 128], F32, kind="ExternalInput").ap()
    d_wo = nc.dram_tensor("wo", [2, 128, 8, 4, 128], F32, kind="ExternalInput").ap()
    d_w1 = nc.dram_tensor("w1", [2, 128, W1N, C, 128], F32, kind="ExternalInput").ap()
    d_w2 = nc.dram_tensor("w2", [2, 128, 8, K2N, 128], F32, kind="ExternalInput").ap()
    d_b1a = nc.dram_tensor("b1a", [2, 128, K2N], F32, kind="ExternalInput").ap()
    d_b1g = nc.dram_tensor("b1g", [2, 128, K2N], F32, kind="ExternalInput").ap()
    d_qs = nc.dram_tensor("qscol", [2, 128, 1], F32, kind="ExternalInput").ap()
    d_ks = nc.dram_tensor("kscol", [2, 128, 1], F32, kind="ExternalInput").ap()
    d_lng = nc.dram_tensor("lng", [2, 128, C], F32, kind="ExternalInput").ap()
    d_lnb = nc.dram_tensor("lnb", [2, 128, C], F32, kind="ExternalInput").ap()
    d_b2 = nc.dram_tensor("b2", [2, 128, C], F32, kind="ExternalInput").ap()
    dc = {}
    for nm in CONST_NAMES:
        a = np.asarray(cst[nm])
        dt = BF16 if a.dtype == ml_dtypes.bfloat16 else F32
        dc[nm] = nc.dram_tensor(nm, list(a.shape), dt, kind="ExternalInput").ap()
    d_x1 = nc.dram_tensor("x1sp", [128, C, T], F32, kind="Internal").ap()
    d_m = nc.dram_tensor("msp", [128, NK2B, T], BF16, kind="Internal").ap()

    dbg = {}

    def tap(name):
        return name in taps

    def tapdram(name, shape, dtype=F32):
        dbg[name] = nc.dram_tensor("dbg_" + name, shape, dtype,
                                   kind="ExternalOutput").ap()
        return dbg[name]

    ctx = ExitStack()
    E = ctx.enter_context

    XR = E(nc.sbuf_tensor("XR", [128, C * T], F32))
    ABF = E(nc.sbuf_tensor("ABF", [128, C * (T + 8)], BF16))
    LBF = E(nc.sbuf_tensor("LBF", [128, C * T], BF16))
    R2 = E(nc.sbuf_tensor("R2", [128, 53376], mybir.dt.uint8))
    r2base = nc.lookup_mloc(R2).addr

    def r2v(name, shape, dtype, off):
        return nc.alloc_sbuf_tensor_at(name, shape, dtype, offset=r2base + off)

    BSEL = r2v("BSEL", [128, K9, 4, 128], BF16, 0)
    CWS = r2v("CWS", [128, C * K9 * 18], BF16, 9216)
    CONVS = r2v("CONVS", [128, 2, 512], BF16, 12096)
    ZR = r2v("ZR", [8, 2 * 512], F32, 14144)
    ACCW = r2v("ACCW", [128, C, W512], F32, 18432)
    STG = r2v("STG", [128, 4, D], F32, 34816)
    COSQ = r2v("COSQ", [128, T], BF16, 0)
    SINQ = r2v("SINQ", [128, T], BF16, 4096)
    COSK = r2v("COSK", [128, T], BF16, 8192)
    SINK = r2v("SINK", [128, T], BF16, 12288)
    QW = r2v("QW", [128, 1, 4, W512], BF16, 16384)
    KW = r2v("KW", [128, 2, 4, W512], BF16, 20480)
    VW = r2v("VW", [128, 2, 4, 520], BF16, 28672)
    OFM = r2v("OFM", [128, 4, T], BF16, 36992)
    MRES = r2v("MRES", [128, NK2A, T], BF16, 0)
    MRB = r2v("MRB", [128, NK2H, W512], BF16, 16384)
    X1RR = r2v("X1RR", [128, 2, 512], F32, 0)
    STGO = r2v("STGO", [128, 4, D], F32, 36992)

    PSTG = E(nc.sbuf_tensor("PSTG", [128, 2, 512], BF16))
    MSTG = E(nc.sbuf_tensor("MSTG", [1, 2, W512], BF16))
    STT = E(nc.sbuf_tensor("STT", [128, 3, 512], F32))
    SQB = E(nc.sbuf_tensor("SQB", [128, 2, 512], BF16))
    QKS = SQB[:, 1, :]
    MAXC = E(nc.sbuf_tensor("MAXC", [128, 2], F32))
    IDB = E(nc.sbuf_tensor("IDB", [128, 128], BF16))
    IDF = E(nc.sbuf_tensor("IDF", [128, 128], F32))
    ROTM = E(nc.sbuf_tensor("ROTM", [128, 128], BF16))
    TRID = E(nc.sbuf_tensor("TRID", [128, 128], BF16))
    TRIE = E(nc.sbuf_tensor("TRIE", [128, 128], BF16))
    HON = E(nc.sbuf_tensor("HON", [128, 128], BF16))
    LNC32 = E(nc.sbuf_tensor("LNC32", [128, 128], F32))
    LNCBF = E(nc.sbuf_tensor("LNCBF", [128, 128], BF16))
    ONES1 = E(nc.sbuf_tensor("ONES1", [1, 128], BF16))
    ONE1F = E(nc.sbuf_tensor("ONE1F", [128, 128], F32))
    ONECB = E(nc.sbuf_tensor("ONECB", [128, 1], BF16))
    ZERB = E(nc.sbuf_tensor("ZERB", [128, 128], BF16))
    SEL4 = E(nc.sbuf_tensor("SEL4", [128, 8], BF16))
    SELZ = E(nc.sbuf_tensor("SELZ", [8, 128], F32))
    WQB = E(nc.sbuf_tensor("WQB", [128, 2, C * 128], BF16))
    WOB = E(nc.sbuf_tensor("WOB", [128, 4 * 128], BF16))
    WPB = E(nc.sbuf_tensor("WPB", [128, 3, C * 128], BF16))
    QSC = E(nc.sbuf_tensor("QSC", [128, 2], F32))
    GBC = E(nc.sbuf_tensor("GBC", [128, 3 * C], F32))
    B1C = E(nc.sbuf_tensor("B1C", [128, 2 * K2N], F32))

    PB = E(nc.psum_tensor("PB", [128, 3584], F32))
    PBF = E(nc.psum_tensor("PBF", [128, 1024], BF16))

    def bank(i, col0=0, ncol=512):
        return PB[:, 512 * i + col0: 512 * i + col0 + ncol]

    sems = {e: E(nc.semaphore("sem_" + e)) for e in ENGINES}
    blk = E(nc.Block())
    S = Sched()

    def XRc(c, lo, n):
        return XR[:, c * T + lo: c * T + lo + n]

    def ABFc(c, lo, n):
        return ABF[:, c * (T + 8) + lo: c * (T + 8) + lo + n]

    def LBFc(c, lo, n):
        return LBF[:, c * T + lo: c * T + lo + n]

    _consts = ((IDB[:], dc["ident_bf"]), (IDF[:], dc["ident_f32"]),
               (ROTM[:], dc["rotmat"]), (TRID[:], dc["tri_diag"]),
               (TRIE[:], dc["tri_edge"]), (HON[:], dc["hones"]),
               (LNC32[:], dc["lncol_f32"]), (LNCBF[:], dc["lncol_bf"]),
               (ONES1[:], dc["ones1_bf"]), (ONE1F[:], dc["ones1_f32"]),
               (ONECB[:], dc["onescol_bf"]), (ZERB[:], dc["zeros_bf"]),
               (SEL4[:], dc["sel4"]), (SELZ[:], dc["selzrow"]))
    for i, (dst, src) in enumerate(_consts):
        S.add("sync", (lambda e, dst=dst, src=src: e.dma_start(out=dst, in_=src)),
              writes=["const"] if i == len(_consts) - 1 else [],
              name=f"ld_const{i}")
    S.add("gpsimd", lambda e: e.dma_start(out=BSEL[:], in_=dc["bsel"]),
          writes=[], name="ld_bsel")
    S.add("gpsimd", lambda e: e.dma_start(out=CWS[:], in_=d_cw),
          writes=["convw"], name="ld_convw")

    def z_guards(e):
        for c in range(C):
            e.memset(ABF[:, c * (T + 8):c * (T + 8) + 4], 0)
            inst = e.memset(ABF[:, c * (T + 8) + T + 4:c * (T + 8) + T + 8], 0)
        return inst
    S.add("vector", z_guards, writes=["guards"], name="zguards")

    xv = xin.rearrange("(tt p) f -> p tt f", p=128)
    for qtr in range(4):
        def ld_q(e, qtr=qtr):
            return e.dma_start(out=STG[:], in_=xv[:, 4 * qtr:4 * (qtr + 1), :])
        S.add("sync", ld_q, writes=[("stg",)], name=f"ldx{qtr}")
        for tt4 in range(4):
            tt = 4 * qtr + tt4
            for cq in range(2):
                bk = (tt4 * 2 + cq) % 2
                def tp(e, tt4=tt4, cq=cq, bk=bk):
                    for c4 in range(4):
                        c = cq * 4 + c4
                        inst = e.transpose(
                            PB[:, 512 * bk + 128 * c4:512 * bk + 128 * (c4 + 1)],
                            STG[:, tt4, 128 * c:128 * (c + 1)], IDF[:])
                    return inst
                S.add("tensor", tp, reads=[("stg",), "const", ("pb", bk)],
                      writes=[("pb", bk)], name=f"tpx{tt}_{cq}")
                def ev(e, tt=tt, cq=cq, bk=bk):
                    for c4 in range(4):
                        c = cq * 4 + c4
                        inst = e.tensor_copy(
                            XRc(c, tt * 128, 128),
                            PB[:, 512 * bk + 128 * c4:512 * bk + 128 * (c4 + 1)])
                    return inst
                S.add("vector", ev, reads=[("pb", bk), ("stg",)],
                      writes=[("xr", tt // 4), ("pb", bk)], name=f"evx{tt}_{cq}")

    if tap("xr"):
        tapdram("xr", [128, C * T])
        S.add("gpsimd", lambda e: e.dma_start(out=dbg["xr"], in_=XR[:]),
              reads=[("xr", i) for i in range(4)], name="tap_xr")

    def ln_stats(tag, rhs, reads, bf=False):
        def mm_mean(e):
            for c in range(C):
                inst = e.matmul(bank(5), LNCBF[:] if bf else LNC32[:], rhs(c),
                                start=(c == 0), stop=(c == C - 1))
            return inst
        S.add("tensor", mm_mean, reads=list(reads) + ["const", ("pb", 5)],
              writes=[("pb", 5)], name=f"lnm_{tag}")
        for c in range(C):
            sl = 0
            def sq1(e, c=c, sl=sl):
                return e.activation(out=SQB[:, sl, :], in_=rhs(c), func=AF.Square)
            S.add("scalar", sq1, reads=list(reads) + [("sqb", sl)],
                  writes=[("sqbd", sl)], name=f"lnsq_{tag}_{c}")
            def mmsq(e, c=c, sl=sl):
                return e.matmul(bank(6), LNCBF[:], SQB[:, sl, :],
                                start=(c == 0), stop=(c == C - 1))
            S.add("tensor", mmsq, reads=[("sqbd", sl), "const", ("pb", 6)],
                  writes=[("pb", 6), ("sqb", sl)], name=f"lnsqm_{tag}_{c}")

        def stats(e):
            e.tensor_copy(STT[:, 2, :], bank(5))
            e.tensor_tensor(STT[:, 1, :], STT[:, 2, :], STT[:, 2, :], ALU.mult)
            e.scalar_tensor_tensor(STT[:, 1, :], STT[:, 1, :], -1.0,
                                   bank(6), ALU.mult, ALU.add)
            e.tensor_scalar_add(STT[:, 1, :], STT[:, 1, :], LNEPS)
            return e.reciprocal(STT[:, 1, :], STT[:, 1, :])
        S.add("vector", stats,
              reads=[("pb", 5), ("pb", 6), ("stt", 2), ("stt", 1), ("stt", 0)],
              writes=[("stt", 2), ("stt", 1), ("stt", 0), ("pb", 6), ("pb", 5)],
              name=f"lnv_{tag}")
        S.add("scalar",
              lambda e: e.activation(out=STT[:, 0, :], in_=STT[:, 1, :], func=AF.Sqrt),
              reads=[("stt", 1), ("stt", 0)], writes=[("stt", 0)], name=f"lnr_{tag}")

        def negm(e):
            return e.scalar_tensor_tensor(STT[:, 1, :], STT[:, 2, :], -1.0,
                                          STT[:, 0, :], ALU.mult, ALU.mult)
        S.add("vector", negm, reads=[("stt", 0), ("stt", 2), ("stt", 1)],
              writes=[("stt", 1)], name=f"lnn_{tag}")

    for w in range(NW):
        ln_stats(f"l1w{w}", lambda c, w=w: XRc(c, w * W512, W512), [("xr", w)])
        for c in range(C):
            def apa(e, c=c, w=w):
                return e.tensor_tensor(STT[:, 2, :], XRc(c, w * W512, W512),
                                       STT[:, 0, :], ALU.mult)
            S.add("vector", apa, reads=[("stt", 0), ("xr", w), ("stt", 2)],
                  writes=[("stt", 2)], name=f"l1a{w}_{c}")
            def apb(e, c=c, w=w):
                return e.tensor_tensor(ABFc(c, w * W512 + 4, W512),
                                       STT[:, 2, :], STT[:, 1, :], ALU.add)
            S.add("vector", apb, reads=[("stt", 2), ("stt", 1), "guards"],
                  writes=[("ln1", c, w)], name=f"l1b{w}_{c}")

    if tap("ln1"):
        tapdram("ln1", [128, C * (T + 8)])
        S.add("gpsimd", lambda e: e.dma_start(out=dbg["ln1"], in_=ABF[:]),
              reads=[("ln1", c, w) for c in range(C) for w in range(NW)],
              name="tap_ln1")

    if stage >= 2:
        for w in range(NW):
            for half in range(2):
                def mml(e, w=w, half=half):
                    for ct in range(4):
                        c = half * 4 + ct
                        tp = (0, 32 * ct) if ct > 0 else None
                        for k in range(K9):
                            inst = e.matmul(
                                PB[32 * ct:32 * ct + 18,
                                   512 * half:512 * half + 512],
                                CWS[:, (c * K9 + k) * 18:(c * K9 + k) * 18 + 18],
                                ABFc(c, w * W512 + k, W512),
                                start=(k == 0), stop=(k == K9 - 1),
                                tile_position=tp)
                    return inst
                S.add("tensor", mml,
                      reads=[("ln1", half * 4 + ct, w) for ct in range(4)] +
                            ["convw", ("pb", half)],
                      writes=[("pb", half)], name=f"cvl{w}_{half}")
                def clamp(e, half=half):
                    return e.tensor_scalar_min(bank(half), bank(half), 60.0)
                S.add("vector", clamp, reads=[("pb", half)], writes=[("pb", half)],
                      name=f"cvc{w}_{half}")
                def expE(e, half=half):
                    return e.activation(out=CONVS[:, half, :], in_=bank(half),
                                        func=AF.Exp)
                S.add("scalar", expE, reads=[("pb", half), ("cv_e", half)],
                      writes=[("cv_e", half), ("pb", half)], name=f"cve{w}_{half}")
                def mmz(e, half=half):
                    return e.matmul(PB[0:8, 1024:1536], SEL4[:], CONVS[:, half, :],
                                    start=True, stop=True)
                S.add("tensor", mmz, reads=[("cv_e", half), "const", ("pb", 2)],
                      writes=[("pb", 2)], name=f"cvz{w}_{half}")
                def zinv(e, half=half):
                    e.activation(out=ZR[:, half * 512:(half + 1) * 512],
                                 in_=PB[0:8, 1024:1536], func=AF.Ln)
                    return e.activation(out=ZR[:, half * 512:(half + 1) * 512],
                                        in_=ZR[:, half * 512:(half + 1) * 512],
                                        func=AF.Exp, scale=f32(-1.0))
                S.add("scalar", zinv, reads=[("pb", 2), ("cv_zi", half)],
                      writes=[("cv_zi", half), ("pb", 2)],
                      name=f"cvzi{w}_{half}")
                def mmzr(e, half=half):
                    return e.matmul(bank(3), SELZ[:],
                                    ZR[:, half * 512:(half + 1) * 512],
                                    start=True, stop=True)
                S.add("tensor", mmzr, reads=[("cv_zi", half), "const", ("pb", 3)],
                      writes=[("pb", 3)], name=f"cvzr{w}_{half}")
                def wnorm(e, half=half):
                    return e.tensor_tensor(CONVS[:, half, :], CONVS[:, half, :],
                                           bank(3), ALU.mult)
                S.add("vector", wnorm, reads=[("cv_e", half), ("pb", 3)],
                      writes=[("cv_w", half), ("cv_e", half), ("pb", 3)],
                      name=f"cvw{w}_{half}")
            for c in range(C):
                half, ct = c // 4, c % 4
                for k in range(K9):
                    pb = 3 + (k % 2)
                    def mmb(e, k=k, ct=ct, half=half, pb=pb):
                        return e.matmul(bank(pb), BSEL[:, k, ct, :],
                                        CONVS[:, half, :], start=True, stop=True)
                    S.add("tensor", mmb,
                          reads=[("cv_w", half), "convw", ("pb", pb)],
                          writes=[("pb", pb)], name=f"cvb{w}_{c}_{k}")
                    def mix(e, k=k, c=c, w=w, pb=pb):
                        if k == 0:
                            return e.tensor_tensor(ACCW[:, c, :],
                                                   ABFc(c, w * W512 + k, W512),
                                                   bank(pb), ALU.mult)
                        e.tensor_tensor(STT[:, 2, :], ABFc(c, w * W512 + k, W512),
                                        bank(pb), ALU.mult)
                        return e.tensor_tensor(ACCW[:, c, :], ACCW[:, c, :],
                                               STT[:, 2, :], ALU.add)
                    S.add("vector", mix,
                          reads=[("pb", pb), ("ln1", c, w), ("acc", c), ("stt", 2)],
                          writes=[("acc", c), ("pb", pb), ("stt", 2)],
                          name=f"cvm{w}_{c}_{k}")
            if tap("conv") and w == 0:
                tapdram("conv", [128, C, W512])
                S.add("gpsimd", lambda e: e.dma_start(out=dbg["conv"], in_=ACCW[:]),
                      reads=[("acc", c) for c in range(C)], name="tap_conv")
            ln_stats(f"lcw{w}", lambda c: ACCW[:, c, :],
                     [("acc", c) for c in range(C)])
            for c in range(C):
                def xa(e, c=c):
                    return e.tensor_tensor(STT[:, 2, :], ACCW[:, c, :],
                                           STT[:, 0, :], ALU.mult)
                S.add("vector", xa, reads=[("stt", 0), ("acc", c), ("stt", 2)],
                      writes=[("stt", 2)], name=f"x1a{w}_{c}")
                def xb(e, c=c):
                    return e.tensor_tensor(STT[:, 2, :], STT[:, 2, :],
                                           STT[:, 1, :], ALU.add)
                S.add("vector", xb, reads=[("stt", 1)], writes=[("stt", 2)],
                      name=f"x1b{w}_{c}")
                def xc(e, c=c, w=w):
                    return e.tensor_tensor(XRc(c, w * W512, W512),
                                           XRc(c, w * W512, W512),
                                           STT[:, 2, :], ALU.add)
                S.add("vector", xc, reads=[("stt", 2), ("xr", w)],
                      writes=[("xr", w), ("acc", c)], name=f"x1c{w}_{c}")

        if tap("x1"):
            tapdram("x1", [128, C * T])
            S.add("gpsimd", lambda e: e.dma_start(out=dbg["x1"], in_=XR[:]),
                  reads=[("xr", i) for i in range(4)], name="tap_x1")

    if stage >= 3:
        def spill(e):
            return e.dma_start(out=d_x1, in_=XR[:].rearrange("p (c t) -> p c t", t=T))
        S.add("sync", spill, reads=[("xr", i) for i in range(4)], writes=["x1sp"],
              name="x1spill")
        for w in range(NW):
            ln_stats(f"l2w{w}", lambda c, w=w: XRc(c, w * W512, W512), [("xr", w)])
            for c in range(C):
                def ya(e, c=c, w=w):
                    return e.tensor_tensor(XRc(c, w * W512, W512),
                                           XRc(c, w * W512, W512),
                                           STT[:, 0, :], ALU.mult)
                S.add("vector", ya, reads=[("stt", 0), "x1sp"],
                      writes=[("xr", w)], name=f"y2a{w}_{c}")
                def yb(e, c=c, w=w):
                    return e.tensor_tensor(XRc(c, w * W512, W512),
                                           XRc(c, w * W512, W512),
                                           STT[:, 1, :], ALU.add)
                S.add("vector", yb, reads=[("stt", 1)], writes=[("xr", w)],
                      name=f"y2b{w}_{c}")
                def ymir(e, c=c, w=w):
                    return e.copy(out=ABFc(c, w * W512 + 4, W512),
                                  in_=XRc(c, w * W512, W512))
                S.add("scalar", ymir,
                      reads=[("xr", w)],
                      writes=[("ybf", c, w), ("ln1", c, w)], name=f"ymir{w}_{c}")

        if tap("y0"):
            tapdram("y0", [128, C * T])
            S.add("gpsimd", lambda e: e.dma_start(out=dbg["y0"], in_=XR[:]),
                  reads=[("xr", i) for i in range(4)], name="tap_y0")

    QWK = [("qw", c2) for c2 in range(4)]
    KWK = [("kw", s, c2) for s in range(2) for c2 in range(4)]
    VWK = [("vw", s, v) for s in range(2) for v in range(4)] + \
          [("vw1", s) for s in range(2)]
    OFK = [("ofm", c2, w, hp) for c2 in range(4) for w in range(NW)
           for hp in range(2)]
    MRK = [("mres", j, tq) for j in range(NK2A) for tq in range(NW)] + \
          [("mresd", j, tq) for j in range(NK2A) for tq in range(NW)] + \
          [("mrb", j) for j in range(NK2B)] + [("mrbd", j) for j in range(NK2B)] + \
          [("mrl", j) for j in range(NK2H)]

    if stage >= 4:
        nlayers = 2 if stage >= 5 else 1
        for l in range(nlayers):
            if l == 0:
                S.virt(writes=(["rot", ("stg",), "convw",
                                ("cv_w", 0), ("cv_w", 1), ("cv_e", 0), ("cv_e", 1),
                                ("cv_zi", 0), ("cv_zi", 1)] +
                               [("acc", c) for c in range(C)] +
                               QWK + KWK + VWK + OFK), name="retag_conv2attn")
            else:
                S.virt(writes=["rot"] + QWK + KWK + VWK + OFK + MRK,
                       name=f"retag_ffn2attn{l}")
            _ffn_layer(S, locals(), l)

    if stage >= 9:
        S.virt(writes=[("x1r", 0), ("x1r", 1), ("stgo2",)] + OFK + MRK,
               name="retag_exit")
        outv = out.rearrange("(tt p) f -> p tt f", p=128)
        for w in range(NW):
            ln_stats(f"lfw{w}", lambda c, w=w: XRc(c, w * W512, W512), [("xr", w)])
            for c in range(C):
                def fa(e, c=c, w=w):
                    return e.tensor_tensor(STT[:, 2, :], XRc(c, w * W512, W512),
                                           STT[:, 0, :], ALU.mult)
                S.add("vector", fa, reads=[("stt", 0), ("xr", w), ("stt", 2)],
                      writes=[("stt", 2)], name=f"fna{w}_{c}")
                def fb(e, c=c, w=w):
                    return e.tensor_tensor(STT[:, 2, :], STT[:, 2, :],
                                           STT[:, 1, :], ALU.add)
                S.add("vector", fb, reads=[("stt", 1)], writes=[("stt", 2)],
                      name=f"fnb{w}_{c}")
                sl = c % 2
                def ldx1(e, c=c, w=w, sl=sl):
                    return e.dma_start(out=X1RR[:, sl, :],
                                       in_=d_x1[:, c, w * W512:(w + 1) * W512])
                S.add("sync", ldx1, reads=["x1sp", ("x1r", sl)],
                      writes=[("x1rd", sl)], name=f"ldx1_{w}_{c}")
                def fc(e, c=c, sl=sl):
                    return e.tensor_tensor(STT[:, 2, :], STT[:, 2, :],
                                           X1RR[:, sl, :], ALU.add)
                S.add("vector", fc, reads=[("x1rd", sl)],
                      writes=[("stt", 2), ("x1r", sl)], name=f"fnc{w}_{c}")
                bk = c % 2
                def ftp(e, c=c, bk=bk):
                    for q in range(4):
                        inst = e.transpose(
                            PB[:, 512 * bk + 128 * q:512 * bk + 128 * (q + 1)],
                            STT[:, 2, 128 * q:128 * (q + 1)], IDF[:])
                    return inst
                S.add("tensor", ftp, reads=[("stt", 2), "const", ("pb", bk)],
                      writes=[("pb", bk)], name=f"ftp{w}_{c}")
                def fev(e, c=c, bk=bk):
                    for q in range(4):
                        inst = e.tensor_copy(
                            STGO[:, q, 128 * c:128 * (c + 1)],
                            PB[:, 512 * bk + 128 * q:512 * bk + 128 * (q + 1)])
                    return inst
                S.add("vector", fev, reads=[("pb", bk), ("stgo2",)],
                      writes=[("pb", bk), ("stgo", c)], name=f"fev{w}_{c}")
            def sto(e, w=w):
                return e.dma_start(out=outv[:, 4 * w:4 * (w + 1), :], in_=STGO[:])
            S.add("sync", sto, reads=[("stgo", c) for c in range(C)],
                  writes=["outd", ("stgo2",)], name=f"sto{w}")
        S.add("sync", lambda e: e.nop(), reads=["outd"], name="outwait")

    S.emit(blk, sems)
    ctx.close()
    return nc, dbg


def _ffn_layer(S, L, l):
    nc = L["nc"]
    PB, PBF = L["PB"], L["PBF"]
    bank = L["bank"]
    XRc, ABFc, LBFc = L["XRc"], L["ABFc"], L["LBFc"]
    STT, SQB, QKS, MAXC = L["STT"], L["SQB"], L["QKS"], L["MAXC"]
    WPB = L["WPB"]
    QW, KW, VW, OFM = L["QW"], L["KW"], L["VW"], L["OFM"]
    COSQ, SINQ, COSK, SINK = L["COSQ"], L["SINQ"], L["COSK"], L["SINK"]
    MRES, MRB = L["MRES"], L["MRB"]
    PSTG, MSTG = L["PSTG"], L["MSTG"]
    WQB, WOB = L["WQB"], L["WOB"]
    QSC, GBC, B1C = L["QSC"], L["GBC"], L["B1C"]
    HON, ROTM, IDB, IDF = L["HON"], L["ROTM"], L["IDB"], L["IDF"]
    TRID, TRIE = L["TRID"], L["TRIE"]
    ONES1, ONE1F, ONECB = L["ONES1"], L["ONE1F"], L["ONECB"]
    ZERB = L["ZERB"]
    d_wqkv, d_wo, d_w1, d_w2 = L["d_wqkv"], L["d_wo"], L["d_w1"], L["d_w2"]
    d_qs, d_ks = L["d_qs"], L["d_ks"]
    d_lng, d_lnb, d_b2 = L["d_lng"], L["d_lnb"], L["d_b2"]
    d_b1a, d_b1g, d_m = L["d_b1a"], L["d_b1g"], L["d_m"]
    dc = L["dc"]
    dbg, taps, tapdram = L["dbg"], L["taps"], L["tapdram"]
    ln_stats = L["ln_stats"]
    XRt = L["XR"]

    def tap(name):
        return name in taps

    YK = lambda c, w: ("ybf", c, w)

    for i, (dst, src) in enumerate((
            (QSC[:, 0:1], d_qs[l]), (QSC[:, 1:2], d_ks[l]),
            (GBC[:, 0:C], d_lng[l]), (GBC[:, C:2 * C], d_lnb[l]),
            (GBC[:, 2 * C:3 * C], d_b2[l]),
            (B1C[:, 0:K2N], d_b1a[l]), (B1C[:, K2N:2 * K2N], d_b1g[l]))):
        S.add("sync", (lambda e, dst=dst, src=src: e.dma_start(out=dst, in_=src)),
              writes=[("lsm",)], name=f"ldsm{l}_{i}")

    for i, (dst, src) in enumerate(((COSQ[:], dc["cosq"]), (SINQ[:], dc["sinq"]),
                                    (COSK[:], dc["cosk"]), (SINK[:], dc["sink"]))):
        S.add("sync", (lambda e, dst=dst, src=src: e.dma_start(out=dst, in_=src)),
              writes=["rot"], name=f"ld_rot{l}_{i}")

    for w in range(NW):
        for n in range(QKN):
            wsl = n % 2
            def ldwq(e, n=n, l=l, wsl=wsl):
                return e.dma_start(out=WQB[:, wsl, :], in_=d_wqkv[l, :, n, :, :])
            S.add("gpsimd", ldwq, reads=[("wqb", wsl)], writes=[("wqbd", wsl)],
                  name=f"ldwq{l}_{w}_{n}")
            pbk = n % 2
            yreads = [YK(c, w) for c in range(C)]
            if w > 0:
                yreads += [YK(c, w - 1) for c in range(4, C)]
            def mmq(e, n=n, w=w, pbk=pbk, wsl=wsl):
                for c in range(C):
                    off = w * W512 + (3 if c >= 4 else 4)
                    inst = e.matmul(bank(pbk), WQB[:, wsl, 128 * c:128 * (c + 1)],
                                    ABFc(c, off, W512),
                                    start=(c == 0), stop=(c == C - 1))
                return inst
            S.add("tensor", mmq, reads=[("wqbd", wsl), ("pb", pbk)] + yreads,
                  writes=[("pb", pbk), ("wqb", wsl)], name=f"mmqkv{l}_{w}_{n}")

            if n < 8:
                isq = n < 4
                c2 = n % 4
                tbl_c, tbl_s = (COSQ, SINQ) if isq else (COSK, SINK)
                scol = QSC[:, 0:1] if isq else QSC[:, 1:2]
                def sq(e, pbk=pbk):
                    return e.activation(out=SQB[:, 0, :], in_=bank(pbk),
                                        func=AF.Square)
                S.add("scalar", sq, reads=[("pb", pbk), ("sqb", 0)],
                      writes=[("sqbd", 0)], name=f"qksq{l}_{w}_{n}")
                def mmn(e):
                    return e.matmul(bank(2), HON[:], SQB[:, 0, :],
                                    start=True, stop=True)
                S.add("tensor", mmn, reads=[("sqbd", 0), "const", ("pb", 2)],
                      writes=[("pb", 2), ("sqb", 0)], name=f"qkn{l}_{w}_{n}")
                def ninv(e):
                    return e.tensor_scalar_max(bank(2), bank(2), 1e-24)
                S.add("vector", ninv, reads=[("pb", 2)],
                      writes=[("pb", 2)], name=f"qkni{l}_{w}_{n}")
                def nsq(e):
                    e.activation(out=STT[:, 0, :], in_=bank(2), func=AF.Ln)
                    return e.activation(out=STT[:, 0, :], in_=STT[:, 0, :],
                                        func=AF.Exp, scale=f32(-0.5))
                S.add("scalar", nsq, reads=[("pb", 2), ("stt", 0)],
                      writes=[("stt", 0), ("pb", 2)], name=f"qkns{l}_{w}_{n}")
                def qbf(e, pbk=pbk, scol=scol):
                    return e.activation(out=QKS, in_=bank(pbk),
                                        func=AF.Copy, scale=scol)
                S.add("scalar", qbf, reads=[("pb", pbk), ("lsm",), ("sqb", 1)],
                      writes=[("sqbd", 1), ("pb", pbk)], name=f"qkb{l}_{w}_{n}")
                def mmrot(e):
                    return e.matmul(bank(3), ROTM[:], QKS,
                                    start=True, stop=True)
                S.add("tensor", mmrot, reads=[("sqbd", 1), "const", ("pb", 3)],
                      writes=[("pb", 3)], name=f"qkr{l}_{w}_{n}")
                def rot1(e, w=w, tbl_c=tbl_c):
                    return e.tensor_tensor(STT[:, 2, :], QKS,
                                           tbl_c[:, w * W512:(w + 1) * W512],
                                           ALU.mult)
                S.add("vector", rot1, reads=[("sqbd", 1), "rot", ("stt", 2)],
                      writes=[("stt", 2), ("sqb", 1)], name=f"qkr1{l}_{w}_{n}")
                def rot2(e, w=w, tbl_s=tbl_s):
                    e.tensor_tensor(STT[:, 1, :], bank(3),
                                    tbl_s[:, w * W512:(w + 1) * W512], ALU.mult)
                    return e.tensor_tensor(STT[:, 2, :], STT[:, 2, :],
                                           STT[:, 1, :], ALU.add)
                S.add("vector", rot2,
                      reads=[("pb", 3), "rot", ("stt", 2), ("stt", 1)],
                      writes=[("stt", 2), ("stt", 1), ("pb", 3)],
                      name=f"qkr2{l}_{w}_{n}")
                if isq:
                    dkey, dst = ("qw", c2), (lambda e, c2=c2: QW[:, 0, c2, :])
                else:
                    dkey, dst = ("kw", w % 2, c2), \
                        (lambda e, w=w, c2=c2: KW[:, w % 2, c2, :])
                def rot3(e, dst=dst):
                    return e.tensor_tensor(dst(e), STT[:, 2, :],
                                           STT[:, 0, :], ALU.mult)
                S.add("vector", rot3, reads=[("stt", 2), ("stt", 0), "rot", dkey],
                      writes=[dkey], name=f"qkr3{l}_{w}_{n}")
            else:
                vc = n - 8
                vsl = w % 2
                def vbf(e, pbk=pbk):
                    return e.activation(out=QKS, in_=bank(pbk), func=AF.Copy)
                S.add("scalar", vbf, reads=[("pb", pbk), ("sqb", 1)],
                      writes=[("sqbd", 1), ("pb", pbk)], name=f"vb{l}_{w}_{vc}")
                def vtp(e):
                    for q in range(4):
                        inst = e.transpose(PBF[:, 128 * q:128 * (q + 1)],
                                           SQB[:, 1, 128 * q:128 * (q + 1)], IDB[:])
                    return inst
                S.add("tensor", vtp, reads=[("sqbd", 1), "const", ("pb", 7)],
                      writes=[("pb", 7)], name=f"vt{l}_{w}_{vc}")
                def vev(e, vsl=vsl, vc=vc):
                    for q in range(4):
                        src = PBF[:, 128 * q:128 * (q + 1)].rearrange(
                            "p (hh dd) -> p hh dd", hh=2)
                        inst = e.tensor_copy(
                            VW[:, vsl, q, 65 * 2 * vc:65 * 2 * vc + 130].rearrange(
                                "p (hh dd) -> p hh dd", dd=65)[:, :, 0:64], src)
                    return inst
                S.add("vector", vev,
                      reads=[("pb", 7), "rot", ("vw", vsl, vc), ("sqb", 1)],
                      writes=[("vw", vsl, vc), ("pb", 7), ("sqb", 1)],
                      name=f"vev{l}_{w}_{vc}")
        def vones(e, vsl=w % 2):
            for q in range(4):
                inst = e.memset(
                    VW[:, vsl, q, :].rearrange("p (hh dd) -> p hh dd",
                                               dd=65)[:, :, 64:65], 1.0)
            return inst
        S.add("vector", vones,
              reads=[("vw", w % 2, vc) for vc in range(4)] + ["rot", ("vw1", w % 2)],
              writes=[("vw1", w % 2)], name=f"vones{l}_{w}")

        if w == 0 and l == 0:
            for nm, buf, keys in (
                    ("qw0", QW, [("qw", c2) for c2 in range(4)]),
                    ("kw0", KW, [("kw", 0, c2) for c2 in range(4)]),
                    ("vw0", VW, [("vw", 0, vc) for vc in range(4)] + [("vw1", 0)])):
                if tap(nm):
                    tapdram(nm, [128, buf.shape[1] * buf.shape[2] * buf.shape[3]])
                    S.add("gpsimd",
                          (lambda e, nm=nm, buf=buf:
                           e.dma_start(out=dbg[nm], in_=buf[:].rearrange(
                               "p a b c -> p (a b c)"))),
                          reads=keys, name=f"tap_{nm}")

        for h in range(NH):
            c2, hp = h // 2, h % 2
            rows = slice(64 * hp, 64 * hp + 64)
            for qi in range(4):
                kj_lo = max(0, 4 * w + qi - 4)
                kj_hi = 4 * w + qi
                ncol = 128 * (kj_hi - kj_lo + 1)
                kread = sorted({("kw", (kj // 4) % 2, c2)
                                for kj in range(kj_lo, kj_hi + 1)})
                def mmqk1(e, qi=qi, kj_lo=kj_lo, kj_hi=kj_hi, c2=c2, rows=rows):
                    inst = None
                    for i, kj in enumerate(range(kj_lo, kj_hi + 1)):
                        inst = e.matmul(
                            PB[:, 128 * i:128 * (i + 1)],
                            QW[rows, 0, c2, 128 * qi:128 * (qi + 1)],
                            KW[rows, (kj // 4) % 2, c2,
                               128 * (kj % 4):128 * (kj % 4) + 128],
                            start=True, stop=True)
                    return inst
                S.add("tensor", mmqk1,
                      reads=[("qw", c2), ("pb", 0), ("pb", 1)] + kread,
                      writes=[("pb", 0), ("pb", 1)], name=f"qk1_{l}_{w}_{h}_{qi}")
                def redm(e, ncol=ncol, qi=qi):
                    return e.reduce_max(MAXC[:, qi % 2:qi % 2 + 1],
                                        PB[:, 0:ncol], axis=AXX, negate=True)
                S.add("vector", redm, reads=[("pb", 0), ("pb", 1), ("mxc", qi % 2)],
                      writes=[("mxcd", qi % 2), ("pb", 0), ("pb", 1)],
                      name=f"rmx{l}_{w}_{h}_{qi}")
                def tpm(e, qi=qi):
                    return e.transpose(
                        PB[0:1, 3072 + 128 * qi:3072 + 128 * (qi + 1)],
                        MAXC[:, qi % 2:qi % 2 + 1], IDF[:])
                S.add("tensor", tpm, reads=[("mxcd", qi % 2), "const", ("pb", 6)],
                      writes=[("pb", 6), ("mxc", qi % 2)],
                      name=f"tpm{l}_{w}_{h}_{qi}")
            msl = h % 2
            def mev(e, msl=msl):
                return e.activation(out=MSTG[:, msl, :], in_=PB[0:1, 3072:3584],
                                    func=AF.Copy)
            S.add("scalar", mev, reads=[("pb", 6), ("mst", msl)],
                  writes=[("mstd", msl), ("pb", 6)], name=f"mev{l}_{w}_{h}")
            if l == 0 and w == 0 and h <= 1 and tap("mst0"):
                nm = f"mst{h}"
                tapdram(nm, [1, 512])
                S.add("gpsimd",
                      (lambda e, msl=msl, nm=nm:
                       e.dma_start(out=dbg[nm], in_=MSTG[:, msl, :])),
                      reads=[("mstd", msl)], name=f"tap_{nm}")

            pv_started = {"d": False, "z": False}
            for i_kj, kj in enumerate(range(max(0, 4 * w - 4), 4 * w + 4)):
                qi_lo = max(0, kj - 4 * w)
                qi_hi = min(3, kj - 4 * w + 4)
                M = 128 * (qi_hi - qi_lo + 1)
                qb = 2 + (i_kj % 2)
                def mmqk2(e, kj=kj, qi_lo=qi_lo, M=M, c2=c2, rows=rows,
                          qb=qb, msl=msl):
                    e.matmul(bank(qb, 0, M),
                             KW[rows, (kj // 4) % 2, c2,
                                128 * (kj % 4):128 * (kj % 4) + 128],
                             QW[rows, 0, c2, 128 * qi_lo:128 * qi_lo + M],
                             start=True, stop=False)
                    return e.matmul(bank(qb, 0, M), ONES1[:],
                                    MSTG[:, msl, 128 * qi_lo:128 * qi_lo + M],
                                    start=False, stop=True)
                S.add("tensor", mmqk2,
                      reads=[("qw", c2), ("kw", (kj // 4) % 2, c2),
                             ("mstd", msl), ("pb", qb)],
                      writes=[("pb", qb)], name=f"qk2_{l}_{w}_{h}_{kj}")
                psl = i_kj % 2
                def pexp(e, qb=qb, M=M, psl=psl):
                    return e.activation(out=PSTG[:, psl, 0:M], in_=bank(qb, 0, M),
                                        func=AF.Exp)
                S.add("scalar", pexp, reads=[("pb", qb), ("pst", psl)],
                      writes=[("pstd", psl), ("pb", qb)],
                      name=f"pex{l}_{w}_{h}_{kj}")
                qi_d = kj - 4 * w
                qi_e = kj - 4 * w + 4
                if qi_lo <= qi_d <= qi_hi or qi_lo <= qi_e <= qi_hi:
                    def tri(e, qi_d=qi_d, qi_e=qi_e, qi_lo=qi_lo, qi_hi=qi_hi,
                            psl=psl):
                        inst = None
                        if qi_lo <= qi_d <= qi_hi:
                            co = 128 * (qi_d - qi_lo)
                            inst = e.tensor_tensor(PSTG[:, psl, co:co + 128],
                                                   PSTG[:, psl, co:co + 128],
                                                   TRID[:], ALU.mult)
                        if qi_lo <= qi_e <= qi_hi:
                            co = 128 * (qi_e - qi_lo)
                            inst = e.tensor_tensor(PSTG[:, psl, co:co + 128],
                                                   PSTG[:, psl, co:co + 128],
                                                   TRIE[:], ALU.mult)
                        return inst
                    S.add("vector", tri, reads=[("pstd", psl), "const"],
                          writes=[("pstd", psl)], name=f"tri{l}_{w}_{h}_{kj}")
                if l == 0 and w == 0 and h <= 1 and kj == 0 and tap("pst0"):
                    nm = f"pst{h}"
                    tapdram(nm, [128, 512])
                    S.add("gpsimd",
                          (lambda e, psl=psl, M=M, nm=nm:
                           e.dma_start(out=dbg[nm][:, 0:M],
                                       in_=PSTG[:, psl, 0:M])),
                          reads=[("pstd", psl)], name=f"tap_{nm}")
                def pv(e, kj=kj, qi_lo=qi_lo, qi_hi=qi_hi, psl=psl, h=h, hp=hp,
                       w=w, c2=c2, pv_started=pv_started):
                    vsl = (kj // 4) % 2
                    vq = kj % 4
                    inst = None
                    if hp == 1 and not pv_started["d"]:
                        e.matmul(PB[:, 2048:2560], ZERB[:],
                                 KW[:, vsl, c2, :], start=True, stop=False)
                        pv_started["d"] = True
                    for qi in range(qi_lo, qi_hi + 1):
                        st = (hp == 0) and (not pv_started["d"])
                        pv_started["d"] = True
                        sp = (kj == 4 * w + 3 and qi == qi_hi)
                        co = 128 * (qi - qi_lo)
                        if hp == 0:
                            inst = e.matmul(
                                PB[0:65, 2048 + 128 * qi:2048 + 128 * (qi + 1)],
                                VW[:, vsl, vq, 65 * h:65 * h + 65],
                                PSTG[:, psl, co:co + 128], start=st, stop=sp)
                        else:
                            inst = e.matmul(
                                PB[0:1, 2048 + 128 * qi:2048 + 128 * (qi + 1)],
                                ONECB[:], PSTG[:, psl, co:co + 128],
                                start=False, stop=sp)
                            inst = e.matmul(
                                PB[64:128, 2048 + 128 * qi:2048 + 128 * (qi + 1)],
                                VW[:, vsl, vq, 65 * h:65 * h + 64],
                                PSTG[:, psl, co:co + 128], start=False, stop=sp,
                                tile_position=(0, 64))
                    return inst
                S.add("tensor", pv,
                      reads=[("pstd", psl), ("vw", (kj // 4) % 2, kj % 4),
                             ("vw1", (kj // 4) % 2), "const", ("pb", 4)],
                      writes=[("pb", 4), ("pst", psl)],
                      name=f"pv{l}_{w}_{h}_{kj}")
            if l == 0 and w == 0 and h <= 1 and tap("pv0"):
                nm = f"pv{h}"
                tapdram(nm, [128, 512])
                def pvdump(e):
                    return e.tensor_copy(STT[:, 2, :], PB[:, 2048:2560])
                S.add("vector", pvdump, reads=[("pb", 4), ("stt", 2)],
                      writes=[("stt", 2), ("pvdmp",)], name=f"pvd{h}")
                S.add("gpsimd",
                      (lambda e, nm=nm: e.dma_start(out=dbg[nm], in_=STT[:, 2, :])),
                      reads=[("pvdmp",), ("stt", 2)], name=f"tap_{nm}")
            zrow = 64 if hp == 0 else 0
            def zi(e, zrow=zrow):
                e.activation(out=STT[zrow:zrow + 1, 2, :],
                             in_=PB[zrow:zrow + 1, 2048:2560], func=AF.Ln)
                return e.activation(out=STT[zrow:zrow + 1, 2, :],
                                    in_=STT[zrow:zrow + 1, 2, :], func=AF.Exp,
                                    scale=f32(-1.0))
            S.add("scalar", zi, reads=[("pb", 4), ("stt", 2)],
                  writes=[("sttz",), ("stt", 2)], name=f"zi{l}_{w}_{h}")
            def zb(e, zrow=zrow, hp=hp):
                tp = (0, 64) if hp else None
                return e.matmul(PB[64 * hp:64 * hp + 64, 2560:3072],
                                ONE1F[zrow:zrow + 1, 0:64],
                                STT[zrow:zrow + 1, 2, :],
                                start=True, stop=True, tile_position=tp)
            S.add("tensor", zb, reads=[("sttz",), "const", ("pb", 5)],
                  writes=[("pb", 5)], name=f"zb{l}_{w}_{h}")
            def oev(e, hp=hp, c2=c2, w=w):
                e.tensor_copy(STT[64 * hp:64 * hp + 64, 2, :],
                              PB[64 * hp:64 * hp + 64, 2048:2560])
                return e.tensor_tensor(
                    OFM[64 * hp:64 * hp + 64, c2, w * W512:(w + 1) * W512],
                    STT[64 * hp:64 * hp + 64, 2, :],
                    PB[64 * hp:64 * hp + 64, 2560:3072], ALU.mult)
            S.add("vector", oev,
                  reads=[("pb", 4), ("pb", 5), ("ofm", c2, w, hp), ("stt", 2)],
                  writes=[("ofm", c2, w, hp), ("pb", 4), ("pb", 5), ("stt", 2)],
                  name=f"oev{l}_{w}_{h}")

    if tap(f"ofm{l}"):
        tapdram(f"ofm{l}", [128, 4, T])
        S.add("gpsimd", lambda e: e.dma_start(out=dbg[f"ofm{l}"], in_=OFM[:]),
              reads=[("ofm", c2, w, hp) for c2 in range(4) for w in range(NW)
                     for hp in range(2)], name=f"tap_ofm{l}")

    for n in range(C):
        def ldwo(e, n=n, l=l):
            return e.dma_start(out=WOB[:], in_=d_wo[l, :, n, :, :])
        S.add("gpsimd", ldwo, reads=[("wob",)], writes=[("wobd",)],
              name=f"ldwo{l}_{n}")
        for w in range(NW):
            pbk = (n * NW + w) % 2
            def mmo(e, n=n, w=w, pbk=pbk):
                for c2 in range(4):
                    inst = e.matmul(bank(pbk), WOB[:, 128 * c2:128 * (c2 + 1)],
                                    OFM[:, c2, w * W512:(w + 1) * W512],
                                    start=(c2 == 0), stop=(c2 == 3))
                return inst
            S.add("tensor", mmo,
                  reads=[("wobd",), ("pb", pbk)] +
                        [("ofm", c2, w, hp) for c2 in range(4) for hp in range(2)],
                  writes=[("pb", pbk)] + ([("wob",)] if w == NW - 1 else []),
                  name=f"mmo{l}_{n}_{w}")
            def yadd(e, n=n, w=w, pbk=pbk):
                return e.tensor_tensor(XRc(n, w * W512, W512),
                                       XRc(n, w * W512, W512), bank(pbk), ALU.add)
            S.add("vector", yadd, reads=[("pb", pbk), ("xr", w)],
                  writes=[("xr", w), ("pb", pbk), ("wo_done", n, w)],
                  name=f"yadd{l}_{n}_{w}")
            def ymir2(e, n=n, w=w):
                return e.copy(out=ABFc(n, w * W512 + 4, W512),
                              in_=XRc(n, w * W512, W512))
            S.add("scalar", ymir2, reads=[("wo_done", n, w), ("xr", w)],
                  writes=[YK(n, w)], name=f"ym2{l}_{n}_{w}")

    if tap(f"ywo{l}"):
        tapdram(f"ywo{l}", [128, C * T])
        S.add("gpsimd", lambda e: e.dma_start(out=dbg[f"ywo{l}"], in_=XRt[:]),
              reads=[("wo_done", n, w) for n in range(C) for w in range(NW)],
              name=f"tap_ywo{l}")

    for w in range(NW):
        def rhs_bf(c, w=w):
            off = w * W512 + (3 if c >= 4 else 4)
            return ABFc(c, off, W512)
        rd = [YK(c, w) for c in range(C)]
        if w > 0:
            rd += [YK(c, w - 1) for c in range(4, C)]
        ln_stats(f"li{l}w{w}", rhs_bf, rd, bf=True)
        for c in range(C):
            def ia(e, c=c, w=w):
                off = w * W512 + (3 if c >= 4 else 4)
                return e.tensor_tensor(STT[:, 2, :], ABFc(c, off, W512),
                                       STT[:, 0, :], ALU.mult)
            S.add("vector", ia, reads=[("stt", 0), YK(c, w), ("stt", 2)] +
                  ([YK(c, w - 1)] if (w > 0 and c >= 4) else []),
                  writes=[("stt", 2)], name=f"ila{l}_{w}_{c}")
            def ib(e):
                return e.tensor_tensor(STT[:, 2, :], STT[:, 2, :],
                                       STT[:, 1, :], ALU.add)
            S.add("vector", ib, reads=[("stt", 1)], writes=[("stt", 2)],
                  name=f"ilb{l}_{w}_{c}")
            def ic(e, c=c, w=w):
                return e.activation(out=LBFc(c, w * W512, W512),
                                    in_=STT[:, 2, :], func=AF.Identity,
                                    scale=GBC[:, c:c + 1],
                                    bias=GBC[:, C + c:C + c + 1])
            S.add("scalar", ic, reads=[("stt", 2), ("lsm",)],
                  writes=[("lbf", c, w)], name=f"ilc{l}_{w}_{c}")

    if tap(f"lbf{l}"):
        tapdram(f"lbf{l}", [128, C * T])
        S.add("gpsimd", lambda e: e.dma_start(out=dbg[f"lbf{l}"], in_=L["LBF"][:]),
              reads=[("lbf", c, w) for c in range(C) for w in range(NW)],
              name=f"tap_lbf{l}")

    S.virt(writes=(["rot"] + [("qw", c2) for c2 in range(4)] +
                   [("kw", s, c2) for s in range(2) for c2 in range(4)] +
                   [("vw", s, v) for s in range(2) for v in range(4)] +
                   [("vw1", s) for s in range(2)] +
                   [("mres", j, tq) for j in range(NK2A) for tq in range(NW)] +
                   [("mresd", j, tq) for j in range(NK2A) for tq in range(NW)] +
                   [("mrb", j) for j in range(NK2B)] +
                   [("mrbd", j) for j in range(NK2B)] +
                   [("mrl", j) for j in range(NK2B)]),
           name=f"retag_attn2ffn{l}")

    for j in range(K2N):
        for tq in range(NW):
            for part in range(2):
                n = 2 * j + part
                wsl = n % 3
                fkey = ("wp", wsl)
                dkey = ("wpd", wsl)
                if tq == 0:
                    def ldw1(e, n=n, l=l, wsl=wsl):
                        return e.dma_start(out=WPB[:, wsl, :],
                                           in_=d_w1[l, :, n, :, :])
                    S.add("gpsimd", ldw1, reads=[fkey], writes=[dkey],
                          name=f"ldw1{l}_{n}")
                pbk = part * 2 + tq % 2
                def mm1(e, n=n, tq=tq, pbk=pbk, wsl=wsl):
                    for c in range(C):
                        inst = e.matmul(bank(pbk),
                                        WPB[:, wsl, 128 * c:128 * (c + 1)],
                                        LBFc(c, tq * W512, W512),
                                        start=(c == 0), stop=(c == C - 1))
                    return inst
                S.add("tensor", mm1,
                      reads=[dkey, ("pb", pbk)] +
                            [("lbf", c, tq) for c in range(C)],
                      writes=[("pb", pbk)] + ([fkey] if tq == NW - 1 else []),
                      name=f"mm1{l}_{n}_{tq}")
            pa = tq % 2
            pg = 2 + tq % 2
            def gact(e, j=j, pg=pg):
                return e.activation(out=SQB[:, 0, :], in_=bank(pg), func=AF.Gelu,
                                    bias=B1C[:, K2N + j:K2N + j + 1])
            S.add("scalar", gact, reads=[("pb", pg), ("sqb", 0), ("lsm",)],
                  writes=[("sqbd", 0), ("pb", pg)], name=f"ga{l}_{j}_{tq}")
            if j < NK2A:
                mdst = (lambda j=j, tq=tq: MRES[:, j, tq * W512:(tq + 1) * W512])
                mkey, mdkey = ("mres", j, tq), ("mresd", j, tq)
            else:
                msl2 = (j - NK2A) % NK2H
                mdst = (lambda msl2=msl2: MRB[:, msl2, :])
                mkey, mdkey = ("mrb", msl2), ("mrbd", msl2)
            def mmul(e, j=j, pa=pa, mdst=mdst):
                return e.scalar_tensor_tensor(mdst(), bank(pa),
                                              B1C[:, j:j + 1], SQB[:, 0, :],
                                              ALU.add, ALU.mult)
            S.add("vector", mmul,
                  reads=[("pb", pa), ("sqbd", 0), ("lsm",), mkey],
                  writes=[("pb", pa), ("sqb", 0), mdkey, mkey],
                  name=f"mmul{l}_{j}_{tq}")
            if j >= NK2A:
                def msp(e, j=j, tq=tq):
                    return e.dma_start(
                        out=d_m[:, j - NK2A, tq * W512:(tq + 1) * W512],
                        in_=MRB[:, (j - NK2A) % NK2H, :])
                S.add("sync", msp, reads=[mdkey],
                      writes=[("msp", j - NK2A, tq), mkey],
                      name=f"msp{l}_{j}_{tq}")

    for n in range(C):
        wsl = n % 3
        def ldw2(e, n=n, l=l, wsl=wsl):
            return e.dma_start(out=WPB[:, wsl, 0:128 * NK2A],
                               in_=d_w2[l, :, n, 0:NK2A, :])
        S.add("gpsimd", ldw2, reads=[("wp", wsl)], writes=[("wpd", wsl)],
              name=f"ldw2A{l}_{n}")
        for tq in range(NW):
            pbk = (n * NW + tq) % 2
            def mm2(e, n=n, tq=tq, pbk=pbk, wsl=wsl):
                for j in range(NK2A):
                    inst = e.matmul(bank(pbk), WPB[:, wsl, 128 * j:128 * (j + 1)],
                                    MRES[:, j, tq * W512:(tq + 1) * W512],
                                    start=(j == 0), stop=(j == NK2A - 1))
                return inst
            S.add("tensor", mm2,
                  reads=[("wpd", wsl), ("pb", pbk)] +
                        [("mresd", j, tq) for j in range(NK2A)],
                  writes=[("pb", pbk)] + ([("wp", wsl)] if tq == NW - 1 else []),
                  name=f"mm2A{l}_{n}_{tq}")
            def y2a(e, n=n, tq=tq, pbk=pbk):
                return e.scalar_tensor_tensor(
                    XRc(n, tq * W512, W512), bank(pbk),
                    GBC[:, 2 * C + n:2 * C + n + 1],
                    XRc(n, tq * W512, W512), ALU.add, ALU.add)
            S.add("vector", y2a, reads=[("pb", pbk), ("xr", tq), ("lsm",)],
                  writes=[("xr", tq), ("pb", pbk), ("w2p", n, tq)],
                  name=f"y2a{l}_{n}_{tq}")
    for tq in range(NW):
        for gh in range(2):
            jlo = NK2A + gh * NK2H
            jhi = jlo + NK2H
            for j in range(jlo, jhi):
                def mrl(e, j=j, tq=tq, jlo=jlo):
                    return e.dma_start(
                        out=MRB[:, j - jlo, :],
                        in_=d_m[:, j - NK2A, tq * W512:(tq + 1) * W512])
                S.add("sync", mrl, reads=[("msp", j - NK2A, tq), ("mrb", j - jlo)],
                      writes=[("mrl", j - jlo)], name=f"mrl{l}_{tq}_{j}")
            for n in range(C):
                def ldw2b(e, n=n, l=l, jlo=jlo, jhi=jhi):
                    return e.dma_start(
                        out=WPB[:, 0, :].tensor.ap()[:, 0:128 * NK2H] if False
                        else WPB[:].rearrange("p s n -> p (s n)")[:, 0:128 * NK2H],
                        in_=d_w2[l, :, n, jlo:jhi, :])
                S.add("gpsimd", ldw2b, reads=[("wp", 0), ("wp", 1)],
                      writes=[("wpd", 0), ("wpd", 1)], name=f"ldw2B{l}_{tq}_{gh}_{n}")
                pbk = (n + tq) % 2
                def mm2b(e, n=n, tq=tq, pbk=pbk, jlo=jlo, jhi=jhi):
                    wflat = WPB[:].rearrange("p s n -> p (s n)")
                    for j in range(jlo, jhi):
                        inst = e.matmul(
                            bank(pbk),
                            wflat[:, 128 * (j - jlo):128 * (j - jlo + 1)],
                            MRB[:, j - jlo, :],
                            start=(j == jlo), stop=(j == jhi - 1))
                    return inst
                S.add("tensor", mm2b,
                      reads=[("wpd", 0), ("wpd", 1), ("pb", pbk)] +
                            [("mrl", j) for j in range(NK2H)],
                      writes=[("pb", pbk), ("wp", 0), ("wp", 1)] +
                             ([("mrb", j) for j in range(NK2H)]
                              if n == C - 1 else []),
                      name=f"mm2B{l}_{tq}_{gh}_{n}")
                def y2c(e, n=n, tq=tq, pbk=pbk):
                    return e.tensor_tensor(XRc(n, tq * W512, W512),
                                           XRc(n, tq * W512, W512),
                                           bank(pbk), ALU.add)
                S.add("vector", y2c,
                      reads=[("pb", pbk), ("xr", tq), ("w2p", n, tq)],
                      writes=[("xr", tq), ("pb", pbk), ("w2d", n, tq, gh)],
                      name=f"y2c{l}_{tq}_{gh}_{n}")
                if gh == 1:
                    def ymir3(e, n=n, tq=tq):
                        return e.copy(out=ABFc(n, tq * W512 + 4, W512),
                                      in_=XRc(n, tq * W512, W512))
                    S.add("scalar", ymir3,
                          reads=[("w2d", n, tq, 1), ("xr", tq)],
                          writes=[YK(n, tq)], name=f"ym3{l}_{tq}_{n}")

    if tap(f"y{l + 1}"):
        tapdram(f"y{l + 1}", [128, C * T])
        S.add("gpsimd", lambda e: e.dma_start(out=dbg[f"y{l + 1}"], in_=XRt[:]),
              reads=[("w2d", n, tq, 1) for n in range(C) for tq in range(NW)],
              name=f"tap_y{l + 1}")


def kernel(**inputs):
    cst = _constants()
    w = _prep_weights(inputs)
    x = np.asarray(inputs["L0_attn_query"], np.float32)
    nc, _ = build_program()
    base = {}
    for k in WEIGHT_NAMES:
        base[k] = w[k]
    for k in CONST_NAMES:
        base[k] = np.asarray(cst[k])
    in_maps = []
    for b in range(8):
        m = dict(base)
        m["x"] = np.ascontiguousarray(x[b])
        in_maps.append(m)
    res = run_bass_kernel_spmd(nc, in_maps, core_ids=list(range(8)))
    return np.stack([res.results[b]["out"] for b in range(8)], 0)


# version 04
# speedup vs baseline: 1.1358x; 1.0751x over previous
import numpy as np
import ml_dtypes
from contextlib import ExitStack

import concourse.bass as bass
import concourse.mybir as mybir
from concourse.bass_utils import run_bass_kernel_spmd

F32 = mybir.dt.float32
BF16 = mybir.dt.bfloat16
AF = mybir.ActivationFunctionType
ALU = mybir.AluOpType
AXX = mybir.AxisListType.X
f32 = np.float32

T = 2048
D = 1024
C = 8
NW = 4
W512 = 512
K9 = 9
NH = 8
QKN = 12
HID = 2730
HIDP = 2816
K2N = 22
W1N = 44
NK2A = 11
NK2B = K2N - NK2A
LNEPS = 1e-5

_CONST_CACHE = {}


def _constants():
    if _CONST_CACHE:
        return _CONST_CACHE
    c = {}
    c["ident_bf"] = np.eye(128, dtype=ml_dtypes.bfloat16)
    c["ident_f32"] = np.eye(128, dtype=np.float32)
    R = np.zeros((64, 64), np.float32)
    for d in range(32):
        R[d + 32, d] = -1.0
        R[d, d + 32] = 1.0
    Rb = np.zeros((128, 128), np.float32)
    Rb[0:64, 0:64] = R
    Rb[64:128, 64:128] = R
    c["rotmat"] = Rb.astype(ml_dtypes.bfloat16)
    d = np.arange(64, dtype=np.float64)
    half = (d % 32) * 2.0
    inv_freq = 1.0 / (10000.0 ** (half / 64.0))
    t = np.arange(T, dtype=np.float64)
    freqs = t[None, :] * inv_freq[:, None]
    base = (half + 0.4 * 64.0) / (1.4 * 64.0)
    power = (t - T // 2) / 256.0
    scale = base[:, None] ** power[None, :]
    cs, sn = np.cos(freqs), np.sin(freqs)

    def dup(a):
        return np.vstack([a, a]).astype(np.float32).astype(ml_dtypes.bfloat16)
    c["cosq"] = dup(8.0 * cs * scale)
    c["sinq"] = dup(8.0 * sn * scale)
    c["cosk"] = dup(cs / scale)
    c["sink"] = dup(sn / scale)
    r = np.arange(128)
    c["tri_diag"] = (r[:, None] <= r[None, :]).astype(ml_dtypes.bfloat16)
    c["tri_edge"] = (r[:, None] >= r[None, :]).astype(ml_dtypes.bfloat16)
    hb = np.zeros((128, 128), np.float32)
    hb[0:64, 0:64] = 1.0
    hb[64:128, 64:128] = 1.0
    c["hones"] = hb.astype(ml_dtypes.bfloat16)
    c["lncol_f32"] = np.full((128, 128), 1.0 / D, np.float32)
    c["lncol_bf"] = np.full((128, 128), 1.0 / D, np.float32).astype(ml_dtypes.bfloat16)
    c["ones1_bf"] = np.ones((1, 128), ml_dtypes.bfloat16)
    c["ones1_f32"] = np.ones((128, 128), np.float32)
    c["zeros_bf"] = np.zeros((128, 128), ml_dtypes.bfloat16)
    c["onescol_bf"] = np.ones((128, 1), ml_dtypes.bfloat16)
    s4 = np.zeros((128, 8), np.float32)
    for ct in range(4):
        for hp in range(2):
            for kk in range(K9):
                s4[32 * ct + 9 * hp + kk, 2 * ct + hp] = 1.0
    c["sel4"] = s4.astype(ml_dtypes.bfloat16)
    sz = np.zeros((8, 128), np.float32)
    for ct in range(4):
        for hp in range(2):
            for kk in range(K9):
                sz[2 * ct + hp, 32 * ct + 9 * hp + kk] = 1.0
    c["selzrow"] = sz
    B = np.zeros((K9, 4, 128, 128), np.float32)
    for kk in range(K9):
        for ct in range(4):
            for hp in range(2):
                B[kk, ct, 32 * ct + 9 * hp + kk, 64 * hp:64 * (hp + 1)] = 1.0
    c["bsel"] = np.ascontiguousarray(B.transpose(2, 0, 1, 3)).astype(ml_dtypes.bfloat16)
    _CONST_CACHE.update(c)
    return c


CONST_NAMES = ("ident_bf", "ident_f32", "rotmat", "cosq", "sinq", "cosk", "sink",
               "tri_diag", "tri_edge", "hones", "lncol_f32", "lncol_bf",
               "ones1_bf", "ones1_f32", "onescol_bf", "zeros_bf", "sel4",
               "selzrow", "bsel")
WEIGHT_NAMES = ("cw_stat", "wqkv", "wo", "w1", "w2", "b1a", "b1g", "qscol", "kscol",
                "lng", "lnb", "b2")


def _prep_weights(inp):
    w = {}
    cw = np.asarray(inp["L0_attn_conv_weight"], np.float32)
    Wk = np.zeros((128, C, K9, 18), np.float32)
    for h in range(16):
        cc, hp = h // 2, h % 2
        for kk in range(K9):
            Wk[64 * hp:64 * (hp + 1), cc, kk, 9 * hp:9 * (hp + 1)] = \
                cw[h * 9:(h + 1) * 9, :, kk].T
    w["cw_stat"] = np.ascontiguousarray(Wk.reshape(128, C * K9 * 18))
    wqkv = np.asarray(inp["L0_ffn_wqkv"], np.float32)
    w["wqkv"] = np.ascontiguousarray(
        wqkv.reshape(2, C, 128, QKN, 128).transpose(0, 2, 3, 1, 4))
    wo = np.asarray(inp["L0_ffn_wo"], np.float32)
    w["wo"] = np.ascontiguousarray(
        wo.reshape(2, 4, 128, 8, 128).transpose(0, 2, 3, 1, 4))
    w1 = np.asarray(inp["L0_ffn_w1"], np.float32)
    a, g = w1[:, :, :HID], w1[:, :, HID:]
    ap = np.zeros((2, D, HIDP), np.float32)
    gp = np.zeros((2, D, HIDP), np.float32)
    ap[:, :, :HID] = a
    gp[:, :, :HID] = g
    w1p = np.zeros((2, D, W1N, 128), np.float32)
    w1p[:, :, 0::2, :] = ap.reshape(2, D, K2N, 128)
    w1p[:, :, 1::2, :] = gp.reshape(2, D, K2N, 128)
    w["w1"] = np.ascontiguousarray(
        w1p.reshape(2, C, 128, W1N, 128).transpose(0, 2, 3, 1, 4))
    b1 = np.asarray(inp["L0_ffn_b1"], np.float32)
    b1a = np.zeros((2, HIDP), np.float32)
    b1g = np.zeros((2, HIDP), np.float32)
    b1a[:, :HID] = b1[:, :HID]
    b1g[:, :HID] = b1[:, HID:]
    w["b1a"] = np.ascontiguousarray(b1a.reshape(2, K2N, 128).transpose(0, 2, 1))
    w["b1g"] = np.ascontiguousarray(b1g.reshape(2, K2N, 128).transpose(0, 2, 1))
    w2 = np.asarray(inp["L0_ffn_w2"], np.float32)
    w2p = np.zeros((2, HIDP, D), np.float32)
    w2p[:, :HID, :] = w2
    w["w2"] = np.ascontiguousarray(
        w2p.reshape(2, K2N, 128, 8, 128).transpose(0, 2, 3, 1, 4))
    qs = np.asarray(inp["L0_ffn_q_scale"], np.float32)
    ks = np.asarray(inp["L0_ffn_k_scale"], np.float32)
    w["qscol"] = np.ascontiguousarray(np.concatenate([qs, qs], 1).reshape(2, 128, 1))
    w["kscol"] = np.ascontiguousarray(np.concatenate([ks, ks], 1).reshape(2, 128, 1))
    for nm, key in (("lng", "L0_ffn_ln_g"), ("lnb", "L0_ffn_ln_b"), ("b2", "L0_ffn_b2")):
        v = np.asarray(inp[key], np.float32)
        w[nm] = np.ascontiguousarray(v.reshape(2, C, 128).transpose(0, 2, 1))
    return w


ENGINES = ("tensor", "vector", "scalar", "sync", "gpsimd")
DMA_ENG = ("sync", "gpsimd")


class _Step:
    __slots__ = ("engine", "fn", "name", "deps", "need_inc", "inc_val", "waits",
                 "vdeps")

    def __init__(self, engine, fn, name):
        self.engine = engine
        self.fn = fn
        self.name = name
        self.deps = []
        self.need_inc = False
        self.inc_val = None
        self.waits = None
        self.vdeps = None


SERIAL = False


class Sched:
    def __init__(self):
        self.steps = {e: [] for e in ENGINES}
        self._writer = {}
        self._readers = {}
        self.n = 0
        self._prev = None

    def _collect(self, reads, writes):
        dep = set()
        for r in reads:
            w = self._writer.get(r)
            if w is not None:
                dep.add(w)
        for wk in writes:
            w = self._writer.get(wk)
            if w is not None:
                dep.add(w)
            for r in self._readers.get(wk, ()):
                dep.add(r)
        out = set()
        for d in dep:
            if d.vdeps is not None:
                out.update(d.vdeps)
            else:
                out.add(d)
        return out

    def add(self, engine, fn, reads=(), writes=(), name=""):
        s = _Step(engine, fn, name or f"s{self.n}")
        self.n += 1
        dep = self._collect(reads, writes)
        if SERIAL and self._prev is not None:
            dep.add(self._prev)
        self._prev = s
        for r in reads:
            self._readers.setdefault(r, []).append(s)
        for wk in writes:
            self._writer[wk] = s
            self._readers[wk] = []
        for d in dep:
            if d is s:
                continue
            if d.engine == engine and d.engine not in DMA_ENG:
                continue
            d.need_inc = True
            s.deps.append(d)
        if engine in DMA_ENG:
            s.need_inc = True
        self.steps[engine].append(s)
        return s

    def virt(self, writes=(), reads=(), name="virt"):
        s = _Step(None, None, name)
        s.vdeps = self._collect(reads, writes)
        for r in reads:
            self._readers.setdefault(r, []).append(s)
        for wk in writes:
            self._writer[wk] = s
            self._readers[wk] = []
        return s

    def emit(self, blk, sems):
        for e in ENGINES:
            cnt = 0
            amt = 16 if e in DMA_ENG else 1
            for s in self.steps[e]:
                if s.need_inc:
                    cnt += amt
                    s.inc_val = cnt
        for e in ENGINES:
            hw = {x: 0 for x in ENGINES}
            for s in self.steps[e]:
                ws = []
                for d in sorted(s.deps, key=lambda d: d.inc_val):
                    if d.inc_val > hw[d.engine]:
                        hw[d.engine] = d.inc_val
                        ws.append((d.engine, d.inc_val))
                s.waits = ws
        amt = {e: (16 if e in DMA_ENG else 1) for e in ENGINES}

        def mk(e_name):
            def body(eng):
                for s in self.steps[e_name]:
                    for (de, v) in s.waits:
                        eng.wait_ge(sems[de], v)
                    inst = s.fn(eng)
                    if s.need_inc:
                        assert inst is not None, f"step {s.name} returned None"
                        inst.then_inc(sems[e_name], amt[e_name])
            return body
        blk.tensor(mk("tensor"))
        blk.vector(mk("vector"))
        blk.scalar(mk("scalar"))
        blk.sync(mk("sync"))
        blk.gpsimd(mk("gpsimd"))


def build_program(stage=99, debug_taps=()):
    cst = _constants()
    nc = bass.Bass("TRN2", target_bir_lowering=False)
    taps = set(debug_taps)

    xin = nc.dram_tensor("x", [T, D], F32, kind="ExternalInput").ap()
    out = nc.dram_tensor("out", [T, D], F32, kind="ExternalOutput").ap()
    d_cw = nc.dram_tensor("cw_stat", [128, C * K9 * 18], F32, kind="ExternalInput").ap()
    d_wqkv = nc.dram_tensor("wqkv", [2, 128, QKN, C, 128], F32, kind="ExternalInput").ap()
    d_wo = nc.dram_tensor("wo", [2, 128, 8, 4, 128], F32, kind="ExternalInput").ap()
    d_w1 = nc.dram_tensor("w1", [2, 128, W1N, C, 128], F32, kind="ExternalInput").ap()
    d_w2 = nc.dram_tensor("w2", [2, 128, 8, K2N, 128], F32, kind="ExternalInput").ap()
    d_b1a = nc.dram_tensor("b1a", [2, 128, K2N], F32, kind="ExternalInput").ap()
    d_b1g = nc.dram_tensor("b1g", [2, 128, K2N], F32, kind="ExternalInput").ap()
    d_qs = nc.dram_tensor("qscol", [2, 128, 1], F32, kind="ExternalInput").ap()
    d_ks = nc.dram_tensor("kscol", [2, 128, 1], F32, kind="ExternalInput").ap()
    d_lng = nc.dram_tensor("lng", [2, 128, C], F32, kind="ExternalInput").ap()
    d_lnb = nc.dram_tensor("lnb", [2, 128, C], F32, kind="ExternalInput").ap()
    d_b2 = nc.dram_tensor("b2", [2, 128, C], F32, kind="ExternalInput").ap()
    dc = {}
    for nm in CONST_NAMES:
        a = np.asarray(cst[nm])
        dt = BF16 if a.dtype == ml_dtypes.bfloat16 else F32
        dc[nm] = nc.dram_tensor(nm, list(a.shape), dt, kind="ExternalInput").ap()
    d_x1 = nc.dram_tensor("x1sp", [128, C, T], F32, kind="Internal").ap()
    d_m = nc.dram_tensor("msp", [128, NK2B, T], BF16, kind="Internal").ap()

    dbg = {}

    def tap(name):
        return name in taps

    def tapdram(name, shape, dtype=F32):
        dbg[name] = nc.dram_tensor("dbg_" + name, shape, dtype,
                                   kind="ExternalOutput").ap()
        return dbg[name]

    ctx = ExitStack()
    E = ctx.enter_context

    XR = E(nc.sbuf_tensor("XR", [128, C * T], F32))
    ABF = E(nc.sbuf_tensor("ABF", [128, C * (T + 8)], BF16))
    LBF = E(nc.sbuf_tensor("LBF", [128, C * T], BF16))
    R2 = E(nc.sbuf_tensor("R2", [128, 53376], mybir.dt.uint8))
    r2base = nc.lookup_mloc(R2).addr

    def r2v(name, shape, dtype, off):
        return nc.alloc_sbuf_tensor_at(name, shape, dtype, offset=r2base + off)

    BSEL = r2v("BSEL", [128, K9, 4, 128], BF16, 0)
    CWS = r2v("CWS", [128, C * K9 * 18], BF16, 9216)
    CONVS = r2v("CONVS", [128, 2, 512], BF16, 12096)
    ZR = r2v("ZR", [8, 2 * 512], F32, 14144)
    ACCW = r2v("ACCW", [128, C, W512], F32, 18432)
    STG = r2v("STG", [128, 4, D], F32, 34816)
    COSQ = r2v("COSQ", [128, T], BF16, 0)
    SINQ = r2v("SINQ", [128, T], BF16, 4096)
    COSK = r2v("COSK", [128, T], BF16, 8192)
    SINK = r2v("SINK", [128, T], BF16, 12288)
    QW = r2v("QW", [128, 1, 4, W512], BF16, 16384)
    KW = r2v("KW", [128, 2, 4, W512], BF16, 20480)
    VW = r2v("VW", [128, 2, 4, 520], BF16, 28672)
    OFM = r2v("OFM", [128, 4, T], BF16, 36992)
    MRES = r2v("MRES", [128, NK2A, T], BF16, 0)
    MRB = r2v("MRB", [128, 8, T // 4], BF16, 45056)
    X1RR = r2v("X1RR", [128, 2, 512], F32, 0)
    STGO = r2v("STGO", [128, 4, D], F32, 36992)

    PSTG = E(nc.sbuf_tensor("PSTG", [128, 2, 512], BF16))
    MSTG = E(nc.sbuf_tensor("MSTG", [1, 2, W512], BF16))
    STT = E(nc.sbuf_tensor("STT", [128, 3, 512], F32))
    SQB = E(nc.sbuf_tensor("SQB", [128, 2, 512], BF16))
    QKS = SQB[:, 1, :]
    MAXC = E(nc.sbuf_tensor("MAXC", [128, 2], F32))
    IDB = E(nc.sbuf_tensor("IDB", [128, 128], BF16))
    IDF = E(nc.sbuf_tensor("IDF", [128, 128], F32))
    ROTM = E(nc.sbuf_tensor("ROTM", [128, 128], BF16))
    TRID = E(nc.sbuf_tensor("TRID", [128, 128], BF16))
    TRIE = E(nc.sbuf_tensor("TRIE", [128, 128], BF16))
    HON = E(nc.sbuf_tensor("HON", [128, 128], BF16))
    LNC32 = E(nc.sbuf_tensor("LNC32", [128, 128], F32))
    LNCBF = E(nc.sbuf_tensor("LNCBF", [128, 128], BF16))
    ONES1 = E(nc.sbuf_tensor("ONES1", [1, 128], BF16))
    ONE1F = E(nc.sbuf_tensor("ONE1F", [128, 128], F32))
    ONECB = E(nc.sbuf_tensor("ONECB", [128, 1], BF16))
    ZERB = E(nc.sbuf_tensor("ZERB", [128, 128], BF16))
    SEL4 = E(nc.sbuf_tensor("SEL4", [128, 8], BF16))
    SELZ = E(nc.sbuf_tensor("SELZ", [8, 128], F32))
    WQB = E(nc.sbuf_tensor("WQB", [128, 2, C * 128], BF16))
    WOB = E(nc.sbuf_tensor("WOB", [128, 4 * 128], BF16))
    WPB = E(nc.sbuf_tensor("WPB", [128, 3, C * 128], BF16))
    QSC = E(nc.sbuf_tensor("QSC", [128, 2], F32))
    GBC = E(nc.sbuf_tensor("GBC", [128, 3 * C], F32))
    B1C = E(nc.sbuf_tensor("B1C", [128, 2 * K2N], F32))

    PB = E(nc.psum_tensor("PB", [128, 3584], F32))
    PBF = E(nc.psum_tensor("PBF", [128, 1024], BF16))

    def bank(i, col0=0, ncol=512):
        return PB[:, 512 * i + col0: 512 * i + col0 + ncol]

    sems = {e: E(nc.semaphore("sem_" + e)) for e in ENGINES}
    blk = E(nc.Block())
    S = Sched()

    def XRc(c, lo, n):
        return XR[:, c * T + lo: c * T + lo + n]

    def ABFc(c, lo, n):
        return ABF[:, c * (T + 8) + lo: c * (T + 8) + lo + n]

    def LBFc(c, lo, n):
        return LBF[:, c * T + lo: c * T + lo + n]

    _consts = ((IDB[:], dc["ident_bf"]), (IDF[:], dc["ident_f32"]),
               (ROTM[:], dc["rotmat"]), (TRID[:], dc["tri_diag"]),
               (TRIE[:], dc["tri_edge"]), (HON[:], dc["hones"]),
               (LNC32[:], dc["lncol_f32"]), (LNCBF[:], dc["lncol_bf"]),
               (ONES1[:], dc["ones1_bf"]), (ONE1F[:], dc["ones1_f32"]),
               (ONECB[:], dc["onescol_bf"]), (ZERB[:], dc["zeros_bf"]),
               (SEL4[:], dc["sel4"]), (SELZ[:], dc["selzrow"]))
    for i, (dst, src) in enumerate(_consts):
        S.add("sync", (lambda e, dst=dst, src=src: e.dma_start(out=dst, in_=src)),
              writes=["const"] if i == len(_consts) - 1 else [],
              name=f"ld_const{i}")
    S.add("gpsimd", lambda e: e.dma_start(out=BSEL[:], in_=dc["bsel"]),
          writes=[], name="ld_bsel")
    S.add("gpsimd", lambda e: e.dma_start(out=CWS[:], in_=d_cw),
          writes=["convw"], name="ld_convw")

    def z_guards(e):
        for c in range(C):
            e.memset(ABF[:, c * (T + 8):c * (T + 8) + 4], 0)
            inst = e.memset(ABF[:, c * (T + 8) + T + 4:c * (T + 8) + T + 8], 0)
        return inst
    S.add("vector", z_guards, writes=["guards"], name="zguards")

    xv = xin.rearrange("(tt p) f -> p tt f", p=128)
    for qtr in range(4):
        def ld_q(e, qtr=qtr):
            return e.dma_start(out=STG[:], in_=xv[:, 4 * qtr:4 * (qtr + 1), :])
        S.add("sync", ld_q, writes=[("stg",)], name=f"ldx{qtr}")
        for tt4 in range(4):
            tt = 4 * qtr + tt4
            for cq in range(2):
                bk = (tt4 * 2 + cq) % 2
                def tp(e, tt4=tt4, cq=cq, bk=bk):
                    for c4 in range(4):
                        c = cq * 4 + c4
                        inst = e.transpose(
                            PB[:, 512 * bk + 128 * c4:512 * bk + 128 * (c4 + 1)],
                            STG[:, tt4, 128 * c:128 * (c + 1)], IDF[:])
                    return inst
                S.add("tensor", tp, reads=[("stg",), "const", ("pb", bk)],
                      writes=[("pb", bk)], name=f"tpx{tt}_{cq}")
                def ev(e, tt=tt, cq=cq, bk=bk):
                    for c4 in range(4):
                        c = cq * 4 + c4
                        inst = e.tensor_copy(
                            XRc(c, tt * 128, 128),
                            PB[:, 512 * bk + 128 * c4:512 * bk + 128 * (c4 + 1)])
                    return inst
                S.add("vector", ev, reads=[("pb", bk), ("stg",)],
                      writes=[("xr", tt // 4), ("pb", bk)], name=f"evx{tt}_{cq}")

    if tap("xr"):
        tapdram("xr", [128, C * T])
        S.add("gpsimd", lambda e: e.dma_start(out=dbg["xr"], in_=XR[:]),
              reads=[("xr", i) for i in range(4)], name="tap_xr")

    def ln_stats(tag, rhs, reads, bf=False):
        def mm_mean(e):
            for c in range(C):
                inst = e.matmul(bank(5), LNCBF[:] if bf else LNC32[:], rhs(c),
                                start=(c == 0), stop=(c == C - 1))
            return inst
        S.add("tensor", mm_mean, reads=list(reads) + ["const", ("pb", 5)],
              writes=[("pb", 5)], name=f"lnm_{tag}")
        for c in range(C):
            sl = 0
            def sq1(e, c=c, sl=sl):
                return e.activation(out=SQB[:, sl, :], in_=rhs(c), func=AF.Square)
            S.add("scalar", sq1, reads=list(reads) + [("sqb", sl)],
                  writes=[("sqbd", sl)], name=f"lnsq_{tag}_{c}")
            def mmsq(e, c=c, sl=sl):
                return e.matmul(bank(6), LNCBF[:], SQB[:, sl, :],
                                start=(c == 0), stop=(c == C - 1))
            S.add("tensor", mmsq, reads=[("sqbd", sl), "const", ("pb", 6)],
                  writes=[("pb", 6), ("sqb", sl)], name=f"lnsqm_{tag}_{c}")

        def stats(e):
            e.tensor_copy(STT[:, 2, :], bank(5))
            e.tensor_tensor(STT[:, 1, :], STT[:, 2, :], STT[:, 2, :], ALU.mult)
            e.scalar_tensor_tensor(STT[:, 1, :], STT[:, 1, :], -1.0,
                                   bank(6), ALU.mult, ALU.add)
            e.tensor_scalar_add(STT[:, 1, :], STT[:, 1, :], LNEPS)
            return e.reciprocal(STT[:, 1, :], STT[:, 1, :])
        S.add("vector", stats,
              reads=[("pb", 5), ("pb", 6), ("stt", 2), ("stt", 1), ("stt", 0)],
              writes=[("stt", 2), ("stt", 1), ("stt", 0), ("pb", 6), ("pb", 5)],
              name=f"lnv_{tag}")
        S.add("scalar",
              lambda e: e.activation(out=STT[:, 0, :], in_=STT[:, 1, :], func=AF.Sqrt),
              reads=[("stt", 1), ("stt", 0)], writes=[("stt", 0)], name=f"lnr_{tag}")

        def negm(e):
            return e.scalar_tensor_tensor(STT[:, 1, :], STT[:, 2, :], -1.0,
                                          STT[:, 0, :], ALU.mult, ALU.mult)
        S.add("vector", negm, reads=[("stt", 0), ("stt", 2), ("stt", 1)],
              writes=[("stt", 1)], name=f"lnn_{tag}")

    for w in range(NW):
        ln_stats(f"l1w{w}", lambda c, w=w: XRc(c, w * W512, W512), [("xr", w)])
        for c in range(C):
            def apa(e, c=c, w=w):
                return e.tensor_tensor(STT[:, 2, :], XRc(c, w * W512, W512),
                                       STT[:, 0, :], ALU.mult)
            S.add("vector", apa, reads=[("stt", 0), ("xr", w), ("stt", 2)],
                  writes=[("stt", 2)], name=f"l1a{w}_{c}")
            def apb(e, c=c, w=w):
                return e.tensor_tensor(ABFc(c, w * W512 + 4, W512),
                                       STT[:, 2, :], STT[:, 1, :], ALU.add)
            S.add("vector", apb, reads=[("stt", 2), ("stt", 1), "guards"],
                  writes=[("ln1", c, w)], name=f"l1b{w}_{c}")

    if tap("ln1"):
        tapdram("ln1", [128, C * (T + 8)])
        S.add("gpsimd", lambda e: e.dma_start(out=dbg["ln1"], in_=ABF[:]),
              reads=[("ln1", c, w) for c in range(C) for w in range(NW)],
              name="tap_ln1")

    if stage >= 2:
        for w in range(NW):
            for half in range(2):
                def mml(e, w=w, half=half):
                    for ct in range(4):
                        c = half * 4 + ct
                        tp = (0, 32 * ct) if ct > 0 else None
                        for k in range(K9):
                            inst = e.matmul(
                                PB[32 * ct:32 * ct + 18,
                                   512 * half:512 * half + 512],
                                CWS[:, (c * K9 + k) * 18:(c * K9 + k) * 18 + 18],
                                ABFc(c, w * W512 + k, W512),
                                start=(k == 0), stop=(k == K9 - 1),
                                tile_position=tp)
                    return inst
                S.add("tensor", mml,
                      reads=[("ln1", half * 4 + ct, w) for ct in range(4)] +
                            ["convw", ("pb", half)],
                      writes=[("pb", half)], name=f"cvl{w}_{half}")
                def clamp(e, half=half):
                    return e.tensor_scalar_min(bank(half), bank(half), 60.0)
                S.add("vector", clamp, reads=[("pb", half)], writes=[("pb", half)],
                      name=f"cvc{w}_{half}")
                def expE(e, half=half):
                    return e.activation(out=CONVS[:, half, :], in_=bank(half),
                                        func=AF.Exp)
                S.add("scalar", expE, reads=[("pb", half), ("cv_e", half)],
                      writes=[("cv_e", half), ("pb", half)], name=f"cve{w}_{half}")
                def mmz(e, half=half):
                    return e.matmul(PB[0:8, 1024:1536], SEL4[:], CONVS[:, half, :],
                                    start=True, stop=True)
                S.add("tensor", mmz, reads=[("cv_e", half), "const", ("pb", 2)],
                      writes=[("pb", 2)], name=f"cvz{w}_{half}")
                def zinv(e, half=half):
                    e.activation(out=ZR[:, half * 512:(half + 1) * 512],
                                 in_=PB[0:8, 1024:1536], func=AF.Ln)
                    return e.activation(out=ZR[:, half * 512:(half + 1) * 512],
                                        in_=ZR[:, half * 512:(half + 1) * 512],
                                        func=AF.Exp, scale=f32(-1.0))
                S.add("scalar", zinv, reads=[("pb", 2), ("cv_zi", half)],
                      writes=[("cv_zi", half), ("pb", 2)],
                      name=f"cvzi{w}_{half}")
                def mmzr(e, half=half):
                    return e.matmul(bank(3), SELZ[:],
                                    ZR[:, half * 512:(half + 1) * 512],
                                    start=True, stop=True)
                S.add("tensor", mmzr, reads=[("cv_zi", half), "const", ("pb", 3)],
                      writes=[("pb", 3)], name=f"cvzr{w}_{half}")
                def wnorm(e, half=half):
                    return e.tensor_tensor(CONVS[:, half, :], CONVS[:, half, :],
                                           bank(3), ALU.mult)
                S.add("vector", wnorm, reads=[("cv_e", half), ("pb", 3)],
                      writes=[("cv_w", half), ("cv_e", half), ("pb", 3)],
                      name=f"cvw{w}_{half}")
            for c in range(C):
                half, ct = c // 4, c % 4
                for k in range(K9):
                    pb = 3 + (k % 2)
                    def mmb(e, k=k, ct=ct, half=half, pb=pb):
                        return e.matmul(bank(pb), BSEL[:, k, ct, :],
                                        CONVS[:, half, :], start=True, stop=True)
                    S.add("tensor", mmb,
                          reads=[("cv_w", half), "convw", ("pb", pb)],
                          writes=[("pb", pb)], name=f"cvb{w}_{c}_{k}")
                    def mix(e, k=k, c=c, w=w, pb=pb):
                        if k == 0:
                            return e.tensor_tensor(ACCW[:, c, :],
                                                   ABFc(c, w * W512 + k, W512),
                                                   bank(pb), ALU.mult)
                        e.tensor_tensor(STT[:, 2, :], ABFc(c, w * W512 + k, W512),
                                        bank(pb), ALU.mult)
                        return e.tensor_tensor(ACCW[:, c, :], ACCW[:, c, :],
                                               STT[:, 2, :], ALU.add)
                    S.add("vector", mix,
                          reads=[("pb", pb), ("ln1", c, w), ("acc", c), ("stt", 2)],
                          writes=[("acc", c), ("pb", pb), ("stt", 2)],
                          name=f"cvm{w}_{c}_{k}")
            if tap("conv") and w == 0:
                tapdram("conv", [128, C, W512])
                S.add("gpsimd", lambda e: e.dma_start(out=dbg["conv"], in_=ACCW[:]),
                      reads=[("acc", c) for c in range(C)], name="tap_conv")
            ln_stats(f"lcw{w}", lambda c: ACCW[:, c, :],
                     [("acc", c) for c in range(C)])
            for c in range(C):
                def xa(e, c=c):
                    return e.tensor_tensor(STT[:, 2, :], ACCW[:, c, :],
                                           STT[:, 0, :], ALU.mult)
                S.add("vector", xa, reads=[("stt", 0), ("acc", c), ("stt", 2)],
                      writes=[("stt", 2)], name=f"x1a{w}_{c}")
                def xb(e, c=c):
                    return e.tensor_tensor(STT[:, 2, :], STT[:, 2, :],
                                           STT[:, 1, :], ALU.add)
                S.add("vector", xb, reads=[("stt", 1)], writes=[("stt", 2)],
                      name=f"x1b{w}_{c}")
                def xc(e, c=c, w=w):
                    return e.tensor_tensor(XRc(c, w * W512, W512),
                                           XRc(c, w * W512, W512),
                                           STT[:, 2, :], ALU.add)
                S.add("vector", xc, reads=[("stt", 2), ("xr", w)],
                      writes=[("xr", w), ("acc", c)], name=f"x1c{w}_{c}")

        if tap("x1"):
            tapdram("x1", [128, C * T])
            S.add("gpsimd", lambda e: e.dma_start(out=dbg["x1"], in_=XR[:]),
                  reads=[("xr", i) for i in range(4)], name="tap_x1")

    if stage >= 3:
        def spill(e):
            return e.dma_start(out=d_x1, in_=XR[:].rearrange("p (c t) -> p c t", t=T))
        S.add("sync", spill, reads=[("xr", i) for i in range(4)], writes=["x1sp"],
              name="x1spill")
        for w in range(NW):
            ln_stats(f"l2w{w}", lambda c, w=w: XRc(c, w * W512, W512), [("xr", w)])
            for c in range(C):
                def ya(e, c=c, w=w):
                    return e.tensor_tensor(XRc(c, w * W512, W512),
                                           XRc(c, w * W512, W512),
                                           STT[:, 0, :], ALU.mult)
                S.add("vector", ya, reads=[("stt", 0), "x1sp"],
                      writes=[("xr", w)], name=f"y2a{w}_{c}")
                def yb(e, c=c, w=w):
                    return e.tensor_tensor(XRc(c, w * W512, W512),
                                           XRc(c, w * W512, W512),
                                           STT[:, 1, :], ALU.add)
                S.add("vector", yb, reads=[("stt", 1)], writes=[("xr", w)],
                      name=f"y2b{w}_{c}")
                def ymir(e, c=c, w=w):
                    return e.copy(out=ABFc(c, w * W512 + 4, W512),
                                  in_=XRc(c, w * W512, W512))
                S.add("scalar", ymir,
                      reads=[("xr", w)],
                      writes=[("ybf", c, w), ("ln1", c, w)], name=f"ymir{w}_{c}")

        if tap("y0"):
            tapdram("y0", [128, C * T])
            S.add("gpsimd", lambda e: e.dma_start(out=dbg["y0"], in_=XR[:]),
                  reads=[("xr", i) for i in range(4)], name="tap_y0")

    QWK = [("qw", c2) for c2 in range(4)]
    KWK = [("kw", s, c2) for s in range(2) for c2 in range(4)]
    VWK = [("vw", s, v) for s in range(2) for v in range(4)] + \
          [("vw1", s) for s in range(2)]
    OFK = [("ofm", c2, w, hp) for c2 in range(4) for w in range(NW)
           for hp in range(2)]
    MRK = ([("mres", j, tq) for j in range(NK2A) for tq in range(NW)] +
           [("mresd", j, tq) for j in range(NK2A) for tq in range(NW)] +
           [("mrbs", x) for x in range(NW)] +
           [("mrbsd", x) for x in range(NW)] +
           [("mrl", x) for x in range(2)])

    if stage >= 4:
        nlayers = 2 if stage >= 5 else 1
        for l in range(nlayers):
            if l == 0:
                S.virt(writes=(["rot", ("stg",), "convw",
                                ("cv_w", 0), ("cv_w", 1), ("cv_e", 0), ("cv_e", 1),
                                ("cv_zi", 0), ("cv_zi", 1)] +
                               [("acc", c) for c in range(C)] +
                               QWK + KWK + VWK + OFK), name="retag_conv2attn")
            else:
                S.virt(writes=["rot"] + QWK + KWK + VWK + OFK + MRK,
                       name=f"retag_ffn2attn{l}")
            _ffn_layer(S, locals(), l)

    if stage >= 9:
        S.virt(writes=[("x1r", 0), ("x1r", 1), ("stgo2",)] + OFK + MRK,
               name="retag_exit")
        outv = out.rearrange("(tt p) f -> p tt f", p=128)
        for w in range(NW):
            ln_stats(f"lfw{w}", lambda c, w=w: XRc(c, w * W512, W512), [("xr", w)])
            for c in range(C):
                def fa(e, c=c, w=w):
                    return e.tensor_tensor(STT[:, 2, :], XRc(c, w * W512, W512),
                                           STT[:, 0, :], ALU.mult)
                S.add("vector", fa, reads=[("stt", 0), ("xr", w), ("stt", 2)],
                      writes=[("stt", 2)], name=f"fna{w}_{c}")
                def fb(e, c=c, w=w):
                    return e.tensor_tensor(STT[:, 2, :], STT[:, 2, :],
                                           STT[:, 1, :], ALU.add)
                S.add("vector", fb, reads=[("stt", 1)], writes=[("stt", 2)],
                      name=f"fnb{w}_{c}")
                sl = c % 2
                def ldx1(e, c=c, w=w, sl=sl):
                    return e.dma_start(out=X1RR[:, sl, :],
                                       in_=d_x1[:, c, w * W512:(w + 1) * W512])
                S.add("sync", ldx1, reads=["x1sp", ("x1r", sl)],
                      writes=[("x1rd", sl)], name=f"ldx1_{w}_{c}")
                def fc(e, c=c, sl=sl):
                    return e.tensor_tensor(STT[:, 2, :], STT[:, 2, :],
                                           X1RR[:, sl, :], ALU.add)
                S.add("vector", fc, reads=[("x1rd", sl)],
                      writes=[("stt", 2), ("x1r", sl)], name=f"fnc{w}_{c}")
                bk = c % 2
                def ftp(e, c=c, bk=bk):
                    for q in range(4):
                        inst = e.transpose(
                            PB[:, 512 * bk + 128 * q:512 * bk + 128 * (q + 1)],
                            STT[:, 2, 128 * q:128 * (q + 1)], IDF[:])
                    return inst
                S.add("tensor", ftp, reads=[("stt", 2), "const", ("pb", bk)],
                      writes=[("pb", bk)], name=f"ftp{w}_{c}")
                def fev(e, c=c, bk=bk):
                    for q in range(4):
                        inst = e.tensor_copy(
                            STGO[:, q, 128 * c:128 * (c + 1)],
                            PB[:, 512 * bk + 128 * q:512 * bk + 128 * (q + 1)])
                    return inst
                S.add("vector", fev, reads=[("pb", bk), ("stgo2",)],
                      writes=[("pb", bk), ("stgo", c)], name=f"fev{w}_{c}")
            def sto(e, w=w):
                return e.dma_start(out=outv[:, 4 * w:4 * (w + 1), :], in_=STGO[:])
            S.add("sync", sto, reads=[("stgo", c) for c in range(C)],
                  writes=["outd", ("stgo2",)], name=f"sto{w}")
        S.add("sync", lambda e: e.nop(), reads=["outd"], name="outwait")

    S.emit(blk, sems)
    ctx.close()
    return nc, dbg


def _ffn_layer(S, L, l):
    nc = L["nc"]
    PB, PBF = L["PB"], L["PBF"]
    bank = L["bank"]
    XRc, ABFc, LBFc = L["XRc"], L["ABFc"], L["LBFc"]
    STT, SQB, QKS, MAXC = L["STT"], L["SQB"], L["QKS"], L["MAXC"]
    WPB = L["WPB"]
    QW, KW, VW, OFM = L["QW"], L["KW"], L["VW"], L["OFM"]
    COSQ, SINQ, COSK, SINK = L["COSQ"], L["SINQ"], L["COSK"], L["SINK"]
    MRES, MRB = L["MRES"], L["MRB"]
    PSTG, MSTG = L["PSTG"], L["MSTG"]
    WQB, WOB = L["WQB"], L["WOB"]
    QSC, GBC, B1C = L["QSC"], L["GBC"], L["B1C"]
    HON, ROTM, IDB, IDF = L["HON"], L["ROTM"], L["IDB"], L["IDF"]
    TRID, TRIE = L["TRID"], L["TRIE"]
    ONES1, ONE1F, ONECB = L["ONES1"], L["ONE1F"], L["ONECB"]
    ZERB = L["ZERB"]
    d_wqkv, d_wo, d_w1, d_w2 = L["d_wqkv"], L["d_wo"], L["d_w1"], L["d_w2"]
    d_qs, d_ks = L["d_qs"], L["d_ks"]
    d_lng, d_lnb, d_b2 = L["d_lng"], L["d_lnb"], L["d_b2"]
    d_b1a, d_b1g, d_m = L["d_b1a"], L["d_b1g"], L["d_m"]
    MRB = L["MRB"]
    dc = L["dc"]
    dbg, taps, tapdram = L["dbg"], L["taps"], L["tapdram"]
    ln_stats = L["ln_stats"]
    XRt = L["XR"]

    def tap(name):
        return name in taps

    YK = lambda c, w: ("ybf", c, w)

    for i, (dst, src) in enumerate((
            (QSC[:, 0:1], d_qs[l]), (QSC[:, 1:2], d_ks[l]),
            (GBC[:, 0:C], d_lng[l]), (GBC[:, C:2 * C], d_lnb[l]),
            (GBC[:, 2 * C:3 * C], d_b2[l]),
            (B1C[:, 0:K2N], d_b1a[l]), (B1C[:, K2N:2 * K2N], d_b1g[l]))):
        S.add("sync", (lambda e, dst=dst, src=src: e.dma_start(out=dst, in_=src)),
              writes=[("lsm",)], name=f"ldsm{l}_{i}")

    for i, (dst, src) in enumerate(((COSQ[:], dc["cosq"]), (SINQ[:], dc["sinq"]),
                                    (COSK[:], dc["cosk"]), (SINK[:], dc["sink"]))):
        S.add("sync", (lambda e, dst=dst, src=src: e.dma_start(out=dst, in_=src)),
              writes=["rot"], name=f"ld_rot{l}_{i}")

    for w in range(NW):
        for n in range(QKN):
            wsl = n % 2
            def ldwq(e, n=n, l=l, wsl=wsl):
                return e.dma_start(out=WQB[:, wsl, :], in_=d_wqkv[l, :, n, :, :])
            S.add("gpsimd", ldwq, reads=[("wqb", wsl)], writes=[("wqbd", wsl)],
                  name=f"ldwq{l}_{w}_{n}")
            pbk = n % 2
            yreads = [YK(c, w) for c in range(C)]
            if w > 0:
                yreads += [YK(c, w - 1) for c in range(4, C)]
            def mmq(e, n=n, w=w, pbk=pbk, wsl=wsl):
                for c in range(C):
                    off = w * W512 + (3 if c >= 4 else 4)
                    inst = e.matmul(bank(pbk), WQB[:, wsl, 128 * c:128 * (c + 1)],
                                    ABFc(c, off, W512),
                                    start=(c == 0), stop=(c == C - 1))
                return inst
            S.add("tensor", mmq, reads=[("wqbd", wsl), ("pb", pbk)] + yreads,
                  writes=[("pb", pbk), ("wqb", wsl)], name=f"mmqkv{l}_{w}_{n}")

            if n < 8:
                isq = n < 4
                c2 = n % 4
                tbl_c, tbl_s = (COSQ, SINQ) if isq else (COSK, SINK)
                scol = QSC[:, 0:1] if isq else QSC[:, 1:2]
                def sq(e, pbk=pbk):
                    return e.activation(out=SQB[:, 0, :], in_=bank(pbk),
                                        func=AF.Square)
                S.add("scalar", sq, reads=[("pb", pbk), ("sqb", 0)],
                      writes=[("sqbd", 0)], name=f"qksq{l}_{w}_{n}")
                def mmn(e):
                    return e.matmul(bank(2), HON[:], SQB[:, 0, :],
                                    start=True, stop=True)
                S.add("tensor", mmn, reads=[("sqbd", 0), "const", ("pb", 2)],
                      writes=[("pb", 2), ("sqb", 0)], name=f"qkn{l}_{w}_{n}")
                def ninv(e):
                    return e.tensor_scalar_max(bank(2), bank(2), 1e-24)
                S.add("vector", ninv, reads=[("pb", 2)],
                      writes=[("pb", 2)], name=f"qkni{l}_{w}_{n}")
                def nsq(e):
                    e.activation(out=STT[:, 0, :], in_=bank(2), func=AF.Ln)
                    return e.activation(out=STT[:, 0, :], in_=STT[:, 0, :],
                                        func=AF.Exp, scale=f32(-0.5))
                S.add("scalar", nsq, reads=[("pb", 2), ("stt", 0)],
                      writes=[("stt", 0), ("pb", 2)], name=f"qkns{l}_{w}_{n}")
                def qbf(e, pbk=pbk, scol=scol):
                    return e.activation(out=QKS, in_=bank(pbk),
                                        func=AF.Copy, scale=scol)
                S.add("scalar", qbf, reads=[("pb", pbk), ("lsm",), ("sqb", 1)],
                      writes=[("sqbd", 1), ("pb", pbk)], name=f"qkb{l}_{w}_{n}")
                def mmrot(e):
                    return e.matmul(bank(3), ROTM[:], QKS,
                                    start=True, stop=True)
                S.add("tensor", mmrot, reads=[("sqbd", 1), "const", ("pb", 3)],
                      writes=[("pb", 3)], name=f"qkr{l}_{w}_{n}")
                def rot1(e, w=w, tbl_c=tbl_c):
                    return e.tensor_tensor(STT[:, 2, :], QKS,
                                           tbl_c[:, w * W512:(w + 1) * W512],
                                           ALU.mult)
                S.add("vector", rot1, reads=[("sqbd", 1), "rot", ("stt", 2)],
                      writes=[("stt", 2), ("sqb", 1)], name=f"qkr1{l}_{w}_{n}")
                def rot2(e, w=w, tbl_s=tbl_s):
                    e.tensor_tensor(STT[:, 1, :], bank(3),
                                    tbl_s[:, w * W512:(w + 1) * W512], ALU.mult)
                    return e.tensor_tensor(STT[:, 2, :], STT[:, 2, :],
                                           STT[:, 1, :], ALU.add)
                S.add("vector", rot2,
                      reads=[("pb", 3), "rot", ("stt", 2), ("stt", 1)],
                      writes=[("stt", 2), ("stt", 1), ("pb", 3)],
                      name=f"qkr2{l}_{w}_{n}")
                if isq:
                    dkey, dst = ("qw", c2), (lambda e, c2=c2: QW[:, 0, c2, :])
                else:
                    dkey, dst = ("kw", w % 2, c2), \
                        (lambda e, w=w, c2=c2: KW[:, w % 2, c2, :])
                def rot3(e, dst=dst):
                    return e.tensor_tensor(dst(e), STT[:, 2, :],
                                           STT[:, 0, :], ALU.mult)
                S.add("vector", rot3, reads=[("stt", 2), ("stt", 0), "rot", dkey],
                      writes=[dkey], name=f"qkr3{l}_{w}_{n}")
            else:
                vc = n - 8
                vsl = w % 2
                def vbf(e, pbk=pbk):
                    return e.activation(out=QKS, in_=bank(pbk), func=AF.Copy)
                S.add("scalar", vbf, reads=[("pb", pbk), ("sqb", 1)],
                      writes=[("sqbd", 1), ("pb", pbk)], name=f"vb{l}_{w}_{vc}")
                def vtp(e):
                    for q in range(4):
                        inst = e.transpose(PBF[:, 128 * q:128 * (q + 1)],
                                           SQB[:, 1, 128 * q:128 * (q + 1)], IDB[:])
                    return inst
                S.add("tensor", vtp, reads=[("sqbd", 1), "const", ("pb", 7)],
                      writes=[("pb", 7)], name=f"vt{l}_{w}_{vc}")
                def vev(e, vsl=vsl, vc=vc):
                    for q in range(4):
                        src = PBF[:, 128 * q:128 * (q + 1)].rearrange(
                            "p (hh dd) -> p hh dd", hh=2)
                        inst = e.tensor_copy(
                            VW[:, vsl, q, 65 * 2 * vc:65 * 2 * vc + 130].rearrange(
                                "p (hh dd) -> p hh dd", dd=65)[:, :, 0:64], src)
                    return inst
                S.add("vector", vev,
                      reads=[("pb", 7), "rot", ("vw", vsl, vc), ("sqb", 1)],
                      writes=[("vw", vsl, vc), ("pb", 7), ("sqb", 1)],
                      name=f"vev{l}_{w}_{vc}")
        def vones(e, vsl=w % 2):
            for q in range(4):
                inst = e.memset(
                    VW[:, vsl, q, :].rearrange("p (hh dd) -> p hh dd",
                                               dd=65)[:, :, 64:65], 1.0)
            return inst
        S.add("vector", vones,
              reads=[("vw", w % 2, vc) for vc in range(4)] + ["rot", ("vw1", w % 2)],
              writes=[("vw1", w % 2)], name=f"vones{l}_{w}")

        if w == 0 and l == 0:
            for nm, buf, keys in (
                    ("qw0", QW, [("qw", c2) for c2 in range(4)]),
                    ("kw0", KW, [("kw", 0, c2) for c2 in range(4)]),
                    ("vw0", VW, [("vw", 0, vc) for vc in range(4)] + [("vw1", 0)])):
                if tap(nm):
                    tapdram(nm, [128, buf.shape[1] * buf.shape[2] * buf.shape[3]])
                    S.add("gpsimd",
                          (lambda e, nm=nm, buf=buf:
                           e.dma_start(out=dbg[nm], in_=buf[:].rearrange(
                               "p a b c -> p (a b c)"))),
                          reads=keys, name=f"tap_{nm}")

        for h in range(NH):
            c2, hp = h // 2, h % 2
            rows = slice(64 * hp, 64 * hp + 64)
            for qi in range(4):
                kj_lo = max(0, 4 * w + qi - 4)
                kj_hi = 4 * w + qi
                ncol = 128 * (kj_hi - kj_lo + 1)
                kread = sorted({("kw", (kj // 4) % 2, c2)
                                for kj in range(kj_lo, kj_hi + 1)})
                def mmqk1(e, qi=qi, kj_lo=kj_lo, kj_hi=kj_hi, c2=c2, rows=rows):
                    inst = None
                    for i, kj in enumerate(range(kj_lo, kj_hi + 1)):
                        inst = e.matmul(
                            PB[:, 128 * i:128 * (i + 1)],
                            QW[rows, 0, c2, 128 * qi:128 * (qi + 1)],
                            KW[rows, (kj // 4) % 2, c2,
                               128 * (kj % 4):128 * (kj % 4) + 128],
                            start=True, stop=True)
                    return inst
                S.add("tensor", mmqk1,
                      reads=[("qw", c2), ("pb", 0), ("pb", 1)] + kread,
                      writes=[("pb", 0), ("pb", 1)], name=f"qk1_{l}_{w}_{h}_{qi}")
                def redm(e, ncol=ncol, qi=qi):
                    return e.reduce_max(MAXC[:, qi % 2:qi % 2 + 1],
                                        PB[:, 0:ncol], axis=AXX, negate=True)
                S.add("vector", redm, reads=[("pb", 0), ("pb", 1), ("mxc", qi % 2)],
                      writes=[("mxcd", qi % 2), ("pb", 0), ("pb", 1)],
                      name=f"rmx{l}_{w}_{h}_{qi}")
                def tpm(e, qi=qi):
                    return e.transpose(
                        PB[0:1, 3072 + 128 * qi:3072 + 128 * (qi + 1)],
                        MAXC[:, qi % 2:qi % 2 + 1], IDF[:])
                S.add("tensor", tpm, reads=[("mxcd", qi % 2), "const", ("pb", 6)],
                      writes=[("pb", 6), ("mxc", qi % 2)],
                      name=f"tpm{l}_{w}_{h}_{qi}")
            msl = h % 2
            def mev(e, msl=msl):
                return e.activation(out=MSTG[:, msl, :], in_=PB[0:1, 3072:3584],
                                    func=AF.Copy)
            S.add("scalar", mev, reads=[("pb", 6), ("mst", msl)],
                  writes=[("mstd", msl), ("pb", 6)], name=f"mev{l}_{w}_{h}")
            if l == 0 and w == 0 and h <= 1 and tap("mst0"):
                nm = f"mst{h}"
                tapdram(nm, [1, 512])
                S.add("gpsimd",
                      (lambda e, msl=msl, nm=nm:
                       e.dma_start(out=dbg[nm], in_=MSTG[:, msl, :])),
                      reads=[("mstd", msl)], name=f"tap_{nm}")

            pv_started = {"d": False, "z": False}
            for i_kj, kj in enumerate(range(max(0, 4 * w - 4), 4 * w + 4)):
                qi_lo = max(0, kj - 4 * w)
                qi_hi = min(3, kj - 4 * w + 4)
                M = 128 * (qi_hi - qi_lo + 1)
                qb = 2 + (i_kj % 2)
                def mmqk2(e, kj=kj, qi_lo=qi_lo, M=M, c2=c2, rows=rows,
                          qb=qb, msl=msl):
                    e.matmul(bank(qb, 0, M),
                             KW[rows, (kj // 4) % 2, c2,
                                128 * (kj % 4):128 * (kj % 4) + 128],
                             QW[rows, 0, c2, 128 * qi_lo:128 * qi_lo + M],
                             start=True, stop=False)
                    return e.matmul(bank(qb, 0, M), ONES1[:],
                                    MSTG[:, msl, 128 * qi_lo:128 * qi_lo + M],
                                    start=False, stop=True)
                S.add("tensor", mmqk2,
                      reads=[("qw", c2), ("kw", (kj // 4) % 2, c2),
                             ("mstd", msl), ("pb", qb)],
                      writes=[("pb", qb)], name=f"qk2_{l}_{w}_{h}_{kj}")
                psl = i_kj % 2
                def pexp(e, qb=qb, M=M, psl=psl):
                    return e.activation(out=PSTG[:, psl, 0:M], in_=bank(qb, 0, M),
                                        func=AF.Exp)
                S.add("scalar", pexp, reads=[("pb", qb), ("pst", psl)],
                      writes=[("pstd", psl), ("pb", qb)],
                      name=f"pex{l}_{w}_{h}_{kj}")
                qi_d = kj - 4 * w
                qi_e = kj - 4 * w + 4
                if qi_lo <= qi_d <= qi_hi or qi_lo <= qi_e <= qi_hi:
                    def tri(e, qi_d=qi_d, qi_e=qi_e, qi_lo=qi_lo, qi_hi=qi_hi,
                            psl=psl):
                        inst = None
                        if qi_lo <= qi_d <= qi_hi:
                            co = 128 * (qi_d - qi_lo)
                            inst = e.tensor_tensor(PSTG[:, psl, co:co + 128],
                                                   PSTG[:, psl, co:co + 128],
                                                   TRID[:], ALU.mult)
                        if qi_lo <= qi_e <= qi_hi:
                            co = 128 * (qi_e - qi_lo)
                            inst = e.tensor_tensor(PSTG[:, psl, co:co + 128],
                                                   PSTG[:, psl, co:co + 128],
                                                   TRIE[:], ALU.mult)
                        return inst
                    S.add("vector", tri, reads=[("pstd", psl), "const"],
                          writes=[("pstd", psl)], name=f"tri{l}_{w}_{h}_{kj}")
                if l == 0 and w == 0 and h <= 1 and kj == 0 and tap("pst0"):
                    nm = f"pst{h}"
                    tapdram(nm, [128, 512])
                    S.add("gpsimd",
                          (lambda e, psl=psl, M=M, nm=nm:
                           e.dma_start(out=dbg[nm][:, 0:M],
                                       in_=PSTG[:, psl, 0:M])),
                          reads=[("pstd", psl)], name=f"tap_{nm}")
                def pv(e, kj=kj, qi_lo=qi_lo, qi_hi=qi_hi, psl=psl, h=h, hp=hp,
                       w=w, c2=c2, pv_started=pv_started):
                    vsl = (kj // 4) % 2
                    vq = kj % 4
                    inst = None
                    if hp == 1 and not pv_started["d"]:
                        e.matmul(PB[:, 2048:2560], ZERB[:],
                                 KW[:, vsl, c2, :], start=True, stop=False)
                        pv_started["d"] = True
                    for qi in range(qi_lo, qi_hi + 1):
                        st = (hp == 0) and (not pv_started["d"])
                        pv_started["d"] = True
                        sp = (kj == 4 * w + 3 and qi == qi_hi)
                        co = 128 * (qi - qi_lo)
                        if hp == 0:
                            inst = e.matmul(
                                PB[0:65, 2048 + 128 * qi:2048 + 128 * (qi + 1)],
                                VW[:, vsl, vq, 65 * h:65 * h + 65],
                                PSTG[:, psl, co:co + 128], start=st, stop=sp)
                        else:
                            inst = e.matmul(
                                PB[0:1, 2048 + 128 * qi:2048 + 128 * (qi + 1)],
                                ONECB[:], PSTG[:, psl, co:co + 128],
                                start=False, stop=sp)
                            inst = e.matmul(
                                PB[64:128, 2048 + 128 * qi:2048 + 128 * (qi + 1)],
                                VW[:, vsl, vq, 65 * h:65 * h + 64],
                                PSTG[:, psl, co:co + 128], start=False, stop=sp,
                                tile_position=(0, 64))
                    return inst
                S.add("tensor", pv,
                      reads=[("pstd", psl), ("vw", (kj // 4) % 2, kj % 4),
                             ("vw1", (kj // 4) % 2), "const", ("pb", 4)],
                      writes=[("pb", 4), ("pst", psl)],
                      name=f"pv{l}_{w}_{h}_{kj}")
            if l == 0 and w == 0 and h <= 1 and tap("pv0"):
                nm = f"pv{h}"
                tapdram(nm, [128, 512])
                def pvdump(e):
                    return e.tensor_copy(STT[:, 2, :], PB[:, 2048:2560])
                S.add("vector", pvdump, reads=[("pb", 4), ("stt", 2)],
                      writes=[("stt", 2), ("pvdmp",)], name=f"pvd{h}")
                S.add("gpsimd",
                      (lambda e, nm=nm: e.dma_start(out=dbg[nm], in_=STT[:, 2, :])),
                      reads=[("pvdmp",), ("stt", 2)], name=f"tap_{nm}")
            zrow = 64 if hp == 0 else 0
            def zi(e, zrow=zrow):
                e.activation(out=STT[zrow:zrow + 1, 2, :],
                             in_=PB[zrow:zrow + 1, 2048:2560], func=AF.Ln)
                return e.activation(out=STT[zrow:zrow + 1, 2, :],
                                    in_=STT[zrow:zrow + 1, 2, :], func=AF.Exp,
                                    scale=f32(-1.0))
            S.add("scalar", zi, reads=[("pb", 4), ("stt", 2)],
                  writes=[("sttz",), ("stt", 2)], name=f"zi{l}_{w}_{h}")
            def zb(e, zrow=zrow, hp=hp):
                tp = (0, 64) if hp else None
                return e.matmul(PB[64 * hp:64 * hp + 64, 2560:3072],
                                ONE1F[zrow:zrow + 1, 0:64],
                                STT[zrow:zrow + 1, 2, :],
                                start=True, stop=True, tile_position=tp)
            S.add("tensor", zb, reads=[("sttz",), "const", ("pb", 5)],
                  writes=[("pb", 5)], name=f"zb{l}_{w}_{h}")
            def oev(e, hp=hp, c2=c2, w=w):
                e.tensor_copy(STT[64 * hp:64 * hp + 64, 2, :],
                              PB[64 * hp:64 * hp + 64, 2048:2560])
                return e.tensor_tensor(
                    OFM[64 * hp:64 * hp + 64, c2, w * W512:(w + 1) * W512],
                    STT[64 * hp:64 * hp + 64, 2, :],
                    PB[64 * hp:64 * hp + 64, 2560:3072], ALU.mult)
            S.add("vector", oev,
                  reads=[("pb", 4), ("pb", 5), ("ofm", c2, w, hp), ("stt", 2)],
                  writes=[("ofm", c2, w, hp), ("pb", 4), ("pb", 5), ("stt", 2)],
                  name=f"oev{l}_{w}_{h}")

    if tap(f"ofm{l}"):
        tapdram(f"ofm{l}", [128, 4, T])
        S.add("gpsimd", lambda e: e.dma_start(out=dbg[f"ofm{l}"], in_=OFM[:]),
              reads=[("ofm", c2, w, hp) for c2 in range(4) for w in range(NW)
                     for hp in range(2)], name=f"tap_ofm{l}")

    for n in range(C):
        def ldwo(e, n=n, l=l):
            return e.dma_start(out=WOB[:], in_=d_wo[l, :, n, :, :])
        S.add("gpsimd", ldwo, reads=[("wob",)], writes=[("wobd",)],
              name=f"ldwo{l}_{n}")
        for w in range(NW):
            pbk = (n * NW + w) % 2
            def mmo(e, n=n, w=w, pbk=pbk):
                for c2 in range(4):
                    inst = e.matmul(bank(pbk), WOB[:, 128 * c2:128 * (c2 + 1)],
                                    OFM[:, c2, w * W512:(w + 1) * W512],
                                    start=(c2 == 0), stop=(c2 == 3))
                return inst
            S.add("tensor", mmo,
                  reads=[("wobd",), ("pb", pbk)] +
                        [("ofm", c2, w, hp) for c2 in range(4) for hp in range(2)],
                  writes=[("pb", pbk)] + ([("wob",)] if w == NW - 1 else []),
                  name=f"mmo{l}_{n}_{w}")
            def yadd(e, n=n, w=w, pbk=pbk):
                return e.tensor_tensor(XRc(n, w * W512, W512),
                                       XRc(n, w * W512, W512), bank(pbk), ALU.add)
            S.add("vector", yadd, reads=[("pb", pbk), ("xr", w)],
                  writes=[("xr", w), ("pb", pbk), ("wo_done", n, w)],
                  name=f"yadd{l}_{n}_{w}")
            def ymir2(e, n=n, w=w):
                return e.copy(out=ABFc(n, w * W512 + 4, W512),
                              in_=XRc(n, w * W512, W512))
            S.add("scalar", ymir2, reads=[("wo_done", n, w), ("xr", w)],
                  writes=[YK(n, w)], name=f"ym2{l}_{n}_{w}")

    if tap(f"ywo{l}"):
        tapdram(f"ywo{l}", [128, C * T])
        S.add("gpsimd", lambda e: e.dma_start(out=dbg[f"ywo{l}"], in_=XRt[:]),
              reads=[("wo_done", n, w) for n in range(C) for w in range(NW)],
              name=f"tap_ywo{l}")

    for w in range(NW):
        def rhs_bf(c, w=w):
            off = w * W512 + (3 if c >= 4 else 4)
            return ABFc(c, off, W512)
        rd = [YK(c, w) for c in range(C)]
        if w > 0:
            rd += [YK(c, w - 1) for c in range(4, C)]
        ln_stats(f"li{l}w{w}", rhs_bf, rd, bf=True)
        for c in range(C):
            def ia(e, c=c, w=w):
                off = w * W512 + (3 if c >= 4 else 4)
                return e.tensor_tensor(STT[:, 2, :], ABFc(c, off, W512),
                                       STT[:, 0, :], ALU.mult)
            S.add("vector", ia, reads=[("stt", 0), YK(c, w), ("stt", 2)] +
                  ([YK(c, w - 1)] if (w > 0 and c >= 4) else []),
                  writes=[("stt", 2)], name=f"ila{l}_{w}_{c}")
            def ib(e):
                return e.tensor_tensor(STT[:, 2, :], STT[:, 2, :],
                                       STT[:, 1, :], ALU.add)
            S.add("vector", ib, reads=[("stt", 1)], writes=[("stt", 2)],
                  name=f"ilb{l}_{w}_{c}")
            def ic(e, c=c, w=w):
                return e.activation(out=LBFc(c, w * W512, W512),
                                    in_=STT[:, 2, :], func=AF.Identity,
                                    scale=GBC[:, c:c + 1],
                                    bias=GBC[:, C + c:C + c + 1])
            S.add("scalar", ic, reads=[("stt", 2), ("lsm",)],
                  writes=[("lbf", c, w)], name=f"ilc{l}_{w}_{c}")

    if tap(f"lbf{l}"):
        tapdram(f"lbf{l}", [128, C * T])
        S.add("gpsimd", lambda e: e.dma_start(out=dbg[f"lbf{l}"], in_=L["LBF"][:]),
              reads=[("lbf", c, w) for c in range(C) for w in range(NW)],
              name=f"tap_lbf{l}")

    S.virt(writes=(["rot"] + [("qw", c2) for c2 in range(4)] +
                   [("kw", s, c2) for s in range(2) for c2 in range(4)] +
                   [("vw", s, v) for s in range(2) for v in range(4)] +
                   [("vw1", s) for s in range(2)] +
                   [("mres", j, tq) for j in range(NK2A) for tq in range(NW)] +
                   [("mresd", j, tq) for j in range(NK2A) for tq in range(NW)] +
                   [("mrbs", x) for x in range(NW)] +
                   [("mrbsd", x) for x in range(NW)] +
                   [("mrl", x) for x in range(2)]),
           name=f"retag_attn2ffn{l}")

    for j in range(K2N):
        for tq in range(NW):
            for part in range(2):
                n = 2 * j + part
                wsl = n % 3
                fkey = ("wp", wsl)
                dkey = ("wpd", wsl)
                if tq == 0:
                    def ldw1(e, n=n, l=l, wsl=wsl):
                        return e.dma_start(out=WPB[:, wsl, :],
                                           in_=d_w1[l, :, n, :, :])
                    S.add("gpsimd", ldw1, reads=[fkey], writes=[dkey],
                          name=f"ldw1{l}_{n}")
                pbk = part * 2 + tq % 2
                def mm1(e, n=n, tq=tq, pbk=pbk, wsl=wsl):
                    for c in range(C):
                        inst = e.matmul(bank(pbk),
                                        WPB[:, wsl, 128 * c:128 * (c + 1)],
                                        LBFc(c, tq * W512, W512),
                                        start=(c == 0), stop=(c == C - 1))
                    return inst
                S.add("tensor", mm1,
                      reads=[dkey, ("pb", pbk)] +
                            [("lbf", c, tq) for c in range(C)],
                      writes=[("pb", pbk)] + ([fkey] if tq == NW - 1 else []),
                      name=f"mm1{l}_{n}_{tq}")
            pa = tq % 2
            pg = 2 + tq % 2
            def gact(e, j=j, pg=pg):
                return e.activation(out=SQB[:, 0, :], in_=bank(pg), func=AF.Gelu,
                                    bias=B1C[:, K2N + j:K2N + j + 1])
            S.add("scalar", gact, reads=[("pb", pg), ("sqb", 0), ("lsm",)],
                  writes=[("sqbd", 0), ("pb", pg)], name=f"ga{l}_{j}_{tq}")
            if j < NK2A:
                mdst = (lambda j=j, tq=tq: MRES[:, j, tq * W512:(tq + 1) * W512])
                mkey, mdkey = ("mres", j, tq), ("mresd", j, tq)
            else:
                mdst = (lambda tq=tq: MRB[:, tq, :])
                mkey, mdkey = ("mrbs", tq), ("mrbsd", tq)
            def mmul(e, j=j, pa=pa, mdst=mdst):
                return e.scalar_tensor_tensor(mdst(), bank(pa),
                                              B1C[:, j:j + 1], SQB[:, 0, :],
                                              ALU.add, ALU.mult)
            S.add("vector", mmul,
                  reads=[("pb", pa), ("sqbd", 0), ("lsm",), mkey],
                  writes=[("pb", pa), ("sqb", 0), mdkey, mkey],
                  name=f"mmul{l}_{j}_{tq}")
            if j >= NK2A:
                def msp(e, j=j, tq=tq, mdst=mdst):
                    return e.dma_start(
                        out=d_m[:, j - NK2A, tq * W512:(tq + 1) * W512],
                        in_=mdst())
                S.add("sync", msp, reads=[mdkey],
                      writes=[("msp", j - NK2A, tq), mkey],
                      name=f"msp{l}_{j}_{tq}")

    for n in range(C):
        def ldw2(e, n=n, l=l):
            return e.dma_start(
                out=WPB[:].rearrange("p s n -> p (s n)")[:, 0:128 * NK2A],
                in_=d_w2[l, :, n, 0:NK2A, :])
        S.add("gpsimd", ldw2, reads=[("wp", 0), ("wp", 1), ("wp", 2)],
              writes=[("wpd", 0), ("wpd", 1), ("wpd", 2)], name=f"ldw2{l}_{n}")
        for tq in range(NW):
            pbk = (n * NW + tq) % 2
            def mm2(e, n=n, tq=tq, pbk=pbk):
                wflat = WPB[:].rearrange("p s n -> p (s n)")
                for j in range(NK2A):
                    inst = e.matmul(bank(pbk), wflat[:, 128 * j:128 * (j + 1)],
                                    MRES[:, j, tq * W512:(tq + 1) * W512],
                                    start=(j == 0), stop=(j == NK2A - 1))
                return inst
            S.add("tensor", mm2,
                  reads=[("wpd", 0), ("wpd", 1), ("wpd", 2), ("pb", pbk)] +
                        [("mresd", j, tq) for j in range(NK2A)],
                  writes=[("pb", pbk)] +
                         ([("wp", 0), ("wp", 1), ("wp", 2)]
                          if tq == NW - 1 else []),
                  name=f"mm2{l}_{n}_{tq}")
            def y2a(e, n=n, tq=tq, pbk=pbk):
                return e.scalar_tensor_tensor(
                    XRc(n, tq * W512, W512), bank(pbk),
                    GBC[:, 2 * C + n:2 * C + n + 1],
                    XRc(n, tq * W512, W512), ALU.add, ALU.add)
            S.add("vector", y2a, reads=[("pb", pbk), ("xr", tq), ("lsm",)],
                  writes=[("xr", tq), ("pb", pbk), ("w2p", n, tq)],
                  name=f"y2a{l}_{n}_{tq}")
    for npair in range(C // 2):
        n0 = 2 * npair
        def ldw2b(e, n0=n0, l=l):
            return e.dma_start(
                out=WPB[:].rearrange("p s n -> p (s n)")[:, 0:128 * 2 * NK2B],
                in_=d_w2[l, :, n0:n0 + 2, NK2A:K2N, :])
        S.add("gpsimd", ldw2b, reads=[("wpd", 0), ("wpd", 1), ("wpd", 2)],
              writes=[("wpd", 0), ("wpd", 1), ("wpd", 2)],
              name=f"ldw2B{l}_{npair}")
        for tqp in range(2):
            pp = npair * 2 + tqp
            bks = [(pp * 4 + i) % 7 for i in range(4)]
            for j in range(NK2A, K2N):
                sl = (j - NK2A) % 2
                def mrl(e, j=j, sl=sl, tqp=tqp):
                    dst = MRB[:, 4 + 2 * sl:6 + 2 * sl, :].rearrange(
                        "p a b -> p (a b)")
                    return e.dma_start(
                        out=dst,
                        in_=d_m[:, j - NK2A, tqp * 1024:(tqp + 1) * 1024])
                S.add("sync", mrl,
                      reads=[("msp", j - NK2A, 2 * tqp),
                             ("msp", j - NK2A, 2 * tqp + 1)],
                      writes=[("mrl", sl)], name=f"mrl{l}_{pp}_{j}")
                def mm2b(e, j=j, sl=sl, bks=bks):
                    wflat = WPB[:].rearrange("p s n -> p (s n)")
                    ring = MRB[:, 4 + 2 * sl:6 + 2 * sl, :].rearrange(
                        "p a b -> p (a b)")
                    inst = None
                    for dn in range(2):
                        wcol = 128 * (dn * NK2B + (j - NK2A))
                        for dt in range(2):
                            inst = e.matmul(
                                bank(bks[dn * 2 + dt]),
                                wflat[:, wcol:wcol + 128],
                                ring[:, dt * W512:(dt + 1) * W512],
                                start=(j == NK2A), stop=(j == K2N - 1))
                    return inst
                S.add("tensor", mm2b,
                      reads=[("wpd", 0), ("wpd", 1), ("wpd", 2),
                             ("mrl", sl)] + [("pb", b) for b in bks],
                      writes=[("pb", b) for b in bks] + [("mrl", sl)],
                      name=f"mm2b{l}_{pp}_{j}")
            for dn in range(2):
                n = n0 + dn
                for dt in range(2):
                    tq = tqp * 2 + dt
                    b = bks[dn * 2 + dt]
                    def y2c(e, n=n, tq=tq, b=b):
                        return e.tensor_tensor(XRc(n, tq * W512, W512),
                                               XRc(n, tq * W512, W512),
                                               bank(b), ALU.add)
                    S.add("vector", y2c,
                          reads=[("pb", b), ("xr", tq), ("w2p", n, tq)],
                          writes=[("xr", tq), ("pb", b), ("w2d", n, tq)],
                          name=f"y2c{l}_{n}_{tq}")
                    def ymir3(e, n=n, tq=tq):
                        return e.copy(out=ABFc(n, tq * W512 + 4, W512),
                                      in_=XRc(n, tq * W512, W512))
                    S.add("scalar", ymir3, reads=[("w2d", n, tq), ("xr", tq)],
                          writes=[YK(n, tq)], name=f"ym3{l}_{n}_{tq}")

    if tap(f"y{l + 1}"):
        tapdram(f"y{l + 1}", [128, C * T])
        S.add("gpsimd", lambda e: e.dma_start(out=dbg[f"y{l + 1}"], in_=XRt[:]),
              reads=[("w2d", n, tq) for n in range(C) for tq in range(NW)],
              name=f"tap_y{l + 1}")


def kernel(**inputs):
    cst = _constants()
    w = _prep_weights(inputs)
    x = np.asarray(inputs["L0_attn_query"], np.float32)
    nc, _ = build_program()
    base = {}
    for k in WEIGHT_NAMES:
        base[k] = w[k]
    for k in CONST_NAMES:
        base[k] = np.asarray(cst[k])
    in_maps = []
    for b in range(8):
        m = dict(base)
        m["x"] = np.ascontiguousarray(x[b])
        in_maps.append(m)
    res = run_bass_kernel_spmd(nc, in_maps, core_ids=list(range(8)))
    return np.stack([res.results[b]["out"] for b in range(8)], 0)


# version 05
# speedup vs baseline: 1.2634x; 1.1124x over previous
import numpy as np
import ml_dtypes
from contextlib import ExitStack

import concourse.bass as bass
import concourse.mybir as mybir
from concourse.bass_utils import run_bass_kernel_spmd

F32 = mybir.dt.float32
BF16 = mybir.dt.bfloat16
AF = mybir.ActivationFunctionType
ALU = mybir.AluOpType
AXX = mybir.AxisListType.X
f32 = np.float32

T = 2048
D = 1024
C = 8
NW = 4
W512 = 512
K9 = 9
NH = 8
QKN = 12
HID = 2730
HIDP = 2816
K2N = 22
W1N = 44
NK2A = 11
NK2B = K2N - NK2A
LNEPS = 1e-5

_CONST_CACHE = {}


def _constants():
    if _CONST_CACHE:
        return _CONST_CACHE
    c = {}
    c["ident_bf"] = np.eye(128, dtype=ml_dtypes.bfloat16)
    c["ident_f32"] = np.eye(128, dtype=np.float32)
    R = np.zeros((64, 64), np.float32)
    for d in range(32):
        R[d + 32, d] = -1.0
        R[d, d + 32] = 1.0
    Rb = np.zeros((128, 128), np.float32)
    Rb[0:64, 0:64] = R
    Rb[64:128, 64:128] = R
    c["rotmat"] = Rb.astype(ml_dtypes.bfloat16)
    d = np.arange(64, dtype=np.float64)
    half = (d % 32) * 2.0
    inv_freq = 1.0 / (10000.0 ** (half / 64.0))
    t = np.arange(T, dtype=np.float64)
    freqs = t[None, :] * inv_freq[:, None]
    base = (half + 0.4 * 64.0) / (1.4 * 64.0)
    power = (t - T // 2) / 256.0
    scale = base[:, None] ** power[None, :]
    cs, sn = np.cos(freqs), np.sin(freqs)

    def dup(a):
        return np.vstack([a, a]).astype(np.float32).astype(ml_dtypes.bfloat16)
    c["cosq"] = dup(8.0 * cs * scale)
    c["sinq"] = dup(8.0 * sn * scale)
    c["cosk"] = dup(cs / scale)
    c["sink"] = dup(sn / scale)
    r = np.arange(128)
    c["tri_diag"] = (r[:, None] <= r[None, :]).astype(ml_dtypes.bfloat16)
    c["tri_edge"] = (r[:, None] >= r[None, :]).astype(ml_dtypes.bfloat16)
    hb = np.zeros((128, 128), np.float32)
    hb[0:64, 0:64] = 1.0
    hb[64:128, 64:128] = 1.0
    c["hones"] = hb.astype(ml_dtypes.bfloat16)
    c["lncol_f32"] = np.full((128, 128), 1.0 / D, np.float32)
    c["lncol_bf"] = np.full((128, 128), 1.0 / D, np.float32).astype(ml_dtypes.bfloat16)
    c["ones1_bf"] = np.ones((1, 128), ml_dtypes.bfloat16)
    c["ones1_f32"] = np.ones((128, 128), np.float32)
    c["zeros_bf"] = np.zeros((128, 128), ml_dtypes.bfloat16)
    c["onescol_bf"] = np.ones((128, 1), ml_dtypes.bfloat16)
    s4 = np.zeros((128, 8), np.float32)
    for ct in range(4):
        for hp in range(2):
            for kk in range(K9):
                s4[32 * ct + 9 * hp + kk, 2 * ct + hp] = 1.0
    c["sel4"] = s4.astype(ml_dtypes.bfloat16)
    sz = np.zeros((8, 128), np.float32)
    for ct in range(4):
        for hp in range(2):
            for kk in range(K9):
                sz[2 * ct + hp, 32 * ct + 9 * hp + kk] = 1.0
    c["selzrow"] = sz
    B = np.zeros((K9, 4, 128, 128), np.float32)
    for kk in range(K9):
        for ct in range(4):
            for hp in range(2):
                B[kk, ct, 32 * ct + 9 * hp + kk, 64 * hp:64 * (hp + 1)] = 1.0
    c["bsel"] = np.ascontiguousarray(B.transpose(2, 0, 1, 3)).astype(ml_dtypes.bfloat16)
    _CONST_CACHE.update(c)
    return c


CONST_NAMES = ("ident_bf", "ident_f32", "rotmat", "cosq", "sinq", "cosk", "sink",
               "tri_diag", "tri_edge", "hones", "lncol_f32", "lncol_bf",
               "ones1_bf", "ones1_f32", "onescol_bf", "zeros_bf", "sel4",
               "selzrow", "bsel")
WEIGHT_NAMES = ("cw_stat", "wqkv", "wo", "w1", "w2", "b1a", "b1g", "qscol", "kscol",
                "lng", "lnb", "b2")


def _prep_weights(inp):
    w = {}
    cw = np.asarray(inp["L0_attn_conv_weight"], np.float32)
    Wk = np.zeros((128, C, K9, 18), np.float32)
    for h in range(16):
        cc, hp = h // 2, h % 2
        for kk in range(K9):
            Wk[64 * hp:64 * (hp + 1), cc, kk, 9 * hp:9 * (hp + 1)] = \
                cw[h * 9:(h + 1) * 9, :, kk].T
    w["cw_stat"] = np.ascontiguousarray(Wk.reshape(128, C * K9 * 18))
    wqkv = np.asarray(inp["L0_ffn_wqkv"], np.float32)
    w["wqkv"] = np.ascontiguousarray(
        wqkv.reshape(2, C, 128, QKN, 128).transpose(0, 2, 3, 1, 4))
    wo = np.asarray(inp["L0_ffn_wo"], np.float32)
    w["wo"] = np.ascontiguousarray(
        wo.reshape(2, 4, 128, 8, 128).transpose(0, 2, 3, 1, 4))
    w1 = np.asarray(inp["L0_ffn_w1"], np.float32)
    a, g = w1[:, :, :HID], w1[:, :, HID:]
    ap = np.zeros((2, D, HIDP), np.float32)
    gp = np.zeros((2, D, HIDP), np.float32)
    ap[:, :, :HID] = a
    gp[:, :, :HID] = g
    w1p = np.zeros((2, D, W1N, 128), np.float32)
    w1p[:, :, 0::2, :] = ap.reshape(2, D, K2N, 128)
    w1p[:, :, 1::2, :] = gp.reshape(2, D, K2N, 128)
    w["w1"] = np.ascontiguousarray(
        w1p.reshape(2, C, 128, W1N, 128).transpose(0, 2, 3, 1, 4))
    b1 = np.asarray(inp["L0_ffn_b1"], np.float32)
    b1a = np.zeros((2, HIDP), np.float32)
    b1g = np.zeros((2, HIDP), np.float32)
    b1a[:, :HID] = b1[:, :HID]
    b1g[:, :HID] = b1[:, HID:]
    w["b1a"] = np.ascontiguousarray(b1a.reshape(2, K2N, 128).transpose(0, 2, 1))
    w["b1g"] = np.ascontiguousarray(b1g.reshape(2, K2N, 128).transpose(0, 2, 1))
    w2 = np.asarray(inp["L0_ffn_w2"], np.float32)
    w2p = np.zeros((2, HIDP, D), np.float32)
    w2p[:, :HID, :] = w2
    w["w2"] = np.ascontiguousarray(
        w2p.reshape(2, K2N, 128, 8, 128).transpose(0, 2, 3, 1, 4))
    qs = np.asarray(inp["L0_ffn_q_scale"], np.float32)
    ks = np.asarray(inp["L0_ffn_k_scale"], np.float32)
    w["qscol"] = np.ascontiguousarray(np.concatenate([qs, qs], 1).reshape(2, 128, 1))
    w["kscol"] = np.ascontiguousarray(np.concatenate([ks, ks], 1).reshape(2, 128, 1))
    for nm, key in (("lng", "L0_ffn_ln_g"), ("lnb", "L0_ffn_ln_b"), ("b2", "L0_ffn_b2")):
        v = np.asarray(inp[key], np.float32)
        w[nm] = np.ascontiguousarray(v.reshape(2, C, 128).transpose(0, 2, 1))
    return w


ENGINES = ("tensor", "vector", "scalar", "sync", "gpsimd")
DMA_ENG = ("sync", "gpsimd")


class _Step:
    __slots__ = ("engine", "fn", "name", "deps", "need_inc", "inc_val", "waits",
                 "vdeps")

    def __init__(self, engine, fn, name):
        self.engine = engine
        self.fn = fn
        self.name = name
        self.deps = []
        self.need_inc = False
        self.inc_val = None
        self.waits = None
        self.vdeps = None


SERIAL = False


class Sched:
    def __init__(self):
        self.steps = {e: [] for e in ENGINES}
        self._writer = {}
        self._readers = {}
        self.n = 0
        self._prev = None

    def _collect(self, reads, writes):
        dep = set()
        for r in reads:
            w = self._writer.get(r)
            if w is not None:
                dep.add(w)
        for wk in writes:
            w = self._writer.get(wk)
            if w is not None:
                dep.add(w)
            for r in self._readers.get(wk, ()):
                dep.add(r)
        out = set()
        for d in dep:
            if d.vdeps is not None:
                out.update(d.vdeps)
            else:
                out.add(d)
        return out

    def add(self, engine, fn, reads=(), writes=(), name=""):
        s = _Step(engine, fn, name or f"s{self.n}")
        self.n += 1
        dep = self._collect(reads, writes)
        if SERIAL and self._prev is not None:
            dep.add(self._prev)
        self._prev = s
        for r in reads:
            self._readers.setdefault(r, []).append(s)
        for wk in writes:
            self._writer[wk] = s
            self._readers[wk] = []
        for d in dep:
            if d is s:
                continue
            if d.engine == engine and d.engine not in DMA_ENG:
                continue
            d.need_inc = True
            s.deps.append(d)
        if engine in DMA_ENG:
            s.need_inc = True
        self.steps[engine].append(s)
        return s

    def virt(self, writes=(), reads=(), name="virt"):
        s = _Step(None, None, name)
        s.vdeps = self._collect(reads, writes)
        for r in reads:
            self._readers.setdefault(r, []).append(s)
        for wk in writes:
            self._writer[wk] = s
            self._readers[wk] = []
        return s

    def emit(self, blk, sems):
        for e in ENGINES:
            cnt = 0
            amt = 16 if e in DMA_ENG else 1
            for s in self.steps[e]:
                if s.need_inc:
                    cnt += amt
                    s.inc_val = cnt
        for e in ENGINES:
            hw = {x: 0 for x in ENGINES}
            for s in self.steps[e]:
                ws = []
                for d in sorted(s.deps, key=lambda d: d.inc_val):
                    if d.inc_val > hw[d.engine]:
                        hw[d.engine] = d.inc_val
                        ws.append((d.engine, d.inc_val))
                s.waits = ws
        amt = {e: (16 if e in DMA_ENG else 1) for e in ENGINES}

        def mk(e_name):
            def body(eng):
                for s in self.steps[e_name]:
                    for (de, v) in s.waits:
                        eng.wait_ge(sems[de], v)
                    inst = s.fn(eng)
                    if s.need_inc:
                        assert inst is not None, f"step {s.name} returned None"
                        inst.then_inc(sems[e_name], amt[e_name])
            return body
        blk.tensor(mk("tensor"))
        blk.vector(mk("vector"))
        blk.scalar(mk("scalar"))
        blk.sync(mk("sync"))
        blk.gpsimd(mk("gpsimd"))


def build_program(stage=99, debug_taps=()):
    cst = _constants()
    nc = bass.Bass("TRN2", target_bir_lowering=False)
    taps = set(debug_taps)

    xin = nc.dram_tensor("x", [T, D], F32, kind="ExternalInput").ap()
    out = nc.dram_tensor("out", [T, D], F32, kind="ExternalOutput").ap()
    d_cw = nc.dram_tensor("cw_stat", [128, C * K9 * 18], F32, kind="ExternalInput").ap()
    d_wqkv = nc.dram_tensor("wqkv", [2, 128, QKN, C, 128], F32, kind="ExternalInput").ap()
    d_wo = nc.dram_tensor("wo", [2, 128, 8, 4, 128], F32, kind="ExternalInput").ap()
    d_w1 = nc.dram_tensor("w1", [2, 128, W1N, C, 128], F32, kind="ExternalInput").ap()
    d_w2 = nc.dram_tensor("w2", [2, 128, 8, K2N, 128], F32, kind="ExternalInput").ap()
    d_b1a = nc.dram_tensor("b1a", [2, 128, K2N], F32, kind="ExternalInput").ap()
    d_b1g = nc.dram_tensor("b1g", [2, 128, K2N], F32, kind="ExternalInput").ap()
    d_qs = nc.dram_tensor("qscol", [2, 128, 1], F32, kind="ExternalInput").ap()
    d_ks = nc.dram_tensor("kscol", [2, 128, 1], F32, kind="ExternalInput").ap()
    d_lng = nc.dram_tensor("lng", [2, 128, C], F32, kind="ExternalInput").ap()
    d_lnb = nc.dram_tensor("lnb", [2, 128, C], F32, kind="ExternalInput").ap()
    d_b2 = nc.dram_tensor("b2", [2, 128, C], F32, kind="ExternalInput").ap()
    dc = {}
    for nm in CONST_NAMES:
        a = np.asarray(cst[nm])
        dt = BF16 if a.dtype == ml_dtypes.bfloat16 else F32
        dc[nm] = nc.dram_tensor(nm, list(a.shape), dt, kind="ExternalInput").ap()
    d_x1 = nc.dram_tensor("x1sp", [128, C, T], F32, kind="Internal").ap()
    d_m = nc.dram_tensor("msp", [128, NK2B, T], BF16, kind="Internal").ap()

    dbg = {}

    def tap(name):
        return name in taps

    def tapdram(name, shape, dtype=F32):
        dbg[name] = nc.dram_tensor("dbg_" + name, shape, dtype,
                                   kind="ExternalOutput").ap()
        return dbg[name]

    ctx = ExitStack()
    E = ctx.enter_context

    XR = E(nc.sbuf_tensor("XR", [128, C * T], F32))
    ABF = E(nc.sbuf_tensor("ABF", [128, C * (T + 8)], BF16))
    LBF = E(nc.sbuf_tensor("LBF", [128, C * T], BF16))
    R2 = E(nc.sbuf_tensor("R2", [128, 53376], mybir.dt.uint8))
    r2base = nc.lookup_mloc(R2).addr

    def r2v(name, shape, dtype, off):
        return nc.alloc_sbuf_tensor_at(name, shape, dtype, offset=r2base + off)

    BSEL = r2v("BSEL", [128, K9, 4, 128], BF16, 0)
    CWS = r2v("CWS", [128, C * K9 * 18], BF16, 9216)
    CONVS = r2v("CONVS", [128, 2, 512], BF16, 12096)
    ZR = r2v("ZR", [8, 2 * 512], F32, 14144)
    ACCW = r2v("ACCW", [128, C, W512], F32, 18432)
    STG = r2v("STG", [128, 4, D], F32, 34816)
    COSQ = r2v("COSQ", [128, T], BF16, 0)
    SINQ = r2v("SINQ", [128, T], BF16, 4096)
    COSK = r2v("COSK", [128, T], BF16, 8192)
    SINK = r2v("SINK", [128, T], BF16, 12288)
    QW = r2v("QW", [128, 1, 4, W512], BF16, 16384)
    KW = r2v("KW", [128, 2, 4, W512], BF16, 20480)
    VW = r2v("VW", [128, 2, 4, 520], BF16, 28672)
    OFM = r2v("OFM", [128, 4, T], BF16, 36992)
    MRES = r2v("MRES", [128, NK2A, T], BF16, 0)
    MRB = r2v("MRB", [128, 8, T // 4], BF16, 45056)
    X1RR = r2v("X1RR", [128, 2, 512], F32, 0)
    STGO = r2v("STGO", [128, 4, D], F32, 36992)

    PSTG = E(nc.sbuf_tensor("PSTG", [128, 2, 512], BF16))
    MSTG = E(nc.sbuf_tensor("MSTG", [1, 2, W512], BF16))
    STT = E(nc.sbuf_tensor("STT", [128, 3, 512], F32))
    SQB = E(nc.sbuf_tensor("SQB", [128, 2, 512], BF16))
    QKS = SQB[:, 1, :]
    MAXC = E(nc.sbuf_tensor("MAXC", [128, 2], F32))
    IDB = E(nc.sbuf_tensor("IDB", [128, 128], BF16))
    IDF = E(nc.sbuf_tensor("IDF", [128, 128], F32))
    ROTM = E(nc.sbuf_tensor("ROTM", [128, 128], BF16))
    TRID = E(nc.sbuf_tensor("TRID", [128, 128], BF16))
    TRIE = E(nc.sbuf_tensor("TRIE", [128, 128], BF16))
    HON = E(nc.sbuf_tensor("HON", [128, 128], BF16))
    LNC32 = E(nc.sbuf_tensor("LNC32", [128, 128], F32))
    LNCBF = E(nc.sbuf_tensor("LNCBF", [128, 128], BF16))
    ONES1 = E(nc.sbuf_tensor("ONES1", [1, 128], BF16))
    ONE1F = E(nc.sbuf_tensor("ONE1F", [128, 128], F32))
    ONECB = E(nc.sbuf_tensor("ONECB", [128, 1], BF16))
    ZERB = E(nc.sbuf_tensor("ZERB", [128, 128], BF16))
    SEL4 = E(nc.sbuf_tensor("SEL4", [128, 8], BF16))
    SELZ = E(nc.sbuf_tensor("SELZ", [8, 128], F32))
    WQB = E(nc.sbuf_tensor("WQB", [128, 2, C * 128], BF16))
    WOB = E(nc.sbuf_tensor("WOB", [128, 4 * 128], BF16))
    WPB = E(nc.sbuf_tensor("WPB", [128, 3, C * 128], BF16))
    QSC = E(nc.sbuf_tensor("QSC", [128, 2], F32))
    GBC = E(nc.sbuf_tensor("GBC", [128, 3 * C], F32))
    B1C = E(nc.sbuf_tensor("B1C", [128, 2 * K2N], F32))

    PB = E(nc.psum_tensor("PB", [128, 3584], F32))
    PBF = E(nc.psum_tensor("PBF", [128, 1024], BF16))

    def bank(i, col0=0, ncol=512):
        return PB[:, 512 * i + col0: 512 * i + col0 + ncol]

    sems = {e: E(nc.semaphore("sem_" + e)) for e in ENGINES}
    blk = E(nc.Block())
    S = Sched()

    def XRc(c, lo, n):
        return XR[:, c * T + lo: c * T + lo + n]

    def ABFc(c, lo, n):
        return ABF[:, c * (T + 8) + lo: c * (T + 8) + lo + n]

    def LBFc(c, lo, n):
        return LBF[:, c * T + lo: c * T + lo + n]

    _consts = ((IDB[:], dc["ident_bf"]), (IDF[:], dc["ident_f32"]),
               (ROTM[:], dc["rotmat"]), (TRID[:], dc["tri_diag"]),
               (TRIE[:], dc["tri_edge"]), (HON[:], dc["hones"]),
               (LNC32[:], dc["lncol_f32"]), (LNCBF[:], dc["lncol_bf"]),
               (ONES1[:], dc["ones1_bf"]), (ONE1F[:], dc["ones1_f32"]),
               (ONECB[:], dc["onescol_bf"]), (ZERB[:], dc["zeros_bf"]),
               (SEL4[:], dc["sel4"]), (SELZ[:], dc["selzrow"]))
    for i, (dst, src) in enumerate(_consts):
        S.add("sync", (lambda e, dst=dst, src=src: e.dma_start(out=dst, in_=src)),
              writes=["const"] if i == len(_consts) - 1 else [],
              name=f"ld_const{i}")
    S.add("gpsimd", lambda e: e.dma_start(out=BSEL[:], in_=dc["bsel"]),
          writes=[], name="ld_bsel")
    S.add("gpsimd", lambda e: e.dma_start(out=CWS[:], in_=d_cw),
          writes=["convw"], name="ld_convw")

    def z_guards(e):
        for c in range(C):
            e.memset(ABF[:, c * (T + 8):c * (T + 8) + 4], 0)
            inst = e.memset(ABF[:, c * (T + 8) + T + 4:c * (T + 8) + T + 8], 0)
        return inst
    S.add("vector", z_guards, writes=["guards"], name="zguards")

    xv = xin.rearrange("(tt p) f -> p tt f", p=128)
    for qtr in range(4):
        def ld_q(e, qtr=qtr):
            return e.dma_start(out=STG[:], in_=xv[:, 4 * qtr:4 * (qtr + 1), :])
        S.add("sync", ld_q, writes=[("stg",)], name=f"ldx{qtr}")
        for tt4 in range(4):
            tt = 4 * qtr + tt4
            for cq in range(2):
                bk = (tt4 * 2 + cq) % 2
                def tp(e, tt4=tt4, cq=cq, bk=bk):
                    for c4 in range(4):
                        c = cq * 4 + c4
                        inst = e.transpose(
                            PB[:, 512 * bk + 128 * c4:512 * bk + 128 * (c4 + 1)],
                            STG[:, tt4, 128 * c:128 * (c + 1)], IDF[:])
                    return inst
                S.add("tensor", tp, reads=[("stg",), "const", ("pb", bk)],
                      writes=[("pb", bk)], name=f"tpx{tt}_{cq}")
                def ev(e, tt=tt, cq=cq, bk=bk):
                    for c4 in range(4):
                        c = cq * 4 + c4
                        inst = e.tensor_copy(
                            XRc(c, tt * 128, 128),
                            PB[:, 512 * bk + 128 * c4:512 * bk + 128 * (c4 + 1)])
                    return inst
                S.add("vector", ev, reads=[("pb", bk), ("stg",)],
                      writes=[("xr", tt // 4), ("pb", bk)], name=f"evx{tt}_{cq}")

    if tap("xr"):
        tapdram("xr", [128, C * T])
        S.add("gpsimd", lambda e: e.dma_start(out=dbg["xr"], in_=XR[:]),
              reads=[("xr", i) for i in range(4)], name="tap_xr")

    def ln_stats(tag, rhs, reads, bf=False):
        def mm_mean(e):
            for c in range(C):
                inst = e.matmul(bank(5), LNCBF[:] if bf else LNC32[:], rhs(c),
                                start=(c == 0), stop=(c == C - 1))
            return inst
        S.add("tensor", mm_mean, reads=list(reads) + ["const", ("pb", 5)],
              writes=[("pb", 5)], name=f"lnm_{tag}")
        for c in range(C):
            sl = c % 2
            def sq1(e, c=c, sl=sl):
                return e.activation(out=SQB[:, sl, :], in_=rhs(c), func=AF.Square)
            S.add("scalar", sq1, reads=list(reads) + [("sqb", sl)],
                  writes=[("sqbd", sl)], name=f"lnsq_{tag}_{c}")
            def mmsq(e, c=c, sl=sl):
                return e.matmul(bank(6), LNCBF[:], SQB[:, sl, :],
                                start=(c == 0), stop=(c == C - 1))
            S.add("tensor", mmsq, reads=[("sqbd", sl), "const", ("pb", 6)],
                  writes=[("pb", 6), ("sqb", sl)], name=f"lnsqm_{tag}_{c}")

        def stats(e):
            e.tensor_copy(STT[:, 2, :], bank(5))
            e.tensor_tensor(STT[:, 1, :], STT[:, 2, :], STT[:, 2, :], ALU.mult)
            return e.scalar_tensor_tensor(STT[:, 1, :], STT[:, 1, :], -1.0,
                                          bank(6), ALU.mult, ALU.add)
        S.add("vector", stats,
              reads=[("pb", 5), ("pb", 6), ("stt", 2), ("stt", 1)],
              writes=[("stt", 2), ("stt", 1), ("pb", 6), ("pb", 5)],
              name=f"lnv_{tag}")
        def lnr(e):
            e.activation(out=STT[:, 1, :], in_=STT[:, 1, :], func=AF.Ln,
                         bias=f32(LNEPS))
            return e.activation(out=STT[:, 0, :], in_=STT[:, 1, :],
                                func=AF.Exp, scale=f32(-0.5))
        S.add("scalar", lnr, reads=[("stt", 1), ("stt", 0)],
              writes=[("stt", 0), ("stt", 1)], name=f"lnr_{tag}")

        def negm(e):
            return e.scalar_tensor_tensor(STT[:, 1, :], STT[:, 2, :], -1.0,
                                          STT[:, 0, :], ALU.mult, ALU.mult)
        S.add("vector", negm, reads=[("stt", 0), ("stt", 2), ("stt", 1)],
              writes=[("stt", 1)], name=f"lnn_{tag}")

    for w in range(NW):
        ln_stats(f"l1w{w}", lambda c, w=w: XRc(c, w * W512, W512), [("xr", w)])
        for c in range(C):
            def apa(e, c=c, w=w):
                return e.tensor_tensor(STT[:, 2, :], XRc(c, w * W512, W512),
                                       STT[:, 0, :], ALU.mult)
            S.add("vector", apa, reads=[("stt", 0), ("xr", w), ("stt", 2)],
                  writes=[("stt", 2)], name=f"l1a{w}_{c}")
            def apb(e, c=c, w=w):
                return e.tensor_tensor(ABFc(c, w * W512 + 4, W512),
                                       STT[:, 2, :], STT[:, 1, :], ALU.add)
            S.add("vector", apb, reads=[("stt", 2), ("stt", 1), "guards"],
                  writes=[("ln1", c, w)], name=f"l1b{w}_{c}")

    if tap("ln1"):
        tapdram("ln1", [128, C * (T + 8)])
        S.add("gpsimd", lambda e: e.dma_start(out=dbg["ln1"], in_=ABF[:]),
              reads=[("ln1", c, w) for c in range(C) for w in range(NW)],
              name="tap_ln1")

    if stage >= 2:
        for w in range(NW):
            for half in range(2):
                def mml(e, w=w, half=half):
                    for ct in range(4):
                        c = half * 4 + ct
                        tp = (0, 32 * ct) if ct > 0 else None
                        for k in range(K9):
                            inst = e.matmul(
                                PB[32 * ct:32 * ct + 18,
                                   512 * half:512 * half + 512],
                                CWS[:, (c * K9 + k) * 18:(c * K9 + k) * 18 + 18],
                                ABFc(c, w * W512 + k, W512),
                                start=(k == 0), stop=(k == K9 - 1),
                                tile_position=tp)
                    return inst
                S.add("tensor", mml,
                      reads=[("ln1", half * 4 + ct, w) for ct in range(4)] +
                            ["convw", ("pb", half)],
                      writes=[("pb", half)], name=f"cvl{w}_{half}")
                def clamp(e, half=half):
                    return e.tensor_scalar_min(bank(half), bank(half), 60.0)
                S.add("vector", clamp, reads=[("pb", half)], writes=[("pb", half)],
                      name=f"cvc{w}_{half}")
                def expE(e, half=half):
                    return e.activation(out=CONVS[:, half, :], in_=bank(half),
                                        func=AF.Exp)
                S.add("scalar", expE, reads=[("pb", half), ("cv_e", half)],
                      writes=[("cv_e", half), ("pb", half)], name=f"cve{w}_{half}")
                def mmz(e, half=half):
                    return e.matmul(PB[0:8, 1024:1536], SEL4[:], CONVS[:, half, :],
                                    start=True, stop=True)
                S.add("tensor", mmz, reads=[("cv_e", half), "const", ("pb", 2)],
                      writes=[("pb", 2)], name=f"cvz{w}_{half}")
                def zinv(e, half=half):
                    e.activation(out=ZR[:, half * 512:(half + 1) * 512],
                                 in_=PB[0:8, 1024:1536], func=AF.Ln)
                    return e.activation(out=ZR[:, half * 512:(half + 1) * 512],
                                        in_=ZR[:, half * 512:(half + 1) * 512],
                                        func=AF.Exp, scale=f32(-1.0))
                S.add("scalar", zinv, reads=[("pb", 2), ("cv_zi", half)],
                      writes=[("cv_zi", half), ("pb", 2)],
                      name=f"cvzi{w}_{half}")
                def mmzr(e, half=half):
                    return e.matmul(bank(3), SELZ[:],
                                    ZR[:, half * 512:(half + 1) * 512],
                                    start=True, stop=True)
                S.add("tensor", mmzr, reads=[("cv_zi", half), "const", ("pb", 3)],
                      writes=[("pb", 3)], name=f"cvzr{w}_{half}")
                def wnorm(e, half=half):
                    return e.tensor_tensor(CONVS[:, half, :], CONVS[:, half, :],
                                           bank(3), ALU.mult)
                S.add("vector", wnorm, reads=[("cv_e", half), ("pb", 3)],
                      writes=[("cv_w", half), ("cv_e", half), ("pb", 3)],
                      name=f"cvw{w}_{half}")
            for c in range(C):
                half, ct = c // 4, c % 4
                for k in range(K9):
                    pb = 3 + (k % 2)
                    def mmb(e, k=k, ct=ct, half=half, pb=pb):
                        return e.matmul(bank(pb), BSEL[:, k, ct, :],
                                        CONVS[:, half, :], start=True, stop=True)
                    S.add("tensor", mmb,
                          reads=[("cv_w", half), "convw", ("pb", pb)],
                          writes=[("pb", pb)], name=f"cvb{w}_{c}_{k}")
                    def mix(e, k=k, c=c, w=w, pb=pb):
                        if k == 0:
                            return e.tensor_tensor(ACCW[:, c, :],
                                                   ABFc(c, w * W512 + k, W512),
                                                   bank(pb), ALU.mult)
                        e.tensor_tensor(STT[:, 2, :], ABFc(c, w * W512 + k, W512),
                                        bank(pb), ALU.mult)
                        return e.tensor_tensor(ACCW[:, c, :], ACCW[:, c, :],
                                               STT[:, 2, :], ALU.add)
                    S.add("vector", mix,
                          reads=[("pb", pb), ("ln1", c, w), ("acc", c), ("stt", 2)],
                          writes=[("acc", c), ("pb", pb), ("stt", 2)],
                          name=f"cvm{w}_{c}_{k}")
            if tap("conv") and w == 0:
                tapdram("conv", [128, C, W512])
                S.add("gpsimd", lambda e: e.dma_start(out=dbg["conv"], in_=ACCW[:]),
                      reads=[("acc", c) for c in range(C)], name="tap_conv")
            ln_stats(f"lcw{w}", lambda c: ACCW[:, c, :],
                     [("acc", c) for c in range(C)])
            for c in range(C):
                def xa(e, c=c):
                    return e.tensor_tensor(STT[:, 2, :], ACCW[:, c, :],
                                           STT[:, 0, :], ALU.mult)
                S.add("vector", xa, reads=[("stt", 0), ("acc", c), ("stt", 2)],
                      writes=[("stt", 2)], name=f"x1a{w}_{c}")
                def xb(e, c=c):
                    return e.tensor_tensor(STT[:, 2, :], STT[:, 2, :],
                                           STT[:, 1, :], ALU.add)
                S.add("vector", xb, reads=[("stt", 1)], writes=[("stt", 2)],
                      name=f"x1b{w}_{c}")
                def xc(e, c=c, w=w):
                    return e.tensor_tensor(XRc(c, w * W512, W512),
                                           XRc(c, w * W512, W512),
                                           STT[:, 2, :], ALU.add)
                S.add("vector", xc, reads=[("stt", 2), ("xr", w)],
                      writes=[("xr", w), ("acc", c)], name=f"x1c{w}_{c}")

        if tap("x1"):
            tapdram("x1", [128, C * T])
            S.add("gpsimd", lambda e: e.dma_start(out=dbg["x1"], in_=XR[:]),
                  reads=[("xr", i) for i in range(4)], name="tap_x1")

    if stage >= 3:
        def spill(e):
            return e.dma_start(out=d_x1, in_=XR[:].rearrange("p (c t) -> p c t", t=T))
        S.add("sync", spill, reads=[("xr", i) for i in range(4)], writes=["x1sp"],
              name="x1spill")
        for w in range(NW):
            ln_stats(f"l2w{w}", lambda c, w=w: XRc(c, w * W512, W512), [("xr", w)])
            for c in range(C):
                def ya(e, c=c, w=w):
                    return e.tensor_tensor(XRc(c, w * W512, W512),
                                           XRc(c, w * W512, W512),
                                           STT[:, 0, :], ALU.mult)
                S.add("vector", ya, reads=[("stt", 0), "x1sp"],
                      writes=[("xr", w)], name=f"y2a{w}_{c}")
                def yb(e, c=c, w=w):
                    return e.tensor_tensor(XRc(c, w * W512, W512),
                                           XRc(c, w * W512, W512),
                                           STT[:, 1, :], ALU.add)
                S.add("vector", yb, reads=[("stt", 1)], writes=[("xr", w)],
                      name=f"y2b{w}_{c}")
                def ymir(e, c=c, w=w):
                    return e.copy(out=ABFc(c, w * W512 + 4, W512),
                                  in_=XRc(c, w * W512, W512))
                S.add("scalar", ymir,
                      reads=[("xr", w)],
                      writes=[("ybf", c, w), ("ln1", c, w)], name=f"ymir{w}_{c}")

        if tap("y0"):
            tapdram("y0", [128, C * T])
            S.add("gpsimd", lambda e: e.dma_start(out=dbg["y0"], in_=XR[:]),
                  reads=[("xr", i) for i in range(4)], name="tap_y0")

    QWK = [("qw", c2) for c2 in range(4)]
    KWK = [("kw", s, c2) for s in range(2) for c2 in range(4)]
    VWK = [("vw", s, v) for s in range(2) for v in range(4)] + \
          [("vw1", s) for s in range(2)]
    OFK = [("ofm", c2, w, hp) for c2 in range(4) for w in range(NW)
           for hp in range(2)]
    MRK = ([("mres", j, tq) for j in range(NK2A) for tq in range(NW)] +
           [("mresd", j, tq) for j in range(NK2A) for tq in range(NW)] +
           [("mrbs", x) for x in range(NW)] +
           [("mrbsd", x) for x in range(NW)] +
           [("mrl", x) for x in range(2)])

    if stage >= 4:
        nlayers = 2 if stage >= 5 else 1
        for l in range(nlayers):
            if l == 0:
                S.virt(writes=(["rot", ("stg",), "convw",
                                ("cv_w", 0), ("cv_w", 1), ("cv_e", 0), ("cv_e", 1),
                                ("cv_zi", 0), ("cv_zi", 1)] +
                               [("acc", c) for c in range(C)] +
                               QWK + KWK + VWK + OFK), name="retag_conv2attn")
            else:
                S.virt(writes=["rot"] + QWK + KWK + VWK + OFK + MRK,
                       name=f"retag_ffn2attn{l}")
            _ffn_layer(S, locals(), l)

    if stage >= 9:
        S.virt(writes=[("x1r", 0), ("x1r", 1), ("stgo2",)] + OFK + MRK,
               name="retag_exit")
        outv = out.rearrange("(tt p) f -> p tt f", p=128)
        for w in range(NW):
            ln_stats(f"lfw{w}", lambda c, w=w: XRc(c, w * W512, W512), [("xr", w)])
            for c in range(C):
                def fa(e, c=c, w=w):
                    return e.tensor_tensor(STT[:, 2, :], XRc(c, w * W512, W512),
                                           STT[:, 0, :], ALU.mult)
                S.add("vector", fa, reads=[("stt", 0), ("xr", w), ("stt", 2)],
                      writes=[("stt", 2)], name=f"fna{w}_{c}")
                def fb(e, c=c, w=w):
                    return e.tensor_tensor(STT[:, 2, :], STT[:, 2, :],
                                           STT[:, 1, :], ALU.add)
                S.add("vector", fb, reads=[("stt", 1)], writes=[("stt", 2)],
                      name=f"fnb{w}_{c}")
                sl = c % 2
                def ldx1(e, c=c, w=w, sl=sl):
                    return e.dma_start(out=X1RR[:, sl, :],
                                       in_=d_x1[:, c, w * W512:(w + 1) * W512])
                S.add("sync", ldx1, reads=["x1sp", ("x1r", sl)],
                      writes=[("x1rd", sl)], name=f"ldx1_{w}_{c}")
                def fc(e, c=c, sl=sl):
                    return e.tensor_tensor(STT[:, 2, :], STT[:, 2, :],
                                           X1RR[:, sl, :], ALU.add)
                S.add("vector", fc, reads=[("x1rd", sl)],
                      writes=[("stt", 2), ("x1r", sl)], name=f"fnc{w}_{c}")
                bk = c % 2
                def ftp(e, c=c, bk=bk):
                    for q in range(4):
                        inst = e.transpose(
                            PB[:, 512 * bk + 128 * q:512 * bk + 128 * (q + 1)],
                            STT[:, 2, 128 * q:128 * (q + 1)], IDF[:])
                    return inst
                S.add("tensor", ftp, reads=[("stt", 2), "const", ("pb", bk)],
                      writes=[("pb", bk)], name=f"ftp{w}_{c}")
                def fev(e, c=c, bk=bk):
                    for q in range(4):
                        inst = e.tensor_copy(
                            STGO[:, q, 128 * c:128 * (c + 1)],
                            PB[:, 512 * bk + 128 * q:512 * bk + 128 * (q + 1)])
                    return inst
                S.add("vector", fev, reads=[("pb", bk), ("stgo2",)],
                      writes=[("pb", bk), ("stgo", c)], name=f"fev{w}_{c}")
            def sto(e, w=w):
                return e.dma_start(out=outv[:, 4 * w:4 * (w + 1), :], in_=STGO[:])
            S.add("sync", sto, reads=[("stgo", c) for c in range(C)],
                  writes=["outd", ("stgo2",)], name=f"sto{w}")
        S.add("sync", lambda e: e.nop(), reads=["outd"], name="outwait")

    S.emit(blk, sems)
    ctx.close()
    return nc, dbg


def _ffn_layer(S, L, l):
    nc = L["nc"]
    PB, PBF = L["PB"], L["PBF"]
    bank = L["bank"]
    XRc, ABFc, LBFc = L["XRc"], L["ABFc"], L["LBFc"]
    STT, SQB, QKS, MAXC = L["STT"], L["SQB"], L["QKS"], L["MAXC"]
    WPB = L["WPB"]
    QW, KW, VW, OFM = L["QW"], L["KW"], L["VW"], L["OFM"]
    COSQ, SINQ, COSK, SINK = L["COSQ"], L["SINQ"], L["COSK"], L["SINK"]
    MRES, MRB = L["MRES"], L["MRB"]
    PSTG, MSTG = L["PSTG"], L["MSTG"]
    WQB, WOB = L["WQB"], L["WOB"]
    QSC, GBC, B1C = L["QSC"], L["GBC"], L["B1C"]
    HON, ROTM, IDB, IDF = L["HON"], L["ROTM"], L["IDB"], L["IDF"]
    TRID, TRIE = L["TRID"], L["TRIE"]
    ONES1, ONE1F, ONECB = L["ONES1"], L["ONE1F"], L["ONECB"]
    ZERB = L["ZERB"]
    d_wqkv, d_wo, d_w1, d_w2 = L["d_wqkv"], L["d_wo"], L["d_w1"], L["d_w2"]
    d_qs, d_ks = L["d_qs"], L["d_ks"]
    d_lng, d_lnb, d_b2 = L["d_lng"], L["d_lnb"], L["d_b2"]
    d_b1a, d_b1g, d_m = L["d_b1a"], L["d_b1g"], L["d_m"]
    MRB = L["MRB"]
    dc = L["dc"]
    dbg, taps, tapdram = L["dbg"], L["taps"], L["tapdram"]
    ln_stats = L["ln_stats"]
    XRt = L["XR"]

    def tap(name):
        return name in taps

    YK = lambda c, w: ("ybf", c, w)

    for i, (dst, src) in enumerate((
            (QSC[:, 0:1], d_qs[l]), (QSC[:, 1:2], d_ks[l]),
            (GBC[:, 0:C], d_lng[l]), (GBC[:, C:2 * C], d_lnb[l]),
            (GBC[:, 2 * C:3 * C], d_b2[l]),
            (B1C[:, 0:K2N], d_b1a[l]), (B1C[:, K2N:2 * K2N], d_b1g[l]))):
        S.add("sync", (lambda e, dst=dst, src=src: e.dma_start(out=dst, in_=src)),
              writes=[("lsm",)], name=f"ldsm{l}_{i}")

    for i, (dst, src) in enumerate(((COSQ[:], dc["cosq"]), (SINQ[:], dc["sinq"]),
                                    (COSK[:], dc["cosk"]), (SINK[:], dc["sink"]))):
        S.add("sync", (lambda e, dst=dst, src=src: e.dma_start(out=dst, in_=src)),
              writes=["rot"], name=f"ld_rot{l}_{i}")

    for w in range(NW):
        def emit_ld_mmq(n, w=w, l=l):
            if n >= QKN:
                return
            wsl = n % 2
            pbk = n % 2
            def ldwq(e, n=n, l=l, wsl=wsl):
                return e.dma_start(out=WQB[:, wsl, :], in_=d_wqkv[l, :, n, :, :])
            S.add("gpsimd", ldwq, reads=[("wqb", wsl)], writes=[("wqbd", wsl)],
                  name=f"ldwq{l}_{w}_{n}")
            yreads = [YK(c, w) for c in range(C)]
            if w > 0:
                yreads += [YK(c, w - 1) for c in range(4, C)]
            def mmq(e, n=n, w=w, pbk=pbk, wsl=wsl):
                for c in range(C):
                    off = w * W512 + (3 if c >= 4 else 4)
                    inst = e.matmul(bank(pbk), WQB[:, wsl, 128 * c:128 * (c + 1)],
                                    ABFc(c, off, W512),
                                    start=(c == 0), stop=(c == C - 1))
                return inst
            S.add("tensor", mmq, reads=[("wqbd", wsl), ("pb", pbk)] + yreads,
                  writes=[("pb", pbk), ("wqb", wsl)], name=f"mmqkv{l}_{w}_{n}")

        emit_ld_mmq(0)
        emit_ld_mmq(1)
        for n in range(QKN):
            pbk = n % 2
            if n < 8:
                isq = n < 4
                c2 = n % 4
                tbl_c, tbl_s = (COSQ, SINQ) if isq else (COSK, SINK)
                scol = QSC[:, 0:1] if isq else QSC[:, 1:2]
                def sq(e, pbk=pbk):
                    return e.activation(out=SQB[:, 0, :], in_=bank(pbk),
                                        func=AF.Square)
                S.add("scalar", sq, reads=[("pb", pbk), ("sqb", 0)],
                      writes=[("sqbd", 0)], name=f"qksq{l}_{w}_{n}")
                def qbf(e, pbk=pbk, scol=scol):
                    return e.activation(out=QKS, in_=bank(pbk),
                                        func=AF.Copy, scale=scol)
                S.add("scalar", qbf, reads=[("pb", pbk), ("lsm",), ("sqb", 1)],
                      writes=[("sqbd", 1), ("pb", pbk)], name=f"qkb{l}_{w}_{n}")
                def mmn(e):
                    return e.matmul(bank(2), HON[:], SQB[:, 0, :],
                                    start=True, stop=True)
                S.add("tensor", mmn, reads=[("sqbd", 0), "const", ("pb", 2)],
                      writes=[("pb", 2), ("sqb", 0)], name=f"qkn{l}_{w}_{n}")
                def ninv(e):
                    return e.tensor_scalar_max(bank(2), bank(2), 1e-24)
                S.add("vector", ninv, reads=[("pb", 2)],
                      writes=[("pb", 2)], name=f"qkni{l}_{w}_{n}")
                def nsq(e):
                    e.activation(out=STT[:, 0, :], in_=bank(2), func=AF.Ln)
                    return e.activation(out=STT[:, 0, :], in_=STT[:, 0, :],
                                        func=AF.Exp, scale=f32(-0.5))
                S.add("scalar", nsq, reads=[("pb", 2), ("stt", 0)],
                      writes=[("stt", 0), ("pb", 2)], name=f"qkns{l}_{w}_{n}")
                emit_ld_mmq(n + 2)
                def mmrot(e):
                    return e.matmul(bank(3), ROTM[:], QKS,
                                    start=True, stop=True)
                S.add("tensor", mmrot, reads=[("sqbd", 1), "const", ("pb", 3)],
                      writes=[("pb", 3)], name=f"qkr{l}_{w}_{n}")
                def rot1(e, w=w, tbl_c=tbl_c):
                    return e.tensor_tensor(STT[:, 2, :], QKS,
                                           tbl_c[:, w * W512:(w + 1) * W512],
                                           ALU.mult)
                S.add("vector", rot1, reads=[("sqbd", 1), "rot", ("stt", 2)],
                      writes=[("stt", 2), ("sqb", 1)], name=f"qkr1{l}_{w}_{n}")
                def rot2(e, w=w, tbl_s=tbl_s):
                    e.tensor_tensor(STT[:, 1, :], bank(3),
                                    tbl_s[:, w * W512:(w + 1) * W512], ALU.mult)
                    return e.tensor_tensor(STT[:, 2, :], STT[:, 2, :],
                                           STT[:, 1, :], ALU.add)
                S.add("vector", rot2,
                      reads=[("pb", 3), "rot", ("stt", 2), ("stt", 1)],
                      writes=[("stt", 2), ("stt", 1), ("pb", 3)],
                      name=f"qkr2{l}_{w}_{n}")
                if isq:
                    dkey, dst = ("qw", c2), (lambda e, c2=c2: QW[:, 0, c2, :])
                else:
                    dkey, dst = ("kw", w % 2, c2), \
                        (lambda e, w=w, c2=c2: KW[:, w % 2, c2, :])
                def rot3(e, dst=dst):
                    return e.tensor_tensor(dst(e), STT[:, 2, :],
                                           STT[:, 0, :], ALU.mult)
                S.add("vector", rot3, reads=[("stt", 2), ("stt", 0), "rot", dkey],
                      writes=[dkey], name=f"qkr3{l}_{w}_{n}")
            else:
                vc = n - 8
                vsl = w % 2
                def vbf(e, pbk=pbk):
                    return e.activation(out=QKS, in_=bank(pbk), func=AF.Copy)
                S.add("scalar", vbf, reads=[("pb", pbk), ("sqb", 1)],
                      writes=[("sqbd", 1), ("pb", pbk)], name=f"vb{l}_{w}_{vc}")
                emit_ld_mmq(n + 2)
                def vtp(e):
                    for q in range(4):
                        inst = e.transpose(PBF[:, 128 * q:128 * (q + 1)],
                                           SQB[:, 1, 128 * q:128 * (q + 1)], IDB[:])
                    return inst
                S.add("tensor", vtp, reads=[("sqbd", 1), "const", ("pb", 7)],
                      writes=[("pb", 7)], name=f"vt{l}_{w}_{vc}")
                def vev(e, vsl=vsl, vc=vc):
                    for q in range(4):
                        src = PBF[:, 128 * q:128 * (q + 1)].rearrange(
                            "p (hh dd) -> p hh dd", hh=2)
                        inst = e.tensor_copy(
                            VW[:, vsl, q, 65 * 2 * vc:65 * 2 * vc + 130].rearrange(
                                "p (hh dd) -> p hh dd", dd=65)[:, :, 0:64], src)
                    return inst
                S.add("vector", vev,
                      reads=[("pb", 7), "rot", ("vw", vsl, vc), ("sqb", 1)],
                      writes=[("vw", vsl, vc), ("pb", 7), ("sqb", 1)],
                      name=f"vev{l}_{w}_{vc}")
        def vones(e, vsl=w % 2):
            for q in range(4):
                inst = e.memset(
                    VW[:, vsl, q, :].rearrange("p (hh dd) -> p hh dd",
                                               dd=65)[:, :, 64:65], 1.0)
            return inst
        S.add("vector", vones,
              reads=[("vw", w % 2, vc) for vc in range(4)] + ["rot", ("vw1", w % 2)],
              writes=[("vw1", w % 2)], name=f"vones{l}_{w}")

        if w == 0 and l == 0:
            for nm, buf, keys in (
                    ("qw0", QW, [("qw", c2) for c2 in range(4)]),
                    ("kw0", KW, [("kw", 0, c2) for c2 in range(4)]),
                    ("vw0", VW, [("vw", 0, vc) for vc in range(4)] + [("vw1", 0)])):
                if tap(nm):
                    tapdram(nm, [128, buf.shape[1] * buf.shape[2] * buf.shape[3]])
                    S.add("gpsimd",
                          (lambda e, nm=nm, buf=buf:
                           e.dma_start(out=dbg[nm], in_=buf[:].rearrange(
                               "p a b c -> p (a b c)"))),
                          reads=keys, name=f"tap_{nm}")

        for h in range(NH):
            c2, hp = h // 2, h % 2
            rows = slice(64 * hp, 64 * hp + 64)
            for qi in range(4):
                kj_lo = max(0, 4 * w + qi - 4)
                kj_hi = 4 * w + qi
                ncol = 128 * (kj_hi - kj_lo + 1)
                kread = sorted({("kw", (kj // 4) % 2, c2)
                                for kj in range(kj_lo, kj_hi + 1)})
                def mmqk1(e, qi=qi, kj_lo=kj_lo, kj_hi=kj_hi, c2=c2, rows=rows):
                    inst = None
                    for i, kj in enumerate(range(kj_lo, kj_hi + 1)):
                        inst = e.matmul(
                            PB[:, 128 * i:128 * (i + 1)],
                            QW[rows, 0, c2, 128 * qi:128 * (qi + 1)],
                            KW[rows, (kj // 4) % 2, c2,
                               128 * (kj % 4):128 * (kj % 4) + 128],
                            start=True, stop=True)
                    return inst
                S.add("tensor", mmqk1,
                      reads=[("qw", c2), ("pb", 0), ("pb", 1)] + kread,
                      writes=[("pb", 0), ("pb", 1)], name=f"qk1_{l}_{w}_{h}_{qi}")
                def redm(e, ncol=ncol, qi=qi):
                    return e.reduce_max(MAXC[:, qi % 2:qi % 2 + 1],
                                        PB[:, 0:ncol], axis=AXX, negate=True)
                S.add("vector", redm, reads=[("pb", 0), ("pb", 1), ("mxc", qi % 2)],
                      writes=[("mxcd", qi % 2), ("pb", 0), ("pb", 1)],
                      name=f"rmx{l}_{w}_{h}_{qi}")
                def tpm(e, qi=qi):
                    return e.transpose(
                        PB[0:1, 3072 + 128 * qi:3072 + 128 * (qi + 1)],
                        MAXC[:, qi % 2:qi % 2 + 1], IDF[:])
                S.add("tensor", tpm, reads=[("mxcd", qi % 2), "const", ("pb", 6)],
                      writes=[("pb", 6), ("mxc", qi % 2)],
                      name=f"tpm{l}_{w}_{h}_{qi}")
            msl = h % 2
            def mev(e, msl=msl):
                return e.activation(out=MSTG[:, msl, :], in_=PB[0:1, 3072:3584],
                                    func=AF.Copy)
            S.add("scalar", mev, reads=[("pb", 6), ("mst", msl)],
                  writes=[("mstd", msl), ("pb", 6)], name=f"mev{l}_{w}_{h}")
            if l == 0 and w == 0 and h <= 1 and tap("mst0"):
                nm = f"mst{h}"
                tapdram(nm, [1, 512])
                S.add("gpsimd",
                      (lambda e, msl=msl, nm=nm:
                       e.dma_start(out=dbg[nm], in_=MSTG[:, msl, :])),
                      reads=[("mstd", msl)], name=f"tap_{nm}")

            pv_started = {"d": False, "z": False}
            for i_kj, kj in enumerate(range(max(0, 4 * w - 4), 4 * w + 4)):
                qi_lo = max(0, kj - 4 * w)
                qi_hi = min(3, kj - 4 * w + 4)
                M = 128 * (qi_hi - qi_lo + 1)
                qb = 2 + (i_kj % 2)
                def mmqk2(e, kj=kj, qi_lo=qi_lo, M=M, c2=c2, rows=rows,
                          qb=qb, msl=msl):
                    e.matmul(bank(qb, 0, M),
                             KW[rows, (kj // 4) % 2, c2,
                                128 * (kj % 4):128 * (kj % 4) + 128],
                             QW[rows, 0, c2, 128 * qi_lo:128 * qi_lo + M],
                             start=True, stop=False)
                    return e.matmul(bank(qb, 0, M), ONES1[:],
                                    MSTG[:, msl, 128 * qi_lo:128 * qi_lo + M],
                                    start=False, stop=True)
                S.add("tensor", mmqk2,
                      reads=[("qw", c2), ("kw", (kj // 4) % 2, c2),
                             ("mstd", msl), ("pb", qb)],
                      writes=[("pb", qb)], name=f"qk2_{l}_{w}_{h}_{kj}")
                psl = i_kj % 2
                def pexp(e, qb=qb, M=M, psl=psl):
                    return e.activation(out=PSTG[:, psl, 0:M], in_=bank(qb, 0, M),
                                        func=AF.Exp)
                S.add("scalar", pexp, reads=[("pb", qb), ("pst", psl)],
                      writes=[("pstd", psl), ("pb", qb)],
                      name=f"pex{l}_{w}_{h}_{kj}")
                qi_d = kj - 4 * w
                qi_e = kj - 4 * w + 4
                if qi_lo <= qi_d <= qi_hi or qi_lo <= qi_e <= qi_hi:
                    def tri(e, qi_d=qi_d, qi_e=qi_e, qi_lo=qi_lo, qi_hi=qi_hi,
                            psl=psl):
                        inst = None
                        if qi_lo <= qi_d <= qi_hi:
                            co = 128 * (qi_d - qi_lo)
                            inst = e.tensor_tensor(PSTG[:, psl, co:co + 128],
                                                   PSTG[:, psl, co:co + 128],
                                                   TRID[:], ALU.mult)
                        if qi_lo <= qi_e <= qi_hi:
                            co = 128 * (qi_e - qi_lo)
                            inst = e.tensor_tensor(PSTG[:, psl, co:co + 128],
                                                   PSTG[:, psl, co:co + 128],
                                                   TRIE[:], ALU.mult)
                        return inst
                    S.add("vector", tri, reads=[("pstd", psl), "const"],
                          writes=[("pstd", psl)], name=f"tri{l}_{w}_{h}_{kj}")
                if l == 0 and w == 0 and h <= 1 and kj == 0 and tap("pst0"):
                    nm = f"pst{h}"
                    tapdram(nm, [128, 512])
                    S.add("gpsimd",
                          (lambda e, psl=psl, M=M, nm=nm:
                           e.dma_start(out=dbg[nm][:, 0:M],
                                       in_=PSTG[:, psl, 0:M])),
                          reads=[("pstd", psl)], name=f"tap_{nm}")
                def pv(e, kj=kj, qi_lo=qi_lo, qi_hi=qi_hi, psl=psl, h=h, hp=hp,
                       w=w, c2=c2, pv_started=pv_started):
                    vsl = (kj // 4) % 2
                    vq = kj % 4
                    inst = None
                    if hp == 1 and not pv_started["d"]:
                        e.matmul(PB[:, 2048:2560], ZERB[:],
                                 KW[:, vsl, c2, :], start=True, stop=False)
                        pv_started["d"] = True
                    for qi in range(qi_lo, qi_hi + 1):
                        st = (hp == 0) and (not pv_started["d"])
                        pv_started["d"] = True
                        sp = (kj == 4 * w + 3 and qi == qi_hi)
                        co = 128 * (qi - qi_lo)
                        if hp == 0:
                            inst = e.matmul(
                                PB[0:65, 2048 + 128 * qi:2048 + 128 * (qi + 1)],
                                VW[:, vsl, vq, 65 * h:65 * h + 65],
                                PSTG[:, psl, co:co + 128], start=st, stop=sp)
                        else:
                            inst = e.matmul(
                                PB[0:1, 2048 + 128 * qi:2048 + 128 * (qi + 1)],
                                ONECB[:], PSTG[:, psl, co:co + 128],
                                start=False, stop=sp)
                            inst = e.matmul(
                                PB[64:128, 2048 + 128 * qi:2048 + 128 * (qi + 1)],
                                VW[:, vsl, vq, 65 * h:65 * h + 64],
                                PSTG[:, psl, co:co + 128], start=False, stop=sp,
                                tile_position=(0, 64))
                    return inst
                S.add("tensor", pv,
                      reads=[("pstd", psl), ("vw", (kj // 4) % 2, kj % 4),
                             ("vw1", (kj // 4) % 2), "const", ("pb", 4)],
                      writes=[("pb", 4), ("pst", psl)],
                      name=f"pv{l}_{w}_{h}_{kj}")
            if l == 0 and w == 0 and h <= 1 and tap("pv0"):
                nm = f"pv{h}"
                tapdram(nm, [128, 512])
                def pvdump(e):
                    return e.tensor_copy(STT[:, 2, :], PB[:, 2048:2560])
                S.add("vector", pvdump, reads=[("pb", 4), ("stt", 2)],
                      writes=[("stt", 2), ("pvdmp",)], name=f"pvd{h}")
                S.add("gpsimd",
                      (lambda e, nm=nm: e.dma_start(out=dbg[nm], in_=STT[:, 2, :])),
                      reads=[("pvdmp",), ("stt", 2)], name=f"tap_{nm}")
            zrow = 64 if hp == 0 else 0
            def zi(e, zrow=zrow):
                e.activation(out=STT[zrow:zrow + 1, 2, :],
                             in_=PB[zrow:zrow + 1, 2048:2560], func=AF.Ln)
                return e.activation(out=STT[zrow:zrow + 1, 2, :],
                                    in_=STT[zrow:zrow + 1, 2, :], func=AF.Exp,
                                    scale=f32(-1.0))
            S.add("scalar", zi, reads=[("pb", 4), ("stt", 2)],
                  writes=[("sttz",), ("stt", 2)], name=f"zi{l}_{w}_{h}")
            def zb(e, zrow=zrow, hp=hp):
                tp = (0, 64) if hp else None
                return e.matmul(PB[64 * hp:64 * hp + 64, 2560:3072],
                                ONE1F[zrow:zrow + 1, 0:64],
                                STT[zrow:zrow + 1, 2, :],
                                start=True, stop=True, tile_position=tp)
            S.add("tensor", zb, reads=[("sttz",), "const", ("pb", 5)],
                  writes=[("pb", 5)], name=f"zb{l}_{w}_{h}")
            def oev(e, hp=hp, c2=c2, w=w):
                e.tensor_copy(STT[64 * hp:64 * hp + 64, 2, :],
                              PB[64 * hp:64 * hp + 64, 2048:2560])
                return e.tensor_tensor(
                    OFM[64 * hp:64 * hp + 64, c2, w * W512:(w + 1) * W512],
                    STT[64 * hp:64 * hp + 64, 2, :],
                    PB[64 * hp:64 * hp + 64, 2560:3072], ALU.mult)
            S.add("vector", oev,
                  reads=[("pb", 4), ("pb", 5), ("ofm", c2, w, hp), ("stt", 2)],
                  writes=[("ofm", c2, w, hp), ("pb", 4), ("pb", 5), ("stt", 2)],
                  name=f"oev{l}_{w}_{h}")

    if tap(f"ofm{l}"):
        tapdram(f"ofm{l}", [128, 4, T])
        S.add("gpsimd", lambda e: e.dma_start(out=dbg[f"ofm{l}"], in_=OFM[:]),
              reads=[("ofm", c2, w, hp) for c2 in range(4) for w in range(NW)
                     for hp in range(2)], name=f"tap_ofm{l}")

    for n in range(C):
        def ldwo(e, n=n, l=l):
            return e.dma_start(out=WOB[:], in_=d_wo[l, :, n, :, :])
        S.add("gpsimd", ldwo, reads=[("wob",)], writes=[("wobd",)],
              name=f"ldwo{l}_{n}")
        for w in range(NW):
            pbk = (n * NW + w) % 2
            def mmo(e, n=n, w=w, pbk=pbk):
                for c2 in range(4):
                    inst = e.matmul(bank(pbk), WOB[:, 128 * c2:128 * (c2 + 1)],
                                    OFM[:, c2, w * W512:(w + 1) * W512],
                                    start=(c2 == 0), stop=(c2 == 3))
                return inst
            S.add("tensor", mmo,
                  reads=[("wobd",), ("pb", pbk)] +
                        [("ofm", c2, w, hp) for c2 in range(4) for hp in range(2)],
                  writes=[("pb", pbk)] + ([("wob",)] if w == NW - 1 else []),
                  name=f"mmo{l}_{n}_{w}")
            def yadd(e, n=n, w=w, pbk=pbk):
                return e.tensor_tensor(XRc(n, w * W512, W512),
                                       XRc(n, w * W512, W512), bank(pbk), ALU.add)
            S.add("vector", yadd, reads=[("pb", pbk), ("xr", w)],
                  writes=[("xr", w), ("pb", pbk), ("wo_done", n, w)],
                  name=f"yadd{l}_{n}_{w}")
            def ymir2(e, n=n, w=w):
                return e.copy(out=ABFc(n, w * W512 + 4, W512),
                              in_=XRc(n, w * W512, W512))
            S.add("scalar", ymir2, reads=[("wo_done", n, w), ("xr", w)],
                  writes=[YK(n, w)], name=f"ym2{l}_{n}_{w}")

    if tap(f"ywo{l}"):
        tapdram(f"ywo{l}", [128, C * T])
        S.add("gpsimd", lambda e: e.dma_start(out=dbg[f"ywo{l}"], in_=XRt[:]),
              reads=[("wo_done", n, w) for n in range(C) for w in range(NW)],
              name=f"tap_ywo{l}")

    for w in range(NW):
        def rhs_bf(c, w=w):
            off = w * W512 + (3 if c >= 4 else 4)
            return ABFc(c, off, W512)
        rd = [YK(c, w) for c in range(C)]
        if w > 0:
            rd += [YK(c, w - 1) for c in range(4, C)]
        ln_stats(f"li{l}w{w}", rhs_bf, rd, bf=True)
        for c in range(C):
            def ia(e, c=c, w=w):
                off = w * W512 + (3 if c >= 4 else 4)
                return e.tensor_tensor(STT[:, 2, :], ABFc(c, off, W512),
                                       STT[:, 0, :], ALU.mult)
            S.add("vector", ia, reads=[("stt", 0), YK(c, w), ("stt", 2)] +
                  ([YK(c, w - 1)] if (w > 0 and c >= 4) else []),
                  writes=[("stt", 2)], name=f"ila{l}_{w}_{c}")
            def ib(e):
                return e.tensor_tensor(STT[:, 2, :], STT[:, 2, :],
                                       STT[:, 1, :], ALU.add)
            S.add("vector", ib, reads=[("stt", 1)], writes=[("stt", 2)],
                  name=f"ilb{l}_{w}_{c}")
            def ic(e, c=c, w=w):
                return e.activation(out=LBFc(c, w * W512, W512),
                                    in_=STT[:, 2, :], func=AF.Identity,
                                    scale=GBC[:, c:c + 1],
                                    bias=GBC[:, C + c:C + c + 1])
            S.add("scalar", ic, reads=[("stt", 2), ("lsm",)],
                  writes=[("lbf", c, w)], name=f"ilc{l}_{w}_{c}")

    if tap(f"lbf{l}"):
        tapdram(f"lbf{l}", [128, C * T])
        S.add("gpsimd", lambda e: e.dma_start(out=dbg[f"lbf{l}"], in_=L["LBF"][:]),
              reads=[("lbf", c, w) for c in range(C) for w in range(NW)],
              name=f"tap_lbf{l}")

    S.virt(writes=(["rot"] + [("qw", c2) for c2 in range(4)] +
                   [("kw", s, c2) for s in range(2) for c2 in range(4)] +
                   [("vw", s, v) for s in range(2) for v in range(4)] +
                   [("vw1", s) for s in range(2)] +
                   [("mres", j, tq) for j in range(NK2A) for tq in range(NW)] +
                   [("mresd", j, tq) for j in range(NK2A) for tq in range(NW)] +
                   [("mrbs", x) for x in range(NW)] +
                   [("mrbsd", x) for x in range(NW)] +
                   [("mrl", x) for x in range(2)]),
           name=f"retag_attn2ffn{l}")

    for j in range(K2N):
        for tq in range(NW):
            for part in range(2):
                n = 2 * j + part
                wsl = n % 3
                fkey = ("wp", wsl)
                dkey = ("wpd", wsl)
                if tq == 0:
                    def ldw1(e, n=n, l=l, wsl=wsl):
                        return e.dma_start(out=WPB[:, wsl, :],
                                           in_=d_w1[l, :, n, :, :])
                    S.add("gpsimd", ldw1, reads=[fkey], writes=[dkey],
                          name=f"ldw1{l}_{n}")
                pbk = part * 2 + tq % 2
                def mm1(e, n=n, tq=tq, pbk=pbk, wsl=wsl):
                    for c in range(C):
                        inst = e.matmul(bank(pbk),
                                        WPB[:, wsl, 128 * c:128 * (c + 1)],
                                        LBFc(c, tq * W512, W512),
                                        start=(c == 0), stop=(c == C - 1))
                    return inst
                S.add("tensor", mm1,
                      reads=[dkey, ("pb", pbk)] +
                            [("lbf", c, tq) for c in range(C)],
                      writes=[("pb", pbk)] + ([fkey] if tq == NW - 1 else []),
                      name=f"mm1{l}_{n}_{tq}")
            pa = tq % 2
            pg = 2 + tq % 2
            def gact(e, j=j, pg=pg):
                return e.activation(out=SQB[:, 0, :], in_=bank(pg), func=AF.Gelu,
                                    bias=B1C[:, K2N + j:K2N + j + 1])
            S.add("scalar", gact, reads=[("pb", pg), ("sqb", 0), ("lsm",)],
                  writes=[("sqbd", 0), ("pb", pg)], name=f"ga{l}_{j}_{tq}")
            if j < NK2A:
                mdst = (lambda j=j, tq=tq: MRES[:, j, tq * W512:(tq + 1) * W512])
                mkey, mdkey = ("mres", j, tq), ("mresd", j, tq)
            else:
                mdst = (lambda tq=tq: MRB[:, tq, :])
                mkey, mdkey = ("mrbs", tq), ("mrbsd", tq)
            def mmul(e, j=j, pa=pa, mdst=mdst):
                return e.scalar_tensor_tensor(mdst(), bank(pa),
                                              B1C[:, j:j + 1], SQB[:, 0, :],
                                              ALU.add, ALU.mult)
            S.add("vector", mmul,
                  reads=[("pb", pa), ("sqbd", 0), ("lsm",), mkey],
                  writes=[("pb", pa), ("sqb", 0), mdkey, mkey],
                  name=f"mmul{l}_{j}_{tq}")
            if j >= NK2A:
                def msp(e, j=j, tq=tq, mdst=mdst):
                    return e.dma_start(
                        out=d_m[:, j - NK2A, tq * W512:(tq + 1) * W512],
                        in_=mdst())
                S.add("sync", msp, reads=[mdkey],
                      writes=[("msp", j - NK2A, tq), mkey],
                      name=f"msp{l}_{j}_{tq}")

    WPALL = ([("wp", x) for x in range(3)] + [("wpd", x) for x in range(3)] +
             [("wpa", x) for x in range(2)] + [("wpad", x) for x in range(2)])
    S.virt(writes=WPALL, name=f"retag_w1w2a{l}")
    for n in range(C):
        hh = n % 2
        def ldw2(e, n=n, l=l, hh=hh):
            return e.dma_start(
                out=WPB[:].rearrange(
                    "p s n -> p (s n)")[:, hh * 1536:hh * 1536 + 128 * NK2A],
                in_=d_w2[l, :, n, 0:NK2A, :])
        S.add("gpsimd", ldw2, reads=[("wpa", hh)], writes=[("wpad", hh)],
              name=f"ldw2{l}_{n}")
        for tq in range(NW):
            pbk = (n * NW + tq) % 2
            def mm2(e, n=n, tq=tq, pbk=pbk, hh=hh):
                wflat = WPB[:].rearrange("p s n -> p (s n)")
                for j in range(NK2A):
                    inst = e.matmul(
                        bank(pbk),
                        wflat[:, hh * 1536 + 128 * j:hh * 1536 + 128 * (j + 1)],
                        MRES[:, j, tq * W512:(tq + 1) * W512],
                        start=(j == 0), stop=(j == NK2A - 1))
                return inst
            S.add("tensor", mm2,
                  reads=[("wpad", hh), ("pb", pbk)] +
                        [("mresd", j, tq) for j in range(NK2A)],
                  writes=[("pb", pbk)] +
                         ([("wpa", hh)] if tq == NW - 1 else []),
                  name=f"mm2{l}_{n}_{tq}")
            def y2a(e, n=n, tq=tq, pbk=pbk):
                return e.scalar_tensor_tensor(
                    XRc(n, tq * W512, W512), bank(pbk),
                    GBC[:, 2 * C + n:2 * C + n + 1],
                    XRc(n, tq * W512, W512), ALU.add, ALU.add)
            S.add("vector", y2a, reads=[("pb", pbk), ("xr", tq), ("lsm",)],
                  writes=[("xr", tq), ("pb", pbk), ("w2p", n, tq)],
                  name=f"y2a{l}_{n}_{tq}")
    S.virt(writes=WPALL, name=f"retag_w2ab{l}")
    for npair in range(C // 2):
        n0 = 2 * npair
        def ldw2b(e, n0=n0, l=l):
            return e.dma_start(
                out=WPB[:].rearrange("p s n -> p (s n)")[:, 0:128 * 2 * NK2B],
                in_=d_w2[l, :, n0:n0 + 2, NK2A:K2N, :])
        S.add("gpsimd", ldw2b, reads=[("wpd", 0), ("wpd", 1), ("wpd", 2)],
              writes=[("wpd", 0), ("wpd", 1), ("wpd", 2)],
              name=f"ldw2B{l}_{npair}")
        for tqp in range(2):
            pp = npair * 2 + tqp
            bks = [(pp * 4 + i) % 7 for i in range(4)]
            for j in range(NK2A, K2N):
                sl = (j - NK2A) % 2
                def mrl(e, j=j, sl=sl, tqp=tqp):
                    dst = MRB[:, 4 + 2 * sl:6 + 2 * sl, :].rearrange(
                        "p a b -> p (a b)")
                    return e.dma_start(
                        out=dst,
                        in_=d_m[:, j - NK2A, tqp * 1024:(tqp + 1) * 1024])
                S.add("sync", mrl,
                      reads=[("msp", j - NK2A, 2 * tqp),
                             ("msp", j - NK2A, 2 * tqp + 1)],
                      writes=[("mrl", sl)], name=f"mrl{l}_{pp}_{j}")
                def mm2b(e, j=j, sl=sl, bks=bks):
                    wflat = WPB[:].rearrange("p s n -> p (s n)")
                    ring = MRB[:, 4 + 2 * sl:6 + 2 * sl, :].rearrange(
                        "p a b -> p (a b)")
                    inst = None
                    for dn in range(2):
                        wcol = 128 * (dn * NK2B + (j - NK2A))
                        for dt in range(2):
                            inst = e.matmul(
                                bank(bks[dn * 2 + dt]),
                                wflat[:, wcol:wcol + 128],
                                ring[:, dt * W512:(dt + 1) * W512],
                                start=(j == NK2A), stop=(j == K2N - 1))
                    return inst
                S.add("tensor", mm2b,
                      reads=[("wpd", 0), ("wpd", 1), ("wpd", 2),
                             ("mrl", sl)] + [("pb", b) for b in bks],
                      writes=[("pb", b) for b in bks] + [("mrl", sl)],
                      name=f"mm2b{l}_{pp}_{j}")
            for dn in range(2):
                n = n0 + dn
                for dt in range(2):
                    tq = tqp * 2 + dt
                    b = bks[dn * 2 + dt]
                    def y2c(e, n=n, tq=tq, b=b):
                        return e.tensor_tensor(XRc(n, tq * W512, W512),
                                               XRc(n, tq * W512, W512),
                                               bank(b), ALU.add)
                    S.add("vector", y2c,
                          reads=[("pb", b), ("xr", tq), ("w2p", n, tq)],
                          writes=[("xr", tq), ("pb", b), ("w2d", n, tq)],
                          name=f"y2c{l}_{n}_{tq}")
                    if l == 0:
                        def ymir3(e, n=n, tq=tq):
                            return e.copy(out=ABFc(n, tq * W512 + 4, W512),
                                          in_=XRc(n, tq * W512, W512))
                        S.add("scalar", ymir3,
                              reads=[("w2d", n, tq), ("xr", tq)],
                              writes=[YK(n, tq)], name=f"ym3{l}_{n}_{tq}")

    if tap(f"y{l + 1}"):
        tapdram(f"y{l + 1}", [128, C * T])
        S.add("gpsimd", lambda e: e.dma_start(out=dbg[f"y{l + 1}"], in_=XRt[:]),
              reads=[("w2d", n, tq) for n in range(C) for tq in range(NW)],
              name=f"tap_y{l + 1}")


def kernel(**inputs):
    cst = _constants()
    w = _prep_weights(inputs)
    x = np.asarray(inputs["L0_attn_query"], np.float32)
    nc, _ = build_program()
    base = {}
    for k in WEIGHT_NAMES:
        base[k] = w[k]
    for k in CONST_NAMES:
        base[k] = np.asarray(cst[k])
    in_maps = []
    for b in range(8):
        m = dict(base)
        m["x"] = np.ascontiguousarray(x[b])
        in_maps.append(m)
    res = run_bass_kernel_spmd(nc, in_maps, core_ids=list(range(8)))
    return np.stack([res.results[b]["out"] for b in range(8)], 0)


# version 06
# speedup vs baseline: 1.4681x; 1.1620x over previous
import numpy as np
import ml_dtypes
from contextlib import ExitStack

import concourse.bass as bass
import concourse.mybir as mybir
from concourse.bass_utils import run_bass_kernel_spmd

F32 = mybir.dt.float32
BF16 = mybir.dt.bfloat16
AF = mybir.ActivationFunctionType
ALU = mybir.AluOpType
AXX = mybir.AxisListType.X
f32 = np.float32

T = 2048
D = 1024
C = 8
NW = 4
W512 = 512
K9 = 9
NH = 8
QKN = 12
HID = 2730
HIDP = 2816
K2N = 22
W1N = 44
NK2A = 11
NK2B = K2N - NK2A
LNEPS = 1e-5

_CONST_CACHE = {}


def _constants():
    if _CONST_CACHE:
        return _CONST_CACHE
    c = {}
    c["ident_bf"] = np.eye(128, dtype=ml_dtypes.bfloat16)
    c["ident_f32"] = np.eye(128, dtype=np.float32)
    R = np.zeros((64, 64), np.float32)
    for d in range(32):
        R[d + 32, d] = -1.0
        R[d, d + 32] = 1.0
    Rb = np.zeros((128, 128), np.float32)
    Rb[0:64, 0:64] = R
    Rb[64:128, 64:128] = R
    c["rotmat"] = Rb.astype(ml_dtypes.bfloat16)
    d = np.arange(64, dtype=np.float64)
    half = (d % 32) * 2.0
    inv_freq = 1.0 / (10000.0 ** (half / 64.0))
    t = np.arange(T, dtype=np.float64)
    freqs = t[None, :] * inv_freq[:, None]
    base = (half + 0.4 * 64.0) / (1.4 * 64.0)
    power = (t - T // 2) / 256.0
    scale = base[:, None] ** power[None, :]
    cs, sn = np.cos(freqs), np.sin(freqs)

    def dup(a):
        return np.vstack([a, a]).astype(np.float32).astype(ml_dtypes.bfloat16)
    c["cosq"] = dup(8.0 * cs * scale)
    c["sinq"] = dup(8.0 * sn * scale)
    c["cosk"] = dup(cs / scale)
    c["sink"] = dup(sn / scale)
    r = np.arange(128)
    c["tri_diag"] = (r[:, None] <= r[None, :]).astype(ml_dtypes.bfloat16)
    c["tri_edge"] = (r[:, None] >= r[None, :]).astype(ml_dtypes.bfloat16)
    hb = np.zeros((128, 128), np.float32)
    hb[0:64, 0:64] = 1.0
    hb[64:128, 64:128] = 1.0
    c["hones"] = hb.astype(ml_dtypes.bfloat16)
    c["lncol_f32"] = np.full((128, 128), 1.0 / D, np.float32)
    c["lncol_bf"] = np.full((128, 128), 1.0 / D, np.float32).astype(ml_dtypes.bfloat16)
    c["ones1_bf"] = np.ones((1, 128), ml_dtypes.bfloat16)
    c["ones1_f32"] = np.ones((128, 128), np.float32)
    c["zeros_bf"] = np.zeros((128, 128), ml_dtypes.bfloat16)
    c["onescol_bf"] = np.ones((128, 1), ml_dtypes.bfloat16)
    s4 = np.zeros((128, 8), np.float32)
    for ct in range(4):
        for hp in range(2):
            for kk in range(K9):
                s4[32 * ct + 9 * hp + kk, 2 * ct + hp] = 1.0
    c["sel4"] = s4.astype(ml_dtypes.bfloat16)
    sz = np.zeros((8, 128), np.float32)
    for ct in range(4):
        for hp in range(2):
            for kk in range(K9):
                sz[2 * ct + hp, 32 * ct + 9 * hp + kk] = 1.0
    c["selzrow"] = sz
    B = np.zeros((K9, 4, 128, 128), np.float32)
    for kk in range(K9):
        for ct in range(4):
            for hp in range(2):
                B[kk, ct, 32 * ct + 9 * hp + kk, 64 * hp:64 * (hp + 1)] = 1.0
    c["bsel"] = np.ascontiguousarray(B.transpose(2, 0, 1, 3)).astype(ml_dtypes.bfloat16)
    _CONST_CACHE.update(c)
    return c


CONST_NAMES = ("ident_bf", "ident_f32", "rotmat", "cosq", "sinq", "cosk", "sink",
               "tri_diag", "tri_edge", "hones", "lncol_f32", "lncol_bf",
               "ones1_bf", "ones1_f32", "onescol_bf", "zeros_bf", "sel4",
               "selzrow", "bsel")
WEIGHT_NAMES = ("cw_stat", "wqkv", "wo", "w1", "w2", "b1a", "b1g", "qscol", "kscol",
                "lng", "lnb", "b2")


def _prep_weights(inp):
    w = {}
    cw = np.asarray(inp["L0_attn_conv_weight"], np.float32)
    Wk = np.zeros((128, C, K9, 18), np.float32)
    for h in range(16):
        cc, hp = h // 2, h % 2
        for kk in range(K9):
            Wk[64 * hp:64 * (hp + 1), cc, kk, 9 * hp:9 * (hp + 1)] = \
                cw[h * 9:(h + 1) * 9, :, kk].T
    w["cw_stat"] = np.ascontiguousarray(Wk.reshape(128, C * K9 * 18))
    wqkv = np.asarray(inp["L0_ffn_wqkv"], np.float32)
    w["wqkv"] = np.ascontiguousarray(
        wqkv.reshape(2, C, 128, QKN, 128).transpose(0, 2, 3, 1, 4))
    wo = np.asarray(inp["L0_ffn_wo"], np.float32)
    w["wo"] = np.ascontiguousarray(
        wo.reshape(2, 4, 128, 8, 128).transpose(0, 2, 3, 1, 4))
    w1 = np.asarray(inp["L0_ffn_w1"], np.float32)
    a, g = w1[:, :, :HID], w1[:, :, HID:]
    ap = np.zeros((2, D, HIDP), np.float32)
    gp = np.zeros((2, D, HIDP), np.float32)
    ap[:, :, :HID] = a
    gp[:, :, :HID] = g
    w1p = np.zeros((2, D, W1N, 128), np.float32)
    w1p[:, :, 0::2, :] = ap.reshape(2, D, K2N, 128)
    w1p[:, :, 1::2, :] = gp.reshape(2, D, K2N, 128)
    w["w1"] = np.ascontiguousarray(
        w1p.reshape(2, C, 128, W1N, 128).transpose(0, 2, 3, 1, 4))
    b1 = np.asarray(inp["L0_ffn_b1"], np.float32)
    b1a = np.zeros((2, HIDP), np.float32)
    b1g = np.zeros((2, HIDP), np.float32)
    b1a[:, :HID] = b1[:, :HID]
    b1g[:, :HID] = b1[:, HID:]
    w["b1a"] = np.ascontiguousarray(b1a.reshape(2, K2N, 128).transpose(0, 2, 1))
    w["b1g"] = np.ascontiguousarray(b1g.reshape(2, K2N, 128).transpose(0, 2, 1))
    w2 = np.asarray(inp["L0_ffn_w2"], np.float32)
    w2p = np.zeros((2, HIDP, D), np.float32)
    w2p[:, :HID, :] = w2
    w["w2"] = np.ascontiguousarray(
        w2p.reshape(2, K2N, 128, 8, 128).transpose(0, 2, 3, 1, 4))
    qs = np.asarray(inp["L0_ffn_q_scale"], np.float32)
    ks = np.asarray(inp["L0_ffn_k_scale"], np.float32)
    w["qscol"] = np.ascontiguousarray(np.concatenate([qs, qs], 1).reshape(2, 128, 1))
    w["kscol"] = np.ascontiguousarray(np.concatenate([ks, ks], 1).reshape(2, 128, 1))
    for nm, key in (("lng", "L0_ffn_ln_g"), ("lnb", "L0_ffn_ln_b"), ("b2", "L0_ffn_b2")):
        v = np.asarray(inp[key], np.float32)
        w[nm] = np.ascontiguousarray(v.reshape(2, C, 128).transpose(0, 2, 1))
    return w


ENGINES = ("tensor", "vector", "scalar", "sync", "gpsimd")
DMA_ENG = ("sync", "gpsimd")


class _Step:
    __slots__ = ("engine", "fn", "name", "deps", "need_inc", "inc_val", "waits",
                 "vdeps")

    def __init__(self, engine, fn, name):
        self.engine = engine
        self.fn = fn
        self.name = name
        self.deps = []
        self.need_inc = False
        self.inc_val = None
        self.waits = None
        self.vdeps = None


SERIAL = False


class Sched:
    def __init__(self):
        self.steps = {e: [] for e in ENGINES}
        self._writer = {}
        self._readers = {}
        self.n = 0
        self._prev = None

    def _collect(self, reads, writes):
        dep = set()
        for r in reads:
            w = self._writer.get(r)
            if w is not None:
                dep.add(w)
        for wk in writes:
            w = self._writer.get(wk)
            if w is not None:
                dep.add(w)
            for r in self._readers.get(wk, ()):
                dep.add(r)
        out = set()
        for d in dep:
            if d.vdeps is not None:
                out.update(d.vdeps)
            else:
                out.add(d)
        return out

    def add(self, engine, fn, reads=(), writes=(), name=""):
        s = _Step(engine, fn, name or f"s{self.n}")
        self.n += 1
        dep = self._collect(reads, writes)
        if SERIAL and self._prev is not None:
            dep.add(self._prev)
        self._prev = s
        for r in reads:
            self._readers.setdefault(r, []).append(s)
        for wk in writes:
            self._writer[wk] = s
            self._readers[wk] = []
        for d in dep:
            if d is s:
                continue
            if d.engine == engine and d.engine not in DMA_ENG:
                continue
            d.need_inc = True
            s.deps.append(d)
        if engine in DMA_ENG:
            s.need_inc = True
        self.steps[engine].append(s)
        return s

    def virt(self, writes=(), reads=(), name="virt"):
        s = _Step(None, None, name)
        s.vdeps = self._collect(reads, writes)
        for r in reads:
            self._readers.setdefault(r, []).append(s)
        for wk in writes:
            self._writer[wk] = s
            self._readers[wk] = []
        return s

    def emit(self, blk, sems):
        for e in ENGINES:
            cnt = 0
            amt = 16 if e in DMA_ENG else 1
            for s in self.steps[e]:
                if s.need_inc:
                    cnt += amt
                    s.inc_val = cnt
        for e in ENGINES:
            hw = {x: 0 for x in ENGINES}
            for s in self.steps[e]:
                ws = []
                for d in sorted(s.deps, key=lambda d: d.inc_val):
                    if d.inc_val > hw[d.engine]:
                        hw[d.engine] = d.inc_val
                        ws.append((d.engine, d.inc_val))
                s.waits = ws
        amt = {e: (16 if e in DMA_ENG else 1) for e in ENGINES}

        def mk(e_name):
            def body(eng):
                for s in self.steps[e_name]:
                    for (de, v) in s.waits:
                        eng.wait_ge(sems[de], v)
                    inst = s.fn(eng)
                    if s.need_inc:
                        assert inst is not None, f"step {s.name} returned None"
                        inst.then_inc(sems[e_name], amt[e_name])
            return body
        blk.tensor(mk("tensor"))
        blk.vector(mk("vector"))
        blk.scalar(mk("scalar"))
        blk.sync(mk("sync"))
        blk.gpsimd(mk("gpsimd"))


def build_program(stage=99, debug_taps=()):
    cst = _constants()
    nc = bass.Bass("TRN2", target_bir_lowering=False)
    taps = set(debug_taps)

    xin = nc.dram_tensor("x", [T, D], F32, kind="ExternalInput").ap()
    out = nc.dram_tensor("out", [T, D], F32, kind="ExternalOutput").ap()
    d_cw = nc.dram_tensor("cw_stat", [128, C * K9 * 18], F32, kind="ExternalInput").ap()
    d_wqkv = nc.dram_tensor("wqkv", [2, 128, QKN, C, 128], F32, kind="ExternalInput").ap()
    d_wo = nc.dram_tensor("wo", [2, 128, 8, 4, 128], F32, kind="ExternalInput").ap()
    d_w1 = nc.dram_tensor("w1", [2, 128, W1N, C, 128], F32, kind="ExternalInput").ap()
    d_w2 = nc.dram_tensor("w2", [2, 128, 8, K2N, 128], F32, kind="ExternalInput").ap()
    d_b1a = nc.dram_tensor("b1a", [2, 128, K2N], F32, kind="ExternalInput").ap()
    d_b1g = nc.dram_tensor("b1g", [2, 128, K2N], F32, kind="ExternalInput").ap()
    d_qs = nc.dram_tensor("qscol", [2, 128, 1], F32, kind="ExternalInput").ap()
    d_ks = nc.dram_tensor("kscol", [2, 128, 1], F32, kind="ExternalInput").ap()
    d_lng = nc.dram_tensor("lng", [2, 128, C], F32, kind="ExternalInput").ap()
    d_lnb = nc.dram_tensor("lnb", [2, 128, C], F32, kind="ExternalInput").ap()
    d_b2 = nc.dram_tensor("b2", [2, 128, C], F32, kind="ExternalInput").ap()
    dc = {}
    for nm in CONST_NAMES:
        a = np.asarray(cst[nm])
        dt = BF16 if a.dtype == ml_dtypes.bfloat16 else F32
        dc[nm] = nc.dram_tensor(nm, list(a.shape), dt, kind="ExternalInput").ap()
    d_x1 = nc.dram_tensor("x1sp", [128, C, T], F32, kind="Internal").ap()
    d_m = nc.dram_tensor("msp", [128, NK2B, T], BF16, kind="Internal").ap()

    dbg = {}

    def tap(name):
        return name in taps

    def tapdram(name, shape, dtype=F32):
        dbg[name] = nc.dram_tensor("dbg_" + name, shape, dtype,
                                   kind="ExternalOutput").ap()
        return dbg[name]

    ctx = ExitStack()
    E = ctx.enter_context

    XR = E(nc.sbuf_tensor("XR", [128, C * T], F32))
    ABF = E(nc.sbuf_tensor("ABF", [128, C * (T + 8)], BF16))
    LBF = E(nc.sbuf_tensor("LBF", [128, C * T], BF16))
    R2 = E(nc.sbuf_tensor("R2", [128, 53376], mybir.dt.uint8))
    r2base = nc.lookup_mloc(R2).addr

    def r2v(name, shape, dtype, off):
        return nc.alloc_sbuf_tensor_at(name, shape, dtype, offset=r2base + off)

    BSEL = r2v("BSEL", [128, K9, 4, 128], BF16, 0)
    CWS = r2v("CWS", [128, C * K9 * 18], BF16, 9216)
    CONVS = r2v("CONVS", [128, 2, 512], BF16, 12096)
    ZR = r2v("ZR", [8, 2 * 512], F32, 14144)
    ACCW = r2v("ACCW", [128, C, W512], F32, 18432)
    STG = r2v("STG", [128, 4, D], F32, 34816)
    COSQ = r2v("COSQ", [128, T], BF16, 0)
    SINQ = r2v("SINQ", [128, T], BF16, 4096)
    COSK = r2v("COSK", [128, T], BF16, 8192)
    SINK = r2v("SINK", [128, T], BF16, 12288)
    QW = r2v("QW", [128, 1, 4, W512], BF16, 16384)
    KW = r2v("KW", [128, 2, 4, W512], BF16, 20480)
    VW = r2v("VW", [128, 2, 4, 520], BF16, 28672)
    OFM = r2v("OFM", [128, 4, T], BF16, 36992)
    MRES = r2v("MRES", [128, NK2A, T], BF16, 0)
    MRB = r2v("MRB", [128, 8, T // 4], BF16, 45056)
    X1RR = r2v("X1RR", [128, 2, 512], F32, 0)
    STGO = r2v("STGO", [128, 4, D], F32, 36992)

    PSTG = E(nc.sbuf_tensor("PSTG", [128, 2, 512], BF16))
    MSTG = E(nc.sbuf_tensor("MSTG", [1, 2, W512], BF16))
    STT = E(nc.sbuf_tensor("STT", [128, 3, 512], F32))
    SQB = E(nc.sbuf_tensor("SQB", [128, 2, 512], BF16))
    QKS = SQB[:, 1, :]
    MAXC = E(nc.sbuf_tensor("MAXC", [128, 2], F32))
    IDB = E(nc.sbuf_tensor("IDB", [128, 128], BF16))
    IDF = E(nc.sbuf_tensor("IDF", [128, 128], F32))
    ROTM = E(nc.sbuf_tensor("ROTM", [128, 128], BF16))
    TRID = E(nc.sbuf_tensor("TRID", [128, 128], BF16))
    TRIE = E(nc.sbuf_tensor("TRIE", [128, 128], BF16))
    HON = E(nc.sbuf_tensor("HON", [128, 128], BF16))
    LNC32 = E(nc.sbuf_tensor("LNC32", [128, 128], F32))
    LNCBF = E(nc.sbuf_tensor("LNCBF", [128, 128], BF16))
    ONES1 = E(nc.sbuf_tensor("ONES1", [1, 128], BF16))
    ONE1F = E(nc.sbuf_tensor("ONE1F", [128, 128], F32))
    ONECB = E(nc.sbuf_tensor("ONECB", [128, 1], BF16))
    ZERB = E(nc.sbuf_tensor("ZERB", [128, 128], BF16))
    SEL4 = E(nc.sbuf_tensor("SEL4", [128, 8], BF16))
    SELZ = E(nc.sbuf_tensor("SELZ", [8, 128], F32))
    WQB = E(nc.sbuf_tensor("WQB", [128, 2, C * 128], BF16))
    WOB = E(nc.sbuf_tensor("WOB", [128, 4 * 128], BF16))
    WPB = E(nc.sbuf_tensor("WPB", [128, 3, C * 128], BF16))
    QSC = E(nc.sbuf_tensor("QSC", [128, 2], F32))
    GBC = E(nc.sbuf_tensor("GBC", [128, 3 * C], F32))
    B1C = E(nc.sbuf_tensor("B1C", [128, 2 * K2N], F32))

    PB = E(nc.psum_tensor("PB", [128, 3584], F32))
    PBF = E(nc.psum_tensor("PBF", [128, 1024], BF16))

    def bank(i, col0=0, ncol=512):
        return PB[:, 512 * i + col0: 512 * i + col0 + ncol]

    sems = {e: E(nc.semaphore("sem_" + e)) for e in ENGINES}
    blk = E(nc.Block())
    S = Sched()

    def XRc(c, lo, n):
        return XR[:, c * T + lo: c * T + lo + n]

    def ABFc(c, lo, n):
        return ABF[:, c * (T + 8) + lo: c * (T + 8) + lo + n]

    def LBFc(c, lo, n):
        return LBF[:, c * T + lo: c * T + lo + n]

    _consts = ((IDB[:], dc["ident_bf"]), (IDF[:], dc["ident_f32"]),
               (ROTM[:], dc["rotmat"]), (TRID[:], dc["tri_diag"]),
               (TRIE[:], dc["tri_edge"]), (HON[:], dc["hones"]),
               (LNC32[:], dc["lncol_f32"]), (LNCBF[:], dc["lncol_bf"]),
               (ONES1[:], dc["ones1_bf"]), (ONE1F[:], dc["ones1_f32"]),
               (ONECB[:], dc["onescol_bf"]), (ZERB[:], dc["zeros_bf"]),
               (SEL4[:], dc["sel4"]), (SELZ[:], dc["selzrow"]))
    for i, (dst, src) in enumerate(_consts):
        S.add("sync", (lambda e, dst=dst, src=src: e.dma_start(out=dst, in_=src)),
              writes=["const"] if i == len(_consts) - 1 else [],
              name=f"ld_const{i}")
    S.add("gpsimd", lambda e: e.dma_start(out=BSEL[:], in_=dc["bsel"]),
          writes=[], name="ld_bsel")
    S.add("gpsimd", lambda e: e.dma_start(out=CWS[:], in_=d_cw),
          writes=["convw"], name="ld_convw")

    def z_guards(e):
        for c in range(C):
            e.memset(ABF[:, c * (T + 8):c * (T + 8) + 4], 0)
            inst = e.memset(ABF[:, c * (T + 8) + T + 4:c * (T + 8) + T + 8], 0)
        return inst
    S.add("vector", z_guards, writes=["guards"], name="zguards")

    xv = xin.rearrange("(tt p) f -> p tt f", p=128)
    for qtr in range(4):
        def ld_q(e, qtr=qtr):
            return e.dma_start(out=STG[:], in_=xv[:, 4 * qtr:4 * (qtr + 1), :])
        S.add("sync", ld_q, writes=[("stg",)], name=f"ldx{qtr}")
        for tt4 in range(4):
            tt = 4 * qtr + tt4
            for cq in range(2):
                bk = (tt4 * 2 + cq) % 2
                def tp(e, tt4=tt4, cq=cq, bk=bk):
                    for c4 in range(4):
                        c = cq * 4 + c4
                        inst = e.transpose(
                            PB[:, 512 * bk + 128 * c4:512 * bk + 128 * (c4 + 1)],
                            STG[:, tt4, 128 * c:128 * (c + 1)], IDF[:])
                    return inst
                S.add("tensor", tp, reads=[("stg",), "const", ("pb", bk)],
                      writes=[("pb", bk)], name=f"tpx{tt}_{cq}")
                def ev(e, tt=tt, cq=cq, bk=bk):
                    for c4 in range(4):
                        c = cq * 4 + c4
                        inst = e.tensor_copy(
                            XRc(c, tt * 128, 128),
                            PB[:, 512 * bk + 128 * c4:512 * bk + 128 * (c4 + 1)])
                    return inst
                S.add("vector", ev, reads=[("pb", bk), ("stg",)],
                      writes=[("xr", tt // 4), ("pb", bk)], name=f"evx{tt}_{cq}")

    if tap("xr"):
        tapdram("xr", [128, C * T])
        S.add("gpsimd", lambda e: e.dma_start(out=dbg["xr"], in_=XR[:]),
              reads=[("xr", i) for i in range(4)], name="tap_xr")

    def ln_stats(tag, rhs, reads, bf=False):
        def mm_mean(e):
            for c in range(C):
                inst = e.matmul(bank(5), LNCBF[:] if bf else LNC32[:], rhs(c),
                                start=(c == 0), stop=(c == C - 1))
            return inst
        S.add("tensor", mm_mean, reads=list(reads) + ["const", ("pb", 5)],
              writes=[("pb", 5)], name=f"lnm_{tag}")
        for c in range(C):
            sl = c % 2
            def sq1(e, c=c, sl=sl):
                return e.activation(out=SQB[:, sl, :], in_=rhs(c), func=AF.Square)
            S.add("scalar", sq1, reads=list(reads) + [("sqb", sl)],
                  writes=[("sqbd", sl)], name=f"lnsq_{tag}_{c}")
            def mmsq(e, c=c, sl=sl):
                return e.matmul(bank(6), LNCBF[:], SQB[:, sl, :],
                                start=(c == 0), stop=(c == C - 1))
            S.add("tensor", mmsq, reads=[("sqbd", sl), "const", ("pb", 6)],
                  writes=[("pb", 6), ("sqb", sl)], name=f"lnsqm_{tag}_{c}")

        def stats(e):
            e.tensor_copy(STT[:, 2, :], bank(5))
            e.tensor_tensor(STT[:, 1, :], STT[:, 2, :], STT[:, 2, :], ALU.mult)
            return e.scalar_tensor_tensor(STT[:, 1, :], STT[:, 1, :], -1.0,
                                          bank(6), ALU.mult, ALU.add)
        S.add("vector", stats,
              reads=[("pb", 5), ("pb", 6), ("stt", 2), ("stt", 1)],
              writes=[("stt", 2), ("stt", 1), ("pb", 6), ("pb", 5)],
              name=f"lnv_{tag}")
        def lnr(e):
            e.activation(out=STT[:, 1, :], in_=STT[:, 1, :], func=AF.Ln,
                         bias=f32(LNEPS))
            return e.activation(out=STT[:, 0, :], in_=STT[:, 1, :],
                                func=AF.Exp, scale=f32(-0.5))
        S.add("scalar", lnr, reads=[("stt", 1), ("stt", 0)],
              writes=[("stt", 0), ("stt", 1)], name=f"lnr_{tag}")

        def negm(e):
            return e.scalar_tensor_tensor(STT[:, 1, :], STT[:, 2, :], -1.0,
                                          STT[:, 0, :], ALU.mult, ALU.mult)
        S.add("vector", negm, reads=[("stt", 0), ("stt", 2), ("stt", 1)],
              writes=[("stt", 1)], name=f"lnn_{tag}")

    for w in range(NW):
        ln_stats(f"l1w{w}", lambda c, w=w: XRc(c, w * W512, W512), [("xr", w)])
        for c in range(C):
            def apa(e, c=c, w=w):
                return e.tensor_tensor(STT[:, 2, :], XRc(c, w * W512, W512),
                                       STT[:, 0, :], ALU.mult)
            S.add("vector", apa, reads=[("stt", 0), ("xr", w), ("stt", 2)],
                  writes=[("stt", 2)], name=f"l1a{w}_{c}")
            def apb(e, c=c, w=w):
                return e.tensor_tensor(ABFc(c, w * W512 + 4, W512),
                                       STT[:, 2, :], STT[:, 1, :], ALU.add)
            S.add("vector", apb, reads=[("stt", 2), ("stt", 1), "guards"],
                  writes=[("ln1", c, w)], name=f"l1b{w}_{c}")

    if tap("ln1"):
        tapdram("ln1", [128, C * (T + 8)])
        S.add("gpsimd", lambda e: e.dma_start(out=dbg["ln1"], in_=ABF[:]),
              reads=[("ln1", c, w) for c in range(C) for w in range(NW)],
              name="tap_ln1")

    if stage >= 2:
        for w in range(NW):
            for half in range(2):
                def mml(e, w=w, half=half):
                    for ct in range(4):
                        c = half * 4 + ct
                        tp = (0, 32 * ct) if ct > 0 else None
                        for k in range(K9):
                            inst = e.matmul(
                                PB[32 * ct:32 * ct + 18,
                                   512 * half:512 * half + 512],
                                CWS[:, (c * K9 + k) * 18:(c * K9 + k) * 18 + 18],
                                ABFc(c, w * W512 + k, W512),
                                start=(k == 0), stop=(k == K9 - 1),
                                tile_position=tp)
                    return inst
                S.add("tensor", mml,
                      reads=[("ln1", half * 4 + ct, w) for ct in range(4)] +
                            ["convw", ("pb", half)],
                      writes=[("pb", half)], name=f"cvl{w}_{half}")
                def clamp(e, half=half):
                    return e.tensor_scalar_min(bank(half), bank(half), 60.0)
                S.add("vector", clamp, reads=[("pb", half)], writes=[("pb", half)],
                      name=f"cvc{w}_{half}")
                def expE(e, half=half):
                    return e.activation(out=CONVS[:, half, :], in_=bank(half),
                                        func=AF.Exp)
                S.add("scalar", expE, reads=[("pb", half), ("cv_e", half)],
                      writes=[("cv_e", half), ("pb", half)], name=f"cve{w}_{half}")
                def mmz(e, half=half):
                    return e.matmul(PB[0:8, 1024:1536], SEL4[:], CONVS[:, half, :],
                                    start=True, stop=True)
                S.add("tensor", mmz, reads=[("cv_e", half), "const", ("pb", 2)],
                      writes=[("pb", 2)], name=f"cvz{w}_{half}")
                def zinv(e, half=half):
                    e.activation(out=ZR[:, half * 512:(half + 1) * 512],
                                 in_=PB[0:8, 1024:1536], func=AF.Ln)
                    return e.activation(out=ZR[:, half * 512:(half + 1) * 512],
                                        in_=ZR[:, half * 512:(half + 1) * 512],
                                        func=AF.Exp, scale=f32(-1.0))
                S.add("scalar", zinv, reads=[("pb", 2), ("cv_zi", half)],
                      writes=[("cv_zi", half), ("pb", 2)],
                      name=f"cvzi{w}_{half}")
                def mmzr(e, half=half):
                    return e.matmul(bank(3), SELZ[:],
                                    ZR[:, half * 512:(half + 1) * 512],
                                    start=True, stop=True)
                S.add("tensor", mmzr, reads=[("cv_zi", half), "const", ("pb", 3)],
                      writes=[("pb", 3)], name=f"cvzr{w}_{half}")
                def wnorm(e, half=half):
                    return e.tensor_tensor(CONVS[:, half, :], CONVS[:, half, :],
                                           bank(3), ALU.mult)
                S.add("vector", wnorm, reads=[("cv_e", half), ("pb", 3)],
                      writes=[("cv_w", half), ("cv_e", half), ("pb", 3)],
                      name=f"cvw{w}_{half}")
            for c in range(C):
                half, ct = c // 4, c % 4
                for k in range(K9):
                    pb = 3 + (k % 2)
                    def mmb(e, k=k, ct=ct, half=half, pb=pb):
                        return e.matmul(bank(pb), BSEL[:, k, ct, :],
                                        CONVS[:, half, :], start=True, stop=True)
                    S.add("tensor", mmb,
                          reads=[("cv_w", half), "convw", ("pb", pb)],
                          writes=[("pb", pb)], name=f"cvb{w}_{c}_{k}")
                    def mix(e, k=k, c=c, w=w, pb=pb):
                        if k == 0:
                            return e.tensor_tensor(ACCW[:, c, :],
                                                   ABFc(c, w * W512 + k, W512),
                                                   bank(pb), ALU.mult)
                        e.tensor_tensor(STT[:, 2, :], ABFc(c, w * W512 + k, W512),
                                        bank(pb), ALU.mult)
                        return e.tensor_tensor(ACCW[:, c, :], ACCW[:, c, :],
                                               STT[:, 2, :], ALU.add)
                    S.add("vector", mix,
                          reads=[("pb", pb), ("ln1", c, w), ("acc", c), ("stt", 2)],
                          writes=[("acc", c), ("pb", pb), ("stt", 2)],
                          name=f"cvm{w}_{c}_{k}")
            if tap("conv") and w == 0:
                tapdram("conv", [128, C, W512])
                S.add("gpsimd", lambda e: e.dma_start(out=dbg["conv"], in_=ACCW[:]),
                      reads=[("acc", c) for c in range(C)], name="tap_conv")
            ln_stats(f"lcw{w}", lambda c: ACCW[:, c, :],
                     [("acc", c) for c in range(C)])
            for c in range(C):
                def xa(e, c=c):
                    return e.tensor_tensor(STT[:, 2, :], ACCW[:, c, :],
                                           STT[:, 0, :], ALU.mult)
                S.add("vector", xa, reads=[("stt", 0), ("acc", c), ("stt", 2)],
                      writes=[("stt", 2)], name=f"x1a{w}_{c}")
                def xb(e, c=c):
                    return e.tensor_tensor(STT[:, 2, :], STT[:, 2, :],
                                           STT[:, 1, :], ALU.add)
                S.add("vector", xb, reads=[("stt", 1)], writes=[("stt", 2)],
                      name=f"x1b{w}_{c}")
                def xc(e, c=c, w=w):
                    return e.tensor_tensor(XRc(c, w * W512, W512),
                                           XRc(c, w * W512, W512),
                                           STT[:, 2, :], ALU.add)
                S.add("vector", xc, reads=[("stt", 2), ("xr", w)],
                      writes=[("xr", w), ("acc", c)], name=f"x1c{w}_{c}")

        if tap("x1"):
            tapdram("x1", [128, C * T])
            S.add("gpsimd", lambda e: e.dma_start(out=dbg["x1"], in_=XR[:]),
                  reads=[("xr", i) for i in range(4)], name="tap_x1")

    if stage >= 3:
        def spill(e):
            return e.dma_start(out=d_x1, in_=XR[:].rearrange("p (c t) -> p c t", t=T))
        S.add("sync", spill, reads=[("xr", i) for i in range(4)], writes=["x1sp"],
              name="x1spill")
        for w in range(NW):
            ln_stats(f"l2w{w}", lambda c, w=w: XRc(c, w * W512, W512), [("xr", w)])
            for c in range(C):
                def ya(e, c=c, w=w):
                    return e.tensor_tensor(XRc(c, w * W512, W512),
                                           XRc(c, w * W512, W512),
                                           STT[:, 0, :], ALU.mult)
                S.add("vector", ya, reads=[("stt", 0), "x1sp"],
                      writes=[("xr", w)], name=f"y2a{w}_{c}")
                def yb(e, c=c, w=w):
                    return e.tensor_tensor(XRc(c, w * W512, W512),
                                           XRc(c, w * W512, W512),
                                           STT[:, 1, :], ALU.add)
                S.add("vector", yb, reads=[("stt", 1)], writes=[("xr", w)],
                      name=f"y2b{w}_{c}")
                def ymir(e, c=c, w=w):
                    return e.copy(out=ABFc(c, w * W512 + 4, W512),
                                  in_=XRc(c, w * W512, W512))
                S.add("scalar", ymir,
                      reads=[("xr", w)],
                      writes=[("ybf", c, w), ("ln1", c, w)], name=f"ymir{w}_{c}")

        if tap("y0"):
            tapdram("y0", [128, C * T])
            S.add("gpsimd", lambda e: e.dma_start(out=dbg["y0"], in_=XR[:]),
                  reads=[("xr", i) for i in range(4)], name="tap_y0")

    QWK = [("qw", c2) for c2 in range(4)]
    KWK = [("kw", s, c2) for s in range(2) for c2 in range(4)]
    VWK = [("vw", s, v) for s in range(2) for v in range(4)] + \
          [("vw1", s) for s in range(2)]
    OFK = [("ofm", c2, w, hp) for c2 in range(4) for w in range(NW)
           for hp in range(2)]
    MRK = ([("mres", j, tq) for j in range(NK2A) for tq in range(NW)] +
           [("mresd", j, tq) for j in range(NK2A) for tq in range(NW)] +
           [("mrbs", x) for x in range(NW)] +
           [("mrbsd", x) for x in range(NW)] +
           [("mrl", x) for x in range(2)])

    if stage >= 4:
        nlayers = 2 if stage >= 5 else 1
        for l in range(nlayers):
            if l == 0:
                S.virt(writes=(["rot", ("stg",), "convw",
                                ("cv_w", 0), ("cv_w", 1), ("cv_e", 0), ("cv_e", 1),
                                ("cv_zi", 0), ("cv_zi", 1)] +
                               [("acc", c) for c in range(C)] +
                               QWK + KWK + VWK + OFK), name="retag_conv2attn")
            else:
                S.virt(writes=["rot"] + QWK + KWK + VWK + OFK + MRK,
                       name=f"retag_ffn2attn{l}")
            _ffn_layer(S, locals(), l)

    if stage >= 9:
        S.virt(writes=[("x1r", 0), ("x1r", 1), ("stgo2",)] + OFK + MRK,
               name="retag_exit")
        outv = out.rearrange("(tt p) f -> p tt f", p=128)
        for w in range(NW):
            ln_stats(f"lfw{w}", lambda c, w=w: XRc(c, w * W512, W512), [("xr", w)])
            for c in range(C):
                def fa(e, c=c, w=w):
                    return e.tensor_tensor(STT[:, 2, :], XRc(c, w * W512, W512),
                                           STT[:, 0, :], ALU.mult)
                S.add("vector", fa, reads=[("stt", 0), ("xr", w), ("stt", 2)],
                      writes=[("stt", 2)], name=f"fna{w}_{c}")
                def fb(e, c=c, w=w):
                    return e.tensor_tensor(STT[:, 2, :], STT[:, 2, :],
                                           STT[:, 1, :], ALU.add)
                S.add("vector", fb, reads=[("stt", 1)], writes=[("stt", 2)],
                      name=f"fnb{w}_{c}")
                sl = c % 2
                def ldx1(e, c=c, w=w, sl=sl):
                    return e.dma_start(out=X1RR[:, sl, :],
                                       in_=d_x1[:, c, w * W512:(w + 1) * W512])
                S.add("sync", ldx1, reads=["x1sp", ("x1r", sl)],
                      writes=[("x1rd", sl)], name=f"ldx1_{w}_{c}")
                def fc(e, c=c, sl=sl):
                    return e.tensor_tensor(STT[:, 2, :], STT[:, 2, :],
                                           X1RR[:, sl, :], ALU.add)
                S.add("vector", fc, reads=[("x1rd", sl)],
                      writes=[("stt", 2), ("x1r", sl)], name=f"fnc{w}_{c}")
                bk = c % 2
                def ftp(e, c=c, bk=bk):
                    for q in range(4):
                        inst = e.transpose(
                            PB[:, 512 * bk + 128 * q:512 * bk + 128 * (q + 1)],
                            STT[:, 2, 128 * q:128 * (q + 1)], IDF[:])
                    return inst
                S.add("tensor", ftp, reads=[("stt", 2), "const", ("pb", bk)],
                      writes=[("pb", bk)], name=f"ftp{w}_{c}")
                def fev(e, c=c, bk=bk):
                    for q in range(4):
                        inst = e.tensor_copy(
                            STGO[:, q, 128 * c:128 * (c + 1)],
                            PB[:, 512 * bk + 128 * q:512 * bk + 128 * (q + 1)])
                    return inst
                S.add("vector", fev, reads=[("pb", bk), ("stgo2",)],
                      writes=[("pb", bk), ("stgo", c)], name=f"fev{w}_{c}")
            def sto(e, w=w):
                return e.dma_start(out=outv[:, 4 * w:4 * (w + 1), :], in_=STGO[:])
            S.add("sync", sto, reads=[("stgo", c) for c in range(C)],
                  writes=["outd", ("stgo2",)], name=f"sto{w}")
        S.add("sync", lambda e: e.nop(), reads=["outd"], name="outwait")

    S.emit(blk, sems)
    ctx.close()
    return nc, dbg


def _ffn_layer(S, L, l):
    nc = L["nc"]
    PB, PBF = L["PB"], L["PBF"]
    bank = L["bank"]
    XRc, ABFc, LBFc = L["XRc"], L["ABFc"], L["LBFc"]
    STT, SQB, QKS, MAXC = L["STT"], L["SQB"], L["QKS"], L["MAXC"]
    WPB = L["WPB"]
    QW, KW, VW, OFM = L["QW"], L["KW"], L["VW"], L["OFM"]
    COSQ, SINQ, COSK, SINK = L["COSQ"], L["SINQ"], L["COSK"], L["SINK"]
    MRES, MRB = L["MRES"], L["MRB"]
    PSTG, MSTG = L["PSTG"], L["MSTG"]
    WQB, WOB = L["WQB"], L["WOB"]
    QSC, GBC, B1C = L["QSC"], L["GBC"], L["B1C"]
    HON, ROTM, IDB, IDF = L["HON"], L["ROTM"], L["IDB"], L["IDF"]
    TRID, TRIE = L["TRID"], L["TRIE"]
    ONES1, ONE1F, ONECB = L["ONES1"], L["ONE1F"], L["ONECB"]
    ZERB = L["ZERB"]
    d_wqkv, d_wo, d_w1, d_w2 = L["d_wqkv"], L["d_wo"], L["d_w1"], L["d_w2"]
    d_qs, d_ks = L["d_qs"], L["d_ks"]
    d_lng, d_lnb, d_b2 = L["d_lng"], L["d_lnb"], L["d_b2"]
    d_b1a, d_b1g, d_m = L["d_b1a"], L["d_b1g"], L["d_m"]
    MRB = L["MRB"]
    dc = L["dc"]
    dbg, taps, tapdram = L["dbg"], L["taps"], L["tapdram"]
    ln_stats = L["ln_stats"]
    XRt = L["XR"]

    def tap(name):
        return name in taps

    YK = lambda c, w: ("ybf", c, w)

    for i, (dst, src) in enumerate((
            (QSC[:, 0:1], d_qs[l]), (QSC[:, 1:2], d_ks[l]),
            (GBC[:, 0:C], d_lng[l]), (GBC[:, C:2 * C], d_lnb[l]),
            (GBC[:, 2 * C:3 * C], d_b2[l]),
            (B1C[:, 0:K2N], d_b1a[l]), (B1C[:, K2N:2 * K2N], d_b1g[l]))):
        S.add("sync", (lambda e, dst=dst, src=src: e.dma_start(out=dst, in_=src)),
              writes=[("lsm",)], name=f"ldsm{l}_{i}")

    for i, (dst, src) in enumerate(((COSQ[:], dc["cosq"]), (SINQ[:], dc["sinq"]),
                                    (COSK[:], dc["cosk"]), (SINK[:], dc["sink"]))):
        S.add("sync", (lambda e, dst=dst, src=src: e.dma_start(out=dst, in_=src)),
              writes=["rot"], name=f"ld_rot{l}_{i}")

    for w in range(NW):
        def emit_ld_mmq(n, w=w, l=l):
            if n >= QKN:
                return
            wsl = n % 2
            pbk = n % 2
            def ldwq(e, n=n, l=l, wsl=wsl):
                return e.dma_start(out=WQB[:, wsl, :], in_=d_wqkv[l, :, n, :, :])
            S.add("gpsimd", ldwq, reads=[("wqb", wsl)], writes=[("wqbd", wsl)],
                  name=f"ldwq{l}_{w}_{n}")
            yreads = [YK(c, w) for c in range(C)]
            if w > 0:
                yreads += [YK(c, w - 1) for c in range(4, C)]
            def mmq(e, n=n, w=w, pbk=pbk, wsl=wsl):
                for c in range(C):
                    off = w * W512 + (3 if c >= 4 else 4)
                    inst = e.matmul(bank(pbk), WQB[:, wsl, 128 * c:128 * (c + 1)],
                                    ABFc(c, off, W512),
                                    start=(c == 0), stop=(c == C - 1))
                return inst
            S.add("tensor", mmq, reads=[("wqbd", wsl), ("pb", pbk)] + yreads,
                  writes=[("pb", pbk), ("wqb", wsl)], name=f"mmqkv{l}_{w}_{n}")

        emit_ld_mmq(0)
        emit_ld_mmq(1)
        for n in range(QKN):
            pbk = n % 2
            if n < 8:
                isq = n < 4
                c2 = n % 4
                tbl_c, tbl_s = (COSQ, SINQ) if isq else (COSK, SINK)
                scol = QSC[:, 0:1] if isq else QSC[:, 1:2]
                def sq(e, pbk=pbk):
                    return e.activation(out=SQB[:, 0, :], in_=bank(pbk),
                                        func=AF.Square)
                S.add("scalar", sq, reads=[("pb", pbk), ("sqb", 0)],
                      writes=[("sqbd", 0)], name=f"qksq{l}_{w}_{n}")
                def qbf(e, pbk=pbk, scol=scol):
                    return e.activation(out=QKS, in_=bank(pbk),
                                        func=AF.Copy, scale=scol)
                S.add("scalar", qbf, reads=[("pb", pbk), ("lsm",), ("sqb", 1)],
                      writes=[("sqbd", 1), ("pb", pbk)], name=f"qkb{l}_{w}_{n}")
                def mmn(e):
                    return e.matmul(bank(2), HON[:], SQB[:, 0, :],
                                    start=True, stop=True)
                S.add("tensor", mmn, reads=[("sqbd", 0), "const", ("pb", 2)],
                      writes=[("pb", 2), ("sqb", 0)], name=f"qkn{l}_{w}_{n}")
                def ninv(e):
                    return e.tensor_scalar_max(bank(2), bank(2), 1e-24)
                S.add("vector", ninv, reads=[("pb", 2)],
                      writes=[("pb", 2)], name=f"qkni{l}_{w}_{n}")
                def nsq(e):
                    e.activation(out=STT[:, 0, :], in_=bank(2), func=AF.Ln)
                    return e.activation(out=STT[:, 0, :], in_=STT[:, 0, :],
                                        func=AF.Exp, scale=f32(-0.5))
                S.add("scalar", nsq, reads=[("pb", 2), ("stt", 0)],
                      writes=[("stt", 0), ("pb", 2)], name=f"qkns{l}_{w}_{n}")
                emit_ld_mmq(n + 2)
                def mmrot(e):
                    return e.matmul(bank(3), ROTM[:], QKS,
                                    start=True, stop=True)
                S.add("tensor", mmrot, reads=[("sqbd", 1), "const", ("pb", 3)],
                      writes=[("pb", 3)], name=f"qkr{l}_{w}_{n}")
                def rot1(e, w=w, tbl_c=tbl_c):
                    return e.tensor_tensor(STT[:, 2, :], QKS,
                                           tbl_c[:, w * W512:(w + 1) * W512],
                                           ALU.mult)
                S.add("vector", rot1, reads=[("sqbd", 1), "rot", ("stt", 2)],
                      writes=[("stt", 2), ("sqb", 1)], name=f"qkr1{l}_{w}_{n}")
                def rot2(e, w=w, tbl_s=tbl_s):
                    e.tensor_tensor(STT[:, 1, :], bank(3),
                                    tbl_s[:, w * W512:(w + 1) * W512], ALU.mult)
                    return e.tensor_tensor(STT[:, 2, :], STT[:, 2, :],
                                           STT[:, 1, :], ALU.add)
                S.add("vector", rot2,
                      reads=[("pb", 3), "rot", ("stt", 2), ("stt", 1)],
                      writes=[("stt", 2), ("stt", 1), ("pb", 3)],
                      name=f"qkr2{l}_{w}_{n}")
                if isq:
                    dkey, dst = ("qw", c2), (lambda e, c2=c2: QW[:, 0, c2, :])
                else:
                    dkey, dst = ("kw", w % 2, c2), \
                        (lambda e, w=w, c2=c2: KW[:, w % 2, c2, :])
                def rot3(e, dst=dst):
                    return e.tensor_tensor(dst(e), STT[:, 2, :],
                                           STT[:, 0, :], ALU.mult)
                S.add("vector", rot3, reads=[("stt", 2), ("stt", 0), "rot", dkey],
                      writes=[dkey], name=f"qkr3{l}_{w}_{n}")
            else:
                vc = n - 8
                vsl = w % 2
                def vbf(e, pbk=pbk):
                    return e.activation(out=QKS, in_=bank(pbk), func=AF.Copy)
                S.add("scalar", vbf, reads=[("pb", pbk), ("sqb", 1)],
                      writes=[("sqbd", 1), ("pb", pbk)], name=f"vb{l}_{w}_{vc}")
                emit_ld_mmq(n + 2)
                def vtp(e):
                    for q in range(4):
                        inst = e.transpose(PBF[:, 128 * q:128 * (q + 1)],
                                           SQB[:, 1, 128 * q:128 * (q + 1)], IDB[:])
                    return inst
                S.add("tensor", vtp, reads=[("sqbd", 1), "const", ("pb", 7)],
                      writes=[("pb", 7)], name=f"vt{l}_{w}_{vc}")
                def vev(e, vsl=vsl, vc=vc):
                    for q in range(4):
                        src = PBF[:, 128 * q:128 * (q + 1)].rearrange(
                            "p (hh dd) -> p hh dd", hh=2)
                        inst = e.tensor_copy(
                            VW[:, vsl, q, 65 * 2 * vc:65 * 2 * vc + 130].rearrange(
                                "p (hh dd) -> p hh dd", dd=65)[:, :, 0:64], src)
                    return inst
                S.add("vector", vev,
                      reads=[("pb", 7), "rot", ("vw", vsl, vc), ("sqb", 1)],
                      writes=[("vw", vsl, vc), ("pb", 7), ("sqb", 1)],
                      name=f"vev{l}_{w}_{vc}")
        def vones(e, vsl=w % 2):
            for q in range(4):
                inst = e.memset(
                    VW[:, vsl, q, :].rearrange("p (hh dd) -> p hh dd",
                                               dd=65)[:, :, 64:65], 1.0)
            return inst
        S.add("vector", vones,
              reads=[("vw", w % 2, vc) for vc in range(4)] + ["rot", ("vw1", w % 2)],
              writes=[("vw1", w % 2)], name=f"vones{l}_{w}")

        if w == 0 and l == 0:
            for nm, buf, keys in (
                    ("qw0", QW, [("qw", c2) for c2 in range(4)]),
                    ("kw0", KW, [("kw", 0, c2) for c2 in range(4)]),
                    ("vw0", VW, [("vw", 0, vc) for vc in range(4)] + [("vw1", 0)])):
                if tap(nm):
                    tapdram(nm, [128, buf.shape[1] * buf.shape[2] * buf.shape[3]])
                    S.add("gpsimd",
                          (lambda e, nm=nm, buf=buf:
                           e.dma_start(out=dbg[nm], in_=buf[:].rearrange(
                               "p a b c -> p (a b c)"))),
                          reads=keys, name=f"tap_{nm}")

        def emit_qk1(h, qi, w=w, l=l):
            c2, hp = h // 2, h % 2
            rows = slice(64 * hp, 64 * hp + 64)
            kj_lo = max(0, 4 * w + qi - 4)
            kj_hi = 4 * w + qi
            ncol = 128 * (kj_hi - kj_lo + 1)
            kread = sorted({("kw", (kj // 4) % 2, c2)
                            for kj in range(kj_lo, kj_hi + 1)})
            def mmqk1(e, qi=qi, kj_lo=kj_lo, kj_hi=kj_hi, c2=c2, rows=rows):
                inst = None
                for i, kj in enumerate(range(kj_lo, kj_hi + 1)):
                    inst = e.matmul(
                        PB[:, 128 * i:128 * (i + 1)],
                        QW[rows, 0, c2, 128 * qi:128 * (qi + 1)],
                        KW[rows, (kj // 4) % 2, c2,
                           128 * (kj % 4):128 * (kj % 4) + 128],
                        start=True, stop=True)
                return inst
            S.add("tensor", mmqk1,
                  reads=[("qw", c2), ("pb", 0), ("pb", 1)] + kread,
                  writes=[("pb", 0), ("pb", 1)], name=f"qk1_{l}_{w}_{h}_{qi}")
            def redm(e, ncol=ncol, qi=qi):
                return e.reduce_max(MAXC[:, qi % 2:qi % 2 + 1],
                                    PB[:, 0:ncol], axis=AXX, negate=True)
            S.add("vector", redm, reads=[("pb", 0), ("pb", 1), ("mxc", qi % 2)],
                  writes=[("mxcd", qi % 2), ("pb", 0), ("pb", 1)],
                  name=f"rmx{l}_{w}_{h}_{qi}")

        def emit_tpm(h, qi, w=w, l=l):
            def tpm(e, qi=qi):
                return e.transpose(
                    PB[0:1, 3072 + 128 * qi:3072 + 128 * (qi + 1)],
                    MAXC[:, qi % 2:qi % 2 + 1], IDF[:])
            S.add("tensor", tpm, reads=[("mxcd", qi % 2), "const", ("pb", 6)],
                  writes=[("pb", 6), ("mxc", qi % 2)],
                  name=f"tpm{l}_{w}_{h}_{qi}")

        def emit_mev(h, w=w, l=l):
            msl = h % 2
            def mev(e, msl=msl):
                return e.activation(out=MSTG[:, msl, :], in_=PB[0:1, 3072:3584],
                                    func=AF.Copy)
            S.add("scalar", mev, reads=[("pb", 6), ("mst", msl)],
                  writes=[("mstd", msl), ("pb", 6)], name=f"mev{l}_{w}_{h}")
            if l == 0 and w == 0 and h <= 1 and tap("mst0"):
                nm = f"mst{h}"
                tapdram(nm, [1, 512])
                S.add("gpsimd",
                      (lambda e, msl=msl, nm=nm:
                       e.dma_start(out=dbg[nm], in_=MSTG[:, msl, :])),
                      reads=[("mstd", msl)], name=f"tap_{nm}")

        def emit_qk2(h, i_kj, w=w, l=l):
            c2, hp = h // 2, h % 2
            rows = slice(64 * hp, 64 * hp + 64)
            msl = h % 2
            kj = 4 * w - 4 + i_kj
            if kj < 0:
                return
            qi_lo = max(0, kj - 4 * w)
            qi_hi = min(3, kj - 4 * w + 4)
            M = 128 * (qi_hi - qi_lo + 1)
            qb = 2 + (i_kj % 2)
            def mmqk2(e, kj=kj, qi_lo=qi_lo, M=M, c2=c2, rows=rows,
                      qb=qb, msl=msl):
                e.matmul(bank(qb, 0, M),
                         KW[rows, (kj // 4) % 2, c2,
                            128 * (kj % 4):128 * (kj % 4) + 128],
                         QW[rows, 0, c2, 128 * qi_lo:128 * qi_lo + M],
                         start=True, stop=False)
                return e.matmul(bank(qb, 0, M), ONES1[:],
                                MSTG[:, msl, 128 * qi_lo:128 * qi_lo + M],
                                start=False, stop=True)
            S.add("tensor", mmqk2,
                  reads=[("qw", c2), ("kw", (kj // 4) % 2, c2),
                         ("mstd", msl), ("pb", qb)],
                  writes=[("pb", qb)], name=f"qk2_{l}_{w}_{h}_{kj}")
            psl = i_kj % 2
            def pexp(e, qb=qb, M=M, psl=psl):
                return e.activation(out=PSTG[:, psl, 0:M], in_=bank(qb, 0, M),
                                    func=AF.Exp)
            S.add("scalar", pexp, reads=[("pb", qb), ("pst", psl)],
                  writes=[("pstd", psl), ("pb", qb)],
                  name=f"pex{l}_{w}_{h}_{kj}")
            qi_d = kj - 4 * w
            qi_e = kj - 4 * w + 4
            if qi_lo <= qi_d <= qi_hi or qi_lo <= qi_e <= qi_hi:
                def tri(e, qi_d=qi_d, qi_e=qi_e, qi_lo=qi_lo, qi_hi=qi_hi,
                        psl=psl):
                    inst = None
                    if qi_lo <= qi_d <= qi_hi:
                        co = 128 * (qi_d - qi_lo)
                        inst = e.tensor_tensor(PSTG[:, psl, co:co + 128],
                                               PSTG[:, psl, co:co + 128],
                                               TRID[:], ALU.mult)
                    if qi_lo <= qi_e <= qi_hi:
                        co = 128 * (qi_e - qi_lo)
                        inst = e.tensor_tensor(PSTG[:, psl, co:co + 128],
                                               PSTG[:, psl, co:co + 128],
                                               TRIE[:], ALU.mult)
                    return inst
                S.add("vector", tri, reads=[("pstd", psl), "const"],
                      writes=[("pstd", psl)], name=f"tri{l}_{w}_{h}_{kj}")
            if l == 0 and w == 0 and h <= 1 and kj == 0 and tap("pst0"):
                nm = f"pst{h}"
                tapdram(nm, [128, 512])
                S.add("gpsimd",
                      (lambda e, psl=psl, M=M, nm=nm:
                       e.dma_start(out=dbg[nm][:, 0:M],
                                   in_=PSTG[:, psl, 0:M])),
                      reads=[("pstd", psl)], name=f"tap_{nm}")

        def emit_pv(h, i_kj, pvst, w=w, l=l):
            c2, hp = h // 2, h % 2
            kj = 4 * w - 4 + i_kj
            if kj < 0:
                return
            qi_lo = max(0, kj - 4 * w)
            qi_hi = min(3, kj - 4 * w + 4)
            psl = i_kj % 2
            def pv(e, kj=kj, qi_lo=qi_lo, qi_hi=qi_hi, psl=psl, h=h, hp=hp,
                   w=w, c2=c2, pv_started=pvst):
                vsl = (kj // 4) % 2
                vq = kj % 4
                inst = None
                if hp == 1 and not pv_started["d"]:
                    e.matmul(PB[:, 2048:2560], ZERB[:],
                             KW[:, vsl, c2, :], start=True, stop=False)
                    pv_started["d"] = True
                for qi in range(qi_lo, qi_hi + 1):
                    st = (hp == 0) and (not pv_started["d"])
                    pv_started["d"] = True
                    sp = (kj == 4 * w + 3 and qi == qi_hi)
                    co = 128 * (qi - qi_lo)
                    if hp == 0:
                        inst = e.matmul(
                            PB[0:65, 2048 + 128 * qi:2048 + 128 * (qi + 1)],
                            VW[:, vsl, vq, 65 * h:65 * h + 65],
                            PSTG[:, psl, co:co + 128], start=st, stop=sp)
                    else:
                        inst = e.matmul(
                            PB[0:1, 2048 + 128 * qi:2048 + 128 * (qi + 1)],
                            ONECB[:], PSTG[:, psl, co:co + 128],
                            start=False, stop=sp)
                        inst = e.matmul(
                            PB[64:128, 2048 + 128 * qi:2048 + 128 * (qi + 1)],
                            VW[:, vsl, vq, 65 * h:65 * h + 64],
                            PSTG[:, psl, co:co + 128], start=False, stop=sp,
                            tile_position=(0, 64))
                return inst
            S.add("tensor", pv,
                  reads=[("pstd", psl), ("vw", (kj // 4) % 2, kj % 4),
                         ("vw1", (kj // 4) % 2), "const", ("pb", 4)],
                  writes=[("pb", 4), ("pst", psl)],
                  name=f"pv{l}_{w}_{h}_{kj}")

        def emit_fin(h, w=w, l=l):
            c2, hp = h // 2, h % 2
            if l == 0 and w == 0 and h <= 1 and tap("pv0"):
                nm = f"pv{h}"
                tapdram(nm, [128, 512])
                def pvdump(e):
                    return e.tensor_copy(STT[:, 2, :], PB[:, 2048:2560])
                S.add("vector", pvdump, reads=[("pb", 4), ("stt", 2)],
                      writes=[("stt", 2), ("pvdmp",)], name=f"pvd{h}")
                S.add("gpsimd",
                      (lambda e, nm=nm: e.dma_start(out=dbg[nm], in_=STT[:, 2, :])),
                      reads=[("pvdmp",), ("stt", 2)], name=f"tap_{nm}")
            zrow = 64 if hp == 0 else 0
            def zi(e, zrow=zrow):
                e.activation(out=STT[zrow:zrow + 1, 2, :],
                             in_=PB[zrow:zrow + 1, 2048:2560], func=AF.Ln)
                return e.activation(out=STT[zrow:zrow + 1, 2, :],
                                    in_=STT[zrow:zrow + 1, 2, :], func=AF.Exp,
                                    scale=f32(-1.0))
            S.add("scalar", zi, reads=[("pb", 4), ("stt", 2)],
                  writes=[("sttz",), ("stt", 2)], name=f"zi{l}_{w}_{h}")
            def zb(e, zrow=zrow, hp=hp):
                tp = (0, 64) if hp else None
                return e.matmul(PB[64 * hp:64 * hp + 64, 2560:3072],
                                ONE1F[zrow:zrow + 1, 0:64],
                                STT[zrow:zrow + 1, 2, :],
                                start=True, stop=True, tile_position=tp)
            S.add("tensor", zb, reads=[("sttz",), "const", ("pb", 5)],
                  writes=[("pb", 5)], name=f"zb{l}_{w}_{h}")
            def oev(e, hp=hp, c2=c2, w=w):
                e.tensor_copy(STT[64 * hp:64 * hp + 64, 2, :],
                              PB[64 * hp:64 * hp + 64, 2048:2560])
                return e.tensor_tensor(
                    OFM[64 * hp:64 * hp + 64, c2, w * W512:(w + 1) * W512],
                    STT[64 * hp:64 * hp + 64, 2, :],
                    PB[64 * hp:64 * hp + 64, 2560:3072], ALU.mult)
            S.add("vector", oev,
                  reads=[("pb", 4), ("pb", 5), ("ofm", c2, w, hp), ("stt", 2)],
                  writes=[("ofm", c2, w, hp), ("pb", 4), ("pb", 5), ("stt", 2)],
                  name=f"oev{l}_{w}_{h}")

        pvst = [{"d": False} for _ in range(NH)]
        for qi in range(4):
            emit_qk1(0, qi)
        for qi in range(4):
            emit_tpm(0, qi)
        emit_mev(0)
        for h in range(NH):
            for i_kj in range(8):
                emit_qk2(h, i_kj)
                if h + 1 < NH and i_kj < 4:
                    emit_qk1(h + 1, i_kj)
                if i_kj >= 1:
                    emit_pv(h, i_kj - 1, pvst[h])
                if h + 1 < NH and i_kj >= 4:
                    emit_tpm(h + 1, i_kj - 4)
            emit_pv(h, 7, pvst[h])
            if h + 1 < NH:
                emit_mev(h + 1)
            emit_fin(h)

    if tap(f"ofm{l}"):
        tapdram(f"ofm{l}", [128, 4, T])
        S.add("gpsimd", lambda e: e.dma_start(out=dbg[f"ofm{l}"], in_=OFM[:]),
              reads=[("ofm", c2, w, hp) for c2 in range(4) for w in range(NW)
                     for hp in range(2)], name=f"tap_ofm{l}")

    for n in range(C):
        def ldwo(e, n=n, l=l):
            return e.dma_start(out=WOB[:], in_=d_wo[l, :, n, :, :])
        S.add("gpsimd", ldwo, reads=[("wob",)], writes=[("wobd",)],
              name=f"ldwo{l}_{n}")
        for w in range(NW):
            pbk = (n * NW + w) % 2
            def mmo(e, n=n, w=w, pbk=pbk):
                for c2 in range(4):
                    inst = e.matmul(bank(pbk), WOB[:, 128 * c2:128 * (c2 + 1)],
                                    OFM[:, c2, w * W512:(w + 1) * W512],
                                    start=(c2 == 0), stop=(c2 == 3))
                return inst
            S.add("tensor", mmo,
                  reads=[("wobd",), ("pb", pbk)] +
                        [("ofm", c2, w, hp) for c2 in range(4) for hp in range(2)],
                  writes=[("pb", pbk)] + ([("wob",)] if w == NW - 1 else []),
                  name=f"mmo{l}_{n}_{w}")
            def yadd(e, n=n, w=w, pbk=pbk):
                return e.tensor_tensor(XRc(n, w * W512, W512),
                                       XRc(n, w * W512, W512), bank(pbk), ALU.add)
            S.add("vector", yadd, reads=[("pb", pbk), ("xr", w)],
                  writes=[("xr", w), ("pb", pbk), ("wo_done", n, w)],
                  name=f"yadd{l}_{n}_{w}")
            def ymir2(e, n=n, w=w):
                return e.copy(out=ABFc(n, w * W512 + 4, W512),
                              in_=XRc(n, w * W512, W512))
            S.add("scalar", ymir2, reads=[("wo_done", n, w), ("xr", w)],
                  writes=[YK(n, w)], name=f"ym2{l}_{n}_{w}")

    if tap(f"ywo{l}"):
        tapdram(f"ywo{l}", [128, C * T])
        S.add("gpsimd", lambda e: e.dma_start(out=dbg[f"ywo{l}"], in_=XRt[:]),
              reads=[("wo_done", n, w) for n in range(C) for w in range(NW)],
              name=f"tap_ywo{l}")

    for w in range(NW):
        def rhs_bf(c, w=w):
            off = w * W512 + (3 if c >= 4 else 4)
            return ABFc(c, off, W512)
        rd = [YK(c, w) for c in range(C)]
        if w > 0:
            rd += [YK(c, w - 1) for c in range(4, C)]
        ln_stats(f"li{l}w{w}", rhs_bf, rd, bf=True)
        for c in range(C):
            def ia(e, c=c, w=w):
                off = w * W512 + (3 if c >= 4 else 4)
                return e.tensor_tensor(STT[:, 2, :], ABFc(c, off, W512),
                                       STT[:, 0, :], ALU.mult)
            S.add("vector", ia, reads=[("stt", 0), YK(c, w), ("stt", 2)] +
                  ([YK(c, w - 1)] if (w > 0 and c >= 4) else []),
                  writes=[("stt", 2)], name=f"ila{l}_{w}_{c}")
            def ib(e):
                return e.tensor_tensor(STT[:, 2, :], STT[:, 2, :],
                                       STT[:, 1, :], ALU.add)
            S.add("vector", ib, reads=[("stt", 1)], writes=[("stt", 2)],
                  name=f"ilb{l}_{w}_{c}")
            def ic(e, c=c, w=w):
                return e.activation(out=LBFc(c, w * W512, W512),
                                    in_=STT[:, 2, :], func=AF.Identity,
                                    scale=GBC[:, c:c + 1],
                                    bias=GBC[:, C + c:C + c + 1])
            S.add("scalar", ic, reads=[("stt", 2), ("lsm",)],
                  writes=[("lbf", c, w)], name=f"ilc{l}_{w}_{c}")

    if tap(f"lbf{l}"):
        tapdram(f"lbf{l}", [128, C * T])
        S.add("gpsimd", lambda e: e.dma_start(out=dbg[f"lbf{l}"], in_=L["LBF"][:]),
              reads=[("lbf", c, w) for c in range(C) for w in range(NW)],
              name=f"tap_lbf{l}")

    S.virt(writes=(["rot"] + [("qw", c2) for c2 in range(4)] +
                   [("kw", s, c2) for s in range(2) for c2 in range(4)] +
                   [("vw", s, v) for s in range(2) for v in range(4)] +
                   [("vw1", s) for s in range(2)] +
                   [("mres", j, tq) for j in range(NK2A) for tq in range(NW)] +
                   [("mresd", j, tq) for j in range(NK2A) for tq in range(NW)] +
                   [("mrbs", x) for x in range(NW)] +
                   [("mrbsd", x) for x in range(NW)] +
                   [("mrl", x) for x in range(2)]),
           name=f"retag_attn2ffn{l}")

    for j in range(K2N):
        for tq in range(NW):
            for part in range(2):
                n = 2 * j + part
                wsl = n % 3
                fkey = ("wp", wsl)
                dkey = ("wpd", wsl)
                if tq == 0:
                    def ldw1(e, n=n, l=l, wsl=wsl):
                        return e.dma_start(out=WPB[:, wsl, :],
                                           in_=d_w1[l, :, n, :, :])
                    S.add("gpsimd", ldw1, reads=[fkey], writes=[dkey],
                          name=f"ldw1{l}_{n}")
                pbk = part * 2 + tq % 2
                def mm1(e, n=n, tq=tq, pbk=pbk, wsl=wsl):
                    for c in range(C):
                        inst = e.matmul(bank(pbk),
                                        WPB[:, wsl, 128 * c:128 * (c + 1)],
                                        LBFc(c, tq * W512, W512),
                                        start=(c == 0), stop=(c == C - 1))
                    return inst
                S.add("tensor", mm1,
                      reads=[dkey, ("pb", pbk)] +
                            [("lbf", c, tq) for c in range(C)],
                      writes=[("pb", pbk)] + ([fkey] if tq == NW - 1 else []),
                      name=f"mm1{l}_{n}_{tq}")
            pa = tq % 2
            pg = 2 + tq % 2
            def gact(e, j=j, pg=pg):
                return e.activation(out=SQB[:, 0, :], in_=bank(pg), func=AF.Gelu,
                                    bias=B1C[:, K2N + j:K2N + j + 1])
            S.add("scalar", gact, reads=[("pb", pg), ("sqb", 0), ("lsm",)],
                  writes=[("sqbd", 0), ("pb", pg)], name=f"ga{l}_{j}_{tq}")
            if j < NK2A:
                mdst = (lambda j=j, tq=tq: MRES[:, j, tq * W512:(tq + 1) * W512])
                mkey, mdkey = ("mres", j, tq), ("mresd", j, tq)
            else:
                mdst = (lambda tq=tq: MRB[:, tq, :])
                mkey, mdkey = ("mrbs", tq), ("mrbsd", tq)
            def mmul(e, j=j, pa=pa, mdst=mdst):
                return e.scalar_tensor_tensor(mdst(), bank(pa),
                                              B1C[:, j:j + 1], SQB[:, 0, :],
                                              ALU.add, ALU.mult)
            S.add("vector", mmul,
                  reads=[("pb", pa), ("sqbd", 0), ("lsm",), mkey],
                  writes=[("pb", pa), ("sqb", 0), mdkey, mkey],
                  name=f"mmul{l}_{j}_{tq}")
            if j >= NK2A:
                def msp(e, j=j, tq=tq, mdst=mdst):
                    return e.dma_start(
                        out=d_m[:, j - NK2A, tq * W512:(tq + 1) * W512],
                        in_=mdst())
                S.add("sync", msp, reads=[mdkey],
                      writes=[("msp", j - NK2A, tq), mkey],
                      name=f"msp{l}_{j}_{tq}")

    WPALL = ([("wp", x) for x in range(3)] + [("wpd", x) for x in range(3)] +
             [("wpa", x) for x in range(2)] + [("wpad", x) for x in range(2)])
    S.virt(writes=WPALL, name=f"retag_w1w2a{l}")
    for n in range(C):
        hh = n % 2
        def ldw2(e, n=n, l=l, hh=hh):
            return e.dma_start(
                out=WPB[:].rearrange(
                    "p s n -> p (s n)")[:, hh * 1536:hh * 1536 + 128 * NK2A],
                in_=d_w2[l, :, n, 0:NK2A, :])
        S.add("gpsimd", ldw2, reads=[("wpa", hh)], writes=[("wpad", hh)],
              name=f"ldw2{l}_{n}")
        for tq in range(NW):
            pbk = (n * NW + tq) % 2
            def mm2(e, n=n, tq=tq, pbk=pbk, hh=hh):
                wflat = WPB[:].rearrange("p s n -> p (s n)")
                for j in range(NK2A):
                    inst = e.matmul(
                        bank(pbk),
                        wflat[:, hh * 1536 + 128 * j:hh * 1536 + 128 * (j + 1)],
                        MRES[:, j, tq * W512:(tq + 1) * W512],
                        start=(j == 0), stop=(j == NK2A - 1))
                return inst
            S.add("tensor", mm2,
                  reads=[("wpad", hh), ("pb", pbk)] +
                        [("mresd", j, tq) for j in range(NK2A)],
                  writes=[("pb", pbk)] +
                         ([("wpa", hh)] if tq == NW - 1 else []),
                  name=f"mm2{l}_{n}_{tq}")
            def y2a(e, n=n, tq=tq, pbk=pbk):
                return e.scalar_tensor_tensor(
                    XRc(n, tq * W512, W512), bank(pbk),
                    GBC[:, 2 * C + n:2 * C + n + 1],
                    XRc(n, tq * W512, W512), ALU.add, ALU.add)
            S.add("vector", y2a, reads=[("pb", pbk), ("xr", tq), ("lsm",)],
                  writes=[("xr", tq), ("pb", pbk), ("w2p", n, tq)],
                  name=f"y2a{l}_{n}_{tq}")
    S.virt(writes=WPALL, name=f"retag_w2ab{l}")
    for npair in range(C // 2):
        n0 = 2 * npair
        def ldw2b(e, n0=n0, l=l):
            return e.dma_start(
                out=WPB[:].rearrange("p s n -> p (s n)")[:, 0:128 * 2 * NK2B],
                in_=d_w2[l, :, n0:n0 + 2, NK2A:K2N, :])
        S.add("gpsimd", ldw2b, reads=[("wpd", 0), ("wpd", 1), ("wpd", 2)],
              writes=[("wpd", 0), ("wpd", 1), ("wpd", 2)],
              name=f"ldw2B{l}_{npair}")
        for tqp in range(2):
            pp = npair * 2 + tqp
            bks = [(pp * 4 + i) % 7 for i in range(4)]
            for j in range(NK2A, K2N):
                sl = (j - NK2A) % 2
                def mrl(e, j=j, sl=sl, tqp=tqp):
                    dst = MRB[:, 4 + 2 * sl:6 + 2 * sl, :].rearrange(
                        "p a b -> p (a b)")
                    return e.dma_start(
                        out=dst,
                        in_=d_m[:, j - NK2A, tqp * 1024:(tqp + 1) * 1024])
                S.add("sync", mrl,
                      reads=[("msp", j - NK2A, 2 * tqp),
                             ("msp", j - NK2A, 2 * tqp + 1)],
                      writes=[("mrl", sl)], name=f"mrl{l}_{pp}_{j}")
                def mm2b(e, j=j, sl=sl, bks=bks):
                    wflat = WPB[:].rearrange("p s n -> p (s n)")
                    ring = MRB[:, 4 + 2 * sl:6 + 2 * sl, :].rearrange(
                        "p a b -> p (a b)")
                    inst = None
                    for dn in range(2):
                        wcol = 128 * (dn * NK2B + (j - NK2A))
                        for dt in range(2):
                            inst = e.matmul(
                                bank(bks[dn * 2 + dt]),
                                wflat[:, wcol:wcol + 128],
                                ring[:, dt * W512:(dt + 1) * W512],
                                start=(j == NK2A), stop=(j == K2N - 1))
                    return inst
                S.add("tensor", mm2b,
                      reads=[("wpd", 0), ("wpd", 1), ("wpd", 2),
                             ("mrl", sl)] + [("pb", b) for b in bks],
                      writes=[("pb", b) for b in bks] + [("mrl", sl)],
                      name=f"mm2b{l}_{pp}_{j}")
            for dn in range(2):
                n = n0 + dn
                for dt in range(2):
                    tq = tqp * 2 + dt
                    b = bks[dn * 2 + dt]
                    def y2c(e, n=n, tq=tq, b=b):
                        return e.tensor_tensor(XRc(n, tq * W512, W512),
                                               XRc(n, tq * W512, W512),
                                               bank(b), ALU.add)
                    S.add("vector", y2c,
                          reads=[("pb", b), ("xr", tq), ("w2p", n, tq)],
                          writes=[("xr", tq), ("pb", b), ("w2d", n, tq)],
                          name=f"y2c{l}_{n}_{tq}")
                    if l == 0:
                        def ymir3(e, n=n, tq=tq):
                            return e.copy(out=ABFc(n, tq * W512 + 4, W512),
                                          in_=XRc(n, tq * W512, W512))
                        S.add("scalar", ymir3,
                              reads=[("w2d", n, tq), ("xr", tq)],
                              writes=[YK(n, tq)], name=f"ym3{l}_{n}_{tq}")

    if tap(f"y{l + 1}"):
        tapdram(f"y{l + 1}", [128, C * T])
        S.add("gpsimd", lambda e: e.dma_start(out=dbg[f"y{l + 1}"], in_=XRt[:]),
              reads=[("w2d", n, tq) for n in range(C) for tq in range(NW)],
              name=f"tap_y{l + 1}")


def kernel(**inputs):
    cst = _constants()
    w = _prep_weights(inputs)
    x = np.asarray(inputs["L0_attn_query"], np.float32)
    nc, _ = build_program()
    base = {}
    for k in WEIGHT_NAMES:
        base[k] = w[k]
    for k in CONST_NAMES:
        base[k] = np.asarray(cst[k])
    in_maps = []
    for b in range(8):
        m = dict(base)
        m["x"] = np.ascontiguousarray(x[b])
        in_maps.append(m)
    res = run_bass_kernel_spmd(nc, in_maps, core_ids=list(range(8)))
    return np.stack([res.results[b]["out"] for b in range(8)], 0)


# version 07
# speedup vs baseline: 1.4953x; 1.0185x over previous
import numpy as np
import ml_dtypes
from contextlib import ExitStack

import concourse.bass as bass
import concourse.mybir as mybir
from concourse.bass_utils import run_bass_kernel_spmd

F32 = mybir.dt.float32
BF16 = mybir.dt.bfloat16
AF = mybir.ActivationFunctionType
ALU = mybir.AluOpType
AXX = mybir.AxisListType.X
f32 = np.float32

T = 2048
D = 1024
C = 8
NW = 4
W512 = 512
K9 = 9
NH = 8
QKN = 12
HID = 2730
HIDP = 2816
K2N = 22
W1N = 44
NK2A = 11
NK2B = K2N - NK2A
LNEPS = 1e-5

_CONST_CACHE = {}


def _constants():
    if _CONST_CACHE:
        return _CONST_CACHE
    c = {}
    c["ident_bf"] = np.eye(128, dtype=ml_dtypes.bfloat16)
    c["ident_f32"] = np.eye(128, dtype=np.float32)
    R = np.zeros((64, 64), np.float32)
    for d in range(32):
        R[d + 32, d] = -1.0
        R[d, d + 32] = 1.0
    Rb = np.zeros((128, 128), np.float32)
    Rb[0:64, 0:64] = R
    Rb[64:128, 64:128] = R
    c["rotmat"] = Rb.astype(ml_dtypes.bfloat16)
    d = np.arange(64, dtype=np.float64)
    half = (d % 32) * 2.0
    inv_freq = 1.0 / (10000.0 ** (half / 64.0))
    t = np.arange(T, dtype=np.float64)
    freqs = t[None, :] * inv_freq[:, None]
    base = (half + 0.4 * 64.0) / (1.4 * 64.0)
    power = (t - T // 2) / 256.0
    scale = base[:, None] ** power[None, :]
    cs, sn = np.cos(freqs), np.sin(freqs)

    def dup(a):
        return np.vstack([a, a]).astype(np.float32).astype(ml_dtypes.bfloat16)
    c["cosq"] = dup(8.0 * cs * scale)
    c["sinq"] = dup(8.0 * sn * scale)
    c["cosk"] = dup(cs / scale)
    c["sink"] = dup(sn / scale)
    r = np.arange(128)
    c["tri_diag"] = (r[:, None] <= r[None, :]).astype(ml_dtypes.bfloat16)
    c["tri_edge"] = (r[:, None] >= r[None, :]).astype(ml_dtypes.bfloat16)
    hb = np.zeros((128, 128), np.float32)
    hb[0:64, 0:64] = 1.0
    hb[64:128, 64:128] = 1.0
    c["hones"] = hb.astype(ml_dtypes.bfloat16)
    c["lncol_f32"] = np.full((128, 128), 1.0 / D, np.float32)
    c["lncol_bf"] = np.full((128, 128), 1.0 / D, np.float32).astype(ml_dtypes.bfloat16)
    c["ones1_bf"] = np.ones((1, 128), ml_dtypes.bfloat16)
    c["ones1_f32"] = np.ones((128, 128), np.float32)
    c["zeros_bf"] = np.zeros((128, 128), ml_dtypes.bfloat16)
    c["onescol_bf"] = np.ones((128, 1), ml_dtypes.bfloat16)
    s4 = np.zeros((128, 8), np.float32)
    for ct in range(4):
        for hp in range(2):
            for kk in range(K9):
                s4[32 * ct + 9 * hp + kk, 2 * ct + hp] = 1.0
    c["sel4"] = s4.astype(ml_dtypes.bfloat16)
    sz = np.zeros((8, 128), np.float32)
    for ct in range(4):
        for hp in range(2):
            for kk in range(K9):
                sz[2 * ct + hp, 32 * ct + 9 * hp + kk] = 1.0
    c["selzrow"] = sz
    B = np.zeros((K9, 4, 128, 128), np.float32)
    for kk in range(K9):
        for ct in range(4):
            for hp in range(2):
                B[kk, ct, 32 * ct + 9 * hp + kk, 64 * hp:64 * (hp + 1)] = 1.0
    c["bsel"] = np.ascontiguousarray(B.transpose(2, 0, 1, 3)).astype(ml_dtypes.bfloat16)
    _CONST_CACHE.update(c)
    return c


CONST_NAMES = ("ident_bf", "ident_f32", "rotmat", "cosq", "sinq", "cosk", "sink",
               "tri_diag", "tri_edge", "hones", "lncol_f32", "lncol_bf",
               "ones1_bf", "ones1_f32", "onescol_bf", "zeros_bf", "sel4",
               "selzrow", "bsel")
WEIGHT_NAMES = ("cw_stat", "wqkv", "wo", "w1", "w2", "b1a", "b1g", "qscol", "kscol",
                "lng", "lnb", "b2")


def _prep_weights(inp):
    w = {}
    cw = np.asarray(inp["L0_attn_conv_weight"], np.float32)
    Wk = np.zeros((128, C, K9, 18), np.float32)
    for h in range(16):
        cc, hp = h // 2, h % 2
        for kk in range(K9):
            Wk[64 * hp:64 * (hp + 1), cc, kk, 9 * hp:9 * (hp + 1)] = \
                cw[h * 9:(h + 1) * 9, :, kk].T
    w["cw_stat"] = np.ascontiguousarray(Wk.reshape(128, C * K9 * 18))
    wqkv = np.asarray(inp["L0_ffn_wqkv"], np.float32)
    w["wqkv"] = np.ascontiguousarray(
        wqkv.reshape(2, C, 128, QKN, 128).transpose(0, 2, 3, 1, 4))
    wo = np.asarray(inp["L0_ffn_wo"], np.float32)
    w["wo"] = np.ascontiguousarray(
        wo.reshape(2, 4, 128, 8, 128).transpose(0, 2, 3, 1, 4))
    w1 = np.asarray(inp["L0_ffn_w1"], np.float32)
    a, g = w1[:, :, :HID], w1[:, :, HID:]
    ap = np.zeros((2, D, HIDP), np.float32)
    gp = np.zeros((2, D, HIDP), np.float32)
    ap[:, :, :HID] = a
    gp[:, :, :HID] = g
    w1p = np.zeros((2, D, W1N, 128), np.float32)
    w1p[:, :, 0::2, :] = ap.reshape(2, D, K2N, 128)
    w1p[:, :, 1::2, :] = gp.reshape(2, D, K2N, 128)
    w["w1"] = np.ascontiguousarray(
        w1p.reshape(2, C, 128, W1N, 128).transpose(0, 2, 3, 1, 4))
    b1 = np.asarray(inp["L0_ffn_b1"], np.float32)
    b1a = np.zeros((2, HIDP), np.float32)
    b1g = np.zeros((2, HIDP), np.float32)
    b1a[:, :HID] = b1[:, :HID]
    b1g[:, :HID] = b1[:, HID:]
    w["b1a"] = np.ascontiguousarray(b1a.reshape(2, K2N, 128).transpose(0, 2, 1))
    w["b1g"] = np.ascontiguousarray(b1g.reshape(2, K2N, 128).transpose(0, 2, 1))
    w2 = np.asarray(inp["L0_ffn_w2"], np.float32)
    w2p = np.zeros((2, HIDP, D), np.float32)
    w2p[:, :HID, :] = w2
    w["w2"] = np.ascontiguousarray(
        w2p.reshape(2, K2N, 128, 8, 128).transpose(0, 2, 3, 1, 4))
    qs = np.asarray(inp["L0_ffn_q_scale"], np.float32)
    ks = np.asarray(inp["L0_ffn_k_scale"], np.float32)
    w["qscol"] = np.ascontiguousarray(np.concatenate([qs, qs], 1).reshape(2, 128, 1))
    w["kscol"] = np.ascontiguousarray(np.concatenate([ks, ks], 1).reshape(2, 128, 1))
    for nm, key in (("lng", "L0_ffn_ln_g"), ("lnb", "L0_ffn_ln_b"), ("b2", "L0_ffn_b2")):
        v = np.asarray(inp[key], np.float32)
        w[nm] = np.ascontiguousarray(v.reshape(2, C, 128).transpose(0, 2, 1))
    return w


ENGINES = ("tensor", "vector", "scalar", "sync", "gpsimd")
DMA_ENG = ("sync", "gpsimd")


class _Step:
    __slots__ = ("engine", "fn", "name", "deps", "need_inc", "inc_val", "waits",
                 "vdeps")

    def __init__(self, engine, fn, name):
        self.engine = engine
        self.fn = fn
        self.name = name
        self.deps = []
        self.need_inc = False
        self.inc_val = None
        self.waits = None
        self.vdeps = None


SERIAL = False


class Sched:
    def __init__(self):
        self.steps = {e: [] for e in ENGINES}
        self._writer = {}
        self._readers = {}
        self.n = 0
        self._prev = None

    def _collect(self, reads, writes):
        dep = set()
        for r in reads:
            w = self._writer.get(r)
            if w is not None:
                dep.add(w)
        for wk in writes:
            w = self._writer.get(wk)
            if w is not None:
                dep.add(w)
            for r in self._readers.get(wk, ()):
                dep.add(r)
        out = set()
        for d in dep:
            if d.vdeps is not None:
                out.update(d.vdeps)
            else:
                out.add(d)
        return out

    def add(self, engine, fn, reads=(), writes=(), name=""):
        s = _Step(engine, fn, name or f"s{self.n}")
        self.n += 1
        dep = self._collect(reads, writes)
        if SERIAL and self._prev is not None:
            dep.add(self._prev)
        self._prev = s
        for r in reads:
            self._readers.setdefault(r, []).append(s)
        for wk in writes:
            self._writer[wk] = s
            self._readers[wk] = []
        for d in dep:
            if d is s:
                continue
            if d.engine == engine and d.engine not in DMA_ENG:
                continue
            d.need_inc = True
            s.deps.append(d)
        if engine in DMA_ENG:
            s.need_inc = True
        self.steps[engine].append(s)
        return s

    def virt(self, writes=(), reads=(), name="virt"):
        s = _Step(None, None, name)
        s.vdeps = self._collect(reads, writes)
        for r in reads:
            self._readers.setdefault(r, []).append(s)
        for wk in writes:
            self._writer[wk] = s
            self._readers[wk] = []
        return s

    def emit(self, blk, sems):
        for e in ENGINES:
            cnt = 0
            amt = 16 if e in DMA_ENG else 1
            for s in self.steps[e]:
                if s.need_inc:
                    cnt += amt
                    s.inc_val = cnt
        for e in ENGINES:
            hw = {x: 0 for x in ENGINES}
            for s in self.steps[e]:
                ws = []
                for d in sorted(s.deps, key=lambda d: d.inc_val):
                    if d.inc_val > hw[d.engine]:
                        hw[d.engine] = d.inc_val
                        ws.append((d.engine, d.inc_val))
                s.waits = ws
        amt = {e: (16 if e in DMA_ENG else 1) for e in ENGINES}

        def mk(e_name):
            def body(eng):
                for s in self.steps[e_name]:
                    for (de, v) in s.waits:
                        eng.wait_ge(sems[de], v)
                    inst = s.fn(eng)
                    if s.need_inc:
                        assert inst is not None, f"step {s.name} returned None"
                        inst.then_inc(sems[e_name], amt[e_name])
            return body
        blk.tensor(mk("tensor"))
        blk.vector(mk("vector"))
        blk.scalar(mk("scalar"))
        blk.sync(mk("sync"))
        blk.gpsimd(mk("gpsimd"))


def build_program(stage=99, debug_taps=()):
    cst = _constants()
    nc = bass.Bass("TRN2", target_bir_lowering=False)
    taps = set(debug_taps)

    xin = nc.dram_tensor("x", [T, D], F32, kind="ExternalInput").ap()
    out = nc.dram_tensor("out", [T, D], F32, kind="ExternalOutput").ap()
    d_cw = nc.dram_tensor("cw_stat", [128, C * K9 * 18], F32, kind="ExternalInput").ap()
    d_wqkv = nc.dram_tensor("wqkv", [2, 128, QKN, C, 128], F32, kind="ExternalInput").ap()
    d_wo = nc.dram_tensor("wo", [2, 128, 8, 4, 128], F32, kind="ExternalInput").ap()
    d_w1 = nc.dram_tensor("w1", [2, 128, W1N, C, 128], F32, kind="ExternalInput").ap()
    d_w2 = nc.dram_tensor("w2", [2, 128, 8, K2N, 128], F32, kind="ExternalInput").ap()
    d_b1a = nc.dram_tensor("b1a", [2, 128, K2N], F32, kind="ExternalInput").ap()
    d_b1g = nc.dram_tensor("b1g", [2, 128, K2N], F32, kind="ExternalInput").ap()
    d_qs = nc.dram_tensor("qscol", [2, 128, 1], F32, kind="ExternalInput").ap()
    d_ks = nc.dram_tensor("kscol", [2, 128, 1], F32, kind="ExternalInput").ap()
    d_lng = nc.dram_tensor("lng", [2, 128, C], F32, kind="ExternalInput").ap()
    d_lnb = nc.dram_tensor("lnb", [2, 128, C], F32, kind="ExternalInput").ap()
    d_b2 = nc.dram_tensor("b2", [2, 128, C], F32, kind="ExternalInput").ap()
    dc = {}
    for nm in CONST_NAMES:
        a = np.asarray(cst[nm])
        dt = BF16 if a.dtype == ml_dtypes.bfloat16 else F32
        dc[nm] = nc.dram_tensor(nm, list(a.shape), dt, kind="ExternalInput").ap()
    d_x1 = nc.dram_tensor("x1sp", [128, C, T], F32, kind="Internal").ap()
    d_m = nc.dram_tensor("msp", [128, NK2B, T], BF16, kind="Internal").ap()

    dbg = {}

    def tap(name):
        return name in taps

    def tapdram(name, shape, dtype=F32):
        dbg[name] = nc.dram_tensor("dbg_" + name, shape, dtype,
                                   kind="ExternalOutput").ap()
        return dbg[name]

    ctx = ExitStack()
    E = ctx.enter_context

    XR = E(nc.sbuf_tensor("XR", [128, C * T], F32))
    ABF = E(nc.sbuf_tensor("ABF", [128, C * (T + 8)], BF16))
    LBF = E(nc.sbuf_tensor("LBF", [128, C * T], BF16))
    R2 = E(nc.sbuf_tensor("R2", [128, 53376], mybir.dt.uint8))
    r2base = nc.lookup_mloc(R2).addr

    def r2v(name, shape, dtype, off):
        return nc.alloc_sbuf_tensor_at(name, shape, dtype, offset=r2base + off)

    BSEL = r2v("BSEL", [128, K9, 4, 128], BF16, 0)
    CWS = r2v("CWS", [128, C * K9 * 18], BF16, 9216)
    CONVS = r2v("CONVS", [128, 2, 512], BF16, 12096)
    ZR = r2v("ZR", [8, 2 * 512], F32, 14144)
    ACCW = r2v("ACCW", [128, C, W512], F32, 18432)
    STG = r2v("STG", [128, 4, D], F32, 34816)
    COSQ = r2v("COSQ", [128, 2, W512], BF16, 0)
    SINQ = r2v("SINQ", [128, 2, W512], BF16, 2048)
    COSK = r2v("COSK", [128, 2, W512], BF16, 4096)
    SINK = r2v("SINK", [128, 2, W512], BF16, 6144)
    WOB = r2v("WOB", [128, 2, 512], BF16, 8192)
    QW = r2v("QW", [128, 1, 4, W512], BF16, 16384)
    KW = r2v("KW", [128, 2, 4, W512], BF16, 20480)
    VW = r2v("VW", [128, 2, 4, 520], BF16, 28672)
    OFM = r2v("OFM", [128, 4, T], BF16, 36992)
    MRES = r2v("MRES", [128, NK2A, T], BF16, 0)
    MRB = r2v("MRB", [128, 8, T // 4], BF16, 45056)
    X1RR = r2v("X1RR", [128, 2, 512], F32, 0)
    STGO = r2v("STGO", [128, 4, D], F32, 36992)

    PSTG = E(nc.sbuf_tensor("PSTG", [128, 2, 512], BF16))
    MSTG = E(nc.sbuf_tensor("MSTG", [1, 2, W512], BF16))
    STT = E(nc.sbuf_tensor("STT", [128, 3, 512], F32))
    SQB = E(nc.sbuf_tensor("SQB", [128, 2, 512], BF16))
    QKS = SQB[:, 1, :]
    MAXC = E(nc.sbuf_tensor("MAXC", [128, 2], F32))
    IDB = E(nc.sbuf_tensor("IDB", [128, 128], BF16))
    IDF = E(nc.sbuf_tensor("IDF", [128, 128], F32))
    ROTM = E(nc.sbuf_tensor("ROTM", [128, 128], BF16))
    TRID = E(nc.sbuf_tensor("TRID", [128, 128], BF16))
    TRIE = E(nc.sbuf_tensor("TRIE", [128, 128], BF16))
    HON = E(nc.sbuf_tensor("HON", [128, 128], BF16))
    LNC32 = E(nc.sbuf_tensor("LNC32", [128, 128], F32))
    LNCBF = E(nc.sbuf_tensor("LNCBF", [128, 128], BF16))
    ONES1 = E(nc.sbuf_tensor("ONES1", [1, 128], BF16))
    ONE1F = E(nc.sbuf_tensor("ONE1F", [128, 128], F32))
    ONECB = E(nc.sbuf_tensor("ONECB", [128, 1], BF16))
    ZERB = E(nc.sbuf_tensor("ZERB", [128, 128], BF16))
    SEL4 = E(nc.sbuf_tensor("SEL4", [128, 8], BF16))
    SELZ = E(nc.sbuf_tensor("SELZ", [8, 128], F32))
    WQB = E(nc.sbuf_tensor("WQB", [128, 2, C * 128], BF16))
    WPB = E(nc.sbuf_tensor("WPB", [128, 3, C * 128], BF16))
    QSC = E(nc.sbuf_tensor("QSC", [128, 2], F32))
    GBC = E(nc.sbuf_tensor("GBC", [128, 3 * C], F32))
    B1C = E(nc.sbuf_tensor("B1C", [128, 2 * K2N], F32))

    PB = E(nc.psum_tensor("PB", [128, 3584], F32))
    PBF = E(nc.psum_tensor("PBF", [128, 1024], BF16))

    def bank(i, col0=0, ncol=512):
        return PB[:, 512 * i + col0: 512 * i + col0 + ncol]

    sems = {e: E(nc.semaphore("sem_" + e)) for e in ENGINES}
    blk = E(nc.Block())
    S = Sched()

    def XRc(c, lo, n):
        return XR[:, c * T + lo: c * T + lo + n]

    def ABFc(c, lo, n):
        return ABF[:, c * (T + 8) + lo: c * (T + 8) + lo + n]

    def LBFc(c, lo, n):
        return LBF[:, c * T + lo: c * T + lo + n]

    _consts = ((IDB[:], dc["ident_bf"]), (IDF[:], dc["ident_f32"]),
               (ROTM[:], dc["rotmat"]), (TRID[:], dc["tri_diag"]),
               (TRIE[:], dc["tri_edge"]), (HON[:], dc["hones"]),
               (LNC32[:], dc["lncol_f32"]), (LNCBF[:], dc["lncol_bf"]),
               (ONES1[:], dc["ones1_bf"]), (ONE1F[:], dc["ones1_f32"]),
               (ONECB[:], dc["onescol_bf"]), (ZERB[:], dc["zeros_bf"]),
               (SEL4[:], dc["sel4"]), (SELZ[:], dc["selzrow"]))
    for i, (dst, src) in enumerate(_consts):
        S.add("sync", (lambda e, dst=dst, src=src: e.dma_start(out=dst, in_=src)),
              writes=["const"] if i == len(_consts) - 1 else [],
              name=f"ld_const{i}")
    S.add("gpsimd", lambda e: e.dma_start(out=BSEL[:], in_=dc["bsel"]),
          writes=[], name="ld_bsel")
    S.add("gpsimd", lambda e: e.dma_start(out=CWS[:], in_=d_cw),
          writes=["convw"], name="ld_convw")

    def z_guards(e):
        for c in range(C):
            e.memset(ABF[:, c * (T + 8):c * (T + 8) + 4], 0)
            inst = e.memset(ABF[:, c * (T + 8) + T + 4:c * (T + 8) + T + 8], 0)
        return inst
    S.add("vector", z_guards, writes=["guards"], name="zguards")

    xv = xin.rearrange("(tt p) f -> p tt f", p=128)
    for qtr in range(4):
        def ld_q(e, qtr=qtr):
            return e.dma_start(out=STG[:], in_=xv[:, 4 * qtr:4 * (qtr + 1), :])
        S.add("sync", ld_q, writes=[("stg",)], name=f"ldx{qtr}")
        for tt4 in range(4):
            tt = 4 * qtr + tt4
            for cq in range(2):
                bk = (tt4 * 2 + cq) % 2
                def tp(e, tt4=tt4, cq=cq, bk=bk):
                    for c4 in range(4):
                        c = cq * 4 + c4
                        inst = e.transpose(
                            PB[:, 512 * bk + 128 * c4:512 * bk + 128 * (c4 + 1)],
                            STG[:, tt4, 128 * c:128 * (c + 1)], IDF[:])
                    return inst
                S.add("tensor", tp, reads=[("stg",), "const", ("pb", bk)],
                      writes=[("pb", bk)], name=f"tpx{tt}_{cq}")
                def ev(e, tt=tt, cq=cq, bk=bk):
                    for c4 in range(4):
                        c = cq * 4 + c4
                        inst = e.tensor_copy(
                            XRc(c, tt * 128, 128),
                            PB[:, 512 * bk + 128 * c4:512 * bk + 128 * (c4 + 1)])
                    return inst
                S.add("vector", ev, reads=[("pb", bk), ("stg",)],
                      writes=[("xr", tt // 4), ("pb", bk)], name=f"evx{tt}_{cq}")

    if tap("xr"):
        tapdram("xr", [128, C * T])
        S.add("gpsimd", lambda e: e.dma_start(out=dbg["xr"], in_=XR[:]),
              reads=[("xr", i) for i in range(4)], name="tap_xr")

    def ln_stats(tag, rhs, reads, bf=False):
        def mm_mean(e):
            for c in range(C):
                inst = e.matmul(bank(5), LNCBF[:] if bf else LNC32[:], rhs(c),
                                start=(c == 0), stop=(c == C - 1))
            return inst
        S.add("tensor", mm_mean, reads=list(reads) + ["const", ("pb", 5)],
              writes=[("pb", 5)], name=f"lnm_{tag}")
        for c in range(C):
            sl = c % 2
            def sq1(e, c=c, sl=sl):
                return e.activation(out=SQB[:, sl, :], in_=rhs(c), func=AF.Square)
            S.add("scalar", sq1, reads=list(reads) + [("sqb", sl)],
                  writes=[("sqbd", sl)], name=f"lnsq_{tag}_{c}")
            def mmsq(e, c=c, sl=sl):
                return e.matmul(bank(6), LNCBF[:], SQB[:, sl, :],
                                start=(c == 0), stop=(c == C - 1))
            S.add("tensor", mmsq, reads=[("sqbd", sl), "const", ("pb", 6)],
                  writes=[("pb", 6), ("sqb", sl)], name=f"lnsqm_{tag}_{c}")

        def stats(e):
            e.tensor_copy(STT[:, 2, :], bank(5))
            e.tensor_tensor(STT[:, 1, :], STT[:, 2, :], STT[:, 2, :], ALU.mult)
            return e.scalar_tensor_tensor(STT[:, 1, :], STT[:, 1, :], -1.0,
                                          bank(6), ALU.mult, ALU.add)
        S.add("vector", stats,
              reads=[("pb", 5), ("pb", 6), ("stt", 2), ("stt", 1)],
              writes=[("stt", 2), ("stt", 1), ("pb", 6), ("pb", 5)],
              name=f"lnv_{tag}")
        def lnr(e):
            e.activation(out=STT[:, 1, :], in_=STT[:, 1, :], func=AF.Ln,
                         bias=f32(LNEPS))
            return e.activation(out=STT[:, 0, :], in_=STT[:, 1, :],
                                func=AF.Exp, scale=f32(-0.5))
        S.add("scalar", lnr, reads=[("stt", 1), ("stt", 0)],
              writes=[("stt", 0), ("stt", 1)], name=f"lnr_{tag}")

        def negm(e):
            return e.scalar_tensor_tensor(STT[:, 1, :], STT[:, 2, :], -1.0,
                                          STT[:, 0, :], ALU.mult, ALU.mult)
        S.add("vector", negm, reads=[("stt", 0), ("stt", 2), ("stt", 1)],
              writes=[("stt", 1)], name=f"lnn_{tag}")

    for w in range(NW):
        ln_stats(f"l1w{w}", lambda c, w=w: XRc(c, w * W512, W512), [("xr", w)])
        for c in range(C):
            def apa(e, c=c, w=w):
                return e.tensor_tensor(STT[:, 2, :], XRc(c, w * W512, W512),
                                       STT[:, 0, :], ALU.mult)
            S.add("vector", apa, reads=[("stt", 0), ("xr", w), ("stt", 2)],
                  writes=[("stt", 2)], name=f"l1a{w}_{c}")
            def apb(e, c=c, w=w):
                return e.tensor_tensor(ABFc(c, w * W512 + 4, W512),
                                       STT[:, 2, :], STT[:, 1, :], ALU.add)
            S.add("vector", apb, reads=[("stt", 2), ("stt", 1), "guards"],
                  writes=[("ln1", c, w)], name=f"l1b{w}_{c}")

    if tap("ln1"):
        tapdram("ln1", [128, C * (T + 8)])
        S.add("gpsimd", lambda e: e.dma_start(out=dbg["ln1"], in_=ABF[:]),
              reads=[("ln1", c, w) for c in range(C) for w in range(NW)],
              name="tap_ln1")

    if stage >= 2:
        for w in range(NW):
            for half in range(2):
                def mml(e, w=w, half=half):
                    for ct in range(4):
                        c = half * 4 + ct
                        tp = (0, 32 * ct) if ct > 0 else None
                        for k in range(K9):
                            inst = e.matmul(
                                PB[32 * ct:32 * ct + 18,
                                   512 * half:512 * half + 512],
                                CWS[:, (c * K9 + k) * 18:(c * K9 + k) * 18 + 18],
                                ABFc(c, w * W512 + k, W512),
                                start=(k == 0), stop=(k == K9 - 1),
                                tile_position=tp)
                    return inst
                S.add("tensor", mml,
                      reads=[("ln1", half * 4 + ct, w) for ct in range(4)] +
                            ["convw", ("pb", half)],
                      writes=[("pb", half)], name=f"cvl{w}_{half}")
                def clamp(e, half=half):
                    return e.tensor_scalar_min(bank(half), bank(half), 60.0)
                S.add("vector", clamp, reads=[("pb", half)], writes=[("pb", half)],
                      name=f"cvc{w}_{half}")
                def expE(e, half=half):
                    return e.activation(out=CONVS[:, half, :], in_=bank(half),
                                        func=AF.Exp)
                S.add("scalar", expE, reads=[("pb", half), ("cv_e", half)],
                      writes=[("cv_e", half), ("pb", half)], name=f"cve{w}_{half}")
                def mmz(e, half=half):
                    return e.matmul(PB[0:8, 1024:1536], SEL4[:], CONVS[:, half, :],
                                    start=True, stop=True)
                S.add("tensor", mmz, reads=[("cv_e", half), "const", ("pb", 2)],
                      writes=[("pb", 2)], name=f"cvz{w}_{half}")
                def zinv(e, half=half):
                    e.activation(out=ZR[:, half * 512:(half + 1) * 512],
                                 in_=PB[0:8, 1024:1536], func=AF.Ln)
                    return e.activation(out=ZR[:, half * 512:(half + 1) * 512],
                                        in_=ZR[:, half * 512:(half + 1) * 512],
                                        func=AF.Exp, scale=f32(-1.0))
                S.add("scalar", zinv, reads=[("pb", 2), ("cv_zi", half)],
                      writes=[("cv_zi", half), ("pb", 2)],
                      name=f"cvzi{w}_{half}")
                def mmzr(e, half=half):
                    return e.matmul(bank(3), SELZ[:],
                                    ZR[:, half * 512:(half + 1) * 512],
                                    start=True, stop=True)
                S.add("tensor", mmzr, reads=[("cv_zi", half), "const", ("pb", 3)],
                      writes=[("pb", 3)], name=f"cvzr{w}_{half}")
                def wnorm(e, half=half):
                    return e.tensor_tensor(CONVS[:, half, :], CONVS[:, half, :],
                                           bank(3), ALU.mult)
                S.add("vector", wnorm, reads=[("cv_e", half), ("pb", 3)],
                      writes=[("cv_w", half), ("cv_e", half), ("pb", 3)],
                      name=f"cvw{w}_{half}")
            for c in range(C):
                half, ct = c // 4, c % 4
                for k in range(K9):
                    pb = 3 + (k % 2)
                    def mmb(e, k=k, ct=ct, half=half, pb=pb):
                        return e.matmul(bank(pb), BSEL[:, k, ct, :],
                                        CONVS[:, half, :], start=True, stop=True)
                    S.add("tensor", mmb,
                          reads=[("cv_w", half), "convw", ("pb", pb)],
                          writes=[("pb", pb)], name=f"cvb{w}_{c}_{k}")
                    def mix(e, k=k, c=c, w=w, pb=pb):
                        if k == 0:
                            return e.tensor_tensor(ACCW[:, c, :],
                                                   ABFc(c, w * W512 + k, W512),
                                                   bank(pb), ALU.mult)
                        e.tensor_tensor(STT[:, 2, :], ABFc(c, w * W512 + k, W512),
                                        bank(pb), ALU.mult)
                        return e.tensor_tensor(ACCW[:, c, :], ACCW[:, c, :],
                                               STT[:, 2, :], ALU.add)
                    S.add("vector", mix,
                          reads=[("pb", pb), ("ln1", c, w), ("acc", c), ("stt", 2)],
                          writes=[("acc", c), ("pb", pb), ("stt", 2)],
                          name=f"cvm{w}_{c}_{k}")
            if tap("conv") and w == 0:
                tapdram("conv", [128, C, W512])
                S.add("gpsimd", lambda e: e.dma_start(out=dbg["conv"], in_=ACCW[:]),
                      reads=[("acc", c) for c in range(C)], name="tap_conv")
            ln_stats(f"lcw{w}", lambda c: ACCW[:, c, :],
                     [("acc", c) for c in range(C)])
            for c in range(C):
                def xa(e, c=c):
                    return e.tensor_tensor(STT[:, 2, :], ACCW[:, c, :],
                                           STT[:, 0, :], ALU.mult)
                S.add("vector", xa, reads=[("stt", 0), ("acc", c), ("stt", 2)],
                      writes=[("stt", 2)], name=f"x1a{w}_{c}")
                def xb(e, c=c):
                    return e.tensor_tensor(STT[:, 2, :], STT[:, 2, :],
                                           STT[:, 1, :], ALU.add)
                S.add("vector", xb, reads=[("stt", 1)], writes=[("stt", 2)],
                      name=f"x1b{w}_{c}")
                def xc(e, c=c, w=w):
                    return e.tensor_tensor(XRc(c, w * W512, W512),
                                           XRc(c, w * W512, W512),
                                           STT[:, 2, :], ALU.add)
                S.add("vector", xc, reads=[("stt", 2), ("xr", w)],
                      writes=[("xr", w), ("acc", c)], name=f"x1c{w}_{c}")

        if tap("x1"):
            tapdram("x1", [128, C * T])
            S.add("gpsimd", lambda e: e.dma_start(out=dbg["x1"], in_=XR[:]),
                  reads=[("xr", i) for i in range(4)], name="tap_x1")

    if stage >= 3:
        def spill(e):
            return e.dma_start(out=d_x1, in_=XR[:].rearrange("p (c t) -> p c t", t=T))
        S.add("sync", spill, reads=[("xr", i) for i in range(4)], writes=["x1sp"],
              name="x1spill")
        def emit_ymir(w):
            for c in range(C):
                def ymir(e, c=c, w=w):
                    return e.copy(out=ABFc(c, w * W512 + 4, W512),
                                  in_=XRc(c, w * W512, W512))
                S.add("scalar", ymir,
                      reads=[("xr", w)],
                      writes=[("ybf", c, w), ("ln1", c, w)], name=f"ymir{w}_{c}")
        for w in range(NW):
            ln_stats(f"l2w{w}", lambda c, w=w: XRc(c, w * W512, W512), [("xr", w)])
            if w > 0:
                emit_ymir(w - 1)
            for c in range(C):
                def ya(e, c=c, w=w):
                    return e.tensor_tensor(XRc(c, w * W512, W512),
                                           XRc(c, w * W512, W512),
                                           STT[:, 0, :], ALU.mult)
                S.add("vector", ya, reads=[("stt", 0), "x1sp"],
                      writes=[("xr", w)], name=f"y2a{w}_{c}")
                def yb(e, c=c, w=w):
                    return e.tensor_tensor(XRc(c, w * W512, W512),
                                           XRc(c, w * W512, W512),
                                           STT[:, 1, :], ALU.add)
                S.add("vector", yb, reads=[("stt", 1)], writes=[("xr", w)],
                      name=f"y2b{w}_{c}")
        emit_ymir(NW - 1)

        if tap("y0"):
            tapdram("y0", [128, C * T])
            S.add("gpsimd", lambda e: e.dma_start(out=dbg["y0"], in_=XR[:]),
                  reads=[("xr", i) for i in range(4)], name="tap_y0")

    QWK = [("qw", c2) for c2 in range(4)]
    KWK = [("kw", s, c2) for s in range(2) for c2 in range(4)]
    VWK = [("vw", s, v) for s in range(2) for v in range(4)] + \
          [("vw1", s) for s in range(2)]
    OFK = [("ofm", c2, w, hp) for c2 in range(4) for w in range(NW)
           for hp in range(2)]
    MRK = ([("mres", j, tq) for j in range(NK2A) for tq in range(NW)] +
           [("mresd", j, tq) for j in range(NK2A) for tq in range(NW)] +
           [("mrbs", x) for x in range(NW)] +
           [("mrbsd", x) for x in range(NW)] +
           [("mrl", x) for x in range(2)] +
           [("wob", x) for x in range(2)] + [("wobd", x) for x in range(2)] +
           [("rotw", p, ti) for p in range(2) for ti in range(4)])

    if stage >= 4:
        nlayers = 2 if stage >= 5 else 1
        for l in range(nlayers):
            if l == 0:
                S.virt(writes=(["rot", ("stg",), "convw",
                                ("cv_w", 0), ("cv_w", 1), ("cv_e", 0), ("cv_e", 1),
                                ("cv_zi", 0), ("cv_zi", 1)] +
                               [("acc", c) for c in range(C)] +
                               QWK + KWK + VWK + OFK), name="retag_conv2attn")
            else:
                S.virt(writes=["rot"] + QWK + KWK + VWK + OFK + MRK,
                       name=f"retag_ffn2attn{l}")
            _ffn_layer(S, locals(), l)

    if stage >= 9:
        S.virt(writes=[("x1r", 0), ("x1r", 1), ("stgo2",)] + OFK + MRK,
               name="retag_exit")
        outv = out.rearrange("(tt p) f -> p tt f", p=128)
        for w in range(NW):
            ln_stats(f"lfw{w}", lambda c, w=w: XRc(c, w * W512, W512), [("xr", w)])
            for c in range(C):
                def fa(e, c=c, w=w):
                    return e.tensor_tensor(STT[:, 2, :], XRc(c, w * W512, W512),
                                           STT[:, 0, :], ALU.mult)
                S.add("vector", fa, reads=[("stt", 0), ("xr", w), ("stt", 2)],
                      writes=[("stt", 2)], name=f"fna{w}_{c}")
                def fb(e, c=c, w=w):
                    return e.tensor_tensor(STT[:, 2, :], STT[:, 2, :],
                                           STT[:, 1, :], ALU.add)
                S.add("vector", fb, reads=[("stt", 1)], writes=[("stt", 2)],
                      name=f"fnb{w}_{c}")
                sl = c % 2
                def ldx1(e, c=c, w=w, sl=sl):
                    return e.dma_start(out=X1RR[:, sl, :],
                                       in_=d_x1[:, c, w * W512:(w + 1) * W512])
                S.add("sync", ldx1, reads=["x1sp", ("x1r", sl)],
                      writes=[("x1rd", sl)], name=f"ldx1_{w}_{c}")
                def fc(e, c=c, sl=sl):
                    return e.tensor_tensor(STT[:, 2, :], STT[:, 2, :],
                                           X1RR[:, sl, :], ALU.add)
                S.add("vector", fc, reads=[("x1rd", sl)],
                      writes=[("stt", 2), ("x1r", sl)], name=f"fnc{w}_{c}")
                bk = c % 2
                def ftp(e, c=c, bk=bk):
                    for q in range(4):
                        inst = e.transpose(
                            PB[:, 512 * bk + 128 * q:512 * bk + 128 * (q + 1)],
                            STT[:, 2, 128 * q:128 * (q + 1)], IDF[:])
                    return inst
                S.add("tensor", ftp, reads=[("stt", 2), "const", ("pb", bk)],
                      writes=[("pb", bk)], name=f"ftp{w}_{c}")
                def fev(e, c=c, bk=bk):
                    for q in range(4):
                        inst = e.tensor_copy(
                            STGO[:, q, 128 * c:128 * (c + 1)],
                            PB[:, 512 * bk + 128 * q:512 * bk + 128 * (q + 1)])
                    return inst
                S.add("vector", fev, reads=[("pb", bk), ("stgo2",)],
                      writes=[("pb", bk), ("stgo", c)], name=f"fev{w}_{c}")
            def sto(e, w=w):
                return e.dma_start(out=outv[:, 4 * w:4 * (w + 1), :], in_=STGO[:])
            S.add("sync", sto, reads=[("stgo", c) for c in range(C)],
                  writes=["outd", ("stgo2",)], name=f"sto{w}")
        S.add("sync", lambda e: e.nop(), reads=["outd"], name="outwait")

    S.emit(blk, sems)
    ctx.close()
    return nc, dbg


def _ffn_layer(S, L, l):
    nc = L["nc"]
    PB, PBF = L["PB"], L["PBF"]
    bank = L["bank"]
    XRc, ABFc, LBFc = L["XRc"], L["ABFc"], L["LBFc"]
    STT, SQB, QKS, MAXC = L["STT"], L["SQB"], L["QKS"], L["MAXC"]
    WPB = L["WPB"]
    QW, KW, VW, OFM = L["QW"], L["KW"], L["VW"], L["OFM"]
    COSQ, SINQ, COSK, SINK = L["COSQ"], L["SINQ"], L["COSK"], L["SINK"]
    MRES, MRB = L["MRES"], L["MRB"]
    PSTG, MSTG = L["PSTG"], L["MSTG"]
    WQB, WOB = L["WQB"], L["WOB"]
    QSC, GBC, B1C = L["QSC"], L["GBC"], L["B1C"]
    HON, ROTM, IDB, IDF = L["HON"], L["ROTM"], L["IDB"], L["IDF"]
    TRID, TRIE = L["TRID"], L["TRIE"]
    ONES1, ONE1F, ONECB = L["ONES1"], L["ONE1F"], L["ONECB"]
    ZERB = L["ZERB"]
    d_wqkv, d_wo, d_w1, d_w2 = L["d_wqkv"], L["d_wo"], L["d_w1"], L["d_w2"]
    d_qs, d_ks = L["d_qs"], L["d_ks"]
    d_lng, d_lnb, d_b2 = L["d_lng"], L["d_lnb"], L["d_b2"]
    d_b1a, d_b1g, d_m = L["d_b1a"], L["d_b1g"], L["d_m"]
    MRB = L["MRB"]
    dc = L["dc"]
    dbg, taps, tapdram = L["dbg"], L["taps"], L["tapdram"]
    ln_stats = L["ln_stats"]
    XRt = L["XR"]

    def tap(name):
        return name in taps

    YK = lambda c, w: ("ybf", c, w)

    for i, (dst, src) in enumerate((
            (QSC[:, 0:1], d_qs[l]), (QSC[:, 1:2], d_ks[l]),
            (GBC[:, 0:C], d_lng[l]), (GBC[:, C:2 * C], d_lnb[l]),
            (GBC[:, 2 * C:3 * C], d_b2[l]),
            (B1C[:, 0:K2N], d_b1a[l]), (B1C[:, K2N:2 * K2N], d_b1g[l]))):
        S.add("sync", (lambda e, dst=dst, src=src: e.dma_start(out=dst, in_=src)),
              writes=[("lsm",)], name=f"ldsm{l}_{i}")

    def emit_ld_rot(w, l=l):
        wsl = w % 2
        for ti, (dst, src) in enumerate(
                ((COSQ, dc["cosq"]), (SINQ, dc["sinq"]),
                 (COSK, dc["cosk"]), (SINK, dc["sink"]))):
            def ldr(e, dst=dst, src=src, w=w, wsl=wsl):
                return e.dma_start(out=dst[:, wsl, :],
                                   in_=src[:, w * W512:(w + 1) * W512])
            S.add("sync", ldr, reads=["rot"], writes=[("rotw", wsl, ti)],
                  name=f"ld_rot{l}_{w}_{ti}")

    def emit_wo(w, l=l):
        for n in range(C):
            wsl2 = n % 2
            def ldwo(e, n=n, l=l, wsl2=wsl2):
                return e.dma_start(out=WOB[:, wsl2, :], in_=d_wo[l, :, n, :, :])
            S.add("gpsimd", ldwo, reads=[("wob", wsl2), "rot"],
                  writes=[("wobd", wsl2)], name=f"ldwo{l}_{w}_{n}")
            pbk = n % 2
            def mmo(e, n=n, w=w, pbk=pbk, wsl2=wsl2):
                for c2 in range(4):
                    inst = e.matmul(bank(pbk),
                                    WOB[:, wsl2, 128 * c2:128 * (c2 + 1)],
                                    OFM[:, c2, w * W512:(w + 1) * W512],
                                    start=(c2 == 0), stop=(c2 == 3))
                return inst
            S.add("tensor", mmo,
                  reads=[("wobd", wsl2), ("pb", pbk)] +
                        [("ofm", c2, w, hp) for c2 in range(4) for hp in range(2)],
                  writes=[("pb", pbk), ("wob", wsl2)],
                  name=f"mmo{l}_{n}_{w}")
            def yadd(e, n=n, w=w, pbk=pbk):
                return e.tensor_tensor(XRc(n, w * W512, W512),
                                       XRc(n, w * W512, W512), bank(pbk), ALU.add)
            S.add("vector", yadd, reads=[("pb", pbk), ("xr", w)],
                  writes=[("xr", w), ("pb", pbk), ("wo_done", n, w)],
                  name=f"yadd{l}_{n}_{w}")
            def ymir2(e, n=n, w=w):
                return e.copy(out=ABFc(n, w * W512 + 4, W512),
                              in_=XRc(n, w * W512, W512))
            S.add("scalar", ymir2, reads=[("wo_done", n, w), ("xr", w)],
                  writes=[YK(n, w)], name=f"ym2{l}_{n}_{w}")

    def emit_il(w, l=l):
        def rhs_bf(c, w=w):
            off = w * W512 + (3 if c >= 4 else 4)
            return ABFc(c, off, W512)
        rd = [YK(c, w) for c in range(C)]
        if w > 0:
            rd += [YK(c, w - 1) for c in range(4, C)]
        ln_stats(f"li{l}w{w}", rhs_bf, rd, bf=True)
        for c in range(C):
            def ia(e, c=c, w=w):
                off = w * W512 + (3 if c >= 4 else 4)
                return e.tensor_tensor(STT[:, 2, :], ABFc(c, off, W512),
                                       STT[:, 0, :], ALU.mult)
            S.add("vector", ia, reads=[("stt", 0), YK(c, w), ("stt", 2)] +
                  ([YK(c, w - 1)] if (w > 0 and c >= 4) else []),
                  writes=[("stt", 2)], name=f"ila{l}_{w}_{c}")
            def ib(e):
                return e.tensor_tensor(STT[:, 2, :], STT[:, 2, :],
                                       STT[:, 1, :], ALU.add)
            S.add("vector", ib, reads=[("stt", 1)], writes=[("stt", 2)],
                  name=f"ilb{l}_{w}_{c}")
            def ic(e, c=c, w=w):
                return e.activation(out=LBFc(c, w * W512, W512),
                                    in_=STT[:, 2, :], func=AF.Identity,
                                    scale=GBC[:, c:c + 1],
                                    bias=GBC[:, C + c:C + c + 1])
            S.add("scalar", ic, reads=[("stt", 2), ("lsm",)],
                  writes=[("lbf", c, w)], name=f"ilc{l}_{w}_{c}")

    for w in range(NW):
        emit_ld_rot(w)
        def emit_ld_mmq(n, w=w, l=l):
            if n >= QKN:
                return
            wsl = n % 2
            pbk = n % 2
            def ldwq(e, n=n, l=l, wsl=wsl):
                return e.dma_start(out=WQB[:, wsl, :], in_=d_wqkv[l, :, n, :, :])
            S.add("gpsimd", ldwq, reads=[("wqb", wsl)], writes=[("wqbd", wsl)],
                  name=f"ldwq{l}_{w}_{n}")
            yreads = [YK(c, w) for c in range(C)]
            if w > 0:
                yreads += [YK(c, w - 1) for c in range(4, C)]
            def mmq(e, n=n, w=w, pbk=pbk, wsl=wsl):
                for c in range(C):
                    off = w * W512 + (3 if c >= 4 else 4)
                    inst = e.matmul(bank(pbk), WQB[:, wsl, 128 * c:128 * (c + 1)],
                                    ABFc(c, off, W512),
                                    start=(c == 0), stop=(c == C - 1))
                return inst
            S.add("tensor", mmq, reads=[("wqbd", wsl), ("pb", pbk)] + yreads,
                  writes=[("pb", pbk), ("wqb", wsl)], name=f"mmqkv{l}_{w}_{n}")

        emit_ld_mmq(0)
        emit_ld_mmq(1)
        for n in range(QKN):
            pbk = n % 2
            if n < 8:
                isq = n < 4
                c2 = n % 4
                tbl_c, tbl_s = (COSQ, SINQ) if isq else (COSK, SINK)
                tic = 0 if isq else 2
                scol = QSC[:, 0:1] if isq else QSC[:, 1:2]
                def sq(e, pbk=pbk):
                    return e.activation(out=SQB[:, 0, :], in_=bank(pbk),
                                        func=AF.Square)
                S.add("scalar", sq, reads=[("pb", pbk), ("sqb", 0)],
                      writes=[("sqbd", 0)], name=f"qksq{l}_{w}_{n}")
                def qbf(e, pbk=pbk, scol=scol):
                    return e.activation(out=QKS, in_=bank(pbk),
                                        func=AF.Copy, scale=scol)
                S.add("scalar", qbf, reads=[("pb", pbk), ("lsm",), ("sqb", 1)],
                      writes=[("sqbd", 1), ("pb", pbk)], name=f"qkb{l}_{w}_{n}")
                def mmn(e):
                    return e.matmul(bank(2), HON[:], SQB[:, 0, :],
                                    start=True, stop=True)
                S.add("tensor", mmn, reads=[("sqbd", 0), "const", ("pb", 2)],
                      writes=[("pb", 2), ("sqb", 0)], name=f"qkn{l}_{w}_{n}")
                def ninv(e):
                    return e.tensor_scalar_max(bank(2), bank(2), 1e-24)
                S.add("vector", ninv, reads=[("pb", 2)],
                      writes=[("pb", 2)], name=f"qkni{l}_{w}_{n}")
                def nsq(e):
                    e.activation(out=STT[:, 0, :], in_=bank(2), func=AF.Ln)
                    return e.activation(out=STT[:, 0, :], in_=STT[:, 0, :],
                                        func=AF.Exp, scale=f32(-0.5))
                S.add("scalar", nsq, reads=[("pb", 2), ("stt", 0)],
                      writes=[("stt", 0), ("pb", 2)], name=f"qkns{l}_{w}_{n}")
                emit_ld_mmq(n + 2)
                def mmrot(e):
                    return e.matmul(bank(3), ROTM[:], QKS,
                                    start=True, stop=True)
                S.add("tensor", mmrot, reads=[("sqbd", 1), "const", ("pb", 3)],
                      writes=[("pb", 3)], name=f"qkr{l}_{w}_{n}")
                def rot1(e, w=w, tbl_c=tbl_c):
                    return e.tensor_tensor(STT[:, 2, :], QKS,
                                           tbl_c[:, w % 2, :], ALU.mult)
                S.add("vector", rot1,
                      reads=[("sqbd", 1), ("rotw", w % 2, tic), ("stt", 2)],
                      writes=[("stt", 2), ("sqb", 1)], name=f"qkr1{l}_{w}_{n}")
                def rot2(e, w=w, tbl_s=tbl_s):
                    e.tensor_tensor(STT[:, 1, :], bank(3),
                                    tbl_s[:, w % 2, :], ALU.mult)
                    return e.tensor_tensor(STT[:, 2, :], STT[:, 2, :],
                                           STT[:, 1, :], ALU.add)
                S.add("vector", rot2,
                      reads=[("pb", 3), ("rotw", w % 2, tic + 1),
                             ("stt", 2), ("stt", 1)],
                      writes=[("stt", 2), ("stt", 1), ("pb", 3)],
                      name=f"qkr2{l}_{w}_{n}")
                if isq:
                    dkey, dst = ("qw", c2), (lambda e, c2=c2: QW[:, 0, c2, :])
                else:
                    dkey, dst = ("kw", w % 2, c2), \
                        (lambda e, w=w, c2=c2: KW[:, w % 2, c2, :])
                def rot3(e, dst=dst):
                    return e.tensor_tensor(dst(e), STT[:, 2, :],
                                           STT[:, 0, :], ALU.mult)
                S.add("vector", rot3, reads=[("stt", 2), ("stt", 0), "rot", dkey],
                      writes=[dkey], name=f"qkr3{l}_{w}_{n}")
            else:
                vc = n - 8
                vsl = w % 2
                def vbf(e, pbk=pbk):
                    return e.activation(out=QKS, in_=bank(pbk), func=AF.Copy)
                S.add("scalar", vbf, reads=[("pb", pbk), ("sqb", 1)],
                      writes=[("sqbd", 1), ("pb", pbk)], name=f"vb{l}_{w}_{vc}")
                emit_ld_mmq(n + 2)
                def vtp(e):
                    for q in range(4):
                        inst = e.transpose(PBF[:, 128 * q:128 * (q + 1)],
                                           SQB[:, 1, 128 * q:128 * (q + 1)], IDB[:])
                    return inst
                S.add("tensor", vtp, reads=[("sqbd", 1), "const", ("pb", 7)],
                      writes=[("pb", 7)], name=f"vt{l}_{w}_{vc}")
                def vev(e, vsl=vsl, vc=vc):
                    for q in range(4):
                        src = PBF[:, 128 * q:128 * (q + 1)].rearrange(
                            "p (hh dd) -> p hh dd", hh=2)
                        inst = e.tensor_copy(
                            VW[:, vsl, q, 65 * 2 * vc:65 * 2 * vc + 130].rearrange(
                                "p (hh dd) -> p hh dd", dd=65)[:, :, 0:64], src)
                    return inst
                S.add("vector", vev,
                      reads=[("pb", 7), "rot", ("vw", vsl, vc), ("sqb", 1)],
                      writes=[("vw", vsl, vc), ("pb", 7), ("sqb", 1)],
                      name=f"vev{l}_{w}_{vc}")
        def vones(e, vsl=w % 2):
            for q in range(4):
                inst = e.memset(
                    VW[:, vsl, q, :].rearrange("p (hh dd) -> p hh dd",
                                               dd=65)[:, :, 64:65], 1.0)
            return inst
        S.add("vector", vones,
              reads=[("vw", w % 2, vc) for vc in range(4)] + ["rot", ("vw1", w % 2)],
              writes=[("vw1", w % 2)], name=f"vones{l}_{w}")

        if w == 0 and l == 0:
            for nm, buf, keys in (
                    ("qw0", QW, [("qw", c2) for c2 in range(4)]),
                    ("kw0", KW, [("kw", 0, c2) for c2 in range(4)]),
                    ("vw0", VW, [("vw", 0, vc) for vc in range(4)] + [("vw1", 0)])):
                if tap(nm):
                    tapdram(nm, [128, buf.shape[1] * buf.shape[2] * buf.shape[3]])
                    S.add("gpsimd",
                          (lambda e, nm=nm, buf=buf:
                           e.dma_start(out=dbg[nm], in_=buf[:].rearrange(
                               "p a b c -> p (a b c)"))),
                          reads=keys, name=f"tap_{nm}")

        def emit_qk1(h, qi, w=w, l=l):
            c2, hp = h // 2, h % 2
            rows = slice(64 * hp, 64 * hp + 64)
            kj_lo = max(0, 4 * w + qi - 4)
            kj_hi = 4 * w + qi
            ncol = 128 * (kj_hi - kj_lo + 1)
            kread = sorted({("kw", (kj // 4) % 2, c2)
                            for kj in range(kj_lo, kj_hi + 1)})
            def mmqk1(e, qi=qi, kj_lo=kj_lo, kj_hi=kj_hi, c2=c2, rows=rows):
                inst = None
                for i, kj in enumerate(range(kj_lo, kj_hi + 1)):
                    inst = e.matmul(
                        PB[:, 128 * i:128 * (i + 1)],
                        QW[rows, 0, c2, 128 * qi:128 * (qi + 1)],
                        KW[rows, (kj // 4) % 2, c2,
                           128 * (kj % 4):128 * (kj % 4) + 128],
                        start=True, stop=True)
                return inst
            S.add("tensor", mmqk1,
                  reads=[("qw", c2), ("pb", 0), ("pb", 1)] + kread,
                  writes=[("pb", 0), ("pb", 1)], name=f"qk1_{l}_{w}_{h}_{qi}")
            def redm(e, ncol=ncol, qi=qi):
                return e.reduce_max(MAXC[:, qi % 2:qi % 2 + 1],
                                    PB[:, 0:ncol], axis=AXX, negate=True)
            S.add("vector", redm, reads=[("pb", 0), ("pb", 1), ("mxc", qi % 2)],
                  writes=[("mxcd", qi % 2), ("pb", 0), ("pb", 1)],
                  name=f"rmx{l}_{w}_{h}_{qi}")

        def emit_tpm(h, qi, w=w, l=l):
            def tpm(e, qi=qi):
                return e.transpose(
                    PB[0:1, 3072 + 128 * qi:3072 + 128 * (qi + 1)],
                    MAXC[:, qi % 2:qi % 2 + 1], IDF[:])
            S.add("tensor", tpm, reads=[("mxcd", qi % 2), "const", ("pb", 6)],
                  writes=[("pb", 6), ("mxc", qi % 2)],
                  name=f"tpm{l}_{w}_{h}_{qi}")

        def emit_mev(h, w=w, l=l):
            msl = h % 2
            def mev(e, msl=msl):
                return e.activation(out=MSTG[:, msl, :], in_=PB[0:1, 3072:3584],
                                    func=AF.Copy)
            S.add("scalar", mev, reads=[("pb", 6), ("mst", msl)],
                  writes=[("mstd", msl), ("pb", 6)], name=f"mev{l}_{w}_{h}")
            if l == 0 and w == 0 and h <= 1 and tap("mst0"):
                nm = f"mst{h}"
                tapdram(nm, [1, 512])
                S.add("gpsimd",
                      (lambda e, msl=msl, nm=nm:
                       e.dma_start(out=dbg[nm], in_=MSTG[:, msl, :])),
                      reads=[("mstd", msl)], name=f"tap_{nm}")

        def emit_qk2(h, i_kj, w=w, l=l):
            c2, hp = h // 2, h % 2
            rows = slice(64 * hp, 64 * hp + 64)
            msl = h % 2
            kj = 4 * w - 4 + i_kj
            if kj < 0:
                return
            qi_lo = max(0, kj - 4 * w)
            qi_hi = min(3, kj - 4 * w + 4)
            M = 128 * (qi_hi - qi_lo + 1)
            qb = 2 + (i_kj % 2)
            def mmqk2(e, kj=kj, qi_lo=qi_lo, M=M, c2=c2, rows=rows,
                      qb=qb, msl=msl):
                e.matmul(bank(qb, 0, M),
                         KW[rows, (kj // 4) % 2, c2,
                            128 * (kj % 4):128 * (kj % 4) + 128],
                         QW[rows, 0, c2, 128 * qi_lo:128 * qi_lo + M],
                         start=True, stop=False)
                return e.matmul(bank(qb, 0, M), ONES1[:],
                                MSTG[:, msl, 128 * qi_lo:128 * qi_lo + M],
                                start=False, stop=True)
            S.add("tensor", mmqk2,
                  reads=[("qw", c2), ("kw", (kj // 4) % 2, c2),
                         ("mstd", msl), ("pb", qb)],
                  writes=[("pb", qb)], name=f"qk2_{l}_{w}_{h}_{kj}")
            psl = i_kj % 2
            def pexp(e, qb=qb, M=M, psl=psl):
                return e.activation(out=PSTG[:, psl, 0:M], in_=bank(qb, 0, M),
                                    func=AF.Exp)
            S.add("scalar", pexp, reads=[("pb", qb), ("pst", psl)],
                  writes=[("pstd", psl), ("pb", qb)],
                  name=f"pex{l}_{w}_{h}_{kj}")
            qi_d = kj - 4 * w
            qi_e = kj - 4 * w + 4
            if qi_lo <= qi_d <= qi_hi or qi_lo <= qi_e <= qi_hi:
                def tri(e, qi_d=qi_d, qi_e=qi_e, qi_lo=qi_lo, qi_hi=qi_hi,
                        psl=psl):
                    inst = None
                    if qi_lo <= qi_d <= qi_hi:
                        co = 128 * (qi_d - qi_lo)
                        inst = e.tensor_tensor(PSTG[:, psl, co:co + 128],
                                               PSTG[:, psl, co:co + 128],
                                               TRID[:], ALU.mult)
                    if qi_lo <= qi_e <= qi_hi:
                        co = 128 * (qi_e - qi_lo)
                        inst = e.tensor_tensor(PSTG[:, psl, co:co + 128],
                                               PSTG[:, psl, co:co + 128],
                                               TRIE[:], ALU.mult)
                    return inst
                S.add("vector", tri, reads=[("pstd", psl), "const"],
                      writes=[("pstd", psl)], name=f"tri{l}_{w}_{h}_{kj}")
            if l == 0 and w == 0 and h <= 1 and kj == 0 and tap("pst0"):
                nm = f"pst{h}"
                tapdram(nm, [128, 512])
                S.add("gpsimd",
                      (lambda e, psl=psl, M=M, nm=nm:
                       e.dma_start(out=dbg[nm][:, 0:M],
                                   in_=PSTG[:, psl, 0:M])),
                      reads=[("pstd", psl)], name=f"tap_{nm}")

        def emit_pv(h, i_kj, pvst, w=w, l=l):
            c2, hp = h // 2, h % 2
            kj = 4 * w - 4 + i_kj
            if kj < 0:
                return
            qi_lo = max(0, kj - 4 * w)
            qi_hi = min(3, kj - 4 * w + 4)
            psl = i_kj % 2
            def pv(e, kj=kj, qi_lo=qi_lo, qi_hi=qi_hi, psl=psl, h=h, hp=hp,
                   w=w, c2=c2, pv_started=pvst):
                vsl = (kj // 4) % 2
                vq = kj % 4
                inst = None
                if hp == 1 and not pv_started["d"]:
                    e.matmul(PB[:, 2048:2560], ZERB[:],
                             KW[:, vsl, c2, :], start=True, stop=False)
                    pv_started["d"] = True
                for qi in range(qi_lo, qi_hi + 1):
                    st = (hp == 0) and (not pv_started["d"])
                    pv_started["d"] = True
                    sp = (kj == 4 * w + 3 and qi == qi_hi)
                    co = 128 * (qi - qi_lo)
                    if hp == 0:
                        inst = e.matmul(
                            PB[0:65, 2048 + 128 * qi:2048 + 128 * (qi + 1)],
                            VW[:, vsl, vq, 65 * h:65 * h + 65],
                            PSTG[:, psl, co:co + 128], start=st, stop=sp)
                    else:
                        inst = e.matmul(
                            PB[0:1, 2048 + 128 * qi:2048 + 128 * (qi + 1)],
                            ONECB[:], PSTG[:, psl, co:co + 128],
                            start=False, stop=sp)
                        inst = e.matmul(
                            PB[64:128, 2048 + 128 * qi:2048 + 128 * (qi + 1)],
                            VW[:, vsl, vq, 65 * h:65 * h + 64],
                            PSTG[:, psl, co:co + 128], start=False, stop=sp,
                            tile_position=(0, 64))
                return inst
            S.add("tensor", pv,
                  reads=[("pstd", psl), ("vw", (kj // 4) % 2, kj % 4),
                         ("vw1", (kj // 4) % 2), "const", ("pb", 4)],
                  writes=[("pb", 4), ("pst", psl)],
                  name=f"pv{l}_{w}_{h}_{kj}")

        def emit_fin(h, w=w, l=l):
            c2, hp = h // 2, h % 2
            if l == 0 and w == 0 and h <= 1 and tap("pv0"):
                nm = f"pv{h}"
                tapdram(nm, [128, 512])
                def pvdump(e):
                    return e.tensor_copy(STT[:, 2, :], PB[:, 2048:2560])
                S.add("vector", pvdump, reads=[("pb", 4), ("stt", 2)],
                      writes=[("stt", 2), ("pvdmp",)], name=f"pvd{h}")
                S.add("gpsimd",
                      (lambda e, nm=nm: e.dma_start(out=dbg[nm], in_=STT[:, 2, :])),
                      reads=[("pvdmp",), ("stt", 2)], name=f"tap_{nm}")
            zrow = 64 if hp == 0 else 0
            def zi(e, zrow=zrow):
                e.activation(out=STT[zrow:zrow + 1, 2, :],
                             in_=PB[zrow:zrow + 1, 2048:2560], func=AF.Ln)
                return e.activation(out=STT[zrow:zrow + 1, 2, :],
                                    in_=STT[zrow:zrow + 1, 2, :], func=AF.Exp,
                                    scale=f32(-1.0))
            S.add("scalar", zi, reads=[("pb", 4), ("stt", 2)],
                  writes=[("sttz",), ("stt", 2)], name=f"zi{l}_{w}_{h}")
            def zb(e, zrow=zrow, hp=hp):
                tp = (0, 64) if hp else None
                return e.matmul(PB[64 * hp:64 * hp + 64, 2560:3072],
                                ONE1F[zrow:zrow + 1, 0:64],
                                STT[zrow:zrow + 1, 2, :],
                                start=True, stop=True, tile_position=tp)
            S.add("tensor", zb, reads=[("sttz",), "const", ("pb", 5)],
                  writes=[("pb", 5)], name=f"zb{l}_{w}_{h}")
            def oev(e, hp=hp, c2=c2, w=w):
                e.tensor_copy(STT[64 * hp:64 * hp + 64, 2, :],
                              PB[64 * hp:64 * hp + 64, 2048:2560])
                return e.tensor_tensor(
                    OFM[64 * hp:64 * hp + 64, c2, w * W512:(w + 1) * W512],
                    STT[64 * hp:64 * hp + 64, 2, :],
                    PB[64 * hp:64 * hp + 64, 2560:3072], ALU.mult)
            S.add("vector", oev,
                  reads=[("pb", 4), ("pb", 5), ("ofm", c2, w, hp), ("stt", 2)],
                  writes=[("ofm", c2, w, hp), ("pb", 4), ("pb", 5), ("stt", 2)],
                  name=f"oev{l}_{w}_{h}")

        pvst = [{"d": False} for _ in range(NH)]
        for qi in range(4):
            emit_qk1(0, qi)
        for qi in range(4):
            emit_tpm(0, qi)
        emit_mev(0)
        for h in range(NH):
            for i_kj in range(8):
                emit_qk2(h, i_kj)
                if h + 1 < NH and i_kj < 4:
                    emit_qk1(h + 1, i_kj)
                if i_kj >= 1:
                    emit_pv(h, i_kj - 1, pvst[h])
                if h + 1 < NH and i_kj >= 4:
                    emit_tpm(h + 1, i_kj - 4)
            emit_pv(h, 7, pvst[h])
            if h + 1 < NH:
                emit_mev(h + 1)
            emit_fin(h)
        if w > 0:
            emit_wo(w - 1)
            emit_il(w - 1)
    emit_wo(NW - 1)
    emit_il(NW - 1)

    if tap(f"ofm{l}"):
        tapdram(f"ofm{l}", [128, 4, T])
        S.add("gpsimd", lambda e: e.dma_start(out=dbg[f"ofm{l}"], in_=OFM[:]),
              reads=[("ofm", c2, w, hp) for c2 in range(4) for w in range(NW)
                     for hp in range(2)], name=f"tap_ofm{l}")

    if tap(f"ywo{l}"):
        tapdram(f"ywo{l}", [128, C * T])
        S.add("gpsimd", lambda e: e.dma_start(out=dbg[f"ywo{l}"], in_=XRt[:]),
              reads=[("wo_done", n, w) for n in range(C) for w in range(NW)],
              name=f"tap_ywo{l}")

    if tap(f"lbf{l}"):
        tapdram(f"lbf{l}", [128, C * T])
        S.add("gpsimd", lambda e: e.dma_start(out=dbg[f"lbf{l}"], in_=L["LBF"][:]),
              reads=[("lbf", c, w) for c in range(C) for w in range(NW)],
              name=f"tap_lbf{l}")

    S.virt(writes=(["rot"] + [("qw", c2) for c2 in range(4)] +
                   [("kw", s, c2) for s in range(2) for c2 in range(4)] +
                   [("vw", s, v) for s in range(2) for v in range(4)] +
                   [("vw1", s) for s in range(2)] +
                   [("mres", j, tq) for j in range(NK2A) for tq in range(NW)] +
                   [("mresd", j, tq) for j in range(NK2A) for tq in range(NW)] +
                   [("mrbs", x) for x in range(NW)] +
                   [("mrbsd", x) for x in range(NW)] +
                   [("mrl", x) for x in range(2)] +
                   [("wob", x) for x in range(2)] +
                   [("wobd", x) for x in range(2)] +
                   [("rotw", p, ti) for p in range(2) for ti in range(4)]),
           name=f"retag_attn2ffn{l}")

    for j in range(K2N):
        for tq in range(NW):
            for part in range(2):
                n = 2 * j + part
                wsl = n % 3
                fkey = ("wp", wsl)
                dkey = ("wpd", wsl)
                if tq == 0:
                    def ldw1(e, n=n, l=l, wsl=wsl):
                        return e.dma_start(out=WPB[:, wsl, :],
                                           in_=d_w1[l, :, n, :, :])
                    S.add("gpsimd", ldw1, reads=[fkey], writes=[dkey],
                          name=f"ldw1{l}_{n}")
                pbk = part * 2 + tq % 2
                def mm1(e, n=n, tq=tq, pbk=pbk, wsl=wsl):
                    for c in range(C):
                        inst = e.matmul(bank(pbk),
                                        WPB[:, wsl, 128 * c:128 * (c + 1)],
                                        LBFc(c, tq * W512, W512),
                                        start=(c == 0), stop=(c == C - 1))
                    return inst
                S.add("tensor", mm1,
                      reads=[dkey, ("pb", pbk)] +
                            [("lbf", c, tq) for c in range(C)],
                      writes=[("pb", pbk)] + ([fkey] if tq == NW - 1 else []),
                      name=f"mm1{l}_{n}_{tq}")
            pa = tq % 2
            pg = 2 + tq % 2
            def gact(e, j=j, pg=pg):
                return e.activation(out=SQB[:, 0, :], in_=bank(pg), func=AF.Gelu,
                                    bias=B1C[:, K2N + j:K2N + j + 1])
            S.add("scalar", gact, reads=[("pb", pg), ("sqb", 0), ("lsm",)],
                  writes=[("sqbd", 0), ("pb", pg)], name=f"ga{l}_{j}_{tq}")
            if j < NK2A:
                mdst = (lambda j=j, tq=tq: MRES[:, j, tq * W512:(tq + 1) * W512])
                mkey, mdkey = ("mres", j, tq), ("mresd", j, tq)
            else:
                mdst = (lambda tq=tq: MRB[:, tq, :])
                mkey, mdkey = ("mrbs", tq), ("mrbsd", tq)
            def mmul(e, j=j, pa=pa, mdst=mdst):
                return e.scalar_tensor_tensor(mdst(), bank(pa),
                                              B1C[:, j:j + 1], SQB[:, 0, :],
                                              ALU.add, ALU.mult)
            S.add("vector", mmul,
                  reads=[("pb", pa), ("sqbd", 0), ("lsm",), mkey],
                  writes=[("pb", pa), ("sqb", 0), mdkey, mkey],
                  name=f"mmul{l}_{j}_{tq}")
            if j >= NK2A:
                def msp(e, j=j, tq=tq, mdst=mdst):
                    return e.dma_start(
                        out=d_m[:, j - NK2A, tq * W512:(tq + 1) * W512],
                        in_=mdst())
                S.add("sync", msp, reads=[mdkey],
                      writes=[("msp", j - NK2A, tq), mkey],
                      name=f"msp{l}_{j}_{tq}")

    WPALL = ([("wp", x) for x in range(3)] + [("wpd", x) for x in range(3)] +
             [("wpa", x) for x in range(2)] + [("wpad", x) for x in range(2)])
    S.virt(writes=WPALL, name=f"retag_w1w2a{l}")
    for n in range(C):
        hh = n % 2
        def ldw2(e, n=n, l=l, hh=hh):
            return e.dma_start(
                out=WPB[:].rearrange(
                    "p s n -> p (s n)")[:, hh * 1536:hh * 1536 + 128 * NK2A],
                in_=d_w2[l, :, n, 0:NK2A, :])
        S.add("gpsimd", ldw2, reads=[("wpa", hh)], writes=[("wpad", hh)],
              name=f"ldw2{l}_{n}")
        for tq in range(NW):
            pbk = (n * NW + tq) % 2
            def mm2(e, n=n, tq=tq, pbk=pbk, hh=hh):
                wflat = WPB[:].rearrange("p s n -> p (s n)")
                for j in range(NK2A):
                    inst = e.matmul(
                        bank(pbk),
                        wflat[:, hh * 1536 + 128 * j:hh * 1536 + 128 * (j + 1)],
                        MRES[:, j, tq * W512:(tq + 1) * W512],
                        start=(j == 0), stop=(j == NK2A - 1))
                return inst
            S.add("tensor", mm2,
                  reads=[("wpad", hh), ("pb", pbk)] +
                        [("mresd", j, tq) for j in range(NK2A)],
                  writes=[("pb", pbk)] +
                         ([("wpa", hh)] if tq == NW - 1 else []),
                  name=f"mm2{l}_{n}_{tq}")
            def y2a(e, n=n, tq=tq, pbk=pbk):
                return e.scalar_tensor_tensor(
                    XRc(n, tq * W512, W512), bank(pbk),
                    GBC[:, 2 * C + n:2 * C + n + 1],
                    XRc(n, tq * W512, W512), ALU.add, ALU.add)
            S.add("vector", y2a, reads=[("pb", pbk), ("xr", tq), ("lsm",)],
                  writes=[("xr", tq), ("pb", pbk), ("w2p", n, tq)],
                  name=f"y2a{l}_{n}_{tq}")
    S.virt(writes=WPALL, name=f"retag_w2ab{l}")
    for npair in range(C // 2):
        n0 = 2 * npair
        def ldw2b(e, n0=n0, l=l):
            return e.dma_start(
                out=WPB[:].rearrange("p s n -> p (s n)")[:, 0:128 * 2 * NK2B],
                in_=d_w2[l, :, n0:n0 + 2, NK2A:K2N, :])
        S.add("gpsimd", ldw2b, reads=[("wpd", 0), ("wpd", 1), ("wpd", 2)],
              writes=[("wpd", 0), ("wpd", 1), ("wpd", 2)],
              name=f"ldw2B{l}_{npair}")
        for tqp in range(2):
            pp = npair * 2 + tqp
            bks = [(pp * 4 + i) % 7 for i in range(4)]
            for j in range(NK2A, K2N):
                sl = (j - NK2A) % 2
                def mrl(e, j=j, sl=sl, tqp=tqp):
                    dst = MRB[:, 4 + 2 * sl:6 + 2 * sl, :].rearrange(
                        "p a b -> p (a b)")
                    return e.dma_start(
                        out=dst,
                        in_=d_m[:, j - NK2A, tqp * 1024:(tqp + 1) * 1024])
                S.add("sync", mrl,
                      reads=[("msp", j - NK2A, 2 * tqp),
                             ("msp", j - NK2A, 2 * tqp + 1)],
                      writes=[("mrl", sl)], name=f"mrl{l}_{pp}_{j}")
                def mm2b(e, j=j, sl=sl, bks=bks):
                    wflat = WPB[:].rearrange("p s n -> p (s n)")
                    ring = MRB[:, 4 + 2 * sl:6 + 2 * sl, :].rearrange(
                        "p a b -> p (a b)")
                    inst = None
                    for dn in range(2):
                        wcol = 128 * (dn * NK2B + (j - NK2A))
                        for dt in range(2):
                            inst = e.matmul(
                                bank(bks[dn * 2 + dt]),
                                wflat[:, wcol:wcol + 128],
                                ring[:, dt * W512:(dt + 1) * W512],
                                start=(j == NK2A), stop=(j == K2N - 1))
                    return inst
                S.add("tensor", mm2b,
                      reads=[("wpd", 0), ("wpd", 1), ("wpd", 2),
                             ("mrl", sl)] + [("pb", b) for b in bks],
                      writes=[("pb", b) for b in bks] + [("mrl", sl)],
                      name=f"mm2b{l}_{pp}_{j}")
            for dn in range(2):
                n = n0 + dn
                for dt in range(2):
                    tq = tqp * 2 + dt
                    b = bks[dn * 2 + dt]
                    def y2c(e, n=n, tq=tq, b=b):
                        return e.tensor_tensor(XRc(n, tq * W512, W512),
                                               XRc(n, tq * W512, W512),
                                               bank(b), ALU.add)
                    S.add("vector", y2c,
                          reads=[("pb", b), ("xr", tq), ("w2p", n, tq)],
                          writes=[("xr", tq), ("pb", b), ("w2d", n, tq)],
                          name=f"y2c{l}_{n}_{tq}")
                    if l == 0:
                        def ymir3(e, n=n, tq=tq):
                            return e.copy(out=ABFc(n, tq * W512 + 4, W512),
                                          in_=XRc(n, tq * W512, W512))
                        S.add("scalar", ymir3,
                              reads=[("w2d", n, tq), ("xr", tq)],
                              writes=[YK(n, tq)], name=f"ym3{l}_{n}_{tq}")

    if tap(f"y{l + 1}"):
        tapdram(f"y{l + 1}", [128, C * T])
        S.add("gpsimd", lambda e: e.dma_start(out=dbg[f"y{l + 1}"], in_=XRt[:]),
              reads=[("w2d", n, tq) for n in range(C) for tq in range(NW)],
              name=f"tap_y{l + 1}")


def kernel(**inputs):
    cst = _constants()
    w = _prep_weights(inputs)
    x = np.asarray(inputs["L0_attn_query"], np.float32)
    nc, _ = build_program()
    base = {}
    for k in WEIGHT_NAMES:
        base[k] = w[k]
    for k in CONST_NAMES:
        base[k] = np.asarray(cst[k])
    in_maps = []
    for b in range(8):
        m = dict(base)
        m["x"] = np.ascontiguousarray(x[b])
        in_maps.append(m)
    res = run_bass_kernel_spmd(nc, in_maps, core_ids=list(range(8)))
    return np.stack([res.results[b]["out"] for b in range(8)], 0)
